# version 02
# baseline (speedup 1.0000x reference)
import os
import numpy as np
import ml_dtypes
import concourse.bass as bass
import concourse.mybir as mybir
from concourse.bass_utils import run_bass_kernel_spmd

F32 = mybir.dt.float32
BF16 = mybir.dt.bfloat16
F16 = mybir.dt.float16
I32 = mybir.dt.int32
AF = mybir.ActivationFunctionType
ALU = mybir.AluOpType
AXX = mybir.AxisListType.X

NCORES = 8
E = 2048
ECH = E // 128
H = 16
D = 128
HC = H // NCORES
ICR = 5632 // NCORES
ICP = 768
GU = 2 * ICP
NEG = -1e30
INV_SQRT_D = 1.0 / np.sqrt(128.0)
PARANOID = bool(int(os.environ.get("MK_PARANOID", "0")))

BF = ml_dtypes.bfloat16


def _hilo(a):
    hi = a.astype(BF)
    lo = (a - hi.astype(np.float32)).astype(BF)
    return hi, lo


class PhysSem:
    def __init__(self, es, nc, name):
        self.h = es.enter_context(nc.semaphore(name))
        self.n = 0


class Sem:

    def __init__(self, phys):
        self.p = phys
        self.base = phys.n
        self.waits = []

    @property
    def n(self):
        return self.p.n - self.base

    def inc(self, instr, k=1):
        instr.then_inc(self.p.h, k)
        self.p.n += k
        return self.n

    def wait(self, eng, value):
        assert value >= 0
        self.waits.append(value)
        eng.wait_ge(self.p.h, self.base + value)


SEM_ALIAS = {}
for _nm, _ph in {
    "P1A": "E00", "P1B": "E01", "P1C": "E02", "P1T": "E03", "P1S": "E04",
    "P2W": "E00", "P2M": "E01", "P2E": "E02", "P2V": "E03", "P2T": "E04",
    "P2VE": "E05", "P2N1": "E06", "P2N2": "E07", "P2N3": "E08", "P2N4": "E09",
    "P2N5": "E10", "D_sc": "E11", "D_bias": "E12", "D_exp": "E13", "D_spl": "E14",
    "D_pv": "E15", "D_brow": "E16", "D_bc": "E17", "D_btile": "E18",
    "D_rden": "E19", "D_rb": "E20", "D_qtev": "E21", "P2B": "E22",
    "P3M": "E23", "P3E": "E24", "P3W": "E25",
    "P4X": "E03", "P4B": "E04", "P4W": "E05", "P4G": "E06", "P4S": "E07",
    "P4H": "E08", "P4D2": "E09", "P4Y": "E10", "P4C1": "E11", "P4C2": "E12",
    "P4C3": "E13", "P4C4": "E14", "P4C5": "E15",
    "P5R1": "E16", "P5R2": "E17", "P5R3": "E18", "P5R4": "E19", "P5R5": "E20",
    "P5T": "E21", "P5S": "E22", "P5BO": "E06",
    "P6W": "E00", "P6M": "E01", "P6E": "E02", "P6Q": "E03",
    "P7KT": "E04", "P7KE": "E05", "P7VE": "E07", "P7KN1": "E08", "P7KN2": "E09",
    "P7KN3": "E10", "P7KM": "E11", "P7KSQ": "E12", "P7QP": "E13", "P7QS": "E14",
    "P7QN1": "E15", "P7QN2": "E16", "P7QN3": "E17", "P7QM": "E18", "P7QSQ": "E19",
    "P7VM": "E20", "P7TM": "E21", "P7VC": "E22",
    "A_sc": "E23", "A_exp": "E24", "A_spl": "E25", "A_pv": "E26",
    "A_rden": "E27", "A_rb": "E28", "A_qtev": "E29",
    "P8W": "E00", "P8M": "E01", "P8E": "E02", "P8O": "E05",
    "P9X": "E03", "P9W": "E04", "P9M": "E05", "P9E": "E06", "P9B": "E07",
    "L1X": "D6", "L1S": "D7",
    "LW0": "D0", "LW1": "D1", "LSL": "D2", "LAQ": "D3", "LOS": "D4", "LCT": "D5",
    "LOH": "D6", "LP3": "D7",
    "LDQ": "D3", "LY0": "D6", "LY1": "D7", "LL0": "D8", "LL1": "D9",
    "LWT0": "D4", "LWT1": "D10",
    "L5": "D6x", "L5S": "D7x", "L5P": "D8x",
    "LQ0": "D6", "LQ1": "D7", "L6KV": "D8",
    "L7K": "D6y", "L7Q": "D7y", "L7S": "D8y",
    "L8OH": "D6", "L8X": "D7", "L8S": "D8",
    "L9A": "D6z", "L9W": "D7z", "L9X": "D8z", "L9O": "D9",
}.items():
    SEM_ALIAS[_nm] = _ph


class MK:
    def __init__(self, S):
        assert S % 1024 == 0
        self.S = S
        self.TOK = S // NCORES
        self.NT = self.TOK // 128
        self.QT0 = S // 512
        self.QT1 = max(1, self.TOK // 512)
        self.QW1 = min(512, self.TOK)
        self.KC = S // 128
        self.SLAB = 256
        self.NSLAB = S // self.SLAB
        self.nc = bass.Bass("TRN2", target_bir_lowering=False,
                            dynamic_dma_scratch_size=8192,
                            detect_race_conditions=False)
        import contextlib
        self.es = contextlib.ExitStack()
        self.block = None
        self.sems = {}
        self.phys = {}
        self.g0 = 0
        self.qt0 = 0
        self.g1 = 0
        self._prev_ost = 0

    def sem(self, name):
        if name not in self.sems:
            ph = SEM_ALIAS.get(name, name)
            if ph not in self.phys:
                self.phys[ph] = PhysSem(self.es, self.nc, ph)
            self.sems[name] = Sem(self.phys[ph])
        return self.sems[name]

    def declare(self):
        nc, S, TOK = self.nc, self.S, self.TOK
        di = lambda n, sh, dt: nc.dram_tensor(n, sh, dt, kind="ExternalInput").ap()
        self.i_x = di("x_c", [TOK, E], F32)
        self.i_wq = di("wq_t", [E, HC * 128], F32)
        self.i_wk = di("wk_t", [E, HC * 128], F32)
        self.i_wv = di("wv_t", [E, HC * 128], F32)
        self.i_bqkv = di("bqkv", [128, HC * 3], F32)
        self.i_wo = di("wo_t", [HC * 128, E], F32)
        self.i_bo = di("bo", [1, E], F32)
        self.i_gup = di("gu_rep", [E, GU], I32)
        self.i_gus = di("gu_sc", [ECH, GU], F16)
        self.i_dwp = di("dw_rep", [ICP, E], I32)
        self.i_dws = di("dw_scsel", [2 * (ICP // 128), E], F16)
        self.i_dsel = di("dsel", [2, 128], F32)
        self.i_wa = di("w_attn", [E, 2304], F32)
        self.i_wu = di("w_out", [E, E], F32)
        self.i_wl1 = di("w_l1t", [E, E], F32)
        self.i_bl1 = di("b_l1", [1, E], F32)
        self.i_x2 = di("x_c2", [TOK, E], F32)
        self.i_idf = di("identf", [128, 128], F32)
        self.i_idb = di("identb", [128, 128], BF16)
        self.i_msk = di("masks", [128, 4 * 512], BF16)
        self.i_alq = di("al_q", [HC, S], F32)
        self.i_alk = di("al_k", [128, HC * (S // 128)], F32)
        self.i_shft = di("shifts", [128, 1], I32)
        self.i_nmask = di("nmask", [128, 1], I32)
        self.i_ninv = di("ninv", [128, 1], F32)
        self.i_cos = di("cos_t", [TOK, 128], F32)
        self.i_sin = di("sin_t", [TOK, 128], F32)

        self.o_y = nc.dram_tensor("y_c", [TOK, E], F32, kind="ExternalOutput").ap()

        dr = lambda n, sh, dt: nc.dram_tensor(n, sh, dt)
        self.d_xh_in = dr("xh_in", [E, TOK], BF16)
        self.d_xl_in = dr("xl_in", [E, TOK], BF16)
        self.d_xh = dr("xh_all", [NCORES * E, TOK], BF16)
        self.d_xl = dr("xl_all", [NCORES * E, TOK], BF16)
        self.d_ohh = dr("ohT_hi", [HC * 128, S], BF16)
        self.d_ohl = dr("ohT_lo", [HC * 128, S], BF16)
        self.NCH = S // 1024
        self.d_part = [dr(f"part{ch}", [1024, 2 * E], F32) for ch in range(self.NCH)]
        self.d_rs = [dr(f"rsout{ch}", [128, 2 * E], F32) for ch in range(self.NCH)]
        self.d_q1 = dr("q1_spill", [TOK, E], F32)
        self.d_k1in = dr("k1_in", [TOK, D], F32)
        self.d_v1in = dr("v1_in", [TOK, D], F32)
        self.d_k1 = dr("k1_all", [S, D], F32)
        self.d_v1 = dr("v1_all", [S, D], F32)
        self.d_x1h = dr("xn1T_hi", [E, TOK], BF16)
        self.d_x1l = dr("xn1T_lo", [E, TOK], BF16)
        self.d_x1 = dr("x1_spill", [TOK, E], F32)
        self.d_x1p = dr("x1p_spill", [TOK, E], F32)
        self.d_o1h = dr("o1T_hi", [E, TOK], BF16)
        self.d_o1l = dr("o1T_lo", [E, TOK], BF16)
        self.d_wt = dr("wt_spill", [E, E], BF16)
        self.d_bar = dr("bar_scratch", [128, 1], F32)

    def alloc(self):
        nc, es, S, TOK = self.nc, self.es, self.S, self.TOK
        sb = lambda n, sh, dt: es.enter_context(nc.sbuf_tensor(n, sh, dt))
        self.AB = sb("AB", [128, 84992], BF16)
        self.AF = sb("AF", [128, 9216], F32)
        self.c_i32a = sb("c_i32a", [128, 512], I32)
        self.c_i32b = sb("c_i32b", [128, 512], I32)
        self.c_idf = sb("c_idf", [128, 128], F32)
        self.c_idb = sb("c_idb", [128, 128], BF16)
        self.c_shft = sb("c_shft", [128, 1], I32)
        self.c_nmask = sb("c_nmask", [128, 1], I32)
        self.c_ninv = sb("c_ninv", [128, 1], F32)
        self.c_zero = sb("c_zero", [128, 1], F32)
        self.c_onef = sb("c_onef", [1, 128], F32)
        self.c_onecb = sb("c_onecb", [128, 1], BF16)
        self.c_onecf = sb("c_onecf", [128, 1], F32)
        self.c_dsel = sb("c_dsel", [2, 128], F32)
        self.c_bqkv = sb("c_bqkv", [128, HC * 3], F32)
        self.c_sc = sb("c_sc", [1, 32], F32)
        self.c_col = sb("c_col", [128, 8], F32)
        self.c_stats = sb("c_stats", [128, 26], F32)
        self.c_alk = sb("c_alk", [128, HC * (S // 128)], F32)
        self.c_bar = sb("c_bar", [128, 4], F32)
        self.c_f16 = sb("c_f16", [2, 2048], F16)
        self.pb = [es.enter_context(nc.psum_tensor(f"pb{i}", [128, 512], F32))
                   for i in range(7)]
        self.pt_b = es.enter_context(nc.psum_tensor("pt_b", [128, 512], BF16))

    def ab(self, off, cols, part0=None, nparts=None):
        assert off + cols <= 84992, (off, cols)
        if part0 is not None:
            return self.AB[part0:part0 + nparts, off:off + cols]
        return self.AB[:, off:off + cols]

    def af(self, off, cols, part0=None, nparts=None):
        assert off + cols <= 9216, (off, cols)
        if part0 is not None:
            return self.AF[part0:part0 + nparts, off:off + cols]
        return self.AF[:, off:off + cols]

    def barrier(self):
        BAR, LD, CC = self.sem("BAR"), self.sem("LD"), self.sem("CC")
        base = BAR.n
        ccn = CC.n
        r1 = base + 19
        tgt = base + 20

        BI = self.sem("BARINIT")

        @self.block.vector
        def _(vector):
            BAR.inc(vector.tensor_copy(self.c_bar[:, 0:1], self.c_bar[:, 0:1]), 1)

        @self.block.scalar
        def _(scalar):
            BI.wait(scalar, 1)
            BAR.inc(scalar.activation(out=self.c_bar[:, 1:2], in_=self.c_bar[:, 1:2],
                                      func=AF.Copy, bias=0.0), 1)

        @self.block.gpsimd
        def _(gpsimd):
            BI.wait(gpsimd, 1)
            BAR.inc(gpsimd.tensor_copy(self.c_bar[:, 2:3], self.c_bar[:, 2:3]), 1)

        @self.block.sync
        def _(sync):
            BI.wait(sync, 1)
            BAR.inc(sync.dma_start(out=self.d_bar.ap(), in_=self.c_bar[:, 3:4]), 16)

        @self.block.tensor
        def _(tensor):
            BAR.wait(tensor, r1)
            BAR.inc(tensor.matmul(self.pb[0][0:1, 0:1], self.c_onecb[:], self.c_onecb[:],
                                  start=True, stop=True), 1)

        dma_totals = [(sem, sem.n) for name, sem in self.sems.items()
                      if name.startswith("L") and sem.n > 0]
        for engname in ("sync", "gpsimd", "tensor", "vector", "scalar"):
            def mk(engname=engname):
                def _(eng):
                    BAR.wait(eng, tgt)
                    for sem, n in dma_totals:
                        sem.wait(eng, n)
                    if ccn:
                        CC.wait(eng, ccn)
                return _
            getattr(self.block, engname)(mk())

    def mm3(self, tensor, out, l_hi, l_lo, r_hi, r_lo, start, stop):
        tensor.matmul(out, l_hi, r_hi, start=start, stop=False)
        tensor.matmul(out, l_lo, r_hi, start=False, stop=False)
        return tensor.matmul(out, l_hi, r_lo, start=False, stop=stop)

    def split_hilo(self, vector, src, hi, lo):
        vector.tensor_copy(hi, src)
        vector.tensor_tensor(lo, src, hi, op=ALU.subtract)



def host_prep(core, S, ins):
    TOK = S // NCORES
    t0 = core * TOK
    f32 = np.float32
    out = {}
    x = np.asarray(ins["L0_attn_x"], dtype=f32)
    out["x_c"] = np.ascontiguousarray(x[0, t0:t0 + TOK, :])
    h0 = core * HC
    for nm, w in (("wq_t", "L0_attn_Wq"), ("wk_t", "L0_attn_Wk"), ("wv_t", "L0_attn_Wv")):
        out[nm] = np.ascontiguousarray(np.asarray(ins[w], f32).T[:, h0 * 128:(h0 + HC) * 128])
    bq = np.stack([np.asarray(ins["L0_attn_bq"], f32), np.asarray(ins["L0_attn_bk"], f32),
                   np.asarray(ins["L0_attn_bv"], f32)], axis=1)
    bqkv = np.zeros((128, HC * 3), f32)
    for h in range(HC):
        for j in range(3):
            bqkv[:, h * 3 + j] = bq[(h0 + h) * 128:(h0 + h + 1) * 128, j]
    out["bqkv"] = bqkv
    out["wo_t"] = np.ascontiguousarray(np.asarray(ins["L0_attn_Wo"], f32).T[h0 * 128:(h0 + HC) * 128, :])
    out["bo"] = np.asarray(ins["L0_attn_bo"], f32).reshape(1, E)

    gup = np.asarray(ins["L0_ffn_gate_up_packed"])
    gus = np.asarray(ins["L0_ffn_gate_up_scales"])
    I = gup.shape[0] // 2
    g_lo, g_hi = core * ICR, (core + 1) * ICR
    rows = np.concatenate([gup[g_lo:g_hi], np.zeros((ICP - ICR, E // 8), np.int32),
                           gup[I + g_lo:I + g_hi], np.zeros((ICP - ICR, E // 8), np.int32)], axis=0)
    out["gu_rep"] = np.ascontiguousarray(np.repeat(rows.T, 8, axis=0)).astype(np.int32)
    sc = np.concatenate([gus[:, g_lo:g_hi], np.zeros((ECH, ICP - ICR), gus.dtype),
                         gus[:, I + g_lo:I + g_hi], np.zeros((ECH, ICP - ICR), gus.dtype)], axis=1)
    out["gu_sc"] = np.ascontiguousarray(sc)
    dwp = np.asarray(ins["L0_ffn_down_packed"])
    dsc = np.asarray(ins["L0_ffn_down_scales"])
    w_lo8, w_hi8 = g_lo // 8, g_hi // 8
    drows = np.concatenate([dwp[:, w_lo8:w_hi8].T, np.zeros((ICP // 8 - ICR // 8, E), np.int32)], axis=0)
    out["dw_rep"] = np.ascontiguousarray(np.repeat(drows, 8, axis=0)).astype(np.int32)
    sel = np.zeros((2, 128), f32)
    sel[0, :64] = 1.0
    sel[1, 64:] = 1.0
    out["dsel"] = sel
    nic = ICP // 128
    scsel = np.zeros((2 * nic, E), dsc.dtype)
    for ic in range(nic):
        scsel[2 * ic + 0] = dsc[min((g_lo + ic * 128) // 128, dsc.shape[0] - 1)]
        scsel[2 * ic + 1] = dsc[min((g_lo + ic * 128 + 64) // 128, dsc.shape[0] - 1)]
    out["dw_scsel"] = np.ascontiguousarray(scsel)

    out["w_attn"] = np.ascontiguousarray(np.asarray(ins["L1_attn_w_attn"], f32))
    out["w_out"] = np.ascontiguousarray(np.asarray(ins["L1_attn_w_out"], f32))
    out["w_l1t"] = np.ascontiguousarray(np.asarray(ins["L1_ffn_weight"], f32).T)
    out["b_l1"] = np.asarray(ins["L1_ffn_bias"], f32).reshape(1, E)

    out["identf"] = np.eye(128, dtype=f32)
    out["identb"] = np.eye(128, dtype=f32).astype(BF)
    p = np.arange(128)[:, None]
    j = np.arange(512)[None, :]
    masks = np.concatenate([np.where(p + 128 * m <= j, 0.0, NEG).astype(f32) for m in range(4)], axis=1)
    out["masks"] = masks.astype(BF)
    slopes = np.array([2.0 ** (-0.5 * (i + 1)) for i in range(H)], dtype=np.float64)
    alq = np.stack([(-slopes[h0 + h] * np.arange(S)).astype(f32) for h in range(HC)], axis=0)
    out["al_q"] = np.ascontiguousarray(alq)
    alk = np.zeros((128, HC * (S // 128)), f32)
    for h in range(HC):
        for kc in range(S // 128):
            alk[:, h * (S // 128) + kc] = (slopes[h0 + h] * (kc * 128 + np.arange(128))).astype(f32)
    out["al_k"] = alk
    out["shifts"] = (4 * (np.arange(128) % 8)).astype(np.int32).reshape(128, 1)
    sh = 4 * (np.arange(128) % 8)
    out["nmask"] = (15 << sh).astype(np.int32).reshape(128, 1)
    out["ninv"] = (2.0 ** (-sh)).astype(f32).reshape(128, 1)
    idx = owned_rows(core, S)
    out["x_c2"] = np.ascontiguousarray(x[0, idx, :])
    hd = 128
    pos = 10000.0 ** ((-2.0 * np.arange(0, hd, 2, dtype=f32) - 1.0) / hd)
    t_idx = idx.astype(f32)[:, None]
    ang = t_idx * pos[None, :]
    rot = np.concatenate([ang, ang], axis=1)
    out["cos_t"] = np.cos(rot).astype(f32)
    out["sin_t"] = np.sin(rot).astype(f32)
    return out


def owned_rows(core, S):
    nch = S // 1024
    return np.concatenate([np.arange(1024 * ch + 128 * core, 1024 * ch + 128 * core + 128)
                           for ch in range(nch)])


def ph0(self):
    nc, LD = self.nc, self.sem("LD")
    ld_marks = {}

    @self.block.sync
    def _(sync):
        for dst, src in [(self.c_idf[:], self.i_idf), (self.c_idb[:], self.i_idb),
                         (self.c_nmask[:], self.i_nmask), (self.c_ninv[:], self.i_ninv),
                         (self.c_dsel[:], self.i_dsel),
                         (self.c_bqkv[:], self.i_bqkv), (self.c_alk[:], self.i_alk)]:
            LD.inc(sync.dma_start(out=dst, in_=src), 16)
        ld_marks["done"] = LD.n

    @self.block.vector
    def _(vector):
        BI = self.sem("BARINIT")
        vector.memset(self.c_zero[:], 0.0)
        vector.memset(self.c_onef[:], 1.0)
        vector.memset(self.c_onecb[:], 1.0)
        vector.memset(self.c_onecf[:], 1.0)
        BI.inc(vector.memset(self.c_bar[:], 0.0))


def ph1(self):
    nc, S, TOK, NT = self.nc, self.S, self.TOK, self.NT
    CC = self.sem("CC")
    L1X, L1S = self.sem("L1X"), self.sem("L1S")
    A = self.sem("P1A"); B = self.sem("P1B"); C = self.sem("P1C")
    T = self.sem("P1T"); SP = self.sem("P1S")
    xtile = self.af(0, 2048)
    xntile = self.af(2048, 2048)
    o_xh, o_xl = 0, ECH * TOK
    b0 = dict(A=A.n, B=B.n, C=C.n, T=T.n, SP=SP.n)
    ldm = {}

    @self.block.sync
    def _(sync):
        for tt in range(NT):
            if tt > 0:
                C.wait(sync, b0["C"] + tt)
            L1X.inc(sync.dma_start(out=xtile, in_=self.i_x[tt * 128:(tt + 1) * 128, :]), 16)
            ldm[("x", tt)] = L1X.n
            for ec in range(ECH):
                g = tt * ECH + ec
                SP.wait(sync, b0["SP"] + g + 1)
                L1S.inc(sync.dma_start(
                    out=self.d_xh_in.ap()[128 * ec:128 * (ec + 1), 128 * tt:128 * (tt + 1)],
                    in_=self.ab(o_xh + ec * TOK + 128 * tt, 128)), 16)
                L1S.inc(sync.dma_start(
                    out=self.d_xl_in.ap()[128 * ec:128 * (ec + 1), 128 * tt:128 * (tt + 1)],
                    in_=self.ab(o_xl + ec * TOK + 128 * tt, 128)), 16)
        ldm["end"] = L1S.n

    @self.block.tensor
    def _(tensor):
        for tt in range(NT):
            C.wait(tensor, b0["C"] + tt + 1)
            for ec in range(ECH):
                g = tt * ECH + ec
                if g >= 2:
                    SP.wait(tensor, b0["SP"] + g - 1)
                T.inc(tensor.transpose(self.pb[g % 2][:, 0:128],
                                       xntile[:, 128 * ec:128 * (ec + 1)], self.c_idf[:]))

    @self.block.scalar
    def _(scalar):
        for tt in range(NT):
            A.wait(scalar, b0["A"] + tt + 1)
            B.inc(scalar.activation(out=self.c_col[:, 1:2], in_=self.c_col[:, 0:1],
                                    func=AF.Sqrt, bias=self.c_zero[:, 0:1]))

    @self.block.vector
    def _(vector):
        HZ = self.sem("HZV")
        for tt in range(NT):
            L1X.wait(vector, ldm[("x", tt)])
            for k in range(4):
                HZ.inc(vector.bn_stats(self.c_stats[:, 6 * k:6 * k + 6],
                                       xtile[:, 512 * k:512 * (k + 1)]))
            HZ.wait(vector, HZ.n)
            HZ.inc(vector.bn_aggr(self.c_stats[:, 24:26], self.c_stats[:, 0:24]))
            HZ.wait(vector, HZ.n)
            A.inc(vector.tensor_scalar(self.c_col[:, 0:1], self.c_stats[:, 25:26], 1e-5, None,
                                       op0=ALU.add))
            B.wait(vector, b0["B"] + tt + 1)
            HZ.inc(vector.reciprocal(self.c_col[:, 2:3], self.c_col[:, 1:2]))
            HZ.wait(vector, HZ.n)
            vector.tensor_scalar(xntile, xtile, self.c_stats[:, 24:25], None, op0=ALU.subtract)
            C.inc(vector.tensor_scalar(xntile, xntile, self.c_col[:, 2:3], None, op0=ALU.mult))
            for ec in range(ECH):
                g = tt * ECH + ec
                T.wait(vector, b0["T"] + g + 1)
                hi_ap = self.ab(o_xh + ec * TOK + 128 * tt, 128)
                lo_ap = self.ab(o_xl + ec * TOK + 128 * tt, 128)
                vector.tensor_copy(hi_ap, self.pb[g % 2][:, 0:128])
                SP.inc(vector.tensor_tensor(lo_ap, self.pb[g % 2][:, 0:128],
                                            hi_ap, op=ALU.subtract))

    @self.block.gpsimd
    def _(gpsimd):
        L1S.wait(gpsimd, ldm["end"])
        CC.inc(gpsimd.collective_compute(
            "AllGather", ALU.bypass, replica_groups=[list(range(NCORES))],
            ins=[self.d_xh_in.ap().opt()], outs=[self.d_xh.ap().opt()]))
        CC.inc(gpsimd.collective_compute(
            "AllGather", ALU.bypass, replica_groups=[list(range(NCORES))],
            ins=[self.d_xl_in.ap().opt()], outs=[self.d_xl.ap().opt()]))


MK.ph0 = ph0
MK.ph1 = ph1


def ph2(self, h):
    nc, S, TOK, NT, QT0 = self.nc, self.S, self.TOK, self.NT, self.QT0
    NSL = S // 256
    LW = [self.sem("LW0"), self.sem("LW1")]
    LSL, LAQ, LOS, LCT = (self.sem(x) for x in ("LSL", "LAQ", "LOS", "LCT"))
    P2W, P2M, P2E, P2V, P2T, P2VE = (self.sem(x) for x in
                                     ("P2W", "P2M", "P2E", "P2V", "P2T", "P2VE"))
    N1, N2, N3, N4, N5 = (self.sem(x) for x in ("P2N1", "P2N2", "P2N3", "P2N4", "P2N5"))
    Dsc, Dbias, Dexp, Dspl, Dpv = (self.sem(x) for x in
                                   ("D_sc", "D_bias", "D_exp", "D_spl", "D_pv"))
    Dbrow, Dbc, Dbtile, Drden, Drb, Dqtev = (self.sem(x) for x in
        ("D_brow", "D_bc", "D_btile", "D_rden", "D_rb", "D_qtev"))

    o_q, o_ql, o_kh, o_kl, o_vh, o_vl = 0, S, 2 * S, 3 * S, 4 * S, 5 * S
    o_slh, o_sll = 6 * S, 6 * S + 4096
    o_w = 6 * S + 8192
    o_p = o_w + 12288
    o_vstg = o_p + 2048
    o_ostg = o_vstg + 512
    o_q2 = o_ostg + 1024
    o_msk = o_q2 + 512
    qh = lambda c0, w: self.ab(o_q + c0, w)
    ql = lambda c0, w: self.ab(o_ql + c0, w)
    kh = lambda c0, w: self.ab(o_kh + c0, w)
    kl = lambda c0, w: self.ab(o_kl + c0, w)
    vh = lambda c0, w: self.ab(o_vh + c0, w)
    vl = lambda c0, w: self.ab(o_vl + c0, w)
    slh = lambda ec, w=256: self.ab(o_slh + ec * 256, w)
    sll = lambda ec, w=256: self.ab(o_sll + ec * 256, w)
    w_hi = lambda j, ec: self.ab(o_w + j * 4096 + ec * 128, 128)
    w_lo = lambda j, ec: self.ab(o_w + j * 4096 + 2048 + ec * 128, 128)
    p_hi = lambda g: self.ab(o_p + (g % 2) * 512, 512)
    p_lo = lambda g: self.ab(o_p + 1024 + (g % 2) * 512, 512)
    vstg_hi = self.ab(o_vstg, 256)
    vstg_lo = self.ab(o_vstg + 256, 256)
    ostg_hi = self.ab(o_ostg, 512)
    ostg_lo = self.ab(o_ostg + 512, 512)
    q2t = self.ab(o_q2, 512)
    mskt = lambda m: self.ab(o_msk + 512 * m, 512)
    tmp = lambda g: self.af(2048 + 512 * (g % 2), 512)
    ptil = lambda g: self.af(3072 + 512 * (g % 2), 512)
    bias_tile = self.af(4096, 512)
    comb = lambda m: self.af(4608 + 512 * m, 512)
    rb_tile = self.af(6656, 512)
    qtmp = self.af(0, 256)
    ktmp = self.af(256, 256)
    vtmp = self.af(512, 256)
    wstg = [self.af(1024, 128), self.af(1152, 128)]
    den_r = self.af(7168, 512, part0=0, nparts=1)
    rden_r = self.af(7680, 512, part0=0, nparts=1)
    alq_r = self.af(8192, 512, part0=0, nparts=1)
    brow_r = self.af(8704, 512, part0=0, nparts=1)

    units = [(qt, kc) for qt in range(QT0) for kc in range(4 * qt + 4)]
    qt_last_g = {qt: max(lu for lu, (q2_, _) in enumerate(units) if q2_ == qt)
                 for qt in range(QT0)}
    g0 = self.g0
    qt0 = self.qt0
    bW, bM, bE, bV, bT, bVE = P2W.n, P2M.n, P2E.n, P2V.n, P2T.n, P2VE.n
    bN1, bN2, bN3, bN4, bN5 = N1.n, N2.n, N3.n, N4.n, N5.n
    ldm = {}
    NJ = S // 512

    def slab_pieces(sl):
        out = []
        t0g = 256 * sl
        done = 0
        while done < 256:
            t = t0g + done
            r = t // TOK
            lc = t - r * TOK
            w = min(256 - done, TOK - lc)
            out.append((r, lc, w, done))
            done += w
        return out

    @self.block.sync
    def _(sync):
        if h == 0:
            LCT.inc(sync.dma_start(out=self.ab(o_msk, 2048), in_=self.i_msk), 16)
            ldm["msk"] = LCT.n
        for j, src in enumerate([self.i_wq, self.i_wk, self.i_wv]):
            for ec in range(ECH):
                idx = j * ECH + ec
                if idx >= 2:
                    P2W.wait(sync, bW + idx - 1)
                LW[idx % 2].inc(sync.dma_start(
                    out=wstg[idx % 2],
                    in_=src[128 * ec:128 * (ec + 1), 128 * h:128 * (h + 1)]), 16)
                ldm[("w", idx)] = LW[idx % 2].n
        for sl in range(NSL):
            P2M.wait(sync, bM + 3 * sl)
            for ec in range(ECH):
                for (r, lc, w, doff) in slab_pieces(sl):
                    LSL.inc(sync.dma_start(
                        out=self.ab(o_slh + ec * 256 + doff, w),
                        in_=self.d_xh.ap()[2048 * r + 128 * ec:2048 * r + 128 * (ec + 1),
                                           lc:lc + w]), 16)
            for ec in range(ECH):
                for (r, lc, w, doff) in slab_pieces(sl):
                    LSL.inc(sync.dma_start(
                        out=self.ab(o_sll + ec * 256 + doff, w),
                        in_=self.d_xl.ap()[2048 * r + 128 * ec:2048 * r + 128 * (ec + 1),
                                           lc:lc + w]), 16)
            ldm[("slab", sl)] = LSL.n
        for qt in range(QT0):
            qtg = qt0 + qt
            Dbrow.wait(sync, qtg)
            LAQ.inc(sync.dma_start(out=alq_r, in_=self.i_alq[h:h + 1, 512 * qt:512 * (qt + 1)]), 16)
            ldm[("alq", qt)] = LAQ.n
            Dqtev.wait(sync, qtg + 1)
            LOS.inc(sync.dma_start(
                out=self.d_ohh.ap()[128 * h:128 * (h + 1), 512 * qt:512 * (qt + 1)],
                in_=ostg_hi), 16)
            LOS.inc(sync.dma_start(
                out=self.d_ohl.ap()[128 * h:128 * (h + 1), 512 * qt:512 * (qt + 1)],
                in_=ostg_lo), 16)
            ldm[("ost", qt)] = LOS.n

    @self.block.tensor
    def _(tensor):
        def pv_emit(qt, kc, g):
            qtg = qt0 + qt
            last = (kc == 4 * qt + 3)
            Dspl.wait(tensor, g + 1)
            if kc == 0:
                Dqtev.wait(tensor, qtg)
            tensor.matmul(self.pb[3][0:1, 0:512], self.c_onecb[:], p_hi(g),
                          start=(kc == 0), stop=False)
            tensor.matmul(self.pb[3][0:1, 0:512], self.c_onecb[:], p_lo(g),
                          start=False, stop=last)
            tensor.matmul(self.pb[2][:], vh(kc * 128, 128), p_hi(g), start=(kc == 0), stop=False)
            tensor.matmul(self.pb[2][:], vl(kc * 128, 128), p_hi(g), start=False, stop=False)
            Dpv.inc(tensor.matmul(self.pb[2][:], vh(kc * 128, 128), p_lo(g),
                                  start=False, stop=last))

        for sl in range(NSL):
            P2E.wait(tensor, bE + 3 * sl)
            LSL.wait(tensor, ldm[("slab", sl)])
            if sl == 0:
                P2W.wait(tensor, bW + 3 * ECH)
            for dst, wj, scale_note in ((0, 0, "q"), (1, 1, "k"), (2, 2, "v")):
                last = None
                for ec in range(ECH):
                    last = self.mm3(tensor, self.pb[dst][:, 0:256],
                                    w_hi(wj, ec), w_lo(wj, ec), slh(ec), sll(ec),
                                    start=(ec == 0), stop=(ec == ECH - 1))
                P2M.inc(last)
            if sl >= 1:
                s = sl - 1
                P2V.wait(tensor, bV + s + 1)
                P2VE.wait(tensor, bVE + s)
                for jb in range(2):
                    P2T.inc(tensor.transpose(self.pt_b[:, 128 * jb:128 * (jb + 1)],
                                             self.ab(o_vstg + 128 * jb, 128),
                                             self.c_idb[:]))
                    P2T.inc(tensor.transpose(self.pt_b[:, 256 + 128 * jb:384 + 128 * jb],
                                             self.ab(o_vstg + 256 + 128 * jb, 128),
                                             self.c_idb[:]))
        s = NSL - 1
        P2V.wait(tensor, bV + s + 1)
        P2VE.wait(tensor, bVE + s)
        for jb in range(2):
            P2T.inc(tensor.transpose(self.pt_b[:, 128 * jb:128 * (jb + 1)],
                                     self.ab(o_vstg + 128 * jb, 128), self.c_idb[:]))
            P2T.inc(tensor.transpose(self.pt_b[:, 256 + 128 * jb:384 + 128 * jb],
                                     self.ab(o_vstg + 256 + 128 * jb, 128), self.c_idb[:]))
        for j in range(2 * NJ):
            N1.wait(tensor, bN1 + j + 1)
            N3.wait(tensor, bN3 + j)
            N2.inc(tensor.matmul(self.pb[3][0:1, 0:512], self.c_onecb[:], q2t,
                                 start=True, stop=True))
        P2E.wait(tensor, bE + 3 * NSL)
        N3.wait(tensor, bN3 + 2 * NJ)
        for lu, (qt, kc) in enumerate(units):
            g = g0 + lu
            qtg = qt0 + qt
            if kc == 0:
                Dbrow.wait(tensor, qtg + 1)
                Dbtile.wait(tensor, qtg)
                Dbc.inc(tensor.matmul(self.pb[4][:], self.c_onef[:], brow_r,
                                      start=True, stop=True))
            if PARANOID:
                Dspl.wait(tensor, max(0, g))
            Dbias.wait(tensor, max(0, g - 1))
            Dsc.inc(self.mm3(tensor, self.pb[g % 2][:],
                             kh(kc * 128, 128), kl(kc * 128, 128),
                             qh(512 * qt, 512), ql(512 * qt, 512),
                             start=True, stop=True))
            if kc >= 1:
                pv_emit(qt, kc - 1, g - 1)
            if kc == 4 * qt + 3:
                pv_emit(qt, kc, g)
                Drden.wait(tensor, qtg + 1)
                Drb.inc(tensor.matmul(self.pb[5][:], self.c_onef[:], rden_r,
                                      start=True, stop=True))

    @self.block.scalar
    def _(scalar):
        for sl in range(NSL):
            if sl == 0:
                self.sem("P2B").wait(scalar, h + 1)
            P2M.wait(scalar, bM + 3 * sl + 1)
            P2E.inc(scalar.activation(out=qtmp, in_=self.pb[0][:, 0:256], func=AF.Identity,
                                      bias=self.c_col[:, 3:4], scale=INV_SQRT_D))
            P2M.wait(scalar, bM + 3 * sl + 2)
            P2E.inc(scalar.activation(out=ktmp, in_=self.pb[1][:, 0:256], func=AF.Identity,
                                      bias=self.c_bqkv[:, 3 * h + 1:3 * h + 2], scale=1.0))
            P2M.wait(scalar, bM + 3 * sl + 3)
            P2E.inc(scalar.activation(out=vtmp, in_=self.pb[2][:, 0:256], func=AF.Identity,
                                      bias=self.c_bqkv[:, 3 * h + 2:3 * h + 3], scale=1.0))
        N3.wait(scalar, bN3 + 2 * NJ)
        scalar.activation(out=self.c_sc[0:1, 3:4], in_=self.c_sc[0:1, 0:1], func=AF.Sqrt,
                          bias=self.c_zero[0:1, 0:1])
        N4.inc(scalar.activation(out=self.c_sc[0:1, 4:5], in_=self.c_sc[0:1, 2:3], func=AF.Sqrt,
                                 bias=self.c_zero[0:1, 0:1]))
        for lu, (qt, kc) in enumerate(units):
            g = g0 + lu
            Dbias.wait(scalar, g + 1)
            Dspl.wait(scalar, max(0, g - 1))
            Dexp.inc(scalar.activation(
                out=ptil(g), in_=tmp(g), func=AF.Exp,
                bias=self.c_alk[:, h * (S // 128) + kc:h * (S // 128) + kc + 1], scale=1.0))

    @self.block.vector
    def _(vector):
        HZ = self.sem("HZV")
        for j in range(3):
            for ec in range(ECH):
                idx = j * ECH + ec
                LW[idx % 2].wait(vector, ldm[("w", idx)])
                vector.tensor_copy(w_hi(j, ec), wstg[idx % 2])
                P2W.inc(vector.tensor_tensor(w_lo(j, ec), wstg[idx % 2], w_hi(j, ec),
                                             op=ALU.subtract))
        self.sem("P2B").inc(vector.tensor_scalar(self.c_col[:, 3:4],
                                                 self.c_bqkv[:, 3 * h:3 * h + 1],
                                                 INV_SQRT_D, None, op0=ALU.mult))
        for sl in range(NSL):
            P2E.wait(vector, bE + 3 * sl + 1)
            self.split_hilo(vector, qtmp, qh(256 * sl, 256), ql(256 * sl, 256))
            P2E.wait(vector, bE + 3 * sl + 2)
            self.split_hilo(vector, ktmp, kh(256 * sl, 256), kl(256 * sl, 256))
            P2E.wait(vector, bE + 3 * sl + 3)
            P2T.wait(vector, bT + 4 * sl)
            vector.tensor_copy(vstg_hi, vtmp)
            P2V.inc(vector.tensor_tensor(vstg_lo, vtmp, vstg_hi, op=ALU.subtract))
            if sl >= 1:
                s = sl - 1
                P2T.wait(vector, bT + 4 * (s + 1))
                last = None
                for jb in range(2):
                    vector.tensor_copy(vh((2 * s + jb) * 128, 128),
                                       self.pt_b[:, 128 * jb:128 * (jb + 1)])
                    last = vector.tensor_copy(vl((2 * s + jb) * 128, 128),
                                              self.pt_b[:, 256 + 128 * jb:384 + 128 * jb])
                P2VE.inc(last)
        s = NSL - 1
        P2T.wait(vector, bT + 4 * (s + 1))
        last = None
        for jb in range(2):
            vector.tensor_copy(vh((2 * s + jb) * 128, 128), self.pt_b[:, 128 * jb:128 * (jb + 1)])
            last = vector.tensor_copy(vl((2 * s + jb) * 128, 128),
                                      self.pt_b[:, 256 + 128 * jb:384 + 128 * jb])
        P2VE.inc(last)
        for j in range(2 * NJ):
            if j < NJ:
                src = qh(512 * j, 512)
            else:
                src = kh(512 * (j - NJ), 512)
            N1.inc(vector.tensor_tensor(q2t, src, src, op=ALU.mult))
            N2.wait(vector, bN2 + j + 1)
            slot = 0 if j < NJ else 2
            if j % NJ == 0:
                N3.inc(vector.reduce_max(self.c_sc[0:1, slot:slot + 1],
                                         self.pb[3][0:1, 0:512], axis=AXX))
            else:
                HZ.inc(vector.reduce_max(self.c_sc[0:1, 1:2], self.pb[3][0:1, 0:512], axis=AXX))
                HZ.wait(vector, HZ.n)
                N3.inc(vector.tensor_tensor(self.c_sc[0:1, slot:slot + 1],
                                            self.c_sc[0:1, slot:slot + 1],
                                            self.c_sc[0:1, 1:2], op=ALU.max))
            N3.wait(vector, bN3 + j + 1)
        N4.wait(vector, bN4 + 1)
        HZ.inc(vector.tensor_tensor(self.c_sc[0:1, 1:2], self.c_sc[0:1, 3:4],
                                    self.c_sc[0:1, 4:5], op=ALU.mult))
        HZ.wait(vector, HZ.n)
        N5.inc(vector.tensor_scalar(self.c_sc[0:1, 5:6], self.c_sc[0:1, 1:2], 1.01, 0.5,
                                    op0=ALU.mult, op1=ALU.add))
        for lu, (qt, kc) in enumerate(units):
            g = g0 + lu
            qtg = qt0 + qt
            if kc == 0:
                LAQ.wait(vector, ldm[("alq", qt)])
                if h == 0 and qt == 0:
                    LCT.wait(vector, ldm["msk"])
                N5.wait(vector, bN5 + 1)
                Dbrow.inc(vector.tensor_scalar(brow_r, alq_r, self.c_sc[0:1, 5:6], None,
                                               op0=ALU.subtract))
                Dbc.wait(vector, qtg + 1)
                vector.tensor_copy(bias_tile, self.pb[4][:])
                last = None
                for m in range(4):
                    last = vector.tensor_tensor(comb(m), bias_tile, mskt(m), op=ALU.add)
                Dbtile.inc(last)
            Dsc.wait(vector, g + 1)
            Dexp.wait(vector, max(0, g - 1))
            sel = comb(kc - 4 * qt) if kc >= 4 * qt else bias_tile
            Dbias.inc(vector.tensor_tensor(tmp(g), self.pb[g % 2][:], sel, op=ALU.add))
            Dexp.wait(vector, g + 1)
            Dpv.wait(vector, max(0, g - 1))
            vector.tensor_copy(p_hi(g), ptil(g))
            Dspl.inc(vector.tensor_tensor(p_lo(g), ptil(g), p_hi(g), op=ALU.subtract))
            if kc == 4 * qt + 3:
                Dpv.wait(vector, g + 1)
                HZ.inc(vector.tensor_copy(den_r, self.pb[3][0:1, 0:512]))
                HZ.wait(vector, HZ.n)
                Drden.inc(vector.reciprocal(rden_r, den_r))
                Drb.wait(vector, qtg + 1)
                vector.tensor_copy(rb_tile, self.pb[5][:])
                if qtg >= 1:
                    LOS.wait(vector, ldm[("ost", qt - 1)] if qt >= 1 else self._prev_ost)
                vector.tensor_tensor(self.af(0, 512),
                                     self.pb[2][:], rb_tile, op=ALU.mult)
                vector.tensor_copy(ostg_hi, self.af(0, 512))
                Dqtev.inc(vector.tensor_tensor(ostg_lo, self.af(0, 512), ostg_hi,
                                               op=ALU.subtract))

    self._prev_ost = ldm.get(("ost", QT0 - 1), 0)
    self.g0 += len(units)
    self.qt0 += QT0


MK.ph2 = ph2


def ph3(self):
    nc, S, TOK = self.nc, self.S, self.TOK
    LW = [self.sem("LW0"), self.sem("LW1")]
    LOH, LP3 = self.sem("LOH"), self.sem("LP3")
    WM, WE, WW = self.sem("P3M"), self.sem("P3E"), self.sem("P3W")
    bM, bE, bW = WM.n, WE.n, WW.n
    o_wh, o_wl, o_oh = 0, 4096, 8192
    wo_h = lambda fc, c0, w: self.ab(o_wh + fc * 2048 + c0, w)
    wo_l = lambda fc, c0, w: self.ab(o_wl + fc * 2048 + c0, w)
    oh_hi = lambda fc: self.ab(o_oh + fc * 128, 128)
    oh_lo = lambda fc: self.ab(o_oh + 256 + fc * 128, 128)
    stage = self.af(0, 2048)
    wstg = [self.af(2048, 2048), self.af(4096, 2048)]
    ldm = {}
    NTT = S // 128

    @self.block.sync
    def _(sync):
        for fc in range(HC):
            if fc >= 2:
                WW.wait(sync, bW + fc - 1)
            LW[fc % 2].inc(sync.dma_start(out=wstg[fc % 2],
                                          in_=self.i_wo[128 * fc:128 * (fc + 1), :]), 16)
            ldm[("w", fc)] = LW[fc % 2].n
        for t in range(NTT):
            WM.wait(sync, bM + t)
            for fc in range(HC):
                LOH.inc(sync.dma_start(out=oh_hi(fc),
                                       in_=self.d_ohh.ap()[128 * fc:128 * (fc + 1),
                                                           128 * t:128 * (t + 1)]), 16)
                LOH.inc(sync.dma_start(out=oh_lo(fc),
                                       in_=self.d_ohl.ap()[128 * fc:128 * (fc + 1),
                                                           128 * t:128 * (t + 1)]), 16)
            ldm[("oh", t)] = LOH.n
            WE.wait(sync, bE + t + 1)
            ch, ro = (128 * t) // 1024, (128 * t) % 1024
            LP3.inc(sync.dma_start(out=self.d_part[ch].ap()[ro:ro + 128, 0:2048], in_=stage), 16)
            ldm[("st", t)] = LP3.n

    @self.block.tensor
    def _(tensor):
        for t in range(NTT):
            LOH.wait(tensor, ldm[("oh", t)])
            if t == 0:
                WW.wait(tensor, bW + HC)
            if t >= 1:
                WE.wait(tensor, bE + t)
            last = None
            for og in range(4):
                for fc in range(HC):
                    last = self.mm3(tensor, self.pb[og][:],
                                    oh_hi(fc), oh_lo(fc),
                                    wo_h(fc, 512 * og, 512), wo_l(fc, 512 * og, 512),
                                    start=(fc == 0), stop=(fc == HC - 1))
            WM.inc(last)

    @self.block.vector
    def _(vector):
        for fc in range(HC):
            LW[fc % 2].wait(vector, ldm[("w", fc)])
            vector.tensor_copy(self.ab(o_wh + fc * 2048, 2048), wstg[fc % 2])
            WW.inc(vector.tensor_tensor(self.ab(o_wl + fc * 2048, 2048), wstg[fc % 2],
                                        self.ab(o_wh + fc * 2048, 2048), op=ALU.subtract))
        for t in range(NTT):
            WM.wait(vector, bM + t + 1)
            if t >= 1:
                LP3.wait(vector, ldm[("st", t - 1)])
            for og in range(3):
                vector.tensor_copy(stage[:, 512 * og:512 * (og + 1)], self.pb[og][:])
            WE.inc(vector.tensor_copy(stage[:, 1536:2048], self.pb[3][:]))


MK.ph3 = ph3


def ph4(self):
    nc, S, TOK = self.nc, self.S, self.TOK
    NSL = S // 256
    NIC = ICP // 128
    CC = self.sem("CC")
    LDQ, LSL = self.sem("LDQ"), self.sem("LSL")
    LY = [self.sem("LY0"), self.sem("LY1")]
    LL = [self.sem("LL0"), self.sem("LL1")]
    LWT = [self.sem("LWT0"), self.sem("LWT1")]
    QX, QB, QW = self.sem("P4X"), self.sem("P4B"), self.sem("P4W")
    G, SI, HH, D2, Y = (self.sem(x) for x in ("P4G", "P4S", "P4H", "P4D2", "P4Y"))
    C1, C2, C3, C4, C5 = (self.sem(x) for x in ("P4C1", "P4C2", "P4C3", "P4C4", "P4C5"))
    o_gh, o_gl, o_dh, o_dl = 0, 24576, 49152, 61440
    o_slh, o_sll = 73728, 77824
    o_h = 81920
    gu_h = lambda ec, c0, w: self.ab(o_gh + ec * 1536 + c0, w)
    gu_l = lambda ec, c0, w: self.ab(o_gl + ec * 1536 + c0, w)
    wd_h = lambda ic, c0, w: self.ab(o_dh + ic * 2048 + c0, w)
    wd_l = lambda ic, c0, w: self.ab(o_dl + ic * 2048 + c0, w)
    slh = lambda ec, c0=0, w=256: self.ab(o_slh + ec * 256 + c0, w)
    sll = lambda ec, c0=0, w=256: self.ab(o_sll + ec * 256 + c0, w)
    h_hi = lambda ic, c0, w: self.ab(o_h + ic * 256 + c0, w)
    h_lo = lambda ic, c0, w: self.ab(o_h + 1536 + ic * 256 + c0, w)
    nibf = self.af(2048, 512)
    t1 = self.af(2560, 512)
    t2 = self.af(3072, 512)
    val = self.af(3584, 512)
    silu_t = self.af(4096, 256)
    srow = self.af(8192, 512, part0=0, nparts=2)
    stage = [self.af(0, 2048), self.af(6656, 2048)]
    ldm = {}
    dq = [("gu", ec, ot) for ec in range(ECH) for ot in range(3)] + \
         [("dw", ic, eg) for ic in range(NIC) for eg in range(4)]

    def slab_pieces(sl):
        out, done = [], 0
        while done < 256:
            t = 256 * sl + done
            r, lc = t // TOK, t % TOK
            w = min(256 - done, TOK - lc)
            out.append((r, lc, w, done))
            done += w
        return out

    @self.block.sync
    def _(sync):
        for u, (kind, a, bb) in enumerate(dq):
            QX.wait(sync, u)
            src = self.i_gup if kind == "gu" else self.i_dwp
            LDQ.inc(sync.dma_start(out=self.c_i32a[:],
                                   in_=src[128 * a:128 * (a + 1), 512 * bb:512 * (bb + 1)]), 16)
            if kind == "gu":
                LDQ.inc(sync.dma_start(out=self.c_f16[0:1, 0:512],
                                       in_=self.i_gus[a:a + 1, 512 * bb:512 * (bb + 1)]), 16)
            else:
                LDQ.inc(sync.dma_start(out=self.c_f16[0:2, 0:512],
                                       in_=self.i_dws[2 * a:2 * a + 2, 512 * bb:512 * (bb + 1)]), 16)
            ldm[("dq", u)] = LDQ.n
        for sl in range(NSL):
            G.wait(sync, 6 * sl)
            for ec in range(ECH):
                for (r, lc, w, doff) in slab_pieces(sl):
                    LSL.inc(sync.dma_start(out=slh(ec, doff, w),
                                           in_=self.d_xh.ap()[2048 * r + 128 * ec:
                                                              2048 * r + 128 * (ec + 1),
                                                              lc:lc + w]), 16)
            for ec in range(ECH):
                for (r, lc, w, doff) in slab_pieces(sl):
                    LSL.inc(sync.dma_start(out=sll(ec, doff, w),
                                           in_=self.d_xl.ap()[2048 * r + 128 * ec:
                                                              2048 * r + 128 * (ec + 1),
                                                              lc:lc + w]), 16)
            ldm[("slab", sl)] = LSL.n
            for tsub in range(2):
                ts_idx = 2 * sl + tsub
                Y.wait(sync, 8 * sl + 4 * tsub + 4)
                tg = 256 * sl + 128 * tsub
                ch, ro = tg // 1024, tg % 1024
                LY[ts_idx % 2].inc(sync.dma_start(
                    out=self.d_part[ch].ap()[ro:ro + 128, 2048:4096],
                    in_=stage[tsub % 2]), 16)
                ldm[("st", ts_idx)] = LY[ts_idx % 2].n
                ldm[("rs", ts_idx)] = (self.sem("LY0").n, self.sem("LY1").n)
        for ec in range(ECH):
            if ec < 2:
                LY[0].wait(sync, self.sem("LY0").n)
                LY[1].wait(sync, self.sem("LY1").n)
            else:
                C1.wait(sync, ec - 1)
            LL[ec % 2].inc(sync.dma_start(out=stage[ec % 2],
                                          in_=self.i_wl1[128 * ec:128 * (ec + 1), :]), 16)
            ldm[("l1w", ec)] = LL[ec % 2].n
        for ec in range(ECH):
            u = ECH + ec
            if ec < 2:
                C1.wait(sync, ECH)
            else:
                C5.wait(sync, ec - 1)
            LL[u % 2].inc(sync.dma_start(out=stage[u % 2],
                                         in_=self.i_wl1[128 * ec:128 * (ec + 1), :]), 16)
            ldm[("l1w", u)] = LL[u % 2].n
            C5.wait(sync, ec + 1)
            LWT[ec % 2].inc(sync.dma_start(out=self.d_wt.ap()[128 * ec:128 * (ec + 1), :],
                                           in_=self.ab(73728 + 2048 * (ec % 2), 2048)), 16)
            ldm[("wt", ec)] = LWT[ec % 2].n

    @self.block.tensor
    def _(tensor):
        for u, (kind, a, bb) in enumerate(dq):
            LDQ.wait(tensor, ldm[("dq", u)])
            QX.wait(tensor, u + 1)
            QW.wait(tensor, u)
            if kind == "gu":
                QB.inc(tensor.matmul(self.pb[4][:], self.c_onef[:], srow[0:1, 0:512],
                                     start=True, stop=True))
            else:
                QB.inc(tensor.matmul(self.pb[4][:], self.c_dsel[:], srow[0:2, 0:512],
                                     start=True, stop=True))
        for sl in range(NSL):
            LSL.wait(tensor, ldm[("slab", sl)])
            if sl == 0:
                QW.wait(tensor, len(dq))
            for i in range(6):
                p = 6 * sl + i
                pg, pu = (0, 1) if i % 2 == 0 else (5, 6)
                HH.wait(tensor, max(0, p - 1))
                SI.wait(tensor, max(0, p - 1))
                for ec in range(ECH):
                    self.mm3(tensor, self.pb[pg][:, 0:256],
                             gu_h(ec, 128 * i, 128), gu_l(ec, 128 * i, 128),
                             slh(ec), sll(ec), start=(ec == 0), stop=(ec == ECH - 1))
                last = None
                for ec in range(ECH):
                    last = self.mm3(tensor, self.pb[pu][:, 0:256],
                                    gu_h(ec, 768 + 128 * i, 128), gu_l(ec, 768 + 128 * i, 128),
                                    slh(ec), sll(ec), start=(ec == 0), stop=(ec == ECH - 1))
                G.inc(last)
            HH.wait(tensor, 6 * (sl + 1))
            for tsub in range(2):
                for eg in range(4):
                    d = 8 * sl + 4 * tsub + eg
                    Y.wait(tensor, max(0, d - 1))
                    last = None
                    for ic in range(NIC):
                        last = self.mm3(tensor, self.pb[2 + d % 2][:],
                                        h_hi(ic, 128 * tsub, 128), h_lo(ic, 128 * tsub, 128),
                                        wd_h(ic, 512 * eg, 512), wd_l(ic, 512 * eg, 512),
                                        start=(ic == 0), stop=(ic == NIC - 1))
                    D2.inc(last)
        HH.wait(tensor, 6 * NSL)
        C1.wait(tensor, ECH)
        C2.inc(tensor.matmul(self.pb[6][0:1, 0:1], self.c_col[:, 5:6], self.c_onecf[:],
                             start=True, stop=True))
        C4.wait(tensor, 1)
        tensor.matmul(self.pb[6][:, 1:2], self.c_onef[:], self.c_sc[0:1, 10:11],
                      start=True, stop=True)
        tensor.matmul(self.pb[6][:, 2:3], self.c_onef[:], self.c_sc[0:1, 11:12],
                      start=True, stop=True)
        C3.inc(tensor.matmul(self.pb[6][:, 3:4], self.c_onef[:], self.c_sc[0:1, 8:9],
                             start=True, stop=True))

    @self.block.scalar
    def _(scalar):
        for sl in range(NSL):
            for i in range(6):
                p = 6 * sl + i
                pg = 0 if i % 2 == 0 else 5
                G.wait(scalar, p + 1)
                if p >= 1:
                    HH.wait(scalar, p)
                SI.inc(scalar.activation(out=silu_t, in_=self.pb[pg][:, 0:256], func=AF.Silu,
                                         bias=self.c_zero[:, 0:1]))

    @self.block.vector
    def _(vector):
        HZ = self.sem("HZV")
        for u, (kind, a, bb) in enumerate(dq):
            LDQ.wait(vector, ldm[("dq", u)])
            nparts = 1 if kind == "gu" else 2
            vector.tensor_copy(srow[0:nparts, :], self.c_f16[0:nparts, 0:512])
            QX.inc(vector.tensor_scalar(self.c_i32b[:], self.c_i32a[:],
                                        self.c_nmask[:, 0:1], None,
                                        op0=ALU.bitwise_and))
            vector.tensor_copy(nibf, self.c_i32b[:])
            vector.tensor_scalar(nibf, nibf, self.c_ninv[:, 0:1], None, op0=ALU.mult)
            vector.tensor_scalar(t1, nibf, 8.0, None, op0=ALU.is_ge)
            vector.scalar_tensor_tensor(t2, t1, -8.0, nibf, op0=ALU.mult, op1=ALU.add)
            vector.tensor_scalar(val, t2, 4.0, 0.5, op0=ALU.min, op1=ALU.mult)
            vector.tensor_scalar(nibf, t2, 4.0, 0.0, op0=ALU.subtract, op1=ALU.max)
            vector.scalar_tensor_tensor(nibf, t2, 7.0, nibf, op0=ALU.is_ge, op1=ALU.add)
            vector.tensor_tensor(val, val, nibf, op=ALU.add)
            vector.tensor_scalar(t1, t1, -2.0, 1.0, op0=ALU.mult, op1=ALU.add)
            vector.tensor_tensor(val, val, t1, op=ALU.mult)
            QB.wait(vector, u + 1)
            vector.tensor_tensor(val, val, self.pb[4][:], op=ALU.mult)
            if kind == "gu":
                dh, dl = gu_h(a, 512 * bb, 512), gu_l(a, 512 * bb, 512)
            else:
                dh, dl = wd_h(a, 512 * bb, 512), wd_l(a, 512 * bb, 512)
            vector.tensor_copy(dh, val)
            QW.inc(vector.tensor_tensor(dl, val, dh, op=ALU.subtract))
        for sl in range(NSL):
            for i in range(6):
                p = 6 * sl + i
                pu = 1 if i % 2 == 0 else 6
                SI.wait(vector, p + 1)
                if sl >= 1 and i == 0:
                    D2.wait(vector, 8 * sl)
                vector.tensor_tensor(silu_t, silu_t, self.pb[pu][:, 0:256], op=ALU.mult)
                vector.tensor_copy(h_hi(i, 0, 256), silu_t)
                HH.inc(vector.tensor_tensor(h_lo(i, 0, 256), silu_t, h_hi(i, 0, 256),
                                            op=ALU.subtract))
            for tsub in range(2):
                for eg in range(4):
                    d = 8 * sl + 4 * tsub + eg
                    D2.wait(vector, d + 1)
                    if 2 * sl + tsub >= 2 and eg == 0:
                        LY[(2 * sl + tsub) % 2].wait(vector, ldm[("st", 2 * sl + tsub - 2)])
                    Y.inc(vector.tensor_copy(stage[tsub % 2][:, 512 * eg:512 * (eg + 1)],
                                              self.pb[2 + d % 2][:]))
        for ec in range(ECH):
            LL[ec % 2].wait(vector, ldm[("l1w", ec)])
            if ec == 0:
                C1.inc(vector.reduce_sum(self.c_col[:, 5:6], stage[0], axis=AXX,
                                         apply_absolute_value=True))
            else:
                HZ.inc(vector.reduce_sum(self.c_col[:, 4:5], stage[ec % 2], axis=AXX,
                                         apply_absolute_value=True))
                HZ.wait(vector, HZ.n)
                C1.inc(vector.tensor_tensor(self.c_col[:, 5:6], self.c_col[:, 5:6],
                                            self.c_col[:, 4:5], op=ALU.add))
            C1.wait(vector, ec + 1)
        C2.wait(vector, 1)
        HZ.inc(vector.tensor_scalar(self.c_sc[0:1, 8:9], self.pb[6][0:1, 0:1],
                                    1.0 / (E * E), None, op0=ALU.mult))
        HZ.wait(vector, HZ.n)
        HZ.inc(vector.tensor_scalar(self.c_sc[0:1, 10:11], self.c_sc[0:1, 8:9], 1e-10, 0.5,
                                    op0=ALU.add, op1=ALU.mult))
        HZ.wait(vector, HZ.n)
        C4.inc(vector.tensor_scalar(self.c_sc[0:1, 11:12], self.c_sc[0:1, 10:11],
                                    -1.0, None, op0=ALU.mult))
        C3.wait(vector, 1)
        vector.tensor_copy(self.c_col[:, 4:5], self.pb[6][:, 1:2])
        vector.tensor_copy(self.c_col[:, 5:6], self.pb[6][:, 2:3])
        HZ.inc(vector.tensor_copy(self.c_col[:, 6:7], self.pb[6][:, 3:4]))
        HZ.wait(vector, HZ.n)
        for ec in range(ECH):
            u = ECH + ec
            LL[u % 2].wait(vector, ldm[("l1w", u)])
            if ec >= 2:
                LWT[ec % 2].wait(vector, ldm[("wt", ec - 2)])
            vector.tensor_scalar(self.af(2048, 2048), stage[u % 2], self.c_col[:, 4:5],
                                 None, op0=ALU.is_gt)
            vector.tensor_scalar(self.af(4096, 2048), stage[u % 2], self.c_col[:, 5:6],
                                 None, op0=ALU.is_lt)
            C5.inc(vector.tensor_tensor(self.ab(73728 + 2048 * (ec % 2), 2048),
                                        self.af(2048, 2048), self.af(4096, 2048),
                                        op=ALU.subtract))

    @self.block.gpsimd
    def _(gpsimd):
        for ch in range(self.NCH):
            last_t128 = (1024 * (ch + 1) - 128) // 128
            n0, n1 = ldm[("rs", last_t128)]
            if n0:
                self.sem("LY0").wait(gpsimd, n0)
            if n1:
                self.sem("LY1").wait(gpsimd, n1)
            CC.inc(gpsimd.collective_compute(
                "ReduceScatter", ALU.add, replica_groups=[list(range(NCORES))],
                ins=[self.d_part[ch].ap().opt()], outs=[self.d_rs[ch].ap().opt()]))


MK.ph4 = ph4


def ph5(self):
    nc, S, TOK, NT = self.nc, self.S, self.TOK, self.NT
    L5, L5S, L5P, LCT = self.sem("L5"), self.sem("L5S"), self.sem("L5P"), self.sem("LCT")
    R1, R2, R3, R4, R5 = (self.sem(x) for x in ("P5R1", "P5R2", "P5R3", "P5R4", "P5R5"))
    TT, SP = self.sem("P5T"), self.sem("P5S")
    BO = self.sem("P5BO")
    buf1 = self.af(0, 2048)
    buf2 = self.af(2048, 2048)
    bo_bc = self.af(4096, 2048)
    bo_row = self.af(6656, 2048, part0=0, nparts=1)
    o_xh, o_xl = 0, ECH * TOK
    ldm = {}

    @self.block.sync
    def _(sync):
        LCT.inc(sync.dma_start(out=bo_row, in_=self.i_bo), 16)
        ldm["bo"] = LCT.n
        for tt in range(NT):
            if tt > 0:
                R4.wait(sync, tt)
                TT.wait(sync, ECH * tt)
                L5P.wait(sync, ldm[("spw", tt - 1)])
            L5.inc(sync.dma_start(out=buf1, in_=self.d_rs[tt].ap()[:, 2048:4096]), 16)
            L5.inc(sync.dma_start(out=buf2, in_=self.d_rs[tt].ap()[:, 0:2048]), 16)
            ldm[("y", tt)] = L5.n
            R1.wait(sync, tt + 1)
            L5.inc(sync.dma_start(out=buf2, in_=self.i_x2[128 * tt:128 * (tt + 1), :]), 16)
            ldm[("x", tt)] = L5.n
            R2.wait(sync, tt + 1)
            L5P.inc(sync.dma_start(out=self.d_x1.ap()[128 * tt:128 * (tt + 1), :], in_=buf1), 16)
            ldm[("spw", tt)] = L5P.n
            for ec in range(ECH):
                g = tt * ECH + ec
                SP.wait(sync, g + 1)
                L5S.inc(sync.dma_start(
                    out=self.d_x1h.ap()[128 * ec:128 * (ec + 1), 128 * tt:128 * (tt + 1)],
                    in_=self.ab(o_xh + ec * TOK + 128 * tt, 128)), 16)
                L5S.inc(sync.dma_start(
                    out=self.d_x1l.ap()[128 * ec:128 * (ec + 1), 128 * tt:128 * (tt + 1)],
                    in_=self.ab(o_xl + ec * TOK + 128 * tt, 128)), 16)

    @self.block.tensor
    def _(tensor):
        LCT.wait(tensor, ldm["bo"])
        last = None
        for og in range(4):
            last = tensor.matmul(self.pb[og][:], self.c_onef[:],
                                 bo_row[0:1, 512 * og:512 * (og + 1)], start=True, stop=True)
        BO.inc(last)
        for tt in range(NT):
            R4.wait(tensor, tt + 1)
            for ec in range(ECH):
                g = tt * ECH + ec
                if g >= 2:
                    SP.wait(tensor, g - 1)
                TT.inc(tensor.transpose(self.pb[4 + g % 2][:, 0:128],
                                        buf2[:, 128 * ec:128 * (ec + 1)], self.c_idf[:]))

    @self.block.scalar
    def _(scalar):
        for tt in range(NT):
            R3.wait(scalar, 2 * tt + 1)
            R3.inc(scalar.activation(out=self.c_col[:, 1:2], in_=self.c_col[:, 0:1],
                                     func=AF.Sqrt, bias=self.c_zero[:, 0:1]))
            R5.wait(scalar, 2 * tt + 1)
            R5.inc(scalar.activation(out=self.c_col[:, 4:5], in_=self.c_col[:, 3:4],
                                     func=AF.Sqrt, bias=self.c_zero[:, 0:1]))

    @self.block.vector
    def _(vector):
        HZ = self.sem("HZV")
        BO.wait(vector, 1)
        for og in range(4):
            vector.tensor_copy(bo_bc[:, 512 * og:512 * (og + 1)], self.pb[og][:])
        for tt in range(NT):
            L5.wait(vector, ldm[("y", tt)])
            vector.tensor_tensor(self.af(8192, 512), buf1[:, 0:512], buf1[:, 0:512], op=ALU.mult)
            HZ.inc(vector.reduce_sum(self.c_col[:, 0:1], self.af(8192, 512), axis=AXX))
            for k in range(1, 4):
                vector.tensor_tensor(self.af(8192, 512), buf1[:, 512 * k:512 * (k + 1)],
                                     buf1[:, 512 * k:512 * (k + 1)], op=ALU.mult)
                HZ.inc(vector.reduce_sum(self.c_col[:, 5:6], self.af(8192, 512), axis=AXX))
                HZ.wait(vector, HZ.n)
                HZ.inc(vector.tensor_tensor(self.c_col[:, 0:1], self.c_col[:, 0:1],
                                            self.c_col[:, 5:6], op=ALU.add))
                HZ.wait(vector, HZ.n)
            R3.inc(vector.tensor_scalar(self.c_col[:, 0:1], self.c_col[:, 0:1],
                                        1.0 / E, 1e-6, op0=ALU.mult, op1=ALU.add))
            R3.wait(vector, 2 * (tt + 1))
            HZ.inc(vector.reciprocal(self.c_col[:, 2:3], self.c_col[:, 1:2]))
            HZ.wait(vector, HZ.n)
            vector.tensor_scalar(buf1, buf1, self.c_col[:, 2:3], None, op0=ALU.mult)
            vector.tensor_tensor(buf1, buf1, buf2, op=ALU.add)
            R1.inc(vector.tensor_tensor(buf1, buf1, bo_bc, op=ALU.add))
            L5.wait(vector, ldm[("x", tt)])
            R2.inc(vector.tensor_tensor(buf1, buf1, buf2, op=ALU.add))
            for k in range(4):
                HZ.inc(vector.bn_stats(self.c_stats[:, 6 * k:6 * k + 6],
                                       buf1[:, 512 * k:512 * (k + 1)]))
            HZ.wait(vector, HZ.n)
            HZ.inc(vector.bn_aggr(self.c_stats[:, 24:26], self.c_stats[:, 0:24]))
            HZ.wait(vector, HZ.n)
            R5.inc(vector.tensor_scalar(self.c_col[:, 3:4], self.c_stats[:, 25:26], 1e-5, None,
                                        op0=ALU.add))
            R5.wait(vector, 2 * (tt + 1))
            HZ.inc(vector.reciprocal(self.c_col[:, 5:6], self.c_col[:, 4:5]))
            HZ.wait(vector, HZ.n)
            vector.tensor_scalar(buf2, buf1, self.c_stats[:, 24:25], None, op0=ALU.subtract)
            R4.inc(vector.tensor_scalar(buf2, buf2, self.c_col[:, 5:6], None, op0=ALU.mult))
            for ec in range(ECH):
                g = tt * ECH + ec
                TT.wait(vector, g + 1)
                hi_ap = self.ab(o_xh + ec * TOK + 128 * tt, 128)
                lo_ap = self.ab(o_xl + ec * TOK + 128 * tt, 128)
                vector.tensor_copy(hi_ap, self.pb[4 + g % 2][:, 0:128])
                SP.inc(vector.tensor_tensor(lo_ap, self.pb[4 + g % 2][:, 0:128],
                                            hi_ap, op=ALU.subtract))


MK.ph5 = ph5


def ph6(self):
    nc, S, TOK, NT = self.nc, self.S, self.TOK, self.NT
    CC = self.sem("CC")
    LCT = self.sem("LCT")
    LW = [self.sem("LW0"), self.sem("LW1")]
    LQ = [self.sem("LQ0"), self.sem("LQ1")]
    L6KV = self.sem("L6KV")
    W6, M6, E6, Q6 = (self.sem(x) for x in ("P6W", "P6M", "P6E", "P6Q"))
    o_xh, o_xl = 0, ECH * TOK
    o_wah, o_wal = 2 * ECH * TOK, 2 * ECH * TOK + 8192
    wstg = [self.af(0, 512), self.af(512, 512)]
    cos_t = lambda tt: self.af(2048 + 128 * tt, 128)
    sin_t = lambda tt: self.af(2048 + 128 * NT + 128 * tt, 128)
    qstage = [self.af(4096, 512), self.af(5888, 512)]
    qa = [self.af(4864, 512), self.af(6400, 512)]
    rot = self.af(4608, 128)
    kst = lambda tt: self.af(6912 + 128 * tt, 128)
    vst = lambda tt: self.af(7936 + 128 * tt, 128)
    ldm = {}
    ogs = [4, 0, 1, 2, 3]

    @self.block.sync
    def _(sync):
        for tt in range(NT):
            LCT.inc(sync.dma_start(out=cos_t(tt), in_=self.i_cos[128 * tt:128 * (tt + 1), :]), 16)
            LCT.inc(sync.dma_start(out=sin_t(tt), in_=self.i_sin[128 * tt:128 * (tt + 1), :]), 16)
        ldm["cs"] = LCT.n
        for oi, og in enumerate(ogs):
            wdt = 256 if og == 4 else 512
            for ec in range(ECH):
                idx = oi * ECH + ec
                if idx >= 2:
                    W6.wait(sync, idx - 1)
                LW[idx % 2].inc(sync.dma_start(out=wstg[idx % 2][:, 0:wdt],
                                               in_=self.i_wa[128 * ec:128 * (ec + 1),
                                                             512 * og:512 * og + wdt]), 16)
                ldm[("w", idx)] = LW[idx % 2].n
            for tt in range(NT):
                idx = oi * NT + tt
                Q6.wait(sync, idx + 1)
                if og == 4:
                    L6KV.inc(sync.dma_start(out=self.d_k1in.ap()[128 * tt:128 * (tt + 1), :],
                                            in_=kst(tt)), 16)
                    L6KV.inc(sync.dma_start(out=self.d_v1in.ap()[128 * tt:128 * (tt + 1), :],
                                            in_=vst(tt)), 16)
                else:
                    LQ[idx % 2].inc(sync.dma_start(
                        out=self.d_q1.ap()[128 * tt:128 * (tt + 1), 512 * og:512 * (og + 1)],
                        in_=qa[idx % 2]), 16)
                    ldm[("st", idx)] = LQ[idx % 2].n
        ldm["kv_end"] = L6KV.n

    @self.block.tensor
    def _(tensor):
        for oi, og in enumerate(ogs):
            wdt = 256 if og == 4 else 512
            W6.wait(tensor, ECH * (oi + 1))
            for tt in range(NT):
                idx = oi * NT + tt
                if idx >= 2:
                    E6.wait(tensor, idx - 1)
                last = None
                for ec in range(ECH):
                    last = self.mm3(tensor, self.pb[idx % 2][:, 0:wdt],
                                    self.ab(o_xh + ec * TOK + 128 * tt, 128),
                                    self.ab(o_xl + ec * TOK + 128 * tt, 128),
                                    self.ab(o_wah + ec * 512, wdt),
                                    self.ab(o_wal + ec * 512, wdt),
                                    start=(ec == 0), stop=(ec == ECH - 1))
                M6.inc(last)

    @self.block.scalar
    def _(scalar):
        for oi, og in enumerate(ogs):
            wdt = 256 if og == 4 else 512
            sc = 1.0 if og == 4 else INV_SQRT_D
            for tt in range(NT):
                idx = oi * NT + tt
                M6.wait(scalar, idx + 1)
                if idx >= 2:
                    Q6.wait(scalar, idx - 1)
                E6.inc(scalar.activation(out=qstage[idx % 2][:, 0:wdt],
                                         in_=self.pb[idx % 2][:, 0:wdt],
                                         func=AF.Identity, bias=self.c_zero[:, 0:1], scale=sc))

    @self.block.vector
    def _(vector):
        for oi, og in enumerate(ogs):
            for ec in range(ECH):
                idx = oi * ECH + ec
                wdt = 256 if og == 4 else 512
                LW[idx % 2].wait(vector, ldm[("w", idx)])
                vector.tensor_copy(self.ab(o_wah + ec * 512, wdt), wstg[idx % 2][:, 0:wdt])
                W6.inc(vector.tensor_tensor(self.ab(o_wal + ec * 512, wdt),
                                            wstg[idx % 2][:, 0:wdt],
                                            self.ab(o_wah + ec * 512, wdt), op=ALU.subtract))
            HZ = self.sem("HZV")
            for tt in range(NT):
                idx = oi * NT + tt
                E6.wait(vector, idx + 1)
                if oi == 0 and tt == 0:
                    LCT.wait(vector, ldm["cs"])
                src = qstage[idx % 2]
                if og == 4:
                    vector.tensor_tensor(kst(tt), src[:, 0:128], cos_t(tt), op=ALU.mult)
                    vector.tensor_scalar(rot[:, 0:64], src[:, 64:128], -1.0, None, op0=ALU.mult)
                    HZ.inc(vector.tensor_copy(rot[:, 64:128], src[:, 0:64]))
                    HZ.wait(vector, HZ.n)
                    HZ.inc(vector.tensor_tensor(rot, rot, sin_t(tt), op=ALU.mult))
                    HZ.wait(vector, HZ.n)
                    vector.tensor_tensor(kst(tt), kst(tt), rot, op=ALU.add)
                    Q6.inc(vector.tensor_copy(vst(tt), src[:, 128:256]))
                else:
                    if idx >= 2 and ("st", idx - 2) in ldm:
                        LQ[idx % 2].wait(vector, ldm[("st", idx - 2)])
                    dst = qa[idx % 2]
                    last = None
                    for j in range(4):
                        js = slice(128 * j, 128 * (j + 1))
                        vector.tensor_tensor(dst[:, js], src[:, js], cos_t(tt), op=ALU.mult)
                        vector.tensor_scalar(rot[:, 0:64], src[:, 128 * j + 64:128 * (j + 1)],
                                             -1.0, None, op0=ALU.mult)
                        HZ.inc(vector.tensor_copy(rot[:, 64:128], src[:, 128 * j:128 * j + 64]))
                        HZ.wait(vector, HZ.n)
                        HZ.inc(vector.tensor_tensor(rot, rot, sin_t(tt), op=ALU.mult))
                        HZ.wait(vector, HZ.n)
                        last = vector.tensor_tensor(dst[:, js], dst[:, js], rot, op=ALU.add)
                    Q6.inc(last)

    @self.block.gpsimd
    def _(gpsimd):
        L6KV.wait(gpsimd, ldm["kv_end"])
        CC.inc(gpsimd.collective_compute(
            "AllGather", ALU.bypass, replica_groups=[list(range(NCORES))],
            ins=[self.d_k1in.ap().opt()], outs=[self.d_k1.ap().opt()]))
        CC.inc(gpsimd.collective_compute(
            "AllGather", ALU.bypass, replica_groups=[list(range(NCORES))],
            ins=[self.d_v1in.ap().opt()], outs=[self.d_v1.ap().opt()]))


MK.ph6 = ph6



def ph7(self):
    nc, S, TOK, NT = self.nc, self.S, self.TOK, self.NT
    QT1, QW1, KC = self.QT1, self.QW1, self.KC
    L7K, L7Q, L7S = self.sem("L7K"), self.sem("L7Q"), self.sem("L7S")
    KT, KE, VE = (self.sem(x) for x in ("P7KT", "P7KE", "P7VE"))
    KN1, KN2, KN3, KM, KSQ = (self.sem(x) for x in ("P7KN1", "P7KN2", "P7KN3", "P7KM", "P7KSQ"))
    QP, QS = self.sem("P7QP"), self.sem("P7QS")
    QN1, QN2, QN3, QM, QSQ = (self.sem(x) for x in ("P7QN1", "P7QN2", "P7QN3", "P7QM", "P7QSQ"))
    VM, TM, VC = (self.sem(x) for x in ("P7VM", "P7TM", "P7VC"))
    Asc, Aexp, Aspl, Apv = (self.sem(x) for x in ("A_sc", "A_exp", "A_spl", "A_pv"))
    Arden, Arb, Aqtev = (self.sem(x) for x in ("A_rden", "A_rb", "A_qtev"))
    o_kth, o_ktl, o_vh, o_vl = 0, S, 2 * S, 3 * S
    o_qh, o_ql = 4 * S, 4 * S + TOK
    o_p = 4 * S + 2 * TOK
    o_ostg = o_p + 2048
    o_q2 = o_ostg + 1024
    kth = lambda c0, w: self.ab(o_kth + c0, w)
    ktl = lambda c0, w: self.ab(o_ktl + c0, w)
    vh = lambda c0, w: self.ab(o_vh + c0, w)
    vl = lambda c0, w: self.ab(o_vl + c0, w)
    qhh = lambda c0, w: self.ab(o_qh + c0, w)
    qll = lambda c0, w: self.ab(o_ql + c0, w)
    p_hi = lambda g: self.ab(o_p + (g % 2) * 512, 512)
    p_lo = lambda g: self.ab(o_p + 1024 + (g % 2) * 512, 512)
    ostg_hi = self.ab(o_ostg, 512)
    ostg_lo = self.ab(o_ostg + 512, 512)
    q2t = self.ab(o_q2, 512)
    kstg = self.af(0, 128)
    qstg = self.af(128, 128)
    ptil = lambda g: self.af(3072 + 512 * (g % 2), 512)
    ostg_f = self.af(4096, 512)
    rb_tile = self.af(6656, 512)
    den_r = self.af(7168, 512, part0=0, nparts=1)
    rden_r = self.af(7680, 512, part0=0, nparts=1)
    negM = self.c_col[:, 3:4]
    ldm = {}
    NKB = S // 128
    NJK = S // 512
    NJ1 = max(1, TOK // 512)
    UPH = QT1 * KC
    g1b = self.g1

    def gidx(h, qt, kc):
        return g1b + h * UPH + qt * KC + kc

    @self.block.sync
    def _(sync):
        for kb in range(NKB):
            KT.wait(sync, kb)
            L7K.inc(sync.dma_start(out=kstg, in_=self.d_k1.ap()[128 * kb:128 * (kb + 1), :]), 16)
            ldm[("k", kb)] = L7K.n
        for kb in range(NKB):
            if kb == 0:
                KT.wait(sync, NKB)
            else:
                VE.wait(sync, kb)
            L7K.inc(sync.dma_start(out=kstg, in_=self.d_v1.ap()[128 * kb:128 * (kb + 1), :]), 16)
            ldm[("v", kb)] = L7K.n
        for h in range(H):
            for tt in range(NT):
                u = h * NT + tt
                QP.wait(sync, u)
                L7Q.inc(sync.dma_start(out=qstg,
                                       in_=self.d_q1.ap()[128 * tt:128 * (tt + 1),
                                                          128 * h:128 * (h + 1)]), 16)
                ldm[("q", u)] = L7Q.n
            for qt in range(QT1):
                qtg = h * QT1 + qt
                Aqtev.wait(sync, qtg + 1)
                L7S.inc(sync.dma_start(
                    out=self.d_o1h.ap()[128 * h:128 * (h + 1), QW1 * qt:QW1 * (qt + 1)],
                    in_=ostg_hi[:, 0:QW1]), 16)
                L7S.inc(sync.dma_start(
                    out=self.d_o1l.ap()[128 * h:128 * (h + 1), QW1 * qt:QW1 * (qt + 1)],
                    in_=ostg_lo[:, 0:QW1]), 16)
                ldm[("ost", qtg)] = L7S.n

    @self.block.tensor
    def _(tensor):
        def pv7(qt, kc, g, qtg):
            last = (kc == KC - 1)
            Aspl.wait(tensor, g + 1)
            if kc == 0:
                Aqtev.wait(tensor, qtg)
            tensor.matmul(self.pb[3][0:1, 0:QW1], self.c_onecb[:], p_hi(g)[:, 0:QW1],
                          start=(kc == 0), stop=False)
            tensor.matmul(self.pb[3][0:1, 0:QW1], self.c_onecb[:], p_lo(g)[:, 0:QW1],
                          start=False, stop=last)
            tensor.matmul(self.pb[2][:, 0:QW1], vh(kc * 128, 128), p_hi(g)[:, 0:QW1],
                          start=(kc == 0), stop=False)
            tensor.matmul(self.pb[2][:, 0:QW1], vl(kc * 128, 128), p_hi(g)[:, 0:QW1],
                          start=False, stop=False)
            Apv.inc(tensor.matmul(self.pb[2][:, 0:QW1], vh(kc * 128, 128), p_lo(g)[:, 0:QW1],
                                  start=False, stop=last))

        for kb in range(NKB):
            L7K.wait(tensor, ldm[("k", kb)])
            if kb >= 2:
                KE.wait(tensor, kb - 1)
            KT.inc(tensor.transpose(self.pb[5][:, 128 * (kb % 2):128 * (kb % 2) + 128],
                                    kstg, self.c_idf[:]))
        for j in range(NJK):
            KN1.wait(tensor, j + 1)
            if j >= 1:
                KN3.wait(tensor, j)
            KN2.inc(tensor.matmul(self.pb[3][0:1, 0:512], self.c_onecb[:], q2t,
                                  start=True, stop=True))
        for h in range(H):
            for tt in range(NT):
                u = h * NT + tt
                L7Q.wait(tensor, ldm[("q", u)])
                if u >= 2:
                    QS.wait(tensor, u - 1)
                QP.inc(tensor.transpose(self.pb[6][:, 128 * (u % 2):128 * (u % 2) + 128],
                                        qstg, self.c_idf[:]))
            if h == 0:
                KN3.wait(tensor, NJK)
            Arden.wait(tensor, h * QT1)
            for j in range(NJ1):
                uj = h * NJ1 + j
                QN1.wait(tensor, uj + 1)
                if uj >= 1:
                    QN3.wait(tensor, uj)
                QN2.inc(tensor.matmul(self.pb[3][0:1, 0:QW1], self.c_onecb[:], q2t[:, 0:QW1],
                                      start=True, stop=True))
            VM.wait(tensor, h + 1)
            if h >= 1:
                VC.wait(tensor, h)
            TM.inc(tensor.matmul(self.pb[4][:, 0:1], self.c_onef[:], self.c_sc[0:1, 7:8],
                                 start=True, stop=True))
            for qt in range(QT1):
                qtg = h * QT1 + qt
                for kc in range(KC):
                    g = gidx(h, qt, kc)
                    Aexp.wait(tensor, max(0, g - 1))
                    Asc.inc(self.mm3(tensor, self.pb[g % 2][:, 0:QW1],
                                     kth(kc * 128, 128), ktl(kc * 128, 128),
                                     qhh(QW1 * qt, QW1), qll(QW1 * qt, QW1),
                                     start=True, stop=True))
                    if kc >= 1:
                        pv7(qt, kc - 1, g - 1, qtg)
                pv7(qt, KC - 1, gidx(h, qt, KC - 1), qtg)
                Arden.wait(tensor, qtg + 1)
                Aqtev.wait(tensor, qtg)
                Arb.inc(tensor.matmul(self.pb[5][:, 0:QW1], self.c_onef[:],
                                      rden_r[0:1, 0:QW1], start=True, stop=True))

    @self.block.scalar
    def _(scalar):
        KM.wait(scalar, 1)
        KSQ.inc(scalar.activation(out=self.c_sc[0:1, 5:6], in_=self.c_sc[0:1, 2:3],
                                  func=AF.Sqrt, bias=self.c_zero[0:1, 0:1]))
        for h in range(H):
            QM.wait(scalar, h + 1)
            QSQ.inc(scalar.activation(out=self.c_sc[0:1, 4:5], in_=self.c_sc[0:1, 0:1],
                                      func=AF.Sqrt, bias=self.c_zero[0:1, 0:1]))
            for qt in range(QT1):
                for kc in range(KC):
                    g = gidx(h, qt, kc)
                    Asc.wait(scalar, g + 1)
                    Aspl.wait(scalar, max(0, g - 1))
                    VC.wait(scalar, h + 1)
                    Aexp.inc(scalar.activation(out=ptil(g)[:, 0:QW1],
                                               in_=self.pb[g % 2][:, 0:QW1], func=AF.Exp,
                                               bias=negM, scale=1.0))

    @self.block.vector
    def _(vector):
        HZ = self.sem("HZV")
        for kb in range(NKB):
            KT.wait(vector, kb + 1)
            slot = self.pb[5][:, 128 * (kb % 2):128 * (kb % 2) + 128]
            vector.tensor_copy(kth(128 * kb, 128), slot)
            KE.inc(vector.tensor_tensor(ktl(128 * kb, 128), slot, kth(128 * kb, 128),
                                        op=ALU.subtract))
        for kb in range(NKB):
            L7K.wait(vector, ldm[("v", kb)])
            vector.tensor_copy(vh(128 * kb, 128), kstg)
            VE.inc(vector.tensor_tensor(vl(128 * kb, 128), kstg, vh(128 * kb, 128),
                                        op=ALU.subtract))
        for j in range(NJK):
            KN1.inc(vector.tensor_tensor(q2t, kth(512 * j, 512), kth(512 * j, 512), op=ALU.mult))
            KN2.wait(vector, j + 1)
            if j == 0:
                KN3.inc(vector.reduce_max(self.c_sc[0:1, 2:3], self.pb[3][0:1, 0:512], axis=AXX))
            else:
                HZ.inc(vector.reduce_max(self.c_sc[0:1, 6:7], self.pb[3][0:1, 0:512], axis=AXX))
                HZ.wait(vector, HZ.n)
                KN3.inc(vector.tensor_tensor(self.c_sc[0:1, 2:3], self.c_sc[0:1, 2:3],
                                             self.c_sc[0:1, 6:7], op=ALU.max))
            KN3.wait(vector, j + 1)
        KM.inc(vector.tensor_copy(self.c_sc[0:1, 16:17], self.c_sc[0:1, 2:3]))
        for h in range(H):
            for tt in range(NT):
                u = h * NT + tt
                QP.wait(vector, u + 1)
                slot = self.pb[6][:, 128 * (u % 2):128 * (u % 2) + 128]
                vector.tensor_copy(qhh(128 * tt, 128), slot)
                QS.inc(vector.tensor_tensor(qll(128 * tt, 128), slot, qhh(128 * tt, 128),
                                            op=ALU.subtract))
            for j in range(NJ1):
                uj = h * NJ1 + j
                QN1.inc(vector.tensor_tensor(q2t[:, 0:QW1], qhh(QW1 * j, QW1),
                                             qhh(QW1 * j, QW1), op=ALU.mult))
                QN2.wait(vector, uj + 1)
                if j == 0:
                    QN3.inc(vector.reduce_max(self.c_sc[0:1, 0:1], self.pb[3][0:1, 0:QW1],
                                              axis=AXX))
                else:
                    HZ.inc(vector.reduce_max(self.c_sc[0:1, 6:7], self.pb[3][0:1, 0:QW1],
                                             axis=AXX))
                    HZ.wait(vector, HZ.n)
                    QN3.inc(vector.tensor_tensor(self.c_sc[0:1, 0:1], self.c_sc[0:1, 0:1],
                                                 self.c_sc[0:1, 6:7], op=ALU.max))
                QN3.wait(vector, uj + 1)
            QM.inc(vector.tensor_copy(self.c_sc[0:1, 17:18], self.c_sc[0:1, 0:1]))
            QSQ.wait(vector, h + 1)
            if h == 0:
                KSQ.wait(vector, 1)
            HZ.inc(vector.tensor_tensor(self.c_sc[0:1, 6:7], self.c_sc[0:1, 4:5],
                                        self.c_sc[0:1, 5:6], op=ALU.mult))
            HZ.wait(vector, HZ.n)
            VM.inc(vector.tensor_scalar(self.c_sc[0:1, 7:8], self.c_sc[0:1, 6:7],
                                        -1.01, -0.5, op0=ALU.mult, op1=ALU.add))
            TM.wait(vector, h + 1)
            if h >= 1:
                Aexp.wait(vector, g1b + h * UPH)
            VC.inc(vector.tensor_copy(negM, self.pb[4][:, 0:1]))
            for qt in range(QT1):
                qtg = h * QT1 + qt
                for kc in range(KC):
                    g = gidx(h, qt, kc)
                    Aexp.wait(vector, g + 1)
                    Apv.wait(vector, max(0, g - 1))
                    vector.tensor_copy(p_hi(g)[:, 0:QW1], ptil(g)[:, 0:QW1])
                    Aspl.inc(vector.tensor_tensor(p_lo(g)[:, 0:QW1], ptil(g)[:, 0:QW1],
                                                  p_hi(g)[:, 0:QW1], op=ALU.subtract))
                g = gidx(h, qt, KC - 1)
                Apv.wait(vector, g + 1)
                HZ.inc(vector.tensor_copy(den_r[0:1, 0:QW1], self.pb[3][0:1, 0:QW1]))
                HZ.wait(vector, HZ.n)
                Arden.inc(vector.reciprocal(rden_r[0:1, 0:QW1], den_r[0:1, 0:QW1]))
                Arb.wait(vector, qtg + 1)
                vector.tensor_copy(rb_tile[:, 0:QW1], self.pb[5][:, 0:QW1])
                if qtg >= 1:
                    L7S.wait(vector, ldm[("ost", qtg - 1)])
                vector.tensor_tensor(ostg_f[:, 0:QW1], self.pb[2][:, 0:QW1],
                                     rb_tile[:, 0:QW1], op=ALU.mult)
                vector.tensor_copy(ostg_hi[:, 0:QW1], ostg_f[:, 0:QW1])
                Aqtev.inc(vector.tensor_tensor(ostg_lo[:, 0:QW1], ostg_f[:, 0:QW1],
                                               ostg_hi[:, 0:QW1], op=ALU.subtract))

    self.g1 += H * UPH


MK.ph7 = ph7


def ph8(self):
    nc, S, TOK, NT = self.nc, self.S, self.TOK, self.NT
    LW = [self.sem("LW0"), self.sem("LW1")]
    L8OH, L8X, L8S = self.sem("L8OH"), self.sem("L8X"), self.sem("L8S")
    W8, M8, E8, O8 = (self.sem(x) for x in ("P8W", "P8M", "P8E", "P8O"))
    o_wh, o_wl, o_oh = 0, ECH * 2048, 2 * ECH * 2048
    wo_h = lambda fc, c0, w: self.ab(o_wh + fc * 2048 + c0, w)
    wo_l = lambda fc, c0, w: self.ab(o_wl + fc * 2048 + c0, w)
    oh_hi = lambda fc: self.ab(o_oh + fc * 128, 128)
    oh_lo = lambda fc: self.ab(o_oh + 2048 + fc * 128, 128)
    wstg = [self.af(0, 2048), self.af(2048, 2048)]
    buf1 = self.af(4096, 2048)
    stage = self.af(6656, 2048)
    ldm = {}

    @self.block.sync
    def _(sync):
        for ec in range(ECH):
            if ec >= 2:
                W8.wait(sync, ec - 1)
            LW[ec % 2].inc(sync.dma_start(out=wstg[ec % 2],
                                          in_=self.i_wu[128 * ec:128 * (ec + 1), :]), 16)
            ldm[("w", ec)] = LW[ec % 2].n
        for tt in range(NT):
            M8.wait(sync, tt)
            for fc in range(ECH):
                L8OH.inc(sync.dma_start(out=oh_hi(fc),
                                        in_=self.d_o1h.ap()[128 * fc:128 * (fc + 1),
                                                            128 * tt:128 * (tt + 1)]), 16)
                L8OH.inc(sync.dma_start(out=oh_lo(fc),
                                        in_=self.d_o1l.ap()[128 * fc:128 * (fc + 1),
                                                            128 * tt:128 * (tt + 1)]), 16)
            ldm[("oh", tt)] = L8OH.n
            E8.wait(sync, tt)
            L8X.inc(sync.dma_start(out=buf1, in_=self.d_x1.ap()[128 * tt:128 * (tt + 1), :]), 16)
            ldm[("x1", tt)] = L8X.n
            E8.wait(sync, tt + 1)
            L8S.inc(sync.dma_start(out=self.d_x1p.ap()[128 * tt:128 * (tt + 1), :], in_=stage), 16)
            ldm[("st", tt)] = L8S.n

    @self.block.tensor
    def _(tensor):
        for tt in range(NT):
            L8OH.wait(tensor, ldm[("oh", tt)])
            if tt == 0:
                W8.wait(tensor, ECH)
            if tt >= 1:
                E8.wait(tensor, tt)
            last = None
            for fc in range(ECH):
                for og in range(4):
                    last = self.mm3(tensor, self.pb[og][:],
                                    oh_hi(fc), oh_lo(fc),
                                    wo_h(fc, 512 * og, 512), wo_l(fc, 512 * og, 512),
                                    start=(fc == 0), stop=(fc == ECH - 1))
            M8.inc(last)

    @self.block.vector
    def _(vector):
        for ec in range(ECH):
            LW[ec % 2].wait(vector, ldm[("w", ec)])
            vector.tensor_copy(self.ab(o_wh + ec * 2048, 2048), wstg[ec % 2])
            W8.inc(vector.tensor_tensor(self.ab(o_wl + ec * 2048, 2048), wstg[ec % 2],
                                        self.ab(o_wh + ec * 2048, 2048), op=ALU.subtract))
        for tt in range(NT):
            M8.wait(vector, tt + 1)
            L8X.wait(vector, ldm[("x1", tt)])
            if tt >= 1:
                L8S.wait(vector, ldm[("st", tt - 1)])
            last = None
            for og in range(4):
                last = vector.tensor_tensor(stage[:, 512 * og:512 * (og + 1)], self.pb[og][:],
                                            buf1[:, 512 * og:512 * (og + 1)], op=ALU.add)
            E8.inc(last)


def ph9(self):
    nc, S, TOK, NT = self.nc, self.S, self.TOK, self.NT
    L9A, L9W, L9X, L9O = (self.sem(x) for x in ("L9A", "L9W", "L9X", "L9O"))
    X9, W9, M9, E9, B9 = (self.sem(x) for x in ("P9X", "P9W", "P9M", "P9E", "P9B"))
    o_xh, o_xl = 0, ECH * TOK
    o_wt = 2 * ECH * TOK
    wt_t = lambda ec: self.ab(o_wt + ec * 512, 512)
    biasr = self.af(6656, 2048, part0=0, nparts=1)
    bias_bc = self.af(4096, 2048)
    x1p = self.af(2048, 512)
    outst = self.af(2560, 512)
    ldm = {}

    @self.block.sync
    def _(sync):
        for ec in range(ECH):
            L9A.inc(sync.dma_start(out=self.ab(o_xh + ec * TOK, TOK),
                                   in_=self.d_x1h.ap()[128 * ec:128 * (ec + 1), :]), 16)
            L9A.inc(sync.dma_start(out=self.ab(o_xl + ec * TOK, TOK),
                                   in_=self.d_x1l.ap()[128 * ec:128 * (ec + 1), :]), 16)
        L9A.inc(sync.dma_start(out=biasr, in_=self.i_bl1), 16)
        ldm["xn"] = L9A.n
        for og in range(4):
            for ec in range(ECH):
                idx = og * ECH + ec
                if og >= 1 and ec == 0:
                    M9.wait(sync, NT * og)
                L9W.inc(sync.dma_start(out=wt_t(ec),
                                       in_=self.d_wt.ap()[128 * ec:128 * (ec + 1),
                                                          512 * og:512 * (og + 1)]), 16)
            ldm[("wt", og)] = L9W.n
            for tt in range(NT):
                idx = og * NT + tt
                E9.wait(sync, idx)
                L9X.inc(sync.dma_start(out=x1p,
                                       in_=self.d_x1p.ap()[128 * tt:128 * (tt + 1),
                                                           512 * og:512 * (og + 1)]), 16)
                ldm[("x1p", idx)] = L9X.n
                E9.wait(sync, idx + 1)
                L9O.inc(sync.dma_start(out=self.o_y[128 * tt:128 * (tt + 1),
                                                    512 * og:512 * (og + 1)], in_=outst), 16)
                ldm[("out", idx)] = L9O.n

    @self.block.tensor
    def _(tensor):
        L9A.wait(tensor, ldm["xn"])
        for og in range(4):
            B9.inc(tensor.matmul(self.pb[2 + og][:], self.c_onef[:],
                                 biasr[0:1, 512 * og:512 * (og + 1)], start=True, stop=True))
        for og in range(4):
            L9W.wait(tensor, ldm[("wt", og)])
            for tt in range(NT):
                idx = og * NT + tt
                if idx >= 2:
                    E9.wait(tensor, idx - 1)
                last = None
                for ec in range(ECH):
                    tensor.matmul(self.pb[idx % 2][:],
                                  self.ab(o_xh + ec * TOK + 128 * tt, 128), wt_t(ec),
                                  start=(ec == 0), stop=False)
                    last = tensor.matmul(self.pb[idx % 2][:],
                                         self.ab(o_xl + ec * TOK + 128 * tt, 128), wt_t(ec),
                                         start=False, stop=(ec == ECH - 1))
                M9.inc(last)

    @self.block.vector
    def _(vector):
        for og in range(4):
            B9.wait(vector, og + 1)
            vector.tensor_copy(bias_bc[:, 512 * og:512 * (og + 1)], self.pb[2 + og][:])
        for og in range(4):
            for tt in range(NT):
                idx = og * NT + tt
                M9.wait(vector, idx + 1)
                L9X.wait(vector, ldm[("x1p", idx)])
                if idx >= 1:
                    L9O.wait(vector, ldm[("out", idx - 1)])
                vector.scalar_tensor_tensor(outst, self.pb[idx % 2][:], self.c_col[:, 6:7],
                                            x1p, op0=ALU.mult, op1=ALU.add)
                E9.inc(vector.tensor_tensor(outst, outst,
                                            bias_bc[:, 512 * og:512 * (og + 1)], op=ALU.add))

    @self.block.gpsimd
    def _(gpsimd):
        pass


MK.ph8 = ph8
MK.ph9 = ph9


def build_mk(S, nph=99, dumps=()):
    mk = MK(S)
    mk.declare()
    mk.alloc()
    dump_outs = []
    for i, name in enumerate(dumps):
        if isinstance(name, str) and "[" in name:
            base, idx = name[:-1].split("[")
            t = getattr(mk, base)[int(idx)]
        elif isinstance(name, str):
            t = getattr(mk, name)
        else:
            t = name
        sh = list(t.shape)
        dt = t.dtype
        dump_outs.append((mk.nc.dram_tensor(f"dbg{i}", sh, dt, kind="ExternalOutput").ap(), t))
    phases = [mk.ph0, mk.ph1, lambda: mk.ph2(0), lambda: mk.ph2(1), mk.ph3,
              mk.ph4, mk.ph5, mk.ph6, mk.ph7, mk.ph8, mk.ph9]
    with mk.nc.Block() as block:
        mk.block = block
        for i, ph in enumerate(phases):
            if i >= nph:
                break
            ph()
            mk.barrier()
        if dump_outs:
            LD = mk.sem("LD")

            @block.sync
            def _(sync):
                for o, t in dump_outs:
                    LD.inc(sync.dma_start(out=o, in_=t[:]), 16)
            mk.barrier()
    for name, sem in mk.sems.items():
        for w in sem.waits:
            assert w <= sem.n, f"sem {name}: wait {w} > total {sem.n}"
    mk.es.close()
    return mk


def kernel(**inputs):
    S = int(np.asarray(inputs["L0_attn_x"]).shape[1])
    mk = build_mk(S)
    in_maps = [host_prep(c, S, inputs) for c in range(NCORES)]
    res = run_bass_kernel_spmd(mk.nc, in_maps, core_ids=list(range(NCORES)))
    out = np.zeros((1, S, E), dtype=np.float32)
    for c in range(NCORES):
        out[0, owned_rows(c, S), :] = np.asarray(res.results[c]["y_c"], dtype=np.float32)
    return out


# version 03
# speedup vs baseline: 1.0561x; 1.0561x over previous
import os
import numpy as np
import ml_dtypes
import concourse.bass as bass
import concourse.mybir as mybir
from concourse.bass_utils import run_bass_kernel_spmd

F32 = mybir.dt.float32
BF16 = mybir.dt.bfloat16
F16 = mybir.dt.float16
I32 = mybir.dt.int32
AF = mybir.ActivationFunctionType
ALU = mybir.AluOpType
AXX = mybir.AxisListType.X

NCORES = 8
E = 2048
ECH = E // 128
H = 16
D = 128
HC = H // NCORES
ICR = 5632 // NCORES
ICP = 768
GU = 2 * ICP
NEG = -1e30
INV_SQRT_D = 1.0 / np.sqrt(128.0)
PARANOID = bool(int(os.environ.get("MK_PARANOID", "0")))

BF = ml_dtypes.bfloat16


def _hilo(a):
    hi = a.astype(BF)
    lo = (a - hi.astype(np.float32)).astype(BF)
    return hi, lo


class PhysSem:
    def __init__(self, es, nc, name):
        self.h = es.enter_context(nc.semaphore(name))
        self.n = 0


class Sem:

    def __init__(self, phys):
        self.p = phys
        self.base = phys.n
        self.waits = []

    @property
    def n(self):
        return self.p.n - self.base

    def inc(self, instr, k=1):
        instr.then_inc(self.p.h, k)
        self.p.n += k
        return self.n

    def wait(self, eng, value):
        assert value >= 0
        self.waits.append(value)
        eng.wait_ge(self.p.h, self.base + value)


SEM_ALIAS = {}
for _nm, _ph in {
    "P1A": "E00", "P1B": "E01", "P1C": "E02", "P1T": "E03", "P1S": "E04",
    "P2W": "E00", "P2M": "E01", "P2E": "E02", "P2V": "E03", "P2T": "E04",
    "P2VE": "E05", "P2N1": "E06", "P2N2": "E07", "P2N3": "E08", "P2N4": "E09",
    "P2N5": "E10", "D_sc": "E11", "D_bias": "E12", "D_exp": "E13", "D_spl": "E14",
    "D_pv": "E15", "D_brow": "E16", "D_bc": "E17", "D_btile": "E18",
    "D_rden": "E19", "D_rb": "E20", "D_qtev": "E21", "P2B": "E22",
    "P3M": "E23", "P3E": "E24", "P3W": "E25",
    "P4X": "E03", "P4B": "E04", "P4W": "E05", "P4G": "E06", "P4S": "E07",
    "P4H": "E08", "P4D2": "E09", "P4Y": "E10", "P4C1": "E11", "P4C2": "E12",
    "P4C3": "E13", "P4C4": "E14", "P4C5": "E15",
    "P5R1": "E16", "P5R2": "E17", "P5R3": "E18", "P5R4": "E19", "P5R5": "E20",
    "P5T": "E21", "P5S": "E22", "P5BO": "E06",
    "P6W": "E00", "P6M": "E01", "P6E": "E02", "P6Q": "E03",
    "P7KT": "E04", "P7KE": "E05", "P7VE": "E07", "P7KN1": "E08", "P7KN2": "E09",
    "P7KN3": "E10", "P7KM": "E11", "P7KSQ": "E12", "P7QP": "E13", "P7QS": "E14",
    "P7QN1": "E15", "P7QN2": "E16", "P7QN3": "E17", "P7QM": "E18", "P7QSQ": "E19",
    "P7VM": "E20", "P7TM": "E21", "P7VC": "E22",
    "A_sc": "E23", "A_exp": "E24", "A_spl": "E25", "A_pv": "E26",
    "A_rden": "E27", "A_rb": "E28", "A_qtev": "E29",
    "P8W": "E00", "P8M": "E01", "P8E": "E02", "P8O": "E05",
    "P9X": "E03", "P9W": "E04", "P9M": "E05", "P9E": "E06", "P9B": "E07",
    "L1X": "D6", "L1S": "D7",
    "LW0": "D0", "LW1": "D1", "LSL": "D2", "LAQ": "D3", "LOS": "D4", "LCT": "D5",
    "LOH": "D6", "LP3": "D7",
    "LDQ": "D3", "LY0": "D6", "LY1": "D7", "LL0": "D8", "LL1": "D9",
    "LWT0": "D4", "LWT1": "D10",
    "L5": "D6x", "L5S": "D7x", "L5P": "D8x",
    "LQ0": "D6", "LQ1": "D7", "L6KV": "D8",
    "L7K": "D6y", "L7Q": "D7y", "L7S": "D8y",
    "L8OH": "D6", "L8X": "D7", "L8S": "D8",
    "L9A": "D6z", "L9W": "D7z", "L9X": "D8z", "L9O": "D9",
}.items():
    SEM_ALIAS[_nm] = _ph


class MK:
    def __init__(self, S):
        assert S % 1024 == 0
        self.S = S
        self.TOK = S // NCORES
        self.NT = self.TOK // 128
        self.QT0 = S // 512
        self.QT1 = max(1, self.TOK // 512)
        self.QW1 = min(512, self.TOK)
        self.KC = S // 128
        self.SLAB = 256
        self.NSLAB = S // self.SLAB
        self.nc = bass.Bass("TRN2", target_bir_lowering=False,
                            dynamic_dma_scratch_size=8192,
                            detect_race_conditions=False)
        import contextlib
        self.es = contextlib.ExitStack()
        self.block = None
        self.sems = {}
        self.phys = {}
        self.g0 = 0
        self.qt0 = 0
        self.g1 = 0
        self._prev_ost = 0

    def sem(self, name):
        if name not in self.sems:
            ph = SEM_ALIAS.get(name, name)
            if ph not in self.phys:
                self.phys[ph] = PhysSem(self.es, self.nc, ph)
            self.sems[name] = Sem(self.phys[ph])
        return self.sems[name]

    def declare(self):
        nc, S, TOK = self.nc, self.S, self.TOK
        di = lambda n, sh, dt: nc.dram_tensor(n, sh, dt, kind="ExternalInput").ap()
        self.i_x = di("x_c", [TOK, E], F32)
        self.i_wq = di("wq_t", [E, HC * 128], F32)
        self.i_wk = di("wk_t", [E, HC * 128], F32)
        self.i_wv = di("wv_t", [E, HC * 128], F32)
        self.i_bqkv = di("bqkv", [128, HC * 3], F32)
        self.i_wo = di("wo_t", [HC * 128, E], F32)
        self.i_bo = di("bo", [1, E], F32)
        self.i_gup = di("gu_rep", [E, GU], I32)
        self.i_gus = di("gu_sc", [ECH, GU], F16)
        self.i_dwp = di("dw_rep", [ICP, E], I32)
        self.i_dws = di("dw_scsel", [2 * (ICP // 128), E], F16)
        self.i_dsel = di("dsel", [2, 128], F32)
        self.i_wa = di("w_attn", [E, 2304], F32)
        self.i_wu = di("w_out", [E, E], F32)
        self.i_wl1 = di("w_l1t", [E, E], F32)
        self.i_bl1 = di("b_l1", [1, E], F32)
        self.i_x2 = di("x_c2", [TOK, E], F32)
        self.i_idf = di("identf", [128, 128], F32)
        self.i_idb = di("identb", [128, 128], BF16)
        self.i_msk = di("masks", [128, 4 * 512], BF16)
        self.i_alq = di("al_q", [HC, S], F32)
        self.i_alk = di("al_k", [128, HC * (S // 128)], F32)
        self.i_shft = di("shifts", [128, 1], I32)
        self.i_nmask = di("nmask", [128, 1], I32)
        self.i_ninv = di("ninv", [128, 1], F32)
        self.i_cos = di("cos_t", [TOK, 128], F32)
        self.i_sin = di("sin_t", [TOK, 128], F32)

        self.o_y = nc.dram_tensor("y_c", [TOK, E], F32, kind="ExternalOutput").ap()

        dr = lambda n, sh, dt: nc.dram_tensor(n, sh, dt)
        self.d_xh_in = dr("xh_in", [E, TOK], BF16)
        self.d_xl_in = dr("xl_in", [E, TOK], BF16)
        self.d_xh = dr("xh_all", [NCORES * E, TOK], BF16)
        self.d_xl = dr("xl_all", [NCORES * E, TOK], BF16)
        self.d_ohh = dr("ohT_hi", [HC * 128, S], BF16)
        self.d_ohl = dr("ohT_lo", [HC * 128, S], BF16)
        self.NCH = S // 1024
        self.d_part = [dr(f"part{ch}", [1024, 2 * E], F32) for ch in range(self.NCH)]
        self.d_rs = [dr(f"rsout{ch}", [128, 2 * E], F32) for ch in range(self.NCH)]
        self.d_q1 = dr("q1_spill", [TOK, E], F32)
        self.d_k1in = dr("k1_in", [TOK, D], F32)
        self.d_v1in = dr("v1_in", [TOK, D], F32)
        self.d_k1 = dr("k1_all", [S, D], F32)
        self.d_v1 = dr("v1_all", [S, D], F32)
        self.d_x1h = dr("xn1T_hi", [E, TOK], BF16)
        self.d_x1l = dr("xn1T_lo", [E, TOK], BF16)
        self.d_x1 = dr("x1_spill", [TOK, E], F32)
        self.d_x1p = dr("x1p_spill", [TOK, E], F32)
        self.d_o1h = dr("o1T_hi", [E, TOK], BF16)
        self.d_o1l = dr("o1T_lo", [E, TOK], BF16)
        self.d_wt = dr("wt_spill", [E, E], BF16)
        self.d_bar = dr("bar_scratch", [128, 1], F32)

    def alloc(self):
        nc, es, S, TOK = self.nc, self.es, self.S, self.TOK
        sb = lambda n, sh, dt: es.enter_context(nc.sbuf_tensor(n, sh, dt))
        self.AB = sb("AB", [128, 84992], BF16)
        self.AF = sb("AF", [128, 9216], F32)
        self.c_i32a = sb("c_i32a", [128, 512], I32)
        self.c_i32b = sb("c_i32b", [128, 512], I32)
        self.c_idf = sb("c_idf", [128, 128], F32)
        self.c_idb = sb("c_idb", [128, 128], BF16)
        self.c_shft = sb("c_shft", [128, 1], I32)
        self.c_nmask = sb("c_nmask", [128, 1], I32)
        self.c_ninv = sb("c_ninv", [128, 1], F32)
        self.c_zero = sb("c_zero", [128, 1], F32)
        self.c_onef = sb("c_onef", [1, 128], F32)
        self.c_onecb = sb("c_onecb", [128, 1], BF16)
        self.c_onecf = sb("c_onecf", [128, 1], F32)
        self.c_dsel = sb("c_dsel", [2, 128], F32)
        self.c_bqkv = sb("c_bqkv", [128, HC * 3], F32)
        self.c_sc = sb("c_sc", [1, 32], F32)
        self.c_col = sb("c_col", [128, 8], F32)
        self.c_stats = sb("c_stats", [128, 26], F32)
        self.c_alk = sb("c_alk", [128, HC * (S // 128)], F32)
        self.c_bar = sb("c_bar", [128, 4], F32)
        self.c_f16 = sb("c_f16", [2, 2048], F16)
        self.pb = [es.enter_context(nc.psum_tensor(f"pb{i}", [128, 512], F32))
                   for i in range(7)]
        self.pt_b = es.enter_context(nc.psum_tensor("pt_b", [128, 512], BF16))

    def ab(self, off, cols, part0=None, nparts=None):
        assert off + cols <= 84992, (off, cols)
        if part0 is not None:
            return self.AB[part0:part0 + nparts, off:off + cols]
        return self.AB[:, off:off + cols]

    def af(self, off, cols, part0=None, nparts=None):
        assert off + cols <= 9216, (off, cols)
        if part0 is not None:
            return self.AF[part0:part0 + nparts, off:off + cols]
        return self.AF[:, off:off + cols]

    def barrier(self):
        BAR, LD, CC = self.sem("BAR"), self.sem("LD"), self.sem("CC")
        base = BAR.n
        ccn = CC.n
        r1 = base + 19
        tgt = base + 20

        BI = self.sem("BARINIT")

        @self.block.vector
        def _(vector):
            BAR.inc(vector.tensor_copy(self.c_bar[:, 0:1], self.c_bar[:, 0:1]), 1)

        @self.block.scalar
        def _(scalar):
            BI.wait(scalar, 1)
            BAR.inc(scalar.activation(out=self.c_bar[:, 1:2], in_=self.c_bar[:, 1:2],
                                      func=AF.Copy, bias=0.0), 1)

        @self.block.gpsimd
        def _(gpsimd):
            BI.wait(gpsimd, 1)
            BAR.inc(gpsimd.tensor_copy(self.c_bar[:, 2:3], self.c_bar[:, 2:3]), 1)

        @self.block.sync
        def _(sync):
            BI.wait(sync, 1)
            BAR.inc(sync.dma_start(out=self.d_bar.ap(), in_=self.c_bar[:, 3:4]), 16)

        @self.block.tensor
        def _(tensor):
            BAR.wait(tensor, r1)
            BAR.inc(tensor.matmul(self.pb[0][0:1, 0:1], self.c_onecb[:], self.c_onecb[:],
                                  start=True, stop=True), 1)

        dma_totals = [(sem, sem.n) for name, sem in self.sems.items()
                      if name.startswith("L") and sem.n > 0]
        for engname in ("sync", "gpsimd", "tensor", "vector", "scalar"):
            def mk(engname=engname):
                def _(eng):
                    BAR.wait(eng, tgt)
                    for sem, n in dma_totals:
                        sem.wait(eng, n)
                    if ccn:
                        CC.wait(eng, ccn)
                return _
            getattr(self.block, engname)(mk())

    def mm3(self, tensor, out, l_hi, l_lo, r_hi, r_lo, start, stop):
        tensor.matmul(out, l_hi, r_hi, start=start, stop=False)
        tensor.matmul(out, l_lo, r_hi, start=False, stop=False)
        return tensor.matmul(out, l_hi, r_lo, start=False, stop=stop)

    def split_hilo(self, vector, src, hi, lo):
        vector.tensor_copy(hi, src)
        vector.tensor_tensor(lo, src, hi, op=ALU.subtract)



def host_prep(core, S, ins):
    TOK = S // NCORES
    t0 = core * TOK
    f32 = np.float32
    out = {}
    x = np.asarray(ins["L0_attn_x"], dtype=f32)
    out["x_c"] = np.ascontiguousarray(x[0, t0:t0 + TOK, :])
    h0 = core * HC
    for nm, w in (("wq_t", "L0_attn_Wq"), ("wk_t", "L0_attn_Wk"), ("wv_t", "L0_attn_Wv")):
        out[nm] = np.ascontiguousarray(np.asarray(ins[w], f32).T[:, h0 * 128:(h0 + HC) * 128])
    bq = np.stack([np.asarray(ins["L0_attn_bq"], f32), np.asarray(ins["L0_attn_bk"], f32),
                   np.asarray(ins["L0_attn_bv"], f32)], axis=1)
    bqkv = np.zeros((128, HC * 3), f32)
    for h in range(HC):
        for j in range(3):
            bqkv[:, h * 3 + j] = bq[(h0 + h) * 128:(h0 + h + 1) * 128, j]
    out["bqkv"] = bqkv
    out["wo_t"] = np.ascontiguousarray(np.asarray(ins["L0_attn_Wo"], f32).T[h0 * 128:(h0 + HC) * 128, :])
    out["bo"] = np.asarray(ins["L0_attn_bo"], f32).reshape(1, E)

    gup = np.asarray(ins["L0_ffn_gate_up_packed"])
    gus = np.asarray(ins["L0_ffn_gate_up_scales"])
    I = gup.shape[0] // 2
    g_lo, g_hi = core * ICR, (core + 1) * ICR
    rows = np.concatenate([gup[g_lo:g_hi], np.zeros((ICP - ICR, E // 8), np.int32),
                           gup[I + g_lo:I + g_hi], np.zeros((ICP - ICR, E // 8), np.int32)], axis=0)
    out["gu_rep"] = np.ascontiguousarray(np.repeat(rows.T, 8, axis=0)).astype(np.int32)
    sc = np.concatenate([gus[:, g_lo:g_hi], np.zeros((ECH, ICP - ICR), gus.dtype),
                         gus[:, I + g_lo:I + g_hi], np.zeros((ECH, ICP - ICR), gus.dtype)], axis=1)
    out["gu_sc"] = np.ascontiguousarray(sc)
    dwp = np.asarray(ins["L0_ffn_down_packed"])
    dsc = np.asarray(ins["L0_ffn_down_scales"])
    w_lo8, w_hi8 = g_lo // 8, g_hi // 8
    drows = np.concatenate([dwp[:, w_lo8:w_hi8].T, np.zeros((ICP // 8 - ICR // 8, E), np.int32)], axis=0)
    out["dw_rep"] = np.ascontiguousarray(np.repeat(drows, 8, axis=0)).astype(np.int32)
    sel = np.zeros((2, 128), f32)
    sel[0, :64] = 1.0
    sel[1, 64:] = 1.0
    out["dsel"] = sel
    nic = ICP // 128
    scsel = np.zeros((2 * nic, E), dsc.dtype)
    for ic in range(nic):
        scsel[2 * ic + 0] = dsc[min((g_lo + ic * 128) // 128, dsc.shape[0] - 1)]
        scsel[2 * ic + 1] = dsc[min((g_lo + ic * 128 + 64) // 128, dsc.shape[0] - 1)]
    out["dw_scsel"] = np.ascontiguousarray(scsel)

    out["w_attn"] = np.ascontiguousarray(np.asarray(ins["L1_attn_w_attn"], f32))
    out["w_out"] = np.ascontiguousarray(np.asarray(ins["L1_attn_w_out"], f32))
    out["w_l1t"] = np.ascontiguousarray(np.asarray(ins["L1_ffn_weight"], f32).T)
    out["b_l1"] = np.asarray(ins["L1_ffn_bias"], f32).reshape(1, E)

    out["identf"] = np.eye(128, dtype=f32)
    out["identb"] = np.eye(128, dtype=f32).astype(BF)
    p = np.arange(128)[:, None]
    j = np.arange(512)[None, :]
    masks = np.concatenate([np.where(p + 128 * m <= j, 0.0, NEG).astype(f32) for m in range(4)], axis=1)
    out["masks"] = masks.astype(BF)
    slopes = np.array([2.0 ** (-0.5 * (i + 1)) for i in range(H)], dtype=np.float64)
    alq = np.stack([(-slopes[h0 + h] * np.arange(S)).astype(f32) for h in range(HC)], axis=0)
    out["al_q"] = np.ascontiguousarray(alq)
    alk = np.zeros((128, HC * (S // 128)), f32)
    for h in range(HC):
        for kc in range(S // 128):
            alk[:, h * (S // 128) + kc] = (slopes[h0 + h] * (kc * 128 + np.arange(128))).astype(f32)
    out["al_k"] = alk
    out["shifts"] = (4 * (np.arange(128) % 8)).astype(np.int32).reshape(128, 1)
    sh = 4 * (np.arange(128) % 8)
    out["nmask"] = (15 << sh).astype(np.int32).reshape(128, 1)
    out["ninv"] = (2.0 ** (-sh)).astype(f32).reshape(128, 1)
    idx = owned_rows(core, S)
    out["x_c2"] = np.ascontiguousarray(x[0, idx, :])
    hd = 128
    pos = 10000.0 ** ((-2.0 * np.arange(0, hd, 2, dtype=f32) - 1.0) / hd)
    t_idx = idx.astype(f32)[:, None]
    ang = t_idx * pos[None, :]
    rot = np.concatenate([ang, ang], axis=1)
    out["cos_t"] = np.cos(rot).astype(f32)
    out["sin_t"] = np.sin(rot).astype(f32)
    return out


def owned_rows(core, S):
    nch = S // 1024
    return np.concatenate([np.arange(1024 * ch + 128 * core, 1024 * ch + 128 * core + 128)
                           for ch in range(nch)])


def ph0(self):
    nc, LD = self.nc, self.sem("LD")
    ld_marks = {}

    @self.block.sync
    def _(sync):
        for dst, src in [(self.c_idf[:], self.i_idf), (self.c_idb[:], self.i_idb),
                         (self.c_nmask[:], self.i_nmask), (self.c_ninv[:], self.i_ninv),
                         (self.c_dsel[:], self.i_dsel),
                         (self.c_bqkv[:], self.i_bqkv), (self.c_alk[:], self.i_alk)]:
            LD.inc(sync.dma_start(out=dst, in_=src), 16)
        ld_marks["done"] = LD.n

    @self.block.vector
    def _(vector):
        BI = self.sem("BARINIT")
        vector.memset(self.c_zero[:], 0.0)
        vector.memset(self.c_onef[:], 1.0)
        vector.memset(self.c_onecb[:], 1.0)
        vector.memset(self.c_onecf[:], 1.0)
        BI.inc(vector.memset(self.c_bar[:], 0.0))


def ph1(self):
    nc, S, TOK, NT = self.nc, self.S, self.TOK, self.NT
    CC = self.sem("CC")
    L1X, L1S = self.sem("L1X"), self.sem("L1S")
    A = self.sem("P1A"); B = self.sem("P1B"); C = self.sem("P1C")
    T = self.sem("P1T"); SP = self.sem("P1S")
    xtile = self.af(0, 2048)
    xntile = self.af(2048, 2048)
    o_xh, o_xl = 0, ECH * TOK
    b0 = dict(A=A.n, B=B.n, C=C.n, T=T.n, SP=SP.n)
    ldm = {}

    @self.block.sync
    def _(sync):
        for tt in range(NT):
            if tt > 0:
                C.wait(sync, b0["C"] + tt)
            L1X.inc(sync.dma_start(out=xtile, in_=self.i_x[tt * 128:(tt + 1) * 128, :]), 16)
            ldm[("x", tt)] = L1X.n
            for ec in range(ECH):
                g = tt * ECH + ec
                SP.wait(sync, b0["SP"] + g + 1)
                L1S.inc(sync.dma_start(
                    out=self.d_xh_in.ap()[128 * ec:128 * (ec + 1), 128 * tt:128 * (tt + 1)],
                    in_=self.ab(o_xh + ec * TOK + 128 * tt, 128)), 16)
                L1S.inc(sync.dma_start(
                    out=self.d_xl_in.ap()[128 * ec:128 * (ec + 1), 128 * tt:128 * (tt + 1)],
                    in_=self.ab(o_xl + ec * TOK + 128 * tt, 128)), 16)
        ldm["end"] = L1S.n

    @self.block.tensor
    def _(tensor):
        for tt in range(NT):
            C.wait(tensor, b0["C"] + tt + 1)
            for ec in range(ECH):
                g = tt * ECH + ec
                if g >= 2:
                    SP.wait(tensor, b0["SP"] + g - 1)
                T.inc(tensor.transpose(self.pb[g % 2][:, 0:128],
                                       xntile[:, 128 * ec:128 * (ec + 1)], self.c_idf[:]))

    @self.block.scalar
    def _(scalar):
        for tt in range(NT):
            A.wait(scalar, b0["A"] + tt + 1)
            B.inc(scalar.activation(out=self.c_col[:, 1:2], in_=self.c_col[:, 0:1],
                                    func=AF.Sqrt, bias=self.c_zero[:, 0:1]))

    @self.block.vector
    def _(vector):
        HZ = self.sem("HZV")
        for tt in range(NT):
            L1X.wait(vector, ldm[("x", tt)])
            for k in range(4):
                HZ.inc(vector.bn_stats(self.c_stats[:, 6 * k:6 * k + 6],
                                       xtile[:, 512 * k:512 * (k + 1)]))
            HZ.wait(vector, HZ.n)
            HZ.inc(vector.bn_aggr(self.c_stats[:, 24:26], self.c_stats[:, 0:24]))
            HZ.wait(vector, HZ.n)
            A.inc(vector.tensor_scalar(self.c_col[:, 0:1], self.c_stats[:, 25:26], 1e-5, None,
                                       op0=ALU.add))
            B.wait(vector, b0["B"] + tt + 1)
            HZ.inc(vector.reciprocal(self.c_col[:, 2:3], self.c_col[:, 1:2]))
            HZ.wait(vector, HZ.n)
            vector.tensor_scalar(xntile, xtile, self.c_stats[:, 24:25], None, op0=ALU.subtract)
            C.inc(vector.tensor_scalar(xntile, xntile, self.c_col[:, 2:3], None, op0=ALU.mult))
            for ec in range(ECH):
                g = tt * ECH + ec
                T.wait(vector, b0["T"] + g + 1)
                hi_ap = self.ab(o_xh + ec * TOK + 128 * tt, 128)
                lo_ap = self.ab(o_xl + ec * TOK + 128 * tt, 128)
                vector.tensor_copy(hi_ap, self.pb[g % 2][:, 0:128])
                SP.inc(vector.tensor_tensor(lo_ap, self.pb[g % 2][:, 0:128],
                                            hi_ap, op=ALU.subtract))

    @self.block.gpsimd
    def _(gpsimd):
        L1S.wait(gpsimd, ldm["end"])
        CC.inc(gpsimd.collective_compute(
            "AllGather", ALU.bypass, replica_groups=[list(range(NCORES))],
            ins=[self.d_xh_in.ap().opt()], outs=[self.d_xh.ap().opt()]))
        CC.inc(gpsimd.collective_compute(
            "AllGather", ALU.bypass, replica_groups=[list(range(NCORES))],
            ins=[self.d_xl_in.ap().opt()], outs=[self.d_xl.ap().opt()]))


MK.ph0 = ph0
MK.ph1 = ph1


def ph2(self, h):
    nc, S, TOK, NT, QT0 = self.nc, self.S, self.TOK, self.NT, self.QT0
    NSL = S // 256
    LW = [self.sem("LW0"), self.sem("LW1")]
    LSL, LAQ, LOS, LCT = (self.sem(x) for x in ("LSL", "LAQ", "LOS", "LCT"))
    P2W, P2M, P2E, P2V, P2T, P2VE = (self.sem(x) for x in
                                     ("P2W", "P2M", "P2E", "P2V", "P2T", "P2VE"))
    N1, N2, N3, N4, N5 = (self.sem(x) for x in ("P2N1", "P2N2", "P2N3", "P2N4", "P2N5"))
    Dsc, Dbias, Dexp, Dspl, Dpv = (self.sem(x) for x in
                                   ("D_sc", "D_bias", "D_exp", "D_spl", "D_pv"))
    Dbrow, Dbc, Dbtile, Drden, Drb, Dqtev = (self.sem(x) for x in
        ("D_brow", "D_bc", "D_btile", "D_rden", "D_rb", "D_qtev"))

    o_q, o_ql, o_kh, o_kl, o_vh, o_vl = 0, S, 2 * S, 3 * S, 4 * S, 5 * S
    o_sl = 6 * S
    o_w = 6 * S + 16384
    o_p = o_w + 12288
    o_vstg = o_p + 3072
    o_ostg = o_vstg + 512
    o_q2 = o_ostg + 1024
    o_msk = o_q2 + 512
    qh = lambda c0, w: self.ab(o_q + c0, w)
    ql = lambda c0, w: self.ab(o_ql + c0, w)
    kh = lambda c0, w: self.ab(o_kh + c0, w)
    kl = lambda c0, w: self.ab(o_kl + c0, w)
    vh = lambda c0, w: self.ab(o_vh + c0, w)
    vl = lambda c0, w: self.ab(o_vl + c0, w)
    slh = lambda b, ec, c0=0, w=256: self.ab(o_sl + b * 8192 + ec * 256 + c0, w)
    sll = lambda b, ec, c0=0, w=256: self.ab(o_sl + b * 8192 + 4096 + ec * 256 + c0, w)
    w_hi = lambda j, ec: self.ab(o_w + j * 4096 + ec * 128, 128)
    w_lo = lambda j, ec: self.ab(o_w + j * 4096 + 2048 + ec * 128, 128)
    p_hi = lambda g: self.ab(o_p + (g % 3) * 512, 512)
    p_lo = lambda g: self.ab(o_p + 1536 + (g % 3) * 512, 512)
    vstg_hi = self.ab(o_vstg, 256)
    vstg_lo = self.ab(o_vstg + 256, 256)
    ostg_hi = self.ab(o_ostg, 512)
    ostg_lo = self.ab(o_ostg + 512, 512)
    q2t = self.ab(o_q2, 512)
    mskt = lambda m: self.ab(o_msk + 512 * m, 512)
    tmp = lambda g: self.af(2048 + 512 * (g % 3), 512)
    ptil = lambda g: self.af(3584 + 512 * (g % 3), 512)
    bias_tile = self.af(5120, 512)
    comb = lambda m: self.af(5632 + 512 * m, 512)
    rb_tile = self.af(5120, 512)
    qtmp = self.af(0, 256)
    ktmp = self.af(256, 256)
    vtmp = self.af(512, 256)
    wstg = [self.af(1024, 128), self.af(1152, 128)]
    den_r = self.af(7168, 512, part0=0, nparts=1)
    rden_r = self.af(7680, 512, part0=0, nparts=1)
    alq_r = self.af(8192, 512, part0=0, nparts=1)
    brow_r = self.af(8704, 512, part0=0, nparts=1)

    units = [(qt, kc) for qt in range(QT0) for kc in range(4 * qt + 4)]
    qt_last_g = {qt: max(lu for lu, (q2_, _) in enumerate(units) if q2_ == qt)
                 for qt in range(QT0)}
    g0 = self.g0
    qt0 = self.qt0
    bW, bM, bE, bV, bT, bVE = P2W.n, P2M.n, P2E.n, P2V.n, P2T.n, P2VE.n
    bN1, bN2, bN3, bN4, bN5 = N1.n, N2.n, N3.n, N4.n, N5.n
    ldm = {}
    NJ = S // 512

    def slab_pieces(sl):
        out = []
        t0g = 256 * sl
        done = 0
        while done < 256:
            t = t0g + done
            r = t // TOK
            lc = t - r * TOK
            w = min(256 - done, TOK - lc)
            out.append((r, lc, w, done))
            done += w
        return out

    @self.block.sync
    def _(sync):
        if h == 0:
            LCT.inc(sync.dma_start(out=self.ab(o_msk, 2048), in_=self.i_msk), 16)
            ldm["msk"] = LCT.n
        for j, src in enumerate([self.i_wq, self.i_wk, self.i_wv]):
            for ec in range(ECH):
                idx = j * ECH + ec
                if idx >= 2:
                    P2W.wait(sync, bW + idx - 1)
                LW[idx % 2].inc(sync.dma_start(
                    out=wstg[idx % 2],
                    in_=src[128 * ec:128 * (ec + 1), 128 * h:128 * (h + 1)]), 16)
                ldm[("w", idx)] = LW[idx % 2].n
        for sl in range(NSL):
            P2M.wait(sync, bM + max(0, 3 * (sl - 1)))
            b = sl % 2
            for ec in range(ECH):
                for (r, lc, w, doff) in slab_pieces(sl):
                    LSL.inc(sync.dma_start(
                        out=slh(b, ec, doff, w),
                        in_=self.d_xh.ap()[2048 * r + 128 * ec:2048 * r + 128 * (ec + 1),
                                           lc:lc + w]), 16)
            for ec in range(ECH):
                for (r, lc, w, doff) in slab_pieces(sl):
                    LSL.inc(sync.dma_start(
                        out=sll(b, ec, doff, w),
                        in_=self.d_xl.ap()[2048 * r + 128 * ec:2048 * r + 128 * (ec + 1),
                                           lc:lc + w]), 16)
            ldm[("slab", sl)] = LSL.n
        for qt in range(QT0):
            qtg = qt0 + qt
            Dbrow.wait(sync, qtg)
            LAQ.inc(sync.dma_start(out=alq_r, in_=self.i_alq[h:h + 1, 512 * qt:512 * (qt + 1)]), 16)
            ldm[("alq", qt)] = LAQ.n
            Dqtev.wait(sync, qtg + 1)
            LOS.inc(sync.dma_start(
                out=self.d_ohh.ap()[128 * h:128 * (h + 1), 512 * qt:512 * (qt + 1)],
                in_=ostg_hi), 16)
            LOS.inc(sync.dma_start(
                out=self.d_ohl.ap()[128 * h:128 * (h + 1), 512 * qt:512 * (qt + 1)],
                in_=ostg_lo), 16)
            ldm[("ost", qt)] = LOS.n

    @self.block.tensor
    def _(tensor):
        SBANK = lambda g: self.pb[(0, 1, 6)[g % 3]][:]

        def pv_emit(qt, kc, g):
            qtg = qt0 + qt
            last = (kc == 4 * qt + 3)
            Dspl.wait(tensor, g + 1)
            if kc == 0:
                Dqtev.wait(tensor, qtg)
            tensor.matmul(self.pb[3][0:1, 0:512], self.c_onecb[:], p_hi(g),
                          start=(kc == 0), stop=False)
            tensor.matmul(self.pb[3][0:1, 0:512], self.c_onecb[:], p_lo(g),
                          start=False, stop=last)
            tensor.matmul(self.pb[2][:], vh(kc * 128, 128), p_hi(g), start=(kc == 0), stop=False)
            tensor.matmul(self.pb[2][:], vl(kc * 128, 128), p_hi(g), start=False, stop=False)
            Dpv.inc(tensor.matmul(self.pb[2][:], vh(kc * 128, 128), p_lo(g),
                                  start=False, stop=last))

        for sl in range(NSL):
            P2E.wait(tensor, bE + 3 * sl)
            LSL.wait(tensor, ldm[("slab", sl)])
            if sl == 0:
                P2W.wait(tensor, bW + 3 * ECH)
            b = sl % 2
            for dst, wj, scale_note in ((0, 0, "q"), (1, 1, "k"), (2, 2, "v")):
                last = None
                for ec in range(ECH):
                    last = self.mm3(tensor, self.pb[dst][:, 0:256],
                                    w_hi(wj, ec), w_lo(wj, ec), slh(b, ec), sll(b, ec),
                                    start=(ec == 0), stop=(ec == ECH - 1))
                P2M.inc(last)
            if sl >= 1:
                s = sl - 1
                P2V.wait(tensor, bV + s + 1)
                P2VE.wait(tensor, bVE + s)
                for jb in range(2):
                    P2T.inc(tensor.transpose(self.pt_b[:, 128 * jb:128 * (jb + 1)],
                                             self.ab(o_vstg + 128 * jb, 128),
                                             self.c_idb[:]))
                    P2T.inc(tensor.transpose(self.pt_b[:, 256 + 128 * jb:384 + 128 * jb],
                                             self.ab(o_vstg + 256 + 128 * jb, 128),
                                             self.c_idb[:]))
        s = NSL - 1
        P2V.wait(tensor, bV + s + 1)
        P2VE.wait(tensor, bVE + s)
        for jb in range(2):
            P2T.inc(tensor.transpose(self.pt_b[:, 128 * jb:128 * (jb + 1)],
                                     self.ab(o_vstg + 128 * jb, 128), self.c_idb[:]))
            P2T.inc(tensor.transpose(self.pt_b[:, 256 + 128 * jb:384 + 128 * jb],
                                     self.ab(o_vstg + 256 + 128 * jb, 128), self.c_idb[:]))
        for j in range(2 * NJ):
            N1.wait(tensor, bN1 + j + 1)
            N3.wait(tensor, bN3 + j)
            N2.inc(tensor.matmul(self.pb[3][0:1, 0:512], self.c_onecb[:], q2t,
                                 start=True, stop=True))
        P2E.wait(tensor, bE + 3 * NSL)
        N3.wait(tensor, bN3 + 2 * NJ)
        for lu, (qt, kc) in enumerate(units):
            g = g0 + lu
            qtg = qt0 + qt
            if kc == 0:
                Dbrow.wait(tensor, qtg + 1)
                Dbtile.wait(tensor, qtg)
                Dbc.inc(tensor.matmul(self.pb[4][:], self.c_onef[:], brow_r,
                                      start=True, stop=True))
            if lu == 0:
                Dbias.wait(tensor, g0)
                Dqtev.wait(tensor, qt0)
            Dbias.wait(tensor, max(0, g - 2))
            Dsc.inc(self.mm3(tensor, SBANK(g),
                             kh(kc * 128, 128), kl(kc * 128, 128),
                             qh(512 * qt, 512), ql(512 * qt, 512),
                             start=True, stop=True))
            if kc >= 1:
                pv_emit(qt, kc - 1, g - 1)
            if kc == 4 * qt + 3:
                pv_emit(qt, kc, g)
                Drden.wait(tensor, qtg + 1)
                Drb.inc(tensor.matmul(self.pb[5][:], self.c_onef[:], rden_r,
                                      start=True, stop=True))

    @self.block.scalar
    def _(scalar):
        for sl in range(NSL):
            if sl == 0:
                self.sem("P2B").wait(scalar, h + 1)
            P2M.wait(scalar, bM + 3 * sl + 1)
            P2E.inc(scalar.activation(out=qtmp, in_=self.pb[0][:, 0:256], func=AF.Identity,
                                      bias=self.c_col[:, 3:4], scale=INV_SQRT_D))
            P2M.wait(scalar, bM + 3 * sl + 2)
            P2E.inc(scalar.activation(out=ktmp, in_=self.pb[1][:, 0:256], func=AF.Identity,
                                      bias=self.c_bqkv[:, 3 * h + 1:3 * h + 2], scale=1.0))
            P2M.wait(scalar, bM + 3 * sl + 3)
            P2E.inc(scalar.activation(out=vtmp, in_=self.pb[2][:, 0:256], func=AF.Identity,
                                      bias=self.c_bqkv[:, 3 * h + 2:3 * h + 3], scale=1.0))
        N3.wait(scalar, bN3 + 2 * NJ)
        scalar.activation(out=self.c_sc[0:1, 3:4], in_=self.c_sc[0:1, 0:1], func=AF.Sqrt,
                          bias=self.c_zero[0:1, 0:1])
        N4.inc(scalar.activation(out=self.c_sc[0:1, 4:5], in_=self.c_sc[0:1, 2:3], func=AF.Sqrt,
                                 bias=self.c_zero[0:1, 0:1]))
        for lu, (qt, kc) in enumerate(units):
            g = g0 + lu
            Dbias.wait(scalar, g + 1)
            Dspl.wait(scalar, max(0, g - 2))
            Dexp.inc(scalar.activation(
                out=ptil(g), in_=tmp(g), func=AF.Exp,
                bias=self.c_alk[:, h * (S // 128) + kc:h * (S // 128) + kc + 1], scale=1.0))

    @self.block.vector
    def _(vector):
        HZ = self.sem("HZV")
        for j in range(3):
            for ec in range(ECH):
                idx = j * ECH + ec
                LW[idx % 2].wait(vector, ldm[("w", idx)])
                vector.tensor_copy(w_hi(j, ec), wstg[idx % 2])
                P2W.inc(vector.tensor_tensor(w_lo(j, ec), wstg[idx % 2], w_hi(j, ec),
                                             op=ALU.subtract))
        self.sem("P2B").inc(vector.tensor_scalar(self.c_col[:, 3:4],
                                                 self.c_bqkv[:, 3 * h:3 * h + 1],
                                                 INV_SQRT_D, None, op0=ALU.mult))
        for sl in range(NSL):
            P2E.wait(vector, bE + 3 * sl + 1)
            self.split_hilo(vector, qtmp, qh(256 * sl, 256), ql(256 * sl, 256))
            P2E.wait(vector, bE + 3 * sl + 2)
            self.split_hilo(vector, ktmp, kh(256 * sl, 256), kl(256 * sl, 256))
            P2E.wait(vector, bE + 3 * sl + 3)
            P2T.wait(vector, bT + 4 * sl)
            vector.tensor_copy(vstg_hi, vtmp)
            P2V.inc(vector.tensor_tensor(vstg_lo, vtmp, vstg_hi, op=ALU.subtract))
            if sl >= 1:
                s = sl - 1
                P2T.wait(vector, bT + 4 * (s + 1))
                last = None
                for jb in range(2):
                    vector.tensor_copy(vh((2 * s + jb) * 128, 128),
                                       self.pt_b[:, 128 * jb:128 * (jb + 1)])
                    last = vector.tensor_copy(vl((2 * s + jb) * 128, 128),
                                              self.pt_b[:, 256 + 128 * jb:384 + 128 * jb])
                P2VE.inc(last)
        s = NSL - 1
        P2T.wait(vector, bT + 4 * (s + 1))
        last = None
        for jb in range(2):
            vector.tensor_copy(vh((2 * s + jb) * 128, 128), self.pt_b[:, 128 * jb:128 * (jb + 1)])
            last = vector.tensor_copy(vl((2 * s + jb) * 128, 128),
                                      self.pt_b[:, 256 + 128 * jb:384 + 128 * jb])
        P2VE.inc(last)
        for j in range(2 * NJ):
            if j < NJ:
                src = qh(512 * j, 512)
            else:
                src = kh(512 * (j - NJ), 512)
            N1.inc(vector.tensor_tensor(q2t, src, src, op=ALU.mult))
            N2.wait(vector, bN2 + j + 1)
            slot = 0 if j < NJ else 2
            if j % NJ == 0:
                N3.inc(vector.reduce_max(self.c_sc[0:1, slot:slot + 1],
                                         self.pb[3][0:1, 0:512], axis=AXX))
            else:
                HZ.inc(vector.reduce_max(self.c_sc[0:1, 1:2], self.pb[3][0:1, 0:512], axis=AXX))
                HZ.wait(vector, HZ.n)
                N3.inc(vector.tensor_tensor(self.c_sc[0:1, slot:slot + 1],
                                            self.c_sc[0:1, slot:slot + 1],
                                            self.c_sc[0:1, 1:2], op=ALU.max))
            N3.wait(vector, bN3 + j + 1)
        N4.wait(vector, bN4 + 1)
        HZ.inc(vector.tensor_tensor(self.c_sc[0:1, 1:2], self.c_sc[0:1, 3:4],
                                    self.c_sc[0:1, 4:5], op=ALU.mult))
        HZ.wait(vector, HZ.n)
        N5.inc(vector.tensor_scalar(self.c_sc[0:1, 5:6], self.c_sc[0:1, 1:2], 1.01, 0.5,
                                    op0=ALU.mult, op1=ALU.add))
        for lu, (qt, kc) in enumerate(units):
            g = g0 + lu
            qtg = qt0 + qt
            if kc == 0:
                LAQ.wait(vector, ldm[("alq", qt)])
                if h == 0 and qt == 0:
                    LCT.wait(vector, ldm["msk"])
                N5.wait(vector, bN5 + 1)
                Dbrow.inc(vector.tensor_scalar(brow_r, alq_r, self.c_sc[0:1, 5:6], None,
                                               op0=ALU.subtract))
                Dbc.wait(vector, qtg + 1)
                vector.tensor_copy(bias_tile, self.pb[4][:])
                last = None
                for m in range(4):
                    last = vector.tensor_tensor(comb(m), bias_tile, mskt(m), op=ALU.add)
                Dbtile.inc(last)
            Dsc.wait(vector, g + 1)
            Dexp.wait(vector, max(0, g - 2))
            sel = comb(kc - 4 * qt) if kc >= 4 * qt else bias_tile
            Dbias.inc(vector.tensor_tensor(tmp(g), self.pb[(0, 1, 6)[g % 3]][:], sel, op=ALU.add))
            Dexp.wait(vector, g + 1)
            Dpv.wait(vector, max(0, g - 2))
            vector.tensor_copy(p_hi(g), ptil(g))
            Dspl.inc(vector.tensor_tensor(p_lo(g), ptil(g), p_hi(g), op=ALU.subtract))
            if kc == 4 * qt + 3:
                Dpv.wait(vector, g + 1)
                HZ.inc(vector.tensor_copy(den_r, self.pb[3][0:1, 0:512]))
                HZ.wait(vector, HZ.n)
                Drden.inc(vector.reciprocal(rden_r, den_r))
                Drb.wait(vector, qtg + 1)
                vector.tensor_copy(rb_tile, self.pb[5][:])
                if qtg >= 1:
                    LOS.wait(vector, ldm[("ost", qt - 1)] if qt >= 1 else self._prev_ost)
                vector.tensor_tensor(self.af(0, 512),
                                     self.pb[2][:], rb_tile, op=ALU.mult)
                vector.tensor_copy(ostg_hi, self.af(0, 512))
                Dqtev.inc(vector.tensor_tensor(ostg_lo, self.af(0, 512), ostg_hi,
                                               op=ALU.subtract))

    self._prev_ost = ldm.get(("ost", QT0 - 1), 0)
    self.g0 += len(units)
    self.qt0 += QT0


MK.ph2 = ph2


def ph3(self):
    nc, S, TOK = self.nc, self.S, self.TOK
    LW = [self.sem("LW0"), self.sem("LW1")]
    LOH, LP3 = self.sem("LOH"), self.sem("LP3")
    WM, WE, WW = self.sem("P3M"), self.sem("P3E"), self.sem("P3W")
    bM, bE, bW = WM.n, WE.n, WW.n
    o_wh, o_wl, o_oh = 0, 4096, 8192
    wo_h = lambda fc, c0, w: self.ab(o_wh + fc * 2048 + c0, w)
    wo_l = lambda fc, c0, w: self.ab(o_wl + fc * 2048 + c0, w)
    oh_hi = lambda fc: self.ab(o_oh + fc * 128, 128)
    oh_lo = lambda fc: self.ab(o_oh + 256 + fc * 128, 128)
    stage = self.af(0, 2048)
    wstg = [self.af(2048, 2048), self.af(4096, 2048)]
    ldm = {}
    NTT = S // 128

    @self.block.sync
    def _(sync):
        for fc in range(HC):
            if fc >= 2:
                WW.wait(sync, bW + fc - 1)
            LW[fc % 2].inc(sync.dma_start(out=wstg[fc % 2],
                                          in_=self.i_wo[128 * fc:128 * (fc + 1), :]), 16)
            ldm[("w", fc)] = LW[fc % 2].n
        for t in range(NTT):
            WM.wait(sync, bM + t)
            for fc in range(HC):
                LOH.inc(sync.dma_start(out=oh_hi(fc),
                                       in_=self.d_ohh.ap()[128 * fc:128 * (fc + 1),
                                                           128 * t:128 * (t + 1)]), 16)
                LOH.inc(sync.dma_start(out=oh_lo(fc),
                                       in_=self.d_ohl.ap()[128 * fc:128 * (fc + 1),
                                                           128 * t:128 * (t + 1)]), 16)
            ldm[("oh", t)] = LOH.n
            WE.wait(sync, bE + t + 1)
            ch, ro = (128 * t) // 1024, (128 * t) % 1024
            LP3.inc(sync.dma_start(out=self.d_part[ch].ap()[ro:ro + 128, 0:2048], in_=stage), 16)
            ldm[("st", t)] = LP3.n

    @self.block.tensor
    def _(tensor):
        for t in range(NTT):
            LOH.wait(tensor, ldm[("oh", t)])
            if t == 0:
                WW.wait(tensor, bW + HC)
            if t >= 1:
                WE.wait(tensor, bE + t)
            last = None
            for og in range(4):
                for fc in range(HC):
                    last = self.mm3(tensor, self.pb[og][:],
                                    oh_hi(fc), oh_lo(fc),
                                    wo_h(fc, 512 * og, 512), wo_l(fc, 512 * og, 512),
                                    start=(fc == 0), stop=(fc == HC - 1))
            WM.inc(last)

    @self.block.vector
    def _(vector):
        for fc in range(HC):
            LW[fc % 2].wait(vector, ldm[("w", fc)])
            vector.tensor_copy(self.ab(o_wh + fc * 2048, 2048), wstg[fc % 2])
            WW.inc(vector.tensor_tensor(self.ab(o_wl + fc * 2048, 2048), wstg[fc % 2],
                                        self.ab(o_wh + fc * 2048, 2048), op=ALU.subtract))
        for t in range(NTT):
            WM.wait(vector, bM + t + 1)
            if t >= 1:
                LP3.wait(vector, ldm[("st", t - 1)])
            for og in range(3):
                vector.tensor_copy(stage[:, 512 * og:512 * (og + 1)], self.pb[og][:])
            WE.inc(vector.tensor_copy(stage[:, 1536:2048], self.pb[3][:]))


MK.ph3 = ph3


def ph4(self):
    nc, S, TOK = self.nc, self.S, self.TOK
    NSL = S // 256
    NIC = ICP // 128
    CC = self.sem("CC")
    LDQ, LSL = self.sem("LDQ"), self.sem("LSL")
    LY = [self.sem("LY0"), self.sem("LY1")]
    LL = [self.sem("LL0"), self.sem("LL1")]
    LWT = [self.sem("LWT0"), self.sem("LWT1")]
    QX, QB, QW = self.sem("P4X"), self.sem("P4B"), self.sem("P4W")
    G, SI, HH, D2, Y = (self.sem(x) for x in ("P4G", "P4S", "P4H", "P4D2", "P4Y"))
    C1, C2, C3, C4, C5 = (self.sem(x) for x in ("P4C1", "P4C2", "P4C3", "P4C4", "P4C5"))
    o_gh, o_gl, o_dh, o_dl = 0, 24576, 49152, 61440
    o_slh, o_sll = 73728, 77824
    o_h = 81920
    gu_h = lambda ec, c0, w: self.ab(o_gh + ec * 1536 + c0, w)
    gu_l = lambda ec, c0, w: self.ab(o_gl + ec * 1536 + c0, w)
    wd_h = lambda ic, c0, w: self.ab(o_dh + ic * 2048 + c0, w)
    wd_l = lambda ic, c0, w: self.ab(o_dl + ic * 2048 + c0, w)
    slh = lambda ec, c0=0, w=256: self.ab(o_slh + ec * 256 + c0, w)
    sll = lambda ec, c0=0, w=256: self.ab(o_sll + ec * 256 + c0, w)
    h_hi = lambda ic, c0, w: self.ab(o_h + ic * 256 + c0, w)
    h_lo = lambda ic, c0, w: self.ab(o_h + 1536 + ic * 256 + c0, w)
    nibf = self.af(2048, 512)
    t1 = self.af(2560, 512)
    t2 = self.af(3072, 512)
    val = self.af(3584, 512)
    silu_t = self.af(4096, 256)
    srow = self.af(8192, 512, part0=0, nparts=2)
    stage = [self.af(0, 2048), self.af(6656, 2048)]
    ldm = {}
    dq = [("gu", ec, ot) for ec in range(ECH) for ot in range(3)] + \
         [("dw", ic, eg) for ic in range(NIC) for eg in range(4)]

    def slab_pieces(sl):
        out, done = [], 0
        while done < 256:
            t = 256 * sl + done
            r, lc = t // TOK, t % TOK
            w = min(256 - done, TOK - lc)
            out.append((r, lc, w, done))
            done += w
        return out

    @self.block.sync
    def _(sync):
        for u, (kind, a, bb) in enumerate(dq):
            QX.wait(sync, u)
            src = self.i_gup if kind == "gu" else self.i_dwp
            LDQ.inc(sync.dma_start(out=self.c_i32a[:],
                                   in_=src[128 * a:128 * (a + 1), 512 * bb:512 * (bb + 1)]), 16)
            if kind == "gu":
                LDQ.inc(sync.dma_start(out=self.c_f16[0:1, 0:512],
                                       in_=self.i_gus[a:a + 1, 512 * bb:512 * (bb + 1)]), 16)
            else:
                LDQ.inc(sync.dma_start(out=self.c_f16[0:2, 0:512],
                                       in_=self.i_dws[2 * a:2 * a + 2, 512 * bb:512 * (bb + 1)]), 16)
            ldm[("dq", u)] = LDQ.n
        for sl in range(NSL):
            G.wait(sync, 6 * sl)
            for ec in range(ECH):
                for (r, lc, w, doff) in slab_pieces(sl):
                    LSL.inc(sync.dma_start(out=slh(ec, doff, w),
                                           in_=self.d_xh.ap()[2048 * r + 128 * ec:
                                                              2048 * r + 128 * (ec + 1),
                                                              lc:lc + w]), 16)
            for ec in range(ECH):
                for (r, lc, w, doff) in slab_pieces(sl):
                    LSL.inc(sync.dma_start(out=sll(ec, doff, w),
                                           in_=self.d_xl.ap()[2048 * r + 128 * ec:
                                                              2048 * r + 128 * (ec + 1),
                                                              lc:lc + w]), 16)
            ldm[("slab", sl)] = LSL.n
            for tsub in range(2):
                ts_idx = 2 * sl + tsub
                Y.wait(sync, 8 * sl + 4 * tsub + 4)
                tg = 256 * sl + 128 * tsub
                ch, ro = tg // 1024, tg % 1024
                LY[ts_idx % 2].inc(sync.dma_start(
                    out=self.d_part[ch].ap()[ro:ro + 128, 2048:4096],
                    in_=stage[tsub % 2]), 16)
                ldm[("st", ts_idx)] = LY[ts_idx % 2].n
                ldm[("rs", ts_idx)] = (self.sem("LY0").n, self.sem("LY1").n)
        for ec in range(ECH):
            if ec < 2:
                LY[0].wait(sync, self.sem("LY0").n)
                LY[1].wait(sync, self.sem("LY1").n)
            else:
                C1.wait(sync, ec - 1)
            LL[ec % 2].inc(sync.dma_start(out=stage[ec % 2],
                                          in_=self.i_wl1[128 * ec:128 * (ec + 1), :]), 16)
            ldm[("l1w", ec)] = LL[ec % 2].n
        for ec in range(ECH):
            u = ECH + ec
            if ec < 2:
                C1.wait(sync, ECH)
            else:
                C5.wait(sync, ec - 1)
            LL[u % 2].inc(sync.dma_start(out=stage[u % 2],
                                         in_=self.i_wl1[128 * ec:128 * (ec + 1), :]), 16)
            ldm[("l1w", u)] = LL[u % 2].n
            C5.wait(sync, ec + 1)
            LWT[ec % 2].inc(sync.dma_start(out=self.d_wt.ap()[128 * ec:128 * (ec + 1), :],
                                           in_=self.ab(73728 + 2048 * (ec % 2), 2048)), 16)
            ldm[("wt", ec)] = LWT[ec % 2].n

    @self.block.tensor
    def _(tensor):
        for u, (kind, a, bb) in enumerate(dq):
            LDQ.wait(tensor, ldm[("dq", u)])
            QX.wait(tensor, u + 1)
            QW.wait(tensor, u)
            if kind == "gu":
                QB.inc(tensor.matmul(self.pb[4][:], self.c_onef[:], srow[0:1, 0:512],
                                     start=True, stop=True))
            else:
                QB.inc(tensor.matmul(self.pb[4][:], self.c_dsel[:], srow[0:2, 0:512],
                                     start=True, stop=True))
        for sl in range(NSL):
            LSL.wait(tensor, ldm[("slab", sl)])
            if sl == 0:
                QW.wait(tensor, len(dq))
            for i in range(6):
                p = 6 * sl + i
                pg, pu = (0, 1) if i % 2 == 0 else (5, 6)
                HH.wait(tensor, max(0, p - 1))
                SI.wait(tensor, max(0, p - 1))
                for ec in range(ECH):
                    self.mm3(tensor, self.pb[pg][:, 0:256],
                             gu_h(ec, 128 * i, 128), gu_l(ec, 128 * i, 128),
                             slh(ec), sll(ec), start=(ec == 0), stop=(ec == ECH - 1))
                last = None
                for ec in range(ECH):
                    last = self.mm3(tensor, self.pb[pu][:, 0:256],
                                    gu_h(ec, 768 + 128 * i, 128), gu_l(ec, 768 + 128 * i, 128),
                                    slh(ec), sll(ec), start=(ec == 0), stop=(ec == ECH - 1))
                G.inc(last)
            HH.wait(tensor, 6 * (sl + 1))
            for tsub in range(2):
                for eg in range(4):
                    d = 8 * sl + 4 * tsub + eg
                    Y.wait(tensor, max(0, d - 1))
                    last = None
                    for ic in range(NIC):
                        last = self.mm3(tensor, self.pb[2 + d % 2][:],
                                        h_hi(ic, 128 * tsub, 128), h_lo(ic, 128 * tsub, 128),
                                        wd_h(ic, 512 * eg, 512), wd_l(ic, 512 * eg, 512),
                                        start=(ic == 0), stop=(ic == NIC - 1))
                    D2.inc(last)
        HH.wait(tensor, 6 * NSL)
        C1.wait(tensor, ECH)
        C2.inc(tensor.matmul(self.pb[6][0:1, 0:1], self.c_col[:, 5:6], self.c_onecf[:],
                             start=True, stop=True))
        C4.wait(tensor, 1)
        tensor.matmul(self.pb[6][:, 1:2], self.c_onef[:], self.c_sc[0:1, 10:11],
                      start=True, stop=True)
        tensor.matmul(self.pb[6][:, 2:3], self.c_onef[:], self.c_sc[0:1, 11:12],
                      start=True, stop=True)
        C3.inc(tensor.matmul(self.pb[6][:, 3:4], self.c_onef[:], self.c_sc[0:1, 8:9],
                             start=True, stop=True))

    @self.block.scalar
    def _(scalar):
        for sl in range(NSL):
            for i in range(6):
                p = 6 * sl + i
                pg = 0 if i % 2 == 0 else 5
                G.wait(scalar, p + 1)
                if p >= 1:
                    HH.wait(scalar, p)
                SI.inc(scalar.activation(out=silu_t, in_=self.pb[pg][:, 0:256], func=AF.Silu,
                                         bias=self.c_zero[:, 0:1]))

    @self.block.vector
    def _(vector):
        HZ = self.sem("HZV")
        for u, (kind, a, bb) in enumerate(dq):
            LDQ.wait(vector, ldm[("dq", u)])
            nparts = 1 if kind == "gu" else 2
            vector.tensor_copy(srow[0:nparts, :], self.c_f16[0:nparts, 0:512])
            QX.inc(vector.tensor_scalar(self.c_i32b[:], self.c_i32a[:],
                                        self.c_nmask[:, 0:1], None,
                                        op0=ALU.bitwise_and))
            vector.tensor_copy(nibf, self.c_i32b[:])
            vector.tensor_scalar(nibf, nibf, self.c_ninv[:, 0:1], None, op0=ALU.mult)
            vector.tensor_scalar(t1, nibf, 8.0, None, op0=ALU.is_ge)
            vector.scalar_tensor_tensor(t2, t1, -8.0, nibf, op0=ALU.mult, op1=ALU.add)
            vector.tensor_scalar(val, t2, 4.0, 0.5, op0=ALU.min, op1=ALU.mult)
            vector.tensor_scalar(nibf, t2, 4.0, 0.0, op0=ALU.subtract, op1=ALU.max)
            vector.scalar_tensor_tensor(nibf, t2, 7.0, nibf, op0=ALU.is_ge, op1=ALU.add)
            vector.tensor_tensor(val, val, nibf, op=ALU.add)
            vector.tensor_scalar(t1, t1, -2.0, 1.0, op0=ALU.mult, op1=ALU.add)
            vector.tensor_tensor(val, val, t1, op=ALU.mult)
            QB.wait(vector, u + 1)
            vector.tensor_tensor(val, val, self.pb[4][:], op=ALU.mult)
            if kind == "gu":
                dh, dl = gu_h(a, 512 * bb, 512), gu_l(a, 512 * bb, 512)
            else:
                dh, dl = wd_h(a, 512 * bb, 512), wd_l(a, 512 * bb, 512)
            vector.tensor_copy(dh, val)
            QW.inc(vector.tensor_tensor(dl, val, dh, op=ALU.subtract))
        for sl in range(NSL):
            for i in range(6):
                p = 6 * sl + i
                pu = 1 if i % 2 == 0 else 6
                SI.wait(vector, p + 1)
                if sl >= 1 and i == 0:
                    D2.wait(vector, 8 * sl)
                vector.tensor_tensor(silu_t, silu_t, self.pb[pu][:, 0:256], op=ALU.mult)
                vector.tensor_copy(h_hi(i, 0, 256), silu_t)
                HH.inc(vector.tensor_tensor(h_lo(i, 0, 256), silu_t, h_hi(i, 0, 256),
                                            op=ALU.subtract))
            for tsub in range(2):
                for eg in range(4):
                    d = 8 * sl + 4 * tsub + eg
                    D2.wait(vector, d + 1)
                    if 2 * sl + tsub >= 2 and eg == 0:
                        LY[(2 * sl + tsub) % 2].wait(vector, ldm[("st", 2 * sl + tsub - 2)])
                    Y.inc(vector.tensor_copy(stage[tsub % 2][:, 512 * eg:512 * (eg + 1)],
                                              self.pb[2 + d % 2][:]))
        for ec in range(ECH):
            LL[ec % 2].wait(vector, ldm[("l1w", ec)])
            if ec == 0:
                C1.inc(vector.reduce_sum(self.c_col[:, 5:6], stage[0], axis=AXX,
                                         apply_absolute_value=True))
            else:
                HZ.inc(vector.reduce_sum(self.c_col[:, 4:5], stage[ec % 2], axis=AXX,
                                         apply_absolute_value=True))
                HZ.wait(vector, HZ.n)
                C1.inc(vector.tensor_tensor(self.c_col[:, 5:6], self.c_col[:, 5:6],
                                            self.c_col[:, 4:5], op=ALU.add))
            C1.wait(vector, ec + 1)
        C2.wait(vector, 1)
        HZ.inc(vector.tensor_scalar(self.c_sc[0:1, 8:9], self.pb[6][0:1, 0:1],
                                    1.0 / (E * E), None, op0=ALU.mult))
        HZ.wait(vector, HZ.n)
        HZ.inc(vector.tensor_scalar(self.c_sc[0:1, 10:11], self.c_sc[0:1, 8:9], 1e-10, 0.5,
                                    op0=ALU.add, op1=ALU.mult))
        HZ.wait(vector, HZ.n)
        C4.inc(vector.tensor_scalar(self.c_sc[0:1, 11:12], self.c_sc[0:1, 10:11],
                                    -1.0, None, op0=ALU.mult))
        C3.wait(vector, 1)
        vector.tensor_copy(self.c_col[:, 4:5], self.pb[6][:, 1:2])
        vector.tensor_copy(self.c_col[:, 5:6], self.pb[6][:, 2:3])
        HZ.inc(vector.tensor_copy(self.c_col[:, 6:7], self.pb[6][:, 3:4]))
        HZ.wait(vector, HZ.n)
        for ec in range(ECH):
            u = ECH + ec
            LL[u % 2].wait(vector, ldm[("l1w", u)])
            if ec >= 2:
                LWT[ec % 2].wait(vector, ldm[("wt", ec - 2)])
            vector.tensor_scalar(self.af(2048, 2048), stage[u % 2], self.c_col[:, 4:5],
                                 None, op0=ALU.is_gt)
            vector.tensor_scalar(self.af(4096, 2048), stage[u % 2], self.c_col[:, 5:6],
                                 None, op0=ALU.is_lt)
            C5.inc(vector.tensor_tensor(self.ab(73728 + 2048 * (ec % 2), 2048),
                                        self.af(2048, 2048), self.af(4096, 2048),
                                        op=ALU.subtract))

    @self.block.gpsimd
    def _(gpsimd):
        for ch in range(self.NCH):
            last_t128 = (1024 * (ch + 1) - 128) // 128
            n0, n1 = ldm[("rs", last_t128)]
            if n0:
                self.sem("LY0").wait(gpsimd, n0)
            if n1:
                self.sem("LY1").wait(gpsimd, n1)
            CC.inc(gpsimd.collective_compute(
                "ReduceScatter", ALU.add, replica_groups=[list(range(NCORES))],
                ins=[self.d_part[ch].ap().opt()], outs=[self.d_rs[ch].ap().opt()]))


MK.ph4 = ph4


def ph5(self):
    nc, S, TOK, NT = self.nc, self.S, self.TOK, self.NT
    L5, L5S, L5P, LCT = self.sem("L5"), self.sem("L5S"), self.sem("L5P"), self.sem("LCT")
    R1, R2, R3, R4, R5 = (self.sem(x) for x in ("P5R1", "P5R2", "P5R3", "P5R4", "P5R5"))
    TT, SP = self.sem("P5T"), self.sem("P5S")
    BO = self.sem("P5BO")
    buf1 = self.af(0, 2048)
    buf2 = self.af(2048, 2048)
    bo_bc = self.af(4096, 2048)
    bo_row = self.af(6656, 2048, part0=0, nparts=1)
    o_xh, o_xl = 0, ECH * TOK
    ldm = {}

    @self.block.sync
    def _(sync):
        LCT.inc(sync.dma_start(out=bo_row, in_=self.i_bo), 16)
        ldm["bo"] = LCT.n
        for tt in range(NT):
            if tt > 0:
                R4.wait(sync, tt)
                TT.wait(sync, ECH * tt)
                L5P.wait(sync, ldm[("spw", tt - 1)])
            L5.inc(sync.dma_start(out=buf1, in_=self.d_rs[tt].ap()[:, 2048:4096]), 16)
            L5.inc(sync.dma_start(out=buf2, in_=self.d_rs[tt].ap()[:, 0:2048]), 16)
            ldm[("y", tt)] = L5.n
            R1.wait(sync, tt + 1)
            L5.inc(sync.dma_start(out=buf2, in_=self.i_x2[128 * tt:128 * (tt + 1), :]), 16)
            ldm[("x", tt)] = L5.n
            R2.wait(sync, tt + 1)
            L5P.inc(sync.dma_start(out=self.d_x1.ap()[128 * tt:128 * (tt + 1), :], in_=buf1), 16)
            ldm[("spw", tt)] = L5P.n
            for ec in range(ECH):
                g = tt * ECH + ec
                SP.wait(sync, g + 1)
                L5S.inc(sync.dma_start(
                    out=self.d_x1h.ap()[128 * ec:128 * (ec + 1), 128 * tt:128 * (tt + 1)],
                    in_=self.ab(o_xh + ec * TOK + 128 * tt, 128)), 16)
                L5S.inc(sync.dma_start(
                    out=self.d_x1l.ap()[128 * ec:128 * (ec + 1), 128 * tt:128 * (tt + 1)],
                    in_=self.ab(o_xl + ec * TOK + 128 * tt, 128)), 16)

    @self.block.tensor
    def _(tensor):
        LCT.wait(tensor, ldm["bo"])
        last = None
        for og in range(4):
            last = tensor.matmul(self.pb[og][:], self.c_onef[:],
                                 bo_row[0:1, 512 * og:512 * (og + 1)], start=True, stop=True)
        BO.inc(last)
        for tt in range(NT):
            R4.wait(tensor, tt + 1)
            for ec in range(ECH):
                g = tt * ECH + ec
                if g >= 2:
                    SP.wait(tensor, g - 1)
                TT.inc(tensor.transpose(self.pb[4 + g % 2][:, 0:128],
                                        buf2[:, 128 * ec:128 * (ec + 1)], self.c_idf[:]))

    @self.block.scalar
    def _(scalar):
        for tt in range(NT):
            R3.wait(scalar, 2 * tt + 1)
            R3.inc(scalar.activation(out=self.c_col[:, 1:2], in_=self.c_col[:, 0:1],
                                     func=AF.Sqrt, bias=self.c_zero[:, 0:1]))
            R5.wait(scalar, 2 * tt + 1)
            R5.inc(scalar.activation(out=self.c_col[:, 4:5], in_=self.c_col[:, 3:4],
                                     func=AF.Sqrt, bias=self.c_zero[:, 0:1]))

    @self.block.vector
    def _(vector):
        HZ = self.sem("HZV")
        BO.wait(vector, 1)
        for og in range(4):
            vector.tensor_copy(bo_bc[:, 512 * og:512 * (og + 1)], self.pb[og][:])
        for tt in range(NT):
            L5.wait(vector, ldm[("y", tt)])
            vector.tensor_tensor(self.af(8192, 512), buf1[:, 0:512], buf1[:, 0:512], op=ALU.mult)
            HZ.inc(vector.reduce_sum(self.c_col[:, 0:1], self.af(8192, 512), axis=AXX))
            for k in range(1, 4):
                vector.tensor_tensor(self.af(8192, 512), buf1[:, 512 * k:512 * (k + 1)],
                                     buf1[:, 512 * k:512 * (k + 1)], op=ALU.mult)
                HZ.inc(vector.reduce_sum(self.c_col[:, 5:6], self.af(8192, 512), axis=AXX))
                HZ.wait(vector, HZ.n)
                HZ.inc(vector.tensor_tensor(self.c_col[:, 0:1], self.c_col[:, 0:1],
                                            self.c_col[:, 5:6], op=ALU.add))
                HZ.wait(vector, HZ.n)
            R3.inc(vector.tensor_scalar(self.c_col[:, 0:1], self.c_col[:, 0:1],
                                        1.0 / E, 1e-6, op0=ALU.mult, op1=ALU.add))
            R3.wait(vector, 2 * (tt + 1))
            HZ.inc(vector.reciprocal(self.c_col[:, 2:3], self.c_col[:, 1:2]))
            HZ.wait(vector, HZ.n)
            vector.tensor_scalar(buf1, buf1, self.c_col[:, 2:3], None, op0=ALU.mult)
            vector.tensor_tensor(buf1, buf1, buf2, op=ALU.add)
            R1.inc(vector.tensor_tensor(buf1, buf1, bo_bc, op=ALU.add))
            L5.wait(vector, ldm[("x", tt)])
            R2.inc(vector.tensor_tensor(buf1, buf1, buf2, op=ALU.add))
            for k in range(4):
                HZ.inc(vector.bn_stats(self.c_stats[:, 6 * k:6 * k + 6],
                                       buf1[:, 512 * k:512 * (k + 1)]))
            HZ.wait(vector, HZ.n)
            HZ.inc(vector.bn_aggr(self.c_stats[:, 24:26], self.c_stats[:, 0:24]))
            HZ.wait(vector, HZ.n)
            R5.inc(vector.tensor_scalar(self.c_col[:, 3:4], self.c_stats[:, 25:26], 1e-5, None,
                                        op0=ALU.add))
            R5.wait(vector, 2 * (tt + 1))
            HZ.inc(vector.reciprocal(self.c_col[:, 5:6], self.c_col[:, 4:5]))
            HZ.wait(vector, HZ.n)
            vector.tensor_scalar(buf2, buf1, self.c_stats[:, 24:25], None, op0=ALU.subtract)
            R4.inc(vector.tensor_scalar(buf2, buf2, self.c_col[:, 5:6], None, op0=ALU.mult))
            for ec in range(ECH):
                g = tt * ECH + ec
                TT.wait(vector, g + 1)
                hi_ap = self.ab(o_xh + ec * TOK + 128 * tt, 128)
                lo_ap = self.ab(o_xl + ec * TOK + 128 * tt, 128)
                vector.tensor_copy(hi_ap, self.pb[4 + g % 2][:, 0:128])
                SP.inc(vector.tensor_tensor(lo_ap, self.pb[4 + g % 2][:, 0:128],
                                            hi_ap, op=ALU.subtract))


MK.ph5 = ph5


def ph6(self):
    nc, S, TOK, NT = self.nc, self.S, self.TOK, self.NT
    CC = self.sem("CC")
    LCT = self.sem("LCT")
    LW = [self.sem("LW0"), self.sem("LW1")]
    LQ = [self.sem("LQ0"), self.sem("LQ1")]
    L6KV = self.sem("L6KV")
    W6, M6, E6, Q6 = (self.sem(x) for x in ("P6W", "P6M", "P6E", "P6Q"))
    o_xh, o_xl = 0, ECH * TOK
    o_wah, o_wal = 2 * ECH * TOK, 2 * ECH * TOK + 8192
    wstg = [self.af(0, 512), self.af(512, 512)]
    cos_t = lambda tt: self.af(2048 + 128 * tt, 128)
    sin_t = lambda tt: self.af(2048 + 128 * NT + 128 * tt, 128)
    qstage = [self.af(4096, 512), self.af(5888, 512)]
    qa = [self.af(4864, 512), self.af(6400, 512)]
    rot = self.af(4608, 128)
    kst = lambda tt: self.af(6912 + 128 * tt, 128)
    vst = lambda tt: self.af(7936 + 128 * tt, 128)
    ldm = {}
    ogs = [4, 0, 1, 2, 3]

    @self.block.sync
    def _(sync):
        for tt in range(NT):
            LCT.inc(sync.dma_start(out=cos_t(tt), in_=self.i_cos[128 * tt:128 * (tt + 1), :]), 16)
            LCT.inc(sync.dma_start(out=sin_t(tt), in_=self.i_sin[128 * tt:128 * (tt + 1), :]), 16)
        ldm["cs"] = LCT.n
        for oi, og in enumerate(ogs):
            wdt = 256 if og == 4 else 512
            for ec in range(ECH):
                idx = oi * ECH + ec
                if idx >= 2:
                    W6.wait(sync, idx - 1)
                LW[idx % 2].inc(sync.dma_start(out=wstg[idx % 2][:, 0:wdt],
                                               in_=self.i_wa[128 * ec:128 * (ec + 1),
                                                             512 * og:512 * og + wdt]), 16)
                ldm[("w", idx)] = LW[idx % 2].n
            for tt in range(NT):
                idx = oi * NT + tt
                Q6.wait(sync, idx + 1)
                if og == 4:
                    L6KV.inc(sync.dma_start(out=self.d_k1in.ap()[128 * tt:128 * (tt + 1), :],
                                            in_=kst(tt)), 16)
                    L6KV.inc(sync.dma_start(out=self.d_v1in.ap()[128 * tt:128 * (tt + 1), :],
                                            in_=vst(tt)), 16)
                else:
                    LQ[idx % 2].inc(sync.dma_start(
                        out=self.d_q1.ap()[128 * tt:128 * (tt + 1), 512 * og:512 * (og + 1)],
                        in_=qa[idx % 2]), 16)
                    ldm[("st", idx)] = LQ[idx % 2].n
        ldm["kv_end"] = L6KV.n

    @self.block.tensor
    def _(tensor):
        for oi, og in enumerate(ogs):
            wdt = 256 if og == 4 else 512
            W6.wait(tensor, ECH * (oi + 1))
            for tt in range(NT):
                idx = oi * NT + tt
                if idx >= 2:
                    E6.wait(tensor, idx - 1)
                last = None
                for ec in range(ECH):
                    last = self.mm3(tensor, self.pb[idx % 2][:, 0:wdt],
                                    self.ab(o_xh + ec * TOK + 128 * tt, 128),
                                    self.ab(o_xl + ec * TOK + 128 * tt, 128),
                                    self.ab(o_wah + ec * 512, wdt),
                                    self.ab(o_wal + ec * 512, wdt),
                                    start=(ec == 0), stop=(ec == ECH - 1))
                M6.inc(last)

    @self.block.scalar
    def _(scalar):
        for oi, og in enumerate(ogs):
            wdt = 256 if og == 4 else 512
            sc = 1.0 if og == 4 else INV_SQRT_D
            for tt in range(NT):
                idx = oi * NT + tt
                M6.wait(scalar, idx + 1)
                if idx >= 2:
                    Q6.wait(scalar, idx - 1)
                E6.inc(scalar.activation(out=qstage[idx % 2][:, 0:wdt],
                                         in_=self.pb[idx % 2][:, 0:wdt],
                                         func=AF.Identity, bias=self.c_zero[:, 0:1], scale=sc))

    @self.block.vector
    def _(vector):
        for oi, og in enumerate(ogs):
            for ec in range(ECH):
                idx = oi * ECH + ec
                wdt = 256 if og == 4 else 512
                LW[idx % 2].wait(vector, ldm[("w", idx)])
                vector.tensor_copy(self.ab(o_wah + ec * 512, wdt), wstg[idx % 2][:, 0:wdt])
                W6.inc(vector.tensor_tensor(self.ab(o_wal + ec * 512, wdt),
                                            wstg[idx % 2][:, 0:wdt],
                                            self.ab(o_wah + ec * 512, wdt), op=ALU.subtract))
            HZ = self.sem("HZV")
            for tt in range(NT):
                idx = oi * NT + tt
                E6.wait(vector, idx + 1)
                if oi == 0 and tt == 0:
                    LCT.wait(vector, ldm["cs"])
                src = qstage[idx % 2]
                if og == 4:
                    vector.tensor_tensor(kst(tt), src[:, 0:128], cos_t(tt), op=ALU.mult)
                    vector.tensor_scalar(rot[:, 0:64], src[:, 64:128], -1.0, None, op0=ALU.mult)
                    HZ.inc(vector.tensor_copy(rot[:, 64:128], src[:, 0:64]))
                    HZ.wait(vector, HZ.n)
                    HZ.inc(vector.tensor_tensor(rot, rot, sin_t(tt), op=ALU.mult))
                    HZ.wait(vector, HZ.n)
                    vector.tensor_tensor(kst(tt), kst(tt), rot, op=ALU.add)
                    Q6.inc(vector.tensor_copy(vst(tt), src[:, 128:256]))
                else:
                    if idx >= 2 and ("st", idx - 2) in ldm:
                        LQ[idx % 2].wait(vector, ldm[("st", idx - 2)])
                    dst = qa[idx % 2]
                    last = None
                    for j in range(4):
                        js = slice(128 * j, 128 * (j + 1))
                        vector.tensor_tensor(dst[:, js], src[:, js], cos_t(tt), op=ALU.mult)
                        vector.tensor_scalar(rot[:, 0:64], src[:, 128 * j + 64:128 * (j + 1)],
                                             -1.0, None, op0=ALU.mult)
                        HZ.inc(vector.tensor_copy(rot[:, 64:128], src[:, 128 * j:128 * j + 64]))
                        HZ.wait(vector, HZ.n)
                        HZ.inc(vector.tensor_tensor(rot, rot, sin_t(tt), op=ALU.mult))
                        HZ.wait(vector, HZ.n)
                        last = vector.tensor_tensor(dst[:, js], dst[:, js], rot, op=ALU.add)
                    Q6.inc(last)

    @self.block.gpsimd
    def _(gpsimd):
        L6KV.wait(gpsimd, ldm["kv_end"])
        CC.inc(gpsimd.collective_compute(
            "AllGather", ALU.bypass, replica_groups=[list(range(NCORES))],
            ins=[self.d_k1in.ap().opt()], outs=[self.d_k1.ap().opt()]))
        CC.inc(gpsimd.collective_compute(
            "AllGather", ALU.bypass, replica_groups=[list(range(NCORES))],
            ins=[self.d_v1in.ap().opt()], outs=[self.d_v1.ap().opt()]))


MK.ph6 = ph6



def ph7(self):
    nc, S, TOK, NT = self.nc, self.S, self.TOK, self.NT
    QT1, QW1, KC = self.QT1, self.QW1, self.KC
    L7K, L7Q, L7S = self.sem("L7K"), self.sem("L7Q"), self.sem("L7S")
    KT, KE, VE = (self.sem(x) for x in ("P7KT", "P7KE", "P7VE"))
    KN1, KN2, KN3, KM, KSQ = (self.sem(x) for x in ("P7KN1", "P7KN2", "P7KN3", "P7KM", "P7KSQ"))
    QP, QS = self.sem("P7QP"), self.sem("P7QS")
    QN1, QN2, QN3, QM, QSQ = (self.sem(x) for x in ("P7QN1", "P7QN2", "P7QN3", "P7QM", "P7QSQ"))
    VM, TM, VC = (self.sem(x) for x in ("P7VM", "P7TM", "P7VC"))
    Asc, Aexp, Aspl, Apv = (self.sem(x) for x in ("A_sc", "A_exp", "A_spl", "A_pv"))
    Arden, Arb, Aqtev = (self.sem(x) for x in ("A_rden", "A_rb", "A_qtev"))
    o_kth, o_ktl, o_vh, o_vl = 0, S, 2 * S, 3 * S
    o_qh, o_ql = 4 * S, 4 * S + TOK
    o_p = 4 * S + 2 * TOK
    o_ostg = o_p + 3072
    o_q2 = o_ostg + 1024
    kth = lambda c0, w: self.ab(o_kth + c0, w)
    ktl = lambda c0, w: self.ab(o_ktl + c0, w)
    vh = lambda c0, w: self.ab(o_vh + c0, w)
    vl = lambda c0, w: self.ab(o_vl + c0, w)
    qhh = lambda c0, w: self.ab(o_qh + c0, w)
    qll = lambda c0, w: self.ab(o_ql + c0, w)
    p_hi = lambda g: self.ab(o_p + (g % 3) * 512, 512)
    p_lo = lambda g: self.ab(o_p + 1536 + (g % 3) * 512, 512)
    ostg_hi = self.ab(o_ostg, 512)
    ostg_lo = self.ab(o_ostg + 512, 512)
    q2t = self.ab(o_q2, 512)
    kstg = self.af(0, 128)
    qstg = self.af(128, 128)
    ptil = lambda g: self.af(3584 + 512 * (g % 3), 512)
    ostg_f = self.af(5120, 512)
    rb_tile = self.af(6656, 512)
    den_r = self.af(7168, 512, part0=0, nparts=1)
    rden_r = self.af(7680, 512, part0=0, nparts=1)
    negM = self.c_col[:, 3:4]
    ldm = {}
    NKB = S // 128
    NJK = S // 512
    NJ1 = max(1, TOK // 512)
    UPH = QT1 * KC
    g1b = self.g1

    def gidx(h, qt, kc):
        return g1b + h * UPH + qt * KC + kc

    @self.block.sync
    def _(sync):
        for kb in range(NKB):
            KT.wait(sync, kb)
            L7K.inc(sync.dma_start(out=kstg, in_=self.d_k1.ap()[128 * kb:128 * (kb + 1), :]), 16)
            ldm[("k", kb)] = L7K.n
        for kb in range(NKB):
            if kb == 0:
                KT.wait(sync, NKB)
            else:
                VE.wait(sync, kb)
            L7K.inc(sync.dma_start(out=kstg, in_=self.d_v1.ap()[128 * kb:128 * (kb + 1), :]), 16)
            ldm[("v", kb)] = L7K.n
        for h in range(H):
            for tt in range(NT):
                u = h * NT + tt
                QP.wait(sync, u)
                L7Q.inc(sync.dma_start(out=qstg,
                                       in_=self.d_q1.ap()[128 * tt:128 * (tt + 1),
                                                          128 * h:128 * (h + 1)]), 16)
                ldm[("q", u)] = L7Q.n
            for qt in range(QT1):
                qtg = h * QT1 + qt
                Aqtev.wait(sync, qtg + 1)
                L7S.inc(sync.dma_start(
                    out=self.d_o1h.ap()[128 * h:128 * (h + 1), QW1 * qt:QW1 * (qt + 1)],
                    in_=ostg_hi[:, 0:QW1]), 16)
                L7S.inc(sync.dma_start(
                    out=self.d_o1l.ap()[128 * h:128 * (h + 1), QW1 * qt:QW1 * (qt + 1)],
                    in_=ostg_lo[:, 0:QW1]), 16)
                ldm[("ost", qtg)] = L7S.n

    @self.block.tensor
    def _(tensor):
        def pv7(qt, kc, g, qtg):
            last = (kc == KC - 1)
            Aspl.wait(tensor, g + 1)
            if kc == 0:
                Aqtev.wait(tensor, qtg)
            tensor.matmul(self.pb[3][0:1, 0:QW1], self.c_onecb[:], p_hi(g)[:, 0:QW1],
                          start=(kc == 0), stop=False)
            tensor.matmul(self.pb[3][0:1, 0:QW1], self.c_onecb[:], p_lo(g)[:, 0:QW1],
                          start=False, stop=last)
            tensor.matmul(self.pb[2][:, 0:QW1], vh(kc * 128, 128), p_hi(g)[:, 0:QW1],
                          start=(kc == 0), stop=False)
            tensor.matmul(self.pb[2][:, 0:QW1], vl(kc * 128, 128), p_hi(g)[:, 0:QW1],
                          start=False, stop=False)
            Apv.inc(tensor.matmul(self.pb[2][:, 0:QW1], vh(kc * 128, 128), p_lo(g)[:, 0:QW1],
                                  start=False, stop=last))

        for kb in range(NKB):
            L7K.wait(tensor, ldm[("k", kb)])
            if kb >= 2:
                KE.wait(tensor, kb - 1)
            KT.inc(tensor.transpose(self.pb[5][:, 128 * (kb % 2):128 * (kb % 2) + 128],
                                    kstg, self.c_idf[:]))
        for j in range(NJK):
            KN1.wait(tensor, j + 1)
            if j >= 1:
                KN3.wait(tensor, j)
            KN2.inc(tensor.matmul(self.pb[3][0:1, 0:512], self.c_onecb[:], q2t,
                                  start=True, stop=True))
        for h in range(H):
            for tt in range(NT):
                u = h * NT + tt
                L7Q.wait(tensor, ldm[("q", u)])
                if tt == 0 and h >= 1:
                    Aexp.wait(tensor, h * UPH)
                if u >= 2:
                    QS.wait(tensor, u - 1)
                QP.inc(tensor.transpose(self.pb[6][:, 128 * (u % 2):128 * (u % 2) + 128],
                                        qstg, self.c_idf[:]))
            if h == 0:
                KN3.wait(tensor, NJK)
            Arden.wait(tensor, h * QT1)
            for j in range(NJ1):
                uj = h * NJ1 + j
                QN1.wait(tensor, uj + 1)
                if uj >= 1:
                    QN3.wait(tensor, uj)
                QN2.inc(tensor.matmul(self.pb[3][0:1, 0:QW1], self.c_onecb[:], q2t[:, 0:QW1],
                                      start=True, stop=True))
            VM.wait(tensor, h + 1)
            if h >= 1:
                VC.wait(tensor, h)
            TM.inc(tensor.matmul(self.pb[4][:, 0:1], self.c_onef[:], self.c_sc[0:1, 7:8],
                                 start=True, stop=True))
            QS.wait(tensor, (h + 1) * NT)
            for qt in range(QT1):
                qtg = h * QT1 + qt
                for kc in range(KC):
                    g = gidx(h, qt, kc)
                    Aexp.wait(tensor, max(0, g - 2))
                    Asc.inc(self.mm3(tensor, self.pb[(0, 1, 6)[g % 3]][:, 0:QW1],
                                     kth(kc * 128, 128), ktl(kc * 128, 128),
                                     qhh(QW1 * qt, QW1), qll(QW1 * qt, QW1),
                                     start=True, stop=True))
                    if kc >= 1:
                        pv7(qt, kc - 1, g - 1, qtg)
                pv7(qt, KC - 1, gidx(h, qt, KC - 1), qtg)
                Arden.wait(tensor, qtg + 1)
                Aqtev.wait(tensor, qtg)
                Arb.inc(tensor.matmul(self.pb[5][:, 0:QW1], self.c_onef[:],
                                      rden_r[0:1, 0:QW1], start=True, stop=True))

    @self.block.scalar
    def _(scalar):
        KM.wait(scalar, 1)
        KSQ.inc(scalar.activation(out=self.c_sc[0:1, 5:6], in_=self.c_sc[0:1, 2:3],
                                  func=AF.Sqrt, bias=self.c_zero[0:1, 0:1]))
        for h in range(H):
            QM.wait(scalar, h + 1)
            QSQ.inc(scalar.activation(out=self.c_sc[0:1, 4:5], in_=self.c_sc[0:1, 0:1],
                                      func=AF.Sqrt, bias=self.c_zero[0:1, 0:1]))
            for qt in range(QT1):
                for kc in range(KC):
                    g = gidx(h, qt, kc)
                    Asc.wait(scalar, g + 1)
                    Aspl.wait(scalar, max(0, g - 2))
                    VC.wait(scalar, h + 1)
                    Aexp.inc(scalar.activation(out=ptil(g)[:, 0:QW1],
                                               in_=self.pb[(0, 1, 6)[g % 3]][:, 0:QW1],
                                               func=AF.Exp, bias=negM, scale=1.0))

    @self.block.vector
    def _(vector):
        HZ = self.sem("HZV")
        for kb in range(NKB):
            KT.wait(vector, kb + 1)
            slot = self.pb[5][:, 128 * (kb % 2):128 * (kb % 2) + 128]
            vector.tensor_copy(kth(128 * kb, 128), slot)
            KE.inc(vector.tensor_tensor(ktl(128 * kb, 128), slot, kth(128 * kb, 128),
                                        op=ALU.subtract))
        for kb in range(NKB):
            L7K.wait(vector, ldm[("v", kb)])
            vector.tensor_copy(vh(128 * kb, 128), kstg)
            VE.inc(vector.tensor_tensor(vl(128 * kb, 128), kstg, vh(128 * kb, 128),
                                        op=ALU.subtract))
        for j in range(NJK):
            KN1.inc(vector.tensor_tensor(q2t, kth(512 * j, 512), kth(512 * j, 512), op=ALU.mult))
            KN2.wait(vector, j + 1)
            if j == 0:
                KN3.inc(vector.reduce_max(self.c_sc[0:1, 2:3], self.pb[3][0:1, 0:512], axis=AXX))
            else:
                HZ.inc(vector.reduce_max(self.c_sc[0:1, 6:7], self.pb[3][0:1, 0:512], axis=AXX))
                HZ.wait(vector, HZ.n)
                KN3.inc(vector.tensor_tensor(self.c_sc[0:1, 2:3], self.c_sc[0:1, 2:3],
                                             self.c_sc[0:1, 6:7], op=ALU.max))
            KN3.wait(vector, j + 1)
        KM.inc(vector.tensor_copy(self.c_sc[0:1, 16:17], self.c_sc[0:1, 2:3]))
        for h in range(H):
            for tt in range(NT):
                u = h * NT + tt
                QP.wait(vector, u + 1)
                slot = self.pb[6][:, 128 * (u % 2):128 * (u % 2) + 128]
                vector.tensor_copy(qhh(128 * tt, 128), slot)
                QS.inc(vector.tensor_tensor(qll(128 * tt, 128), slot, qhh(128 * tt, 128),
                                            op=ALU.subtract))
            for j in range(NJ1):
                uj = h * NJ1 + j
                QN1.inc(vector.tensor_tensor(q2t[:, 0:QW1], qhh(QW1 * j, QW1),
                                             qhh(QW1 * j, QW1), op=ALU.mult))
                QN2.wait(vector, uj + 1)
                if j == 0:
                    QN3.inc(vector.reduce_max(self.c_sc[0:1, 0:1], self.pb[3][0:1, 0:QW1],
                                              axis=AXX))
                else:
                    HZ.inc(vector.reduce_max(self.c_sc[0:1, 6:7], self.pb[3][0:1, 0:QW1],
                                             axis=AXX))
                    HZ.wait(vector, HZ.n)
                    QN3.inc(vector.tensor_tensor(self.c_sc[0:1, 0:1], self.c_sc[0:1, 0:1],
                                                 self.c_sc[0:1, 6:7], op=ALU.max))
                QN3.wait(vector, uj + 1)
            QM.inc(vector.tensor_copy(self.c_sc[0:1, 17:18], self.c_sc[0:1, 0:1]))
            QSQ.wait(vector, h + 1)
            if h == 0:
                KSQ.wait(vector, 1)
            HZ.inc(vector.tensor_tensor(self.c_sc[0:1, 6:7], self.c_sc[0:1, 4:5],
                                        self.c_sc[0:1, 5:6], op=ALU.mult))
            HZ.wait(vector, HZ.n)
            VM.inc(vector.tensor_scalar(self.c_sc[0:1, 7:8], self.c_sc[0:1, 6:7],
                                        -1.01, -0.5, op0=ALU.mult, op1=ALU.add))
            TM.wait(vector, h + 1)
            if h >= 1:
                Aexp.wait(vector, g1b + h * UPH)
            VC.inc(vector.tensor_copy(negM, self.pb[4][:, 0:1]))
            for qt in range(QT1):
                qtg = h * QT1 + qt
                for kc in range(KC):
                    g = gidx(h, qt, kc)
                    Aexp.wait(vector, g + 1)
                    Apv.wait(vector, max(0, g - 2))
                    vector.tensor_copy(p_hi(g)[:, 0:QW1], ptil(g)[:, 0:QW1])
                    Aspl.inc(vector.tensor_tensor(p_lo(g)[:, 0:QW1], ptil(g)[:, 0:QW1],
                                                  p_hi(g)[:, 0:QW1], op=ALU.subtract))
                g = gidx(h, qt, KC - 1)
                Apv.wait(vector, g + 1)
                HZ.inc(vector.tensor_copy(den_r[0:1, 0:QW1], self.pb[3][0:1, 0:QW1]))
                HZ.wait(vector, HZ.n)
                Arden.inc(vector.reciprocal(rden_r[0:1, 0:QW1], den_r[0:1, 0:QW1]))
                Arb.wait(vector, qtg + 1)
                vector.tensor_copy(rb_tile[:, 0:QW1], self.pb[5][:, 0:QW1])
                if qtg >= 1:
                    L7S.wait(vector, ldm[("ost", qtg - 1)])
                vector.tensor_tensor(ostg_f[:, 0:QW1], self.pb[2][:, 0:QW1],
                                     rb_tile[:, 0:QW1], op=ALU.mult)
                vector.tensor_copy(ostg_hi[:, 0:QW1], ostg_f[:, 0:QW1])
                Aqtev.inc(vector.tensor_tensor(ostg_lo[:, 0:QW1], ostg_f[:, 0:QW1],
                                               ostg_hi[:, 0:QW1], op=ALU.subtract))

    self.g1 += H * UPH


MK.ph7 = ph7


def ph8(self):
    nc, S, TOK, NT = self.nc, self.S, self.TOK, self.NT
    LW = [self.sem("LW0"), self.sem("LW1")]
    L8OH, L8X, L8S = self.sem("L8OH"), self.sem("L8X"), self.sem("L8S")
    W8, M8, E8, O8 = (self.sem(x) for x in ("P8W", "P8M", "P8E", "P8O"))
    o_wh, o_wl, o_oh = 0, ECH * 2048, 2 * ECH * 2048
    wo_h = lambda fc, c0, w: self.ab(o_wh + fc * 2048 + c0, w)
    wo_l = lambda fc, c0, w: self.ab(o_wl + fc * 2048 + c0, w)
    oh_hi = lambda fc: self.ab(o_oh + fc * 128, 128)
    oh_lo = lambda fc: self.ab(o_oh + 2048 + fc * 128, 128)
    wstg = [self.af(0, 2048), self.af(2048, 2048)]
    buf1 = self.af(4096, 2048)
    stage = self.af(6656, 2048)
    ldm = {}

    @self.block.sync
    def _(sync):
        for ec in range(ECH):
            if ec >= 2:
                W8.wait(sync, ec - 1)
            LW[ec % 2].inc(sync.dma_start(out=wstg[ec % 2],
                                          in_=self.i_wu[128 * ec:128 * (ec + 1), :]), 16)
            ldm[("w", ec)] = LW[ec % 2].n
        for tt in range(NT):
            M8.wait(sync, tt)
            for fc in range(ECH):
                L8OH.inc(sync.dma_start(out=oh_hi(fc),
                                        in_=self.d_o1h.ap()[128 * fc:128 * (fc + 1),
                                                            128 * tt:128 * (tt + 1)]), 16)
                L8OH.inc(sync.dma_start(out=oh_lo(fc),
                                        in_=self.d_o1l.ap()[128 * fc:128 * (fc + 1),
                                                            128 * tt:128 * (tt + 1)]), 16)
            ldm[("oh", tt)] = L8OH.n
            E8.wait(sync, tt)
            L8X.inc(sync.dma_start(out=buf1, in_=self.d_x1.ap()[128 * tt:128 * (tt + 1), :]), 16)
            ldm[("x1", tt)] = L8X.n
            E8.wait(sync, tt + 1)
            L8S.inc(sync.dma_start(out=self.d_x1p.ap()[128 * tt:128 * (tt + 1), :], in_=stage), 16)
            ldm[("st", tt)] = L8S.n

    @self.block.tensor
    def _(tensor):
        for tt in range(NT):
            L8OH.wait(tensor, ldm[("oh", tt)])
            if tt == 0:
                W8.wait(tensor, ECH)
            if tt >= 1:
                E8.wait(tensor, tt)
            last = None
            for fc in range(ECH):
                for og in range(4):
                    last = self.mm3(tensor, self.pb[og][:],
                                    oh_hi(fc), oh_lo(fc),
                                    wo_h(fc, 512 * og, 512), wo_l(fc, 512 * og, 512),
                                    start=(fc == 0), stop=(fc == ECH - 1))
            M8.inc(last)

    @self.block.vector
    def _(vector):
        for ec in range(ECH):
            LW[ec % 2].wait(vector, ldm[("w", ec)])
            vector.tensor_copy(self.ab(o_wh + ec * 2048, 2048), wstg[ec % 2])
            W8.inc(vector.tensor_tensor(self.ab(o_wl + ec * 2048, 2048), wstg[ec % 2],
                                        self.ab(o_wh + ec * 2048, 2048), op=ALU.subtract))
        for tt in range(NT):
            M8.wait(vector, tt + 1)
            L8X.wait(vector, ldm[("x1", tt)])
            if tt >= 1:
                L8S.wait(vector, ldm[("st", tt - 1)])
            last = None
            for og in range(4):
                last = vector.tensor_tensor(stage[:, 512 * og:512 * (og + 1)], self.pb[og][:],
                                            buf1[:, 512 * og:512 * (og + 1)], op=ALU.add)
            E8.inc(last)


def ph9(self):
    nc, S, TOK, NT = self.nc, self.S, self.TOK, self.NT
    L9A, L9W, L9X, L9O = (self.sem(x) for x in ("L9A", "L9W", "L9X", "L9O"))
    X9, W9, M9, E9, B9 = (self.sem(x) for x in ("P9X", "P9W", "P9M", "P9E", "P9B"))
    o_xh, o_xl = 0, ECH * TOK
    o_wt = 2 * ECH * TOK
    wt_t = lambda ec: self.ab(o_wt + ec * 512, 512)
    biasr = self.af(6656, 2048, part0=0, nparts=1)
    bias_bc = self.af(4096, 2048)
    x1p = self.af(2048, 512)
    outst = self.af(2560, 512)
    ldm = {}

    @self.block.sync
    def _(sync):
        for ec in range(ECH):
            L9A.inc(sync.dma_start(out=self.ab(o_xh + ec * TOK, TOK),
                                   in_=self.d_x1h.ap()[128 * ec:128 * (ec + 1), :]), 16)
            L9A.inc(sync.dma_start(out=self.ab(o_xl + ec * TOK, TOK),
                                   in_=self.d_x1l.ap()[128 * ec:128 * (ec + 1), :]), 16)
        L9A.inc(sync.dma_start(out=biasr, in_=self.i_bl1), 16)
        ldm["xn"] = L9A.n
        for og in range(4):
            for ec in range(ECH):
                idx = og * ECH + ec
                if og >= 1 and ec == 0:
                    M9.wait(sync, NT * og)
                L9W.inc(sync.dma_start(out=wt_t(ec),
                                       in_=self.d_wt.ap()[128 * ec:128 * (ec + 1),
                                                          512 * og:512 * (og + 1)]), 16)
            ldm[("wt", og)] = L9W.n
            for tt in range(NT):
                idx = og * NT + tt
                E9.wait(sync, idx)
                L9X.inc(sync.dma_start(out=x1p,
                                       in_=self.d_x1p.ap()[128 * tt:128 * (tt + 1),
                                                           512 * og:512 * (og + 1)]), 16)
                ldm[("x1p", idx)] = L9X.n
                E9.wait(sync, idx + 1)
                L9O.inc(sync.dma_start(out=self.o_y[128 * tt:128 * (tt + 1),
                                                    512 * og:512 * (og + 1)], in_=outst), 16)
                ldm[("out", idx)] = L9O.n

    @self.block.tensor
    def _(tensor):
        L9A.wait(tensor, ldm["xn"])
        for og in range(4):
            B9.inc(tensor.matmul(self.pb[2 + og][:], self.c_onef[:],
                                 biasr[0:1, 512 * og:512 * (og + 1)], start=True, stop=True))
        for og in range(4):
            L9W.wait(tensor, ldm[("wt", og)])
            for tt in range(NT):
                idx = og * NT + tt
                if idx >= 2:
                    E9.wait(tensor, idx - 1)
                last = None
                for ec in range(ECH):
                    tensor.matmul(self.pb[idx % 2][:],
                                  self.ab(o_xh + ec * TOK + 128 * tt, 128), wt_t(ec),
                                  start=(ec == 0), stop=False)
                    last = tensor.matmul(self.pb[idx % 2][:],
                                         self.ab(o_xl + ec * TOK + 128 * tt, 128), wt_t(ec),
                                         start=False, stop=(ec == ECH - 1))
                M9.inc(last)

    @self.block.vector
    def _(vector):
        for og in range(4):
            B9.wait(vector, og + 1)
            vector.tensor_copy(bias_bc[:, 512 * og:512 * (og + 1)], self.pb[2 + og][:])
        for og in range(4):
            for tt in range(NT):
                idx = og * NT + tt
                M9.wait(vector, idx + 1)
                L9X.wait(vector, ldm[("x1p", idx)])
                if idx >= 1:
                    L9O.wait(vector, ldm[("out", idx - 1)])
                vector.scalar_tensor_tensor(outst, self.pb[idx % 2][:], self.c_col[:, 6:7],
                                            x1p, op0=ALU.mult, op1=ALU.add)
                E9.inc(vector.tensor_tensor(outst, outst,
                                            bias_bc[:, 512 * og:512 * (og + 1)], op=ALU.add))

    @self.block.gpsimd
    def _(gpsimd):
        pass


MK.ph8 = ph8
MK.ph9 = ph9


def build_mk(S, nph=99, dumps=()):
    mk = MK(S)
    mk.declare()
    mk.alloc()
    dump_outs = []
    for i, name in enumerate(dumps):
        if isinstance(name, str) and "[" in name:
            base, idx = name[:-1].split("[")
            t = getattr(mk, base)[int(idx)]
        elif isinstance(name, str):
            t = getattr(mk, name)
        else:
            t = name
        sh = list(t.shape)
        dt = t.dtype
        dump_outs.append((mk.nc.dram_tensor(f"dbg{i}", sh, dt, kind="ExternalOutput").ap(), t))
    phases = [mk.ph0, mk.ph1, lambda: mk.ph2(0), lambda: mk.ph2(1), mk.ph3,
              mk.ph4, mk.ph5, mk.ph6, mk.ph7, mk.ph8, mk.ph9]
    with mk.nc.Block() as block:
        mk.block = block
        for i, ph in enumerate(phases):
            if i >= nph:
                break
            ph()
            mk.barrier()
        if dump_outs:
            LD = mk.sem("LD")

            @block.sync
            def _(sync):
                for o, t in dump_outs:
                    LD.inc(sync.dma_start(out=o, in_=t[:]), 16)
            mk.barrier()
    for name, sem in mk.sems.items():
        for w in sem.waits:
            assert w <= sem.n, f"sem {name}: wait {w} > total {sem.n}"
    mk.es.close()
    return mk


def kernel(**inputs):
    S = int(np.asarray(inputs["L0_attn_x"]).shape[1])
    mk = build_mk(S)
    in_maps = [host_prep(c, S, inputs) for c in range(NCORES)]
    res = run_bass_kernel_spmd(mk.nc, in_maps, core_ids=list(range(NCORES)))
    out = np.zeros((1, S, E), dtype=np.float32)
    for c in range(NCORES):
        out[0, owned_rows(c, S), :] = np.asarray(res.results[c]["y_c"], dtype=np.float32)
    return out


# version 04
# speedup vs baseline: 1.1066x; 1.0479x over previous
import os
import numpy as np
import ml_dtypes
import concourse.bass as bass
import concourse.mybir as mybir
from concourse.bass_utils import run_bass_kernel_spmd

F32 = mybir.dt.float32
BF16 = mybir.dt.bfloat16
F16 = mybir.dt.float16
I32 = mybir.dt.int32
AF = mybir.ActivationFunctionType
ALU = mybir.AluOpType
AXX = mybir.AxisListType.X

NCORES = 8
E = 2048
ECH = E // 128
H = 16
D = 128
HC = H // NCORES
ICR = 5632 // NCORES
ICP = 768
GU = 2 * ICP
NEG = -1e30
INV_SQRT_D = 1.0 / np.sqrt(128.0)
PARANOID = bool(int(os.environ.get("MK_PARANOID", "0")))

BF = ml_dtypes.bfloat16


def _hilo(a):
    hi = a.astype(BF)
    lo = (a - hi.astype(np.float32)).astype(BF)
    return hi, lo


class PhysSem:
    def __init__(self, es, nc, name):
        self.h = es.enter_context(nc.semaphore(name))
        self.n = 0


class Sem:

    def __init__(self, phys):
        self.p = phys
        self.base = phys.n
        self.waits = []

    @property
    def n(self):
        return self.p.n - self.base

    def inc(self, instr, k=1):
        instr.then_inc(self.p.h, k)
        self.p.n += k
        return self.n

    def wait(self, eng, value):
        assert value >= 0
        self.waits.append(value)
        eng.wait_ge(self.p.h, self.base + value)


SEM_ALIAS = {}
for _nm, _ph in {
    "P1A": "E00", "P1B": "E01", "P1C": "E02", "P1T": "E03", "P1S": "E04",
    "P2W": "E00", "P2M": "E01", "P2E": "E02", "P2V": "E03", "P2T": "E04",
    "P2VE": "E05", "P2N1": "E06", "P2N2": "E07", "P2N3": "E08", "P2N4": "E09",
    "P2N5": "E10", "D_sc": "E11", "D_bias": "E12", "D_exp": "E13", "D_spl": "E14",
    "D_pv": "E15", "D_brow": "E16", "D_bc": "E17", "D_btile": "E18",
    "D_rden": "E19", "D_rb": "E20", "D_qtev": "E21", "P2B": "E22",
    "P3M": "E23", "P3E": "E24", "P3W": "E25",
    "P4X": "E03", "P4B": "E04", "P4W": "E05", "P4G": "E06", "P4S": "E07",
    "P4H": "E08", "P4D2": "E09", "P4Y": "E10", "P4C1": "E11", "P4C2": "E12",
    "P4C3": "E13", "P4C4": "E14", "P4C5": "E15",
    "P5R1": "E16", "P5R2": "E17", "P5R3": "E18", "P5R4": "E19", "P5R5": "E20",
    "P5T": "E21", "P5S": "E22", "P5BO": "E06",
    "P6W": "E00", "P6M": "E01", "P6E": "E02", "P6Q": "E03",
    "P7KT": "E04", "P7KE": "E05", "P7VE": "E07", "P7KN1": "E08", "P7KN2": "E09",
    "P7KN3": "E10", "P7KM": "E11", "P7KSQ": "E12", "P7QP": "E13", "P7QS": "E14",
    "P7QN1": "E15", "P7QN2": "E16", "P7QN3": "E17", "P7QM": "E18", "P7QSQ": "E19",
    "P7VM": "E20", "P7TM": "E21", "P7VC": "E22",
    "A_sc": "E23", "A_exp": "E24", "A_spl": "E25", "A_pv": "E26",
    "A_rden": "E27", "A_rb": "E28", "A_qtev": "E29",
    "P8W": "E00", "P8M": "E01", "P8E": "E02", "P8O": "E05",
    "P9X": "E03", "P9W": "E04", "P9M": "E05", "P9E": "E06", "P9B": "E07",
    "L1X": "D6", "L1S": "D7",
    "LW0": "D0", "LW1": "D1", "LSL": "D2", "LAQ": "D3", "LOS": "D4", "LCT": "D5",
    "LOH": "D6", "LP3": "D7",
    "LDQ": "D3", "LY0": "D6", "LY1": "D7", "LL0": "D8", "LL1": "D9",
    "LWT0": "D4", "LWT1": "D10",
    "L5": "D6x", "L5S": "D7x", "L5P": "D8x",
    "LQ0": "D6", "LQ1": "D7", "L6KV": "D8",
    "L7K": "D6y", "L7Q": "D7y", "L7S": "D8y",
    "L8OH": "D6", "L8X": "D7", "L8S": "D8",
    "L9A": "D6z", "L9W": "D7z", "L9X": "D8z", "L9O": "D9",
}.items():
    SEM_ALIAS[_nm] = _ph


class MK:
    def __init__(self, S):
        assert S % 1024 == 0
        self.S = S
        self.TOK = S // NCORES
        self.NT = self.TOK // 128
        self.QT0 = S // 512
        self.QT1 = max(1, self.TOK // 512)
        self.QW1 = min(512, self.TOK)
        self.KC = S // 128
        self.SLAB = 256
        self.NSLAB = S // self.SLAB
        self.nc = bass.Bass("TRN2", target_bir_lowering=False,
                            dynamic_dma_scratch_size=8192,
                            detect_race_conditions=False)
        import contextlib
        self.es = contextlib.ExitStack()
        self.block = None
        self.sems = {}
        self.phys = {}
        self.g0 = 0
        self.qt0 = 0
        self.g1 = 0
        self._prev_ost = 0

    def sem(self, name):
        if name not in self.sems:
            ph = SEM_ALIAS.get(name, name)
            if ph not in self.phys:
                self.phys[ph] = PhysSem(self.es, self.nc, ph)
            self.sems[name] = Sem(self.phys[ph])
        return self.sems[name]

    def declare(self):
        nc, S, TOK = self.nc, self.S, self.TOK
        di = lambda n, sh, dt: nc.dram_tensor(n, sh, dt, kind="ExternalInput").ap()
        self.i_x = di("x_c", [TOK, E], F32)
        self.i_wq = di("wq_t", [E, HC * 128], F32)
        self.i_wk = di("wk_t", [E, HC * 128], F32)
        self.i_wv = di("wv_t", [E, HC * 128], F32)
        self.i_bqkv = di("bqkv", [128, HC * 3], F32)
        self.i_wo = di("wo_t", [HC * 128, E], F32)
        self.i_bo = di("bo", [1, E], F32)
        self.i_gup = di("gu_rep", [E, GU], I32)
        self.i_gus = di("gu_sc", [ECH, GU], F16)
        self.i_dwp = di("dw_rep", [ICP, E], I32)
        self.i_dws = di("dw_scsel", [2 * (ICP // 128), E], F16)
        self.i_dsel = di("dsel", [2, 128], F32)
        self.i_wa = di("w_attn", [E, 2304], F32)
        self.i_wu = di("w_out", [E, E], F32)
        self.i_wl1 = di("w_l1t", [E, E], F32)
        self.i_bl1 = di("b_l1", [1, E], F32)
        self.i_x2 = di("x_c2", [TOK, E], F32)
        self.i_idf = di("identf", [128, 128], F32)
        self.i_idb = di("identb", [128, 128], BF16)
        self.i_msk = di("masks", [128, 4 * 512], BF16)
        self.i_alq = di("al_q", [HC, S], F32)
        self.i_alk = di("al_k", [128, HC * (S // 128)], F32)
        self.i_shft = di("shifts", [128, 1], I32)
        self.i_nmask = di("nmask", [128, 1], I32)
        self.i_ninv = di("ninv", [128, 1], F32)
        self.i_cos = di("cos_t", [TOK, 128], F32)
        self.i_sin = di("sin_t", [TOK, 128], F32)

        self.o_y = nc.dram_tensor("y_c", [TOK, E], F32, kind="ExternalOutput").ap()

        dr = lambda n, sh, dt: nc.dram_tensor(n, sh, dt)
        self.d_xh_in = dr("xh_in", [E, TOK], BF16)
        self.d_xl_in = dr("xl_in", [E, TOK], BF16)
        self.d_xh = dr("xh_all", [NCORES * E, TOK], BF16)
        self.d_xl = dr("xl_all", [NCORES * E, TOK], BF16)
        self.d_ohh = dr("ohT_hi", [HC * 128, S], BF16)
        self.d_ohl = dr("ohT_lo", [HC * 128, S], BF16)
        self.NCH = S // 1024
        self.d_part = [dr(f"part{ch}", [1024, 2 * E], F32) for ch in range(self.NCH)]
        self.d_rs = [dr(f"rsout{ch}", [128, 2 * E], F32) for ch in range(self.NCH)]
        self.d_q1 = dr("q1_spill", [TOK, E], F32)
        self.d_k1in = dr("k1_in", [TOK, D], F32)
        self.d_v1in = dr("v1_in", [TOK, D], F32)
        self.d_k1 = dr("k1_all", [S, D], F32)
        self.d_v1 = dr("v1_all", [S, D], F32)
        self.d_x1h = dr("xn1T_hi", [E, TOK], BF16)
        self.d_x1l = dr("xn1T_lo", [E, TOK], BF16)
        self.d_x1 = dr("x1_spill", [TOK, E], F32)
        self.d_x1p = dr("x1p_spill", [TOK, E], F32)
        self.d_o1h = dr("o1T_hi", [E, TOK], BF16)
        self.d_o1l = dr("o1T_lo", [E, TOK], BF16)
        self.d_wt = dr("wt_spill", [E, E], BF16)
        self.d_bar = dr("bar_scratch", [128, 1], F32)

    def alloc(self):
        nc, es, S, TOK = self.nc, self.es, self.S, self.TOK
        sb = lambda n, sh, dt: es.enter_context(nc.sbuf_tensor(n, sh, dt))
        self.AB = sb("AB", [128, 84992], BF16)
        self.AF = sb("AF", [128, 9216], F32)
        self.c_i32a = sb("c_i32a", [128, 512], I32)
        self.c_i32b = sb("c_i32b", [128, 512], I32)
        self.c_idf = sb("c_idf", [128, 128], F32)
        self.c_idb = sb("c_idb", [128, 128], BF16)
        self.c_shft = sb("c_shft", [128, 1], I32)
        self.c_nmask = sb("c_nmask", [128, 1], I32)
        self.c_ninv = sb("c_ninv", [128, 1], F32)
        self.c_zero = sb("c_zero", [128, 1], F32)
        self.c_onef = sb("c_onef", [1, 128], F32)
        self.c_onecb = sb("c_onecb", [128, 1], BF16)
        self.c_onecf = sb("c_onecf", [128, 1], F32)
        self.c_dsel = sb("c_dsel", [2, 128], F32)
        self.c_bqkv = sb("c_bqkv", [128, HC * 3], F32)
        self.c_sc = sb("c_sc", [1, 32], F32)
        self.c_col = sb("c_col", [128, 8], F32)
        self.c_stats = sb("c_stats", [128, 26], F32)
        self.c_alk = sb("c_alk", [128, HC * (S // 128)], F32)
        self.c_bar = sb("c_bar", [128, 4], F32)
        self.c_f16 = sb("c_f16", [2, 2048], F16)
        self.pb = [es.enter_context(nc.psum_tensor(f"pb{i}", [128, 512], F32))
                   for i in range(7)]
        self.pt_b = es.enter_context(nc.psum_tensor("pt_b", [128, 512], BF16))

    def ab(self, off, cols, part0=None, nparts=None):
        assert off + cols <= 84992, (off, cols)
        if part0 is not None:
            return self.AB[part0:part0 + nparts, off:off + cols]
        return self.AB[:, off:off + cols]

    def af(self, off, cols, part0=None, nparts=None):
        assert off + cols <= 9216, (off, cols)
        if part0 is not None:
            return self.AF[part0:part0 + nparts, off:off + cols]
        return self.AF[:, off:off + cols]

    def barrier(self):
        BAR, LD, CC = self.sem("BAR"), self.sem("LD"), self.sem("CC")
        base = BAR.n
        ccn = CC.n
        r1 = base + 19
        tgt = base + 20

        BI = self.sem("BARINIT")

        @self.block.vector
        def _(vector):
            BAR.inc(vector.tensor_copy(self.c_bar[:, 0:1], self.c_bar[:, 0:1]), 1)

        @self.block.scalar
        def _(scalar):
            BI.wait(scalar, 1)
            BAR.inc(scalar.activation(out=self.c_bar[:, 1:2], in_=self.c_bar[:, 1:2],
                                      func=AF.Copy, bias=0.0), 1)

        @self.block.gpsimd
        def _(gpsimd):
            BI.wait(gpsimd, 1)
            BAR.inc(gpsimd.tensor_copy(self.c_bar[:, 2:3], self.c_bar[:, 2:3]), 1)

        @self.block.sync
        def _(sync):
            BI.wait(sync, 1)
            BAR.inc(sync.dma_start(out=self.d_bar.ap(), in_=self.c_bar[:, 3:4]), 16)

        @self.block.tensor
        def _(tensor):
            BAR.wait(tensor, r1)
            BAR.inc(tensor.matmul(self.pb[0][0:1, 0:1], self.c_onecb[:], self.c_onecb[:],
                                  start=True, stop=True), 1)

        dma_totals = [(sem, sem.n) for name, sem in self.sems.items()
                      if name.startswith("L") and sem.n > 0]
        for engname in ("sync", "gpsimd", "tensor", "vector", "scalar"):
            def mk(engname=engname):
                def _(eng):
                    BAR.wait(eng, tgt)
                    for sem, n in dma_totals:
                        sem.wait(eng, n)
                    if ccn:
                        CC.wait(eng, ccn)
                return _
            getattr(self.block, engname)(mk())

    def mm3(self, tensor, out, l_hi, l_lo, r_hi, r_lo, start, stop):
        tensor.matmul(out, l_hi, r_hi, start=start, stop=False)
        tensor.matmul(out, l_lo, r_hi, start=False, stop=False)
        return tensor.matmul(out, l_hi, r_lo, start=False, stop=stop)

    def split_hilo(self, vector, src, hi, lo):
        vector.tensor_copy(hi, src)
        vector.tensor_tensor(lo, src, hi, op=ALU.subtract)



def host_prep(core, S, ins):
    TOK = S // NCORES
    t0 = core * TOK
    f32 = np.float32
    out = {}
    x = np.asarray(ins["L0_attn_x"], dtype=f32)
    out["x_c"] = np.ascontiguousarray(x[0, t0:t0 + TOK, :])
    h0 = core * HC
    for nm, w in (("wq_t", "L0_attn_Wq"), ("wk_t", "L0_attn_Wk"), ("wv_t", "L0_attn_Wv")):
        out[nm] = np.ascontiguousarray(np.asarray(ins[w], f32).T[:, h0 * 128:(h0 + HC) * 128])
    bq = np.stack([np.asarray(ins["L0_attn_bq"], f32), np.asarray(ins["L0_attn_bk"], f32),
                   np.asarray(ins["L0_attn_bv"], f32)], axis=1)
    bqkv = np.zeros((128, HC * 3), f32)
    for h in range(HC):
        for j in range(3):
            bqkv[:, h * 3 + j] = bq[(h0 + h) * 128:(h0 + h + 1) * 128, j]
    out["bqkv"] = bqkv
    out["wo_t"] = np.ascontiguousarray(np.asarray(ins["L0_attn_Wo"], f32).T[h0 * 128:(h0 + HC) * 128, :])
    out["bo"] = np.asarray(ins["L0_attn_bo"], f32).reshape(1, E)

    gup = np.asarray(ins["L0_ffn_gate_up_packed"])
    gus = np.asarray(ins["L0_ffn_gate_up_scales"])
    I = gup.shape[0] // 2
    g_lo, g_hi = core * ICR, (core + 1) * ICR
    rows = np.concatenate([gup[g_lo:g_hi], np.zeros((ICP - ICR, E // 8), np.int32),
                           gup[I + g_lo:I + g_hi], np.zeros((ICP - ICR, E // 8), np.int32)], axis=0)
    out["gu_rep"] = np.ascontiguousarray(np.repeat(rows.T, 8, axis=0)).astype(np.int32)
    sc = np.concatenate([gus[:, g_lo:g_hi], np.zeros((ECH, ICP - ICR), gus.dtype),
                         gus[:, I + g_lo:I + g_hi], np.zeros((ECH, ICP - ICR), gus.dtype)], axis=1)
    out["gu_sc"] = np.ascontiguousarray(sc)
    dwp = np.asarray(ins["L0_ffn_down_packed"])
    dsc = np.asarray(ins["L0_ffn_down_scales"])
    w_lo8, w_hi8 = g_lo // 8, g_hi // 8
    drows = np.concatenate([dwp[:, w_lo8:w_hi8].T, np.zeros((ICP // 8 - ICR // 8, E), np.int32)], axis=0)
    out["dw_rep"] = np.ascontiguousarray(np.repeat(drows, 8, axis=0)).astype(np.int32)
    sel = np.zeros((2, 128), f32)
    sel[0, :64] = 1.0
    sel[1, 64:] = 1.0
    out["dsel"] = sel
    nic = ICP // 128
    scsel = np.zeros((2 * nic, E), dsc.dtype)
    for ic in range(nic):
        scsel[2 * ic + 0] = dsc[min((g_lo + ic * 128) // 128, dsc.shape[0] - 1)]
        scsel[2 * ic + 1] = dsc[min((g_lo + ic * 128 + 64) // 128, dsc.shape[0] - 1)]
    out["dw_scsel"] = np.ascontiguousarray(scsel)

    out["w_attn"] = np.ascontiguousarray(np.asarray(ins["L1_attn_w_attn"], f32))
    out["w_out"] = np.ascontiguousarray(np.asarray(ins["L1_attn_w_out"], f32))
    out["w_l1t"] = np.ascontiguousarray(np.asarray(ins["L1_ffn_weight"], f32).T)
    out["b_l1"] = np.asarray(ins["L1_ffn_bias"], f32).reshape(1, E)

    out["identf"] = np.eye(128, dtype=f32)
    out["identb"] = np.eye(128, dtype=f32).astype(BF)
    p = np.arange(128)[:, None]
    j = np.arange(512)[None, :]
    masks = np.concatenate([np.where(p + 128 * m <= j, 0.0, NEG).astype(f32) for m in range(4)], axis=1)
    out["masks"] = masks.astype(BF)
    slopes = np.array([2.0 ** (-0.5 * (i + 1)) for i in range(H)], dtype=np.float64)
    alq = np.stack([(-slopes[h0 + h] * np.arange(S)).astype(f32) for h in range(HC)], axis=0)
    out["al_q"] = np.ascontiguousarray(alq)
    alk = np.zeros((128, HC * (S // 128)), f32)
    for h in range(HC):
        for kc in range(S // 128):
            alk[:, h * (S // 128) + kc] = (slopes[h0 + h] * (kc * 128 + np.arange(128))).astype(f32)
    out["al_k"] = alk
    out["shifts"] = (4 * (np.arange(128) % 8)).astype(np.int32).reshape(128, 1)
    sh = 4 * (np.arange(128) % 8)
    out["nmask"] = (15 << sh).astype(np.int32).reshape(128, 1)
    out["ninv"] = (2.0 ** (-sh)).astype(f32).reshape(128, 1)
    idx = owned_rows(core, S)
    out["x_c2"] = np.ascontiguousarray(x[0, idx, :])
    hd = 128
    pos = 10000.0 ** ((-2.0 * np.arange(0, hd, 2, dtype=f32) - 1.0) / hd)
    t_idx = idx.astype(f32)[:, None]
    ang = t_idx * pos[None, :]
    rot = np.concatenate([ang, ang], axis=1)
    out["cos_t"] = np.cos(rot).astype(f32)
    out["sin_t"] = np.sin(rot).astype(f32)
    return out


def owned_rows(core, S):
    nch = S // 1024
    return np.concatenate([np.arange(1024 * ch + 128 * core, 1024 * ch + 128 * core + 128)
                           for ch in range(nch)])


def ph0(self):
    nc, LD = self.nc, self.sem("LD")
    ld_marks = {}

    @self.block.sync
    def _(sync):
        for dst, src in [(self.c_idf[:], self.i_idf), (self.c_idb[:], self.i_idb),
                         (self.c_nmask[:], self.i_nmask), (self.c_ninv[:], self.i_ninv),
                         (self.c_dsel[:], self.i_dsel),
                         (self.c_bqkv[:], self.i_bqkv), (self.c_alk[:], self.i_alk)]:
            LD.inc(sync.dma_start(out=dst, in_=src), 16)
        ld_marks["done"] = LD.n

    @self.block.vector
    def _(vector):
        BI = self.sem("BARINIT")
        vector.memset(self.c_zero[:], 0.0)
        vector.memset(self.c_onef[:], 1.0)
        vector.memset(self.c_onecb[:], 1.0)
        vector.memset(self.c_onecf[:], 1.0)
        BI.inc(vector.memset(self.c_bar[:], 0.0))


def ph1(self):
    nc, S, TOK, NT = self.nc, self.S, self.TOK, self.NT
    CC = self.sem("CC")
    L1X, L1S = self.sem("L1X"), self.sem("L1S")
    A = self.sem("P1A"); B = self.sem("P1B"); C = self.sem("P1C")
    T = self.sem("P1T"); SP = self.sem("P1S")
    xtile = self.af(0, 2048)
    xntile = self.af(2048, 2048)
    o_xh, o_xl = 0, ECH * TOK
    b0 = dict(A=A.n, B=B.n, C=C.n, T=T.n, SP=SP.n)
    ldm = {}

    @self.block.sync
    def _(sync):
        for tt in range(NT):
            if tt > 0:
                C.wait(sync, b0["C"] + tt)
            L1X.inc(sync.dma_start(out=xtile, in_=self.i_x[tt * 128:(tt + 1) * 128, :]), 16)
            ldm[("x", tt)] = L1X.n
            for ec in range(ECH):
                g = tt * ECH + ec
                SP.wait(sync, b0["SP"] + g + 1)
                L1S.inc(sync.dma_start(
                    out=self.d_xh_in.ap()[128 * ec:128 * (ec + 1), 128 * tt:128 * (tt + 1)],
                    in_=self.ab(o_xh + ec * TOK + 128 * tt, 128)), 16)
                L1S.inc(sync.dma_start(
                    out=self.d_xl_in.ap()[128 * ec:128 * (ec + 1), 128 * tt:128 * (tt + 1)],
                    in_=self.ab(o_xl + ec * TOK + 128 * tt, 128)), 16)
        ldm["end"] = L1S.n

    @self.block.tensor
    def _(tensor):
        for tt in range(NT):
            C.wait(tensor, b0["C"] + tt + 1)
            for ec in range(ECH):
                g = tt * ECH + ec
                if g >= 2:
                    SP.wait(tensor, b0["SP"] + g - 1)
                T.inc(tensor.transpose(self.pb[g % 2][:, 0:128],
                                       xntile[:, 128 * ec:128 * (ec + 1)], self.c_idf[:]))

    @self.block.scalar
    def _(scalar):
        for tt in range(NT):
            A.wait(scalar, b0["A"] + tt + 1)
            B.inc(scalar.activation(out=self.c_col[:, 1:2], in_=self.c_col[:, 0:1],
                                    func=AF.Sqrt, bias=self.c_zero[:, 0:1]))

    @self.block.vector
    def _(vector):
        HZ = self.sem("HZV")
        for tt in range(NT):
            L1X.wait(vector, ldm[("x", tt)])
            for k in range(4):
                HZ.inc(vector.bn_stats(self.c_stats[:, 6 * k:6 * k + 6],
                                       xtile[:, 512 * k:512 * (k + 1)]))
            HZ.wait(vector, HZ.n)
            HZ.inc(vector.bn_aggr(self.c_stats[:, 24:26], self.c_stats[:, 0:24]))
            HZ.wait(vector, HZ.n)
            A.inc(vector.tensor_scalar(self.c_col[:, 0:1], self.c_stats[:, 25:26], 1e-5, None,
                                       op0=ALU.add))
            B.wait(vector, b0["B"] + tt + 1)
            HZ.inc(vector.reciprocal(self.c_col[:, 2:3], self.c_col[:, 1:2]))
            HZ.wait(vector, HZ.n)
            vector.tensor_scalar(xntile, xtile, self.c_stats[:, 24:25], None, op0=ALU.subtract)
            C.inc(vector.tensor_scalar(xntile, xntile, self.c_col[:, 2:3], None, op0=ALU.mult))
            for ec in range(ECH):
                g = tt * ECH + ec
                T.wait(vector, b0["T"] + g + 1)
                hi_ap = self.ab(o_xh + ec * TOK + 128 * tt, 128)
                lo_ap = self.ab(o_xl + ec * TOK + 128 * tt, 128)
                vector.tensor_copy(hi_ap, self.pb[g % 2][:, 0:128])
                SP.inc(vector.tensor_tensor(lo_ap, self.pb[g % 2][:, 0:128],
                                            hi_ap, op=ALU.subtract))

    @self.block.gpsimd
    def _(gpsimd):
        L1S.wait(gpsimd, ldm["end"])
        CC.inc(gpsimd.collective_compute(
            "AllGather", ALU.bypass, replica_groups=[list(range(NCORES))],
            ins=[self.d_xh_in.ap().opt()], outs=[self.d_xh.ap().opt()]))
        CC.inc(gpsimd.collective_compute(
            "AllGather", ALU.bypass, replica_groups=[list(range(NCORES))],
            ins=[self.d_xl_in.ap().opt()], outs=[self.d_xl.ap().opt()]))


MK.ph0 = ph0
MK.ph1 = ph1


def ph2(self, h):
    nc, S, TOK, NT, QT0 = self.nc, self.S, self.TOK, self.NT, self.QT0
    NSL = S // 256
    LW = [self.sem("LW0"), self.sem("LW1")]
    LSL, LAQ, LOS, LCT = (self.sem(x) for x in ("LSL", "LAQ", "LOS", "LCT"))
    P2W, P2M, P2E, P2V, P2T, P2VE = (self.sem(x) for x in
                                     ("P2W", "P2M", "P2E", "P2V", "P2T", "P2VE"))
    N1, N2, N3, N4, N5 = (self.sem(x) for x in ("P2N1", "P2N2", "P2N3", "P2N4", "P2N5"))
    Dsc, Dbias, Dexp, Dspl, Dpv = (self.sem(x) for x in
                                   ("D_sc", "D_bias", "D_exp", "D_spl", "D_pv"))
    Dbrow, Dbc, Dbtile, Drden, Drb, Dqtev = (self.sem(x) for x in
        ("D_brow", "D_bc", "D_btile", "D_rden", "D_rb", "D_qtev"))

    o_q, o_ql, o_kh, o_kl, o_vh, o_vl = 0, S, 2 * S, 3 * S, 4 * S, 5 * S
    o_sl = 6 * S
    o_w = 6 * S + 16384
    o_p = o_w + 12288
    o_vstg = o_p + 3072
    o_ostg = o_vstg + 512
    o_q2 = o_ostg + 1024
    o_msk = o_q2 + 512
    qh = lambda c0, w: self.ab(o_q + c0, w)
    ql = lambda c0, w: self.ab(o_ql + c0, w)
    kh = lambda c0, w: self.ab(o_kh + c0, w)
    kl = lambda c0, w: self.ab(o_kl + c0, w)
    vh = lambda c0, w: self.ab(o_vh + c0, w)
    vl = lambda c0, w: self.ab(o_vl + c0, w)
    slh = lambda b, ec, c0=0, w=256: self.ab(o_sl + b * 8192 + ec * 256 + c0, w)
    sll = lambda b, ec, c0=0, w=256: self.ab(o_sl + b * 8192 + 4096 + ec * 256 + c0, w)
    w_hi = lambda j, ec: self.ab(o_w + j * 4096 + ec * 128, 128)
    w_lo = lambda j, ec: self.ab(o_w + j * 4096 + 2048 + ec * 128, 128)
    p_hi = lambda g: self.ab(o_p + (g % 3) * 512, 512)
    p_lo = lambda g: self.ab(o_p + 1536 + (g % 3) * 512, 512)
    vstg_hi = self.ab(o_vstg, 256)
    vstg_lo = self.ab(o_vstg + 256, 256)
    ostg_hi = self.ab(o_ostg, 512)
    ostg_lo = self.ab(o_ostg + 512, 512)
    q2t = self.ab(o_q2, 512)
    mskt = lambda m: self.ab(o_msk + 512 * m, 512)
    tmp = lambda g: self.af(2048 + 512 * (g % 3), 512)
    ptil = lambda g: self.af(3584 + 512 * (g % 3), 512)
    bias_tile = self.af(5120, 512)
    comb = lambda m: self.af(5632 + 512 * m, 512)
    rb_tile = self.af(5120, 512)
    qtmp = self.af(0, 256)
    ktmp = self.af(256, 256)
    vtmp = self.af(512, 256)
    wstg = [self.af(1024, 128), self.af(1152, 128)]
    den_r = self.af(7168, 512, part0=0, nparts=1)
    rden_r = self.af(7680, 512, part0=0, nparts=1)
    alq_r = self.af(8192, 512, part0=0, nparts=1)
    brow_r = self.af(8704, 512, part0=0, nparts=1)

    units = [(qt, kc) for qt in range(QT0) for kc in range(4 * qt + 4)]
    qt_last_g = {qt: max(lu for lu, (q2_, _) in enumerate(units) if q2_ == qt)
                 for qt in range(QT0)}
    g0 = self.g0
    qt0 = self.qt0
    bW, bM, bE, bV, bT, bVE = P2W.n, P2M.n, P2E.n, P2V.n, P2T.n, P2VE.n
    bN1, bN2, bN3, bN4, bN5 = N1.n, N2.n, N3.n, N4.n, N5.n
    ldm = {}
    NJ = S // 512

    def slab_pieces(sl):
        out = []
        t0g = 256 * sl
        done = 0
        while done < 256:
            t = t0g + done
            r = t // TOK
            lc = t - r * TOK
            w = min(256 - done, TOK - lc)
            out.append((r, lc, w, done))
            done += w
        return out

    @self.block.sync
    def _(sync):
        if h == 0:
            LCT.inc(sync.dma_start(out=self.ab(o_msk, 2048), in_=self.i_msk), 16)
            ldm["msk"] = LCT.n
        for j, src in enumerate([self.i_wq, self.i_wk, self.i_wv]):
            for ec in range(ECH):
                idx = j * ECH + ec
                if idx >= 2:
                    P2W.wait(sync, bW + idx - 1)
                LW[idx % 2].inc(sync.dma_start(
                    out=wstg[idx % 2],
                    in_=src[128 * ec:128 * (ec + 1), 128 * h:128 * (h + 1)]), 16)
                ldm[("w", idx)] = LW[idx % 2].n
        for sl in range(NSL):
            P2M.wait(sync, bM + max(0, 3 * (sl - 1)))
            b = sl % 2
            for (r, lc, w, doff) in slab_pieces(sl):
                src_h = self.d_xh.ap()[2048 * r:2048 * (r + 1), lc:lc + w] \
                    .rearrange("(n p) m -> p n m", p=128)
                src_l = self.d_xl.ap()[2048 * r:2048 * (r + 1), lc:lc + w] \
                    .rearrange("(n p) m -> p n m", p=128)
                if w == 256 and doff == 0:
                    LSL.inc(sync.dma_start(
                        out=self.ab(o_sl + b * 8192, 4096).rearrange("p (n m) -> p n m", n=ECH),
                        in_=src_h), 16)
                    LSL.inc(sync.dma_start(
                        out=self.ab(o_sl + b * 8192 + 4096, 4096).rearrange("p (n m) -> p n m", n=ECH),
                        in_=src_l), 16)
                else:
                    for ec in range(ECH):
                        LSL.inc(sync.dma_start(out=slh(b, ec, doff, w),
                                               in_=self.d_xh.ap()[2048 * r + 128 * ec:
                                                                  2048 * r + 128 * (ec + 1),
                                                                  lc:lc + w]), 16)
                        LSL.inc(sync.dma_start(out=sll(b, ec, doff, w),
                                               in_=self.d_xl.ap()[2048 * r + 128 * ec:
                                                                  2048 * r + 128 * (ec + 1),
                                                                  lc:lc + w]), 16)
            ldm[("slab", sl)] = LSL.n
        for qt in range(QT0):
            qtg = qt0 + qt
            Dbrow.wait(sync, qtg)
            LAQ.inc(sync.dma_start(out=alq_r, in_=self.i_alq[h:h + 1, 512 * qt:512 * (qt + 1)]), 16)
            ldm[("alq", qt)] = LAQ.n
            Dqtev.wait(sync, qtg + 1)
            LOS.inc(sync.dma_start(
                out=self.d_ohh.ap()[128 * h:128 * (h + 1), 512 * qt:512 * (qt + 1)],
                in_=ostg_hi), 16)
            LOS.inc(sync.dma_start(
                out=self.d_ohl.ap()[128 * h:128 * (h + 1), 512 * qt:512 * (qt + 1)],
                in_=ostg_lo), 16)
            ldm[("ost", qt)] = LOS.n

    @self.block.tensor
    def _(tensor):
        SBANK = lambda g: self.pb[(0, 1, 6)[g % 3]][:]

        def pv_emit(qt, kc, g):
            qtg = qt0 + qt
            last = (kc == 4 * qt + 3)
            Dspl.wait(tensor, g + 1)
            if kc == 0:
                Dqtev.wait(tensor, qtg)
            tensor.matmul(self.pb[3][0:1, 0:512], self.c_onecb[:], p_hi(g),
                          start=(kc == 0), stop=False)
            tensor.matmul(self.pb[3][0:1, 0:512], self.c_onecb[:], p_lo(g),
                          start=False, stop=last)
            tensor.matmul(self.pb[2][:], vh(kc * 128, 128), p_hi(g), start=(kc == 0), stop=False)
            tensor.matmul(self.pb[2][:], vl(kc * 128, 128), p_hi(g), start=False, stop=False)
            Dpv.inc(tensor.matmul(self.pb[2][:], vh(kc * 128, 128), p_lo(g),
                                  start=False, stop=last))

        for sl in range(NSL):
            P2E.wait(tensor, bE + 3 * sl)
            LSL.wait(tensor, ldm[("slab", sl)])
            if sl == 0:
                P2W.wait(tensor, bW + 3 * ECH)
            b = sl % 2
            for dst, wj, scale_note in ((0, 0, "q"), (1, 1, "k"), (2, 2, "v")):
                last = None
                for ec in range(ECH):
                    last = self.mm3(tensor, self.pb[dst][:, 0:256],
                                    w_hi(wj, ec), w_lo(wj, ec), slh(b, ec), sll(b, ec),
                                    start=(ec == 0), stop=(ec == ECH - 1))
                P2M.inc(last)
            if sl >= 1:
                s = sl - 1
                P2V.wait(tensor, bV + s + 1)
                P2VE.wait(tensor, bVE + s)
                for jb in range(2):
                    P2T.inc(tensor.transpose(self.pt_b[:, 128 * jb:128 * (jb + 1)],
                                             self.ab(o_vstg + 128 * jb, 128),
                                             self.c_idb[:]))
                    P2T.inc(tensor.transpose(self.pt_b[:, 256 + 128 * jb:384 + 128 * jb],
                                             self.ab(o_vstg + 256 + 128 * jb, 128),
                                             self.c_idb[:]))
        s = NSL - 1
        P2V.wait(tensor, bV + s + 1)
        P2VE.wait(tensor, bVE + s)
        for jb in range(2):
            P2T.inc(tensor.transpose(self.pt_b[:, 128 * jb:128 * (jb + 1)],
                                     self.ab(o_vstg + 128 * jb, 128), self.c_idb[:]))
            P2T.inc(tensor.transpose(self.pt_b[:, 256 + 128 * jb:384 + 128 * jb],
                                     self.ab(o_vstg + 256 + 128 * jb, 128), self.c_idb[:]))
        for j in range(2 * NJ):
            N1.wait(tensor, bN1 + j + 1)
            N3.wait(tensor, bN3 + j)
            N2.inc(tensor.matmul(self.pb[3][0:1, 0:512], self.c_onecb[:], q2t,
                                 start=True, stop=True))
        P2E.wait(tensor, bE + 3 * NSL)
        N3.wait(tensor, bN3 + 2 * NJ)
        for lu, (qt, kc) in enumerate(units):
            g = g0 + lu
            qtg = qt0 + qt
            if kc == 0:
                Dbrow.wait(tensor, qtg + 1)
                Dbtile.wait(tensor, qtg)
                Dbc.inc(tensor.matmul(self.pb[4][:], self.c_onef[:], brow_r,
                                      start=True, stop=True))
            if lu == 0:
                Dbias.wait(tensor, g0)
                Dqtev.wait(tensor, qt0)
            Dbias.wait(tensor, max(0, g - 2))
            Dsc.inc(self.mm3(tensor, SBANK(g),
                             kh(kc * 128, 128), kl(kc * 128, 128),
                             qh(512 * qt, 512), ql(512 * qt, 512),
                             start=True, stop=True))
            if kc >= 1:
                pv_emit(qt, kc - 1, g - 1)
            if kc == 4 * qt + 3:
                pv_emit(qt, kc, g)
                Drden.wait(tensor, qtg + 1)
                Drb.inc(tensor.matmul(self.pb[5][:], self.c_onef[:], rden_r,
                                      start=True, stop=True))

    @self.block.scalar
    def _(scalar):
        for sl in range(NSL):
            if sl == 0:
                self.sem("P2B").wait(scalar, h + 1)
            P2M.wait(scalar, bM + 3 * sl + 1)
            P2E.inc(scalar.activation(out=qtmp, in_=self.pb[0][:, 0:256], func=AF.Identity,
                                      bias=self.c_col[:, 3:4], scale=INV_SQRT_D))
            P2M.wait(scalar, bM + 3 * sl + 2)
            P2E.inc(scalar.activation(out=ktmp, in_=self.pb[1][:, 0:256], func=AF.Identity,
                                      bias=self.c_bqkv[:, 3 * h + 1:3 * h + 2], scale=1.0))
            P2M.wait(scalar, bM + 3 * sl + 3)
            P2E.inc(scalar.activation(out=vtmp, in_=self.pb[2][:, 0:256], func=AF.Identity,
                                      bias=self.c_bqkv[:, 3 * h + 2:3 * h + 3], scale=1.0))
        N3.wait(scalar, bN3 + 2 * NJ)
        scalar.activation(out=self.c_sc[0:1, 3:4], in_=self.c_sc[0:1, 0:1], func=AF.Sqrt,
                          bias=self.c_zero[0:1, 0:1])
        N4.inc(scalar.activation(out=self.c_sc[0:1, 4:5], in_=self.c_sc[0:1, 2:3], func=AF.Sqrt,
                                 bias=self.c_zero[0:1, 0:1]))
        for lu, (qt, kc) in enumerate(units):
            g = g0 + lu
            Dbias.wait(scalar, g + 1)
            Dspl.wait(scalar, max(0, g - 2))
            Dexp.inc(scalar.activation(
                out=ptil(g), in_=tmp(g), func=AF.Exp,
                bias=self.c_alk[:, h * (S // 128) + kc:h * (S // 128) + kc + 1], scale=1.0))

    @self.block.vector
    def _(vector):
        HZ = self.sem("HZV")
        for j in range(3):
            for ec in range(ECH):
                idx = j * ECH + ec
                LW[idx % 2].wait(vector, ldm[("w", idx)])
                vector.tensor_copy(w_hi(j, ec), wstg[idx % 2])
                P2W.inc(vector.tensor_tensor(w_lo(j, ec), wstg[idx % 2], w_hi(j, ec),
                                             op=ALU.subtract))
        self.sem("P2B").inc(vector.tensor_scalar(self.c_col[:, 3:4],
                                                 self.c_bqkv[:, 3 * h:3 * h + 1],
                                                 INV_SQRT_D, None, op0=ALU.mult))
        for sl in range(NSL):
            P2E.wait(vector, bE + 3 * sl + 1)
            self.split_hilo(vector, qtmp, qh(256 * sl, 256), ql(256 * sl, 256))
            P2E.wait(vector, bE + 3 * sl + 2)
            self.split_hilo(vector, ktmp, kh(256 * sl, 256), kl(256 * sl, 256))
            P2E.wait(vector, bE + 3 * sl + 3)
            P2T.wait(vector, bT + 4 * sl)
            vector.tensor_copy(vstg_hi, vtmp)
            P2V.inc(vector.tensor_tensor(vstg_lo, vtmp, vstg_hi, op=ALU.subtract))
            if sl >= 1:
                s = sl - 1
                P2T.wait(vector, bT + 4 * (s + 1))
                last = None
                for jb in range(2):
                    vector.tensor_copy(vh((2 * s + jb) * 128, 128),
                                       self.pt_b[:, 128 * jb:128 * (jb + 1)])
                    last = vector.tensor_copy(vl((2 * s + jb) * 128, 128),
                                              self.pt_b[:, 256 + 128 * jb:384 + 128 * jb])
                P2VE.inc(last)
        s = NSL - 1
        P2T.wait(vector, bT + 4 * (s + 1))
        last = None
        for jb in range(2):
            vector.tensor_copy(vh((2 * s + jb) * 128, 128), self.pt_b[:, 128 * jb:128 * (jb + 1)])
            last = vector.tensor_copy(vl((2 * s + jb) * 128, 128),
                                      self.pt_b[:, 256 + 128 * jb:384 + 128 * jb])
        P2VE.inc(last)
        for j in range(2 * NJ):
            if j < NJ:
                src = qh(512 * j, 512)
            else:
                src = kh(512 * (j - NJ), 512)
            N1.inc(vector.tensor_tensor(q2t, src, src, op=ALU.mult))
            N2.wait(vector, bN2 + j + 1)
            slot = 0 if j < NJ else 2
            if j % NJ == 0:
                N3.inc(vector.reduce_max(self.c_sc[0:1, slot:slot + 1],
                                         self.pb[3][0:1, 0:512], axis=AXX))
            else:
                HZ.inc(vector.reduce_max(self.c_sc[0:1, 1:2], self.pb[3][0:1, 0:512], axis=AXX))
                HZ.wait(vector, HZ.n)
                N3.inc(vector.tensor_tensor(self.c_sc[0:1, slot:slot + 1],
                                            self.c_sc[0:1, slot:slot + 1],
                                            self.c_sc[0:1, 1:2], op=ALU.max))
            N3.wait(vector, bN3 + j + 1)
        N4.wait(vector, bN4 + 1)
        HZ.inc(vector.tensor_tensor(self.c_sc[0:1, 1:2], self.c_sc[0:1, 3:4],
                                    self.c_sc[0:1, 4:5], op=ALU.mult))
        HZ.wait(vector, HZ.n)
        N5.inc(vector.tensor_scalar(self.c_sc[0:1, 5:6], self.c_sc[0:1, 1:2], 1.01, 0.5,
                                    op0=ALU.mult, op1=ALU.add))
        for lu, (qt, kc) in enumerate(units):
            g = g0 + lu
            qtg = qt0 + qt
            if kc == 0:
                LAQ.wait(vector, ldm[("alq", qt)])
                if h == 0 and qt == 0:
                    LCT.wait(vector, ldm["msk"])
                N5.wait(vector, bN5 + 1)
                Dbrow.inc(vector.tensor_scalar(brow_r, alq_r, self.c_sc[0:1, 5:6], None,
                                               op0=ALU.subtract))
                Dbc.wait(vector, qtg + 1)
                vector.tensor_copy(bias_tile, self.pb[4][:])
                last = None
                for m in range(4):
                    last = vector.tensor_tensor(comb(m), bias_tile, mskt(m), op=ALU.add)
                Dbtile.inc(last)
            Dsc.wait(vector, g + 1)
            Dexp.wait(vector, max(0, g - 2))
            sel = comb(kc - 4 * qt) if kc >= 4 * qt else bias_tile
            Dbias.inc(vector.tensor_tensor(tmp(g), self.pb[(0, 1, 6)[g % 3]][:], sel, op=ALU.add))
            Dexp.wait(vector, g + 1)
            Dpv.wait(vector, max(0, g - 2))
            vector.tensor_copy(p_hi(g), ptil(g))
            Dspl.inc(vector.tensor_tensor(p_lo(g), ptil(g), p_hi(g), op=ALU.subtract))
            if kc == 4 * qt + 3:
                Dpv.wait(vector, g + 1)
                HZ.inc(vector.tensor_copy(den_r, self.pb[3][0:1, 0:512]))
                HZ.wait(vector, HZ.n)
                Drden.inc(vector.reciprocal(rden_r, den_r))
                Drb.wait(vector, qtg + 1)
                vector.tensor_copy(rb_tile, self.pb[5][:])
                if qtg >= 1:
                    LOS.wait(vector, ldm[("ost", qt - 1)] if qt >= 1 else self._prev_ost)
                vector.tensor_tensor(self.af(0, 512),
                                     self.pb[2][:], rb_tile, op=ALU.mult)
                vector.tensor_copy(ostg_hi, self.af(0, 512))
                Dqtev.inc(vector.tensor_tensor(ostg_lo, self.af(0, 512), ostg_hi,
                                               op=ALU.subtract))

    self._prev_ost = ldm.get(("ost", QT0 - 1), 0)
    self.g0 += len(units)
    self.qt0 += QT0


MK.ph2 = ph2


def ph3(self):
    nc, S, TOK = self.nc, self.S, self.TOK
    LW = [self.sem("LW0"), self.sem("LW1")]
    LOH, LP3 = self.sem("LOH"), self.sem("LP3")
    WM, WE, WW = self.sem("P3M"), self.sem("P3E"), self.sem("P3W")
    bM, bE, bW = WM.n, WE.n, WW.n
    o_wh, o_wl, o_oh = 0, 4096, 8192
    wo_h = lambda fc, c0, w: self.ab(o_wh + fc * 2048 + c0, w)
    wo_l = lambda fc, c0, w: self.ab(o_wl + fc * 2048 + c0, w)
    oh_hi = lambda fc: self.ab(o_oh + fc * 128, 128)
    oh_lo = lambda fc: self.ab(o_oh + 256 + fc * 128, 128)
    stage = self.af(0, 2048)
    wstg = [self.af(2048, 2048), self.af(4096, 2048)]
    ldm = {}
    NTT = S // 128

    @self.block.sync
    def _(sync):
        for fc in range(HC):
            if fc >= 2:
                WW.wait(sync, bW + fc - 1)
            LW[fc % 2].inc(sync.dma_start(out=wstg[fc % 2],
                                          in_=self.i_wo[128 * fc:128 * (fc + 1), :]), 16)
            ldm[("w", fc)] = LW[fc % 2].n
        for t in range(NTT):
            WM.wait(sync, bM + t)
            for fc in range(HC):
                LOH.inc(sync.dma_start(out=oh_hi(fc),
                                       in_=self.d_ohh.ap()[128 * fc:128 * (fc + 1),
                                                           128 * t:128 * (t + 1)]), 16)
                LOH.inc(sync.dma_start(out=oh_lo(fc),
                                       in_=self.d_ohl.ap()[128 * fc:128 * (fc + 1),
                                                           128 * t:128 * (t + 1)]), 16)
            ldm[("oh", t)] = LOH.n
            WE.wait(sync, bE + t + 1)
            ch, ro = (128 * t) // 1024, (128 * t) % 1024
            LP3.inc(sync.dma_start(out=self.d_part[ch].ap()[ro:ro + 128, 0:2048], in_=stage), 16)
            ldm[("st", t)] = LP3.n

    @self.block.tensor
    def _(tensor):
        for t in range(NTT):
            LOH.wait(tensor, ldm[("oh", t)])
            if t == 0:
                WW.wait(tensor, bW + HC)
            if t >= 1:
                WE.wait(tensor, bE + t)
            last = None
            for og in range(4):
                for fc in range(HC):
                    last = self.mm3(tensor, self.pb[og][:],
                                    oh_hi(fc), oh_lo(fc),
                                    wo_h(fc, 512 * og, 512), wo_l(fc, 512 * og, 512),
                                    start=(fc == 0), stop=(fc == HC - 1))
            WM.inc(last)

    @self.block.vector
    def _(vector):
        for fc in range(HC):
            LW[fc % 2].wait(vector, ldm[("w", fc)])
            vector.tensor_copy(self.ab(o_wh + fc * 2048, 2048), wstg[fc % 2])
            WW.inc(vector.tensor_tensor(self.ab(o_wl + fc * 2048, 2048), wstg[fc % 2],
                                        self.ab(o_wh + fc * 2048, 2048), op=ALU.subtract))
        for t in range(NTT):
            WM.wait(vector, bM + t + 1)
            if t >= 1:
                LP3.wait(vector, ldm[("st", t - 1)])
            for og in range(3):
                vector.tensor_copy(stage[:, 512 * og:512 * (og + 1)], self.pb[og][:])
            WE.inc(vector.tensor_copy(stage[:, 1536:2048], self.pb[3][:]))


MK.ph3 = ph3


def ph4(self):
    nc, S, TOK = self.nc, self.S, self.TOK
    NSL = S // 256
    NIC = ICP // 128
    CC = self.sem("CC")
    LDQ, LSL = self.sem("LDQ"), self.sem("LSL")
    LY = [self.sem("LY0"), self.sem("LY1")]
    LL = [self.sem("LL0"), self.sem("LL1")]
    LWT = [self.sem("LWT0"), self.sem("LWT1")]
    QX, QB, QW = self.sem("P4X"), self.sem("P4B"), self.sem("P4W")
    G, SI, HH, D2, Y = (self.sem(x) for x in ("P4G", "P4S", "P4H", "P4D2", "P4Y"))
    C1, C2, C3, C4, C5 = (self.sem(x) for x in ("P4C1", "P4C2", "P4C3", "P4C4", "P4C5"))
    o_gh, o_gl, o_dh, o_dl = 0, 24576, 49152, 61440
    o_slh, o_sll = 73728, 77824
    o_h = 81920
    gu_h = lambda ec, c0, w: self.ab(o_gh + ec * 1536 + c0, w)
    gu_l = lambda ec, c0, w: self.ab(o_gl + ec * 1536 + c0, w)
    wd_h = lambda ic, c0, w: self.ab(o_dh + ic * 2048 + c0, w)
    wd_l = lambda ic, c0, w: self.ab(o_dl + ic * 2048 + c0, w)
    slh = lambda ec, c0=0, w=256: self.ab(o_slh + ec * 256 + c0, w)
    sll = lambda ec, c0=0, w=256: self.ab(o_sll + ec * 256 + c0, w)
    h_hi = lambda ic, c0, w: self.ab(o_h + ic * 256 + c0, w)
    h_lo = lambda ic, c0, w: self.ab(o_h + 1536 + ic * 256 + c0, w)
    nibf = self.af(2048, 512)
    t1 = self.af(2560, 512)
    t2 = self.af(3072, 512)
    val = self.af(3584, 512)
    silu_t = self.af(4096, 256)
    srow = self.af(8192, 512, part0=0, nparts=2)
    stage = [self.af(0, 2048), self.af(6656, 2048)]
    ldm = {}
    dq = [("gu", ec, ot) for ec in range(ECH) for ot in range(3)] + \
         [("dw", ic, eg) for ic in range(NIC) for eg in range(4)]

    def slab_pieces(sl):
        out, done = [], 0
        while done < 256:
            t = 256 * sl + done
            r, lc = t // TOK, t % TOK
            w = min(256 - done, TOK - lc)
            out.append((r, lc, w, done))
            done += w
        return out

    @self.block.sync
    def _(sync):
        for u, (kind, a, bb) in enumerate(dq):
            QX.wait(sync, u)
            src = self.i_gup if kind == "gu" else self.i_dwp
            LDQ.inc(sync.dma_start(out=self.c_i32a[:],
                                   in_=src[128 * a:128 * (a + 1), 512 * bb:512 * (bb + 1)]), 16)
            if kind == "gu":
                LDQ.inc(sync.dma_start(out=self.c_f16[0:1, 0:512],
                                       in_=self.i_gus[a:a + 1, 512 * bb:512 * (bb + 1)]), 16)
            else:
                LDQ.inc(sync.dma_start(out=self.c_f16[0:2, 0:512],
                                       in_=self.i_dws[2 * a:2 * a + 2, 512 * bb:512 * (bb + 1)]), 16)
            ldm[("dq", u)] = LDQ.n
        for sl in range(NSL):
            G.wait(sync, 6 * sl)
            for (r, lc, w, doff) in slab_pieces(sl):
                if w == 256 and doff == 0:
                    src_h = self.d_xh.ap()[2048 * r:2048 * (r + 1), lc:lc + w] \
                        .rearrange("(n p) m -> p n m", p=128)
                    src_l = self.d_xl.ap()[2048 * r:2048 * (r + 1), lc:lc + w] \
                        .rearrange("(n p) m -> p n m", p=128)
                    LSL.inc(sync.dma_start(
                        out=self.ab(o_slh, 4096).rearrange("p (n m) -> p n m", n=ECH),
                        in_=src_h), 16)
                    LSL.inc(sync.dma_start(
                        out=self.ab(o_sll, 4096).rearrange("p (n m) -> p n m", n=ECH),
                        in_=src_l), 16)
                else:
                    for ec in range(ECH):
                        LSL.inc(sync.dma_start(out=slh(ec, doff, w),
                                               in_=self.d_xh.ap()[2048 * r + 128 * ec:
                                                                  2048 * r + 128 * (ec + 1),
                                                                  lc:lc + w]), 16)
                        LSL.inc(sync.dma_start(out=sll(ec, doff, w),
                                               in_=self.d_xl.ap()[2048 * r + 128 * ec:
                                                                  2048 * r + 128 * (ec + 1),
                                                                  lc:lc + w]), 16)
            ldm[("slab", sl)] = LSL.n
            for tsub in range(2):
                ts_idx = 2 * sl + tsub
                Y.wait(sync, 8 * sl + 4 * tsub + 4)
                tg = 256 * sl + 128 * tsub
                ch, ro = tg // 1024, tg % 1024
                LY[ts_idx % 2].inc(sync.dma_start(
                    out=self.d_part[ch].ap()[ro:ro + 128, 2048:4096],
                    in_=stage[tsub % 2]), 16)
                ldm[("st", ts_idx)] = LY[ts_idx % 2].n
                ldm[("rs", ts_idx)] = (self.sem("LY0").n, self.sem("LY1").n)
        for ec in range(ECH):
            if ec < 2:
                LY[0].wait(sync, self.sem("LY0").n)
                LY[1].wait(sync, self.sem("LY1").n)
            else:
                C1.wait(sync, ec - 1)
            LL[ec % 2].inc(sync.dma_start(out=stage[ec % 2],
                                          in_=self.i_wl1[128 * ec:128 * (ec + 1), :]), 16)
            ldm[("l1w", ec)] = LL[ec % 2].n
        for ec in range(ECH):
            u = ECH + ec
            if ec < 2:
                C1.wait(sync, ECH)
            else:
                C5.wait(sync, ec - 1)
            LL[u % 2].inc(sync.dma_start(out=stage[u % 2],
                                         in_=self.i_wl1[128 * ec:128 * (ec + 1), :]), 16)
            ldm[("l1w", u)] = LL[u % 2].n
            C5.wait(sync, ec + 1)
            LWT[ec % 2].inc(sync.dma_start(out=self.d_wt.ap()[128 * ec:128 * (ec + 1), :],
                                           in_=self.ab(73728 + 2048 * (ec % 2), 2048)), 16)
            ldm[("wt", ec)] = LWT[ec % 2].n

    @self.block.tensor
    def _(tensor):
        for u, (kind, a, bb) in enumerate(dq):
            LDQ.wait(tensor, ldm[("dq", u)])
            QX.wait(tensor, u + 1)
            QW.wait(tensor, u)
            if kind == "gu":
                QB.inc(tensor.matmul(self.pb[4][:], self.c_onef[:], srow[0:1, 0:512],
                                     start=True, stop=True))
            else:
                QB.inc(tensor.matmul(self.pb[4][:], self.c_dsel[:], srow[0:2, 0:512],
                                     start=True, stop=True))
        for sl in range(NSL):
            LSL.wait(tensor, ldm[("slab", sl)])
            if sl == 0:
                QW.wait(tensor, len(dq))
            for i in range(6):
                p = 6 * sl + i
                pg, pu = (0, 1) if i % 2 == 0 else (5, 6)
                HH.wait(tensor, max(0, p - 1))
                SI.wait(tensor, max(0, p - 1))
                for ec in range(ECH):
                    self.mm3(tensor, self.pb[pg][:, 0:256],
                             gu_h(ec, 128 * i, 128), gu_l(ec, 128 * i, 128),
                             slh(ec), sll(ec), start=(ec == 0), stop=(ec == ECH - 1))
                last = None
                for ec in range(ECH):
                    last = self.mm3(tensor, self.pb[pu][:, 0:256],
                                    gu_h(ec, 768 + 128 * i, 128), gu_l(ec, 768 + 128 * i, 128),
                                    slh(ec), sll(ec), start=(ec == 0), stop=(ec == ECH - 1))
                G.inc(last)
            HH.wait(tensor, 6 * (sl + 1))
            for tsub in range(2):
                for eg in range(4):
                    d = 8 * sl + 4 * tsub + eg
                    Y.wait(tensor, max(0, d - 1))
                    last = None
                    for ic in range(NIC):
                        last = self.mm3(tensor, self.pb[2 + d % 2][:],
                                        h_hi(ic, 128 * tsub, 128), h_lo(ic, 128 * tsub, 128),
                                        wd_h(ic, 512 * eg, 512), wd_l(ic, 512 * eg, 512),
                                        start=(ic == 0), stop=(ic == NIC - 1))
                    D2.inc(last)
        HH.wait(tensor, 6 * NSL)
        C1.wait(tensor, ECH)
        C2.inc(tensor.matmul(self.pb[6][0:1, 0:1], self.c_col[:, 5:6], self.c_onecf[:],
                             start=True, stop=True))
        C4.wait(tensor, 1)
        tensor.matmul(self.pb[6][:, 1:2], self.c_onef[:], self.c_sc[0:1, 10:11],
                      start=True, stop=True)
        tensor.matmul(self.pb[6][:, 2:3], self.c_onef[:], self.c_sc[0:1, 11:12],
                      start=True, stop=True)
        C3.inc(tensor.matmul(self.pb[6][:, 3:4], self.c_onef[:], self.c_sc[0:1, 8:9],
                             start=True, stop=True))

    @self.block.scalar
    def _(scalar):
        for sl in range(NSL):
            for i in range(6):
                p = 6 * sl + i
                pg = 0 if i % 2 == 0 else 5
                G.wait(scalar, p + 1)
                if p >= 1:
                    HH.wait(scalar, p)
                SI.inc(scalar.activation(out=silu_t, in_=self.pb[pg][:, 0:256], func=AF.Silu,
                                         bias=self.c_zero[:, 0:1]))

    @self.block.vector
    def _(vector):
        HZ = self.sem("HZV")
        for u, (kind, a, bb) in enumerate(dq):
            LDQ.wait(vector, ldm[("dq", u)])
            nparts = 1 if kind == "gu" else 2
            vector.tensor_copy(srow[0:nparts, :], self.c_f16[0:nparts, 0:512])
            QX.inc(vector.tensor_scalar(self.c_i32b[:], self.c_i32a[:],
                                        self.c_nmask[:, 0:1], None,
                                        op0=ALU.bitwise_and))
            vector.tensor_copy(nibf, self.c_i32b[:])
            vector.tensor_scalar(nibf, nibf, self.c_ninv[:, 0:1], None, op0=ALU.mult)
            vector.tensor_scalar(t1, nibf, 8.0, None, op0=ALU.is_ge)
            vector.scalar_tensor_tensor(t2, t1, -8.0, nibf, op0=ALU.mult, op1=ALU.add)
            vector.tensor_scalar(val, t2, 4.0, 0.5, op0=ALU.min, op1=ALU.mult)
            vector.tensor_scalar(nibf, t2, 4.0, 0.0, op0=ALU.subtract, op1=ALU.max)
            vector.scalar_tensor_tensor(nibf, t2, 7.0, nibf, op0=ALU.is_ge, op1=ALU.add)
            vector.tensor_tensor(val, val, nibf, op=ALU.add)
            vector.tensor_scalar(t1, t1, -2.0, 1.0, op0=ALU.mult, op1=ALU.add)
            vector.tensor_tensor(val, val, t1, op=ALU.mult)
            QB.wait(vector, u + 1)
            vector.tensor_tensor(val, val, self.pb[4][:], op=ALU.mult)
            if kind == "gu":
                dh, dl = gu_h(a, 512 * bb, 512), gu_l(a, 512 * bb, 512)
            else:
                dh, dl = wd_h(a, 512 * bb, 512), wd_l(a, 512 * bb, 512)
            vector.tensor_copy(dh, val)
            QW.inc(vector.tensor_tensor(dl, val, dh, op=ALU.subtract))
        for sl in range(NSL):
            for i in range(6):
                p = 6 * sl + i
                pu = 1 if i % 2 == 0 else 6
                SI.wait(vector, p + 1)
                if sl >= 1 and i == 0:
                    D2.wait(vector, 8 * sl)
                vector.tensor_tensor(silu_t, silu_t, self.pb[pu][:, 0:256], op=ALU.mult)
                vector.tensor_copy(h_hi(i, 0, 256), silu_t)
                HH.inc(vector.tensor_tensor(h_lo(i, 0, 256), silu_t, h_hi(i, 0, 256),
                                            op=ALU.subtract))
            for tsub in range(2):
                for eg in range(4):
                    d = 8 * sl + 4 * tsub + eg
                    D2.wait(vector, d + 1)
                    if 2 * sl + tsub >= 2 and eg == 0:
                        LY[(2 * sl + tsub) % 2].wait(vector, ldm[("st", 2 * sl + tsub - 2)])
                    Y.inc(vector.tensor_copy(stage[tsub % 2][:, 512 * eg:512 * (eg + 1)],
                                              self.pb[2 + d % 2][:]))
        for ec in range(ECH):
            LL[ec % 2].wait(vector, ldm[("l1w", ec)])
            if ec == 0:
                C1.inc(vector.reduce_sum(self.c_col[:, 5:6], stage[0], axis=AXX,
                                         apply_absolute_value=True))
            else:
                HZ.inc(vector.reduce_sum(self.c_col[:, 4:5], stage[ec % 2], axis=AXX,
                                         apply_absolute_value=True))
                HZ.wait(vector, HZ.n)
                C1.inc(vector.tensor_tensor(self.c_col[:, 5:6], self.c_col[:, 5:6],
                                            self.c_col[:, 4:5], op=ALU.add))
            C1.wait(vector, ec + 1)
        C2.wait(vector, 1)
        HZ.inc(vector.tensor_scalar(self.c_sc[0:1, 8:9], self.pb[6][0:1, 0:1],
                                    1.0 / (E * E), None, op0=ALU.mult))
        HZ.wait(vector, HZ.n)
        HZ.inc(vector.tensor_scalar(self.c_sc[0:1, 10:11], self.c_sc[0:1, 8:9], 1e-10, 0.5,
                                    op0=ALU.add, op1=ALU.mult))
        HZ.wait(vector, HZ.n)
        C4.inc(vector.tensor_scalar(self.c_sc[0:1, 11:12], self.c_sc[0:1, 10:11],
                                    -1.0, None, op0=ALU.mult))
        C3.wait(vector, 1)
        vector.tensor_copy(self.c_col[:, 4:5], self.pb[6][:, 1:2])
        vector.tensor_copy(self.c_col[:, 5:6], self.pb[6][:, 2:3])
        HZ.inc(vector.tensor_copy(self.c_col[:, 6:7], self.pb[6][:, 3:4]))
        HZ.wait(vector, HZ.n)
        for ec in range(ECH):
            u = ECH + ec
            LL[u % 2].wait(vector, ldm[("l1w", u)])
            if ec >= 2:
                LWT[ec % 2].wait(vector, ldm[("wt", ec - 2)])
            vector.tensor_scalar(self.af(2048, 2048), stage[u % 2], self.c_col[:, 4:5],
                                 None, op0=ALU.is_gt)
            vector.tensor_scalar(self.af(4096, 2048), stage[u % 2], self.c_col[:, 5:6],
                                 None, op0=ALU.is_lt)
            C5.inc(vector.tensor_tensor(self.ab(73728 + 2048 * (ec % 2), 2048),
                                        self.af(2048, 2048), self.af(4096, 2048),
                                        op=ALU.subtract))

    @self.block.gpsimd
    def _(gpsimd):
        for ch in range(self.NCH):
            last_t128 = (1024 * (ch + 1) - 128) // 128
            n0, n1 = ldm[("rs", last_t128)]
            if n0:
                self.sem("LY0").wait(gpsimd, n0)
            if n1:
                self.sem("LY1").wait(gpsimd, n1)
            CC.inc(gpsimd.collective_compute(
                "ReduceScatter", ALU.add, replica_groups=[list(range(NCORES))],
                ins=[self.d_part[ch].ap().opt()], outs=[self.d_rs[ch].ap().opt()]))


MK.ph4 = ph4


def ph5(self):
    nc, S, TOK, NT = self.nc, self.S, self.TOK, self.NT
    L5, L5S, L5P, LCT = self.sem("L5"), self.sem("L5S"), self.sem("L5P"), self.sem("LCT")
    R1, R2, R3, R4, R5 = (self.sem(x) for x in ("P5R1", "P5R2", "P5R3", "P5R4", "P5R5"))
    TT, SP = self.sem("P5T"), self.sem("P5S")
    BO = self.sem("P5BO")
    buf1 = self.af(0, 2048)
    buf2 = self.af(2048, 2048)
    bo_bc = self.af(4096, 2048)
    bo_row = self.af(6656, 2048, part0=0, nparts=1)
    o_xh, o_xl = 0, ECH * TOK
    ldm = {}

    @self.block.sync
    def _(sync):
        LCT.inc(sync.dma_start(out=bo_row, in_=self.i_bo), 16)
        ldm["bo"] = LCT.n
        for tt in range(NT):
            if tt > 0:
                R4.wait(sync, tt)
                TT.wait(sync, ECH * tt)
                L5P.wait(sync, ldm[("spw", tt - 1)])
            L5.inc(sync.dma_start(out=buf1, in_=self.d_rs[tt].ap()[:, 2048:4096]), 16)
            L5.inc(sync.dma_start(out=buf2, in_=self.d_rs[tt].ap()[:, 0:2048]), 16)
            ldm[("y", tt)] = L5.n
            R1.wait(sync, tt + 1)
            L5.inc(sync.dma_start(out=buf2, in_=self.i_x2[128 * tt:128 * (tt + 1), :]), 16)
            ldm[("x", tt)] = L5.n
            R2.wait(sync, tt + 1)
            L5P.inc(sync.dma_start(out=self.d_x1.ap()[128 * tt:128 * (tt + 1), :], in_=buf1), 16)
            ldm[("spw", tt)] = L5P.n
            for ec in range(ECH):
                g = tt * ECH + ec
                SP.wait(sync, g + 1)
                L5S.inc(sync.dma_start(
                    out=self.d_x1h.ap()[128 * ec:128 * (ec + 1), 128 * tt:128 * (tt + 1)],
                    in_=self.ab(o_xh + ec * TOK + 128 * tt, 128)), 16)
                L5S.inc(sync.dma_start(
                    out=self.d_x1l.ap()[128 * ec:128 * (ec + 1), 128 * tt:128 * (tt + 1)],
                    in_=self.ab(o_xl + ec * TOK + 128 * tt, 128)), 16)

    @self.block.tensor
    def _(tensor):
        LCT.wait(tensor, ldm["bo"])
        last = None
        for og in range(4):
            last = tensor.matmul(self.pb[og][:], self.c_onef[:],
                                 bo_row[0:1, 512 * og:512 * (og + 1)], start=True, stop=True)
        BO.inc(last)
        for tt in range(NT):
            R4.wait(tensor, tt + 1)
            for ec in range(ECH):
                g = tt * ECH + ec
                if g >= 2:
                    SP.wait(tensor, g - 1)
                TT.inc(tensor.transpose(self.pb[4 + g % 2][:, 0:128],
                                        buf2[:, 128 * ec:128 * (ec + 1)], self.c_idf[:]))

    @self.block.scalar
    def _(scalar):
        for tt in range(NT):
            R3.wait(scalar, 2 * tt + 1)
            R3.inc(scalar.activation(out=self.c_col[:, 1:2], in_=self.c_col[:, 0:1],
                                     func=AF.Sqrt, bias=self.c_zero[:, 0:1]))
            R5.wait(scalar, 2 * tt + 1)
            R5.inc(scalar.activation(out=self.c_col[:, 4:5], in_=self.c_col[:, 3:4],
                                     func=AF.Sqrt, bias=self.c_zero[:, 0:1]))

    @self.block.vector
    def _(vector):
        HZ = self.sem("HZV")
        BO.wait(vector, 1)
        for og in range(4):
            vector.tensor_copy(bo_bc[:, 512 * og:512 * (og + 1)], self.pb[og][:])
        for tt in range(NT):
            L5.wait(vector, ldm[("y", tt)])
            vector.tensor_tensor(self.af(8192, 512), buf1[:, 0:512], buf1[:, 0:512], op=ALU.mult)
            HZ.inc(vector.reduce_sum(self.c_col[:, 0:1], self.af(8192, 512), axis=AXX))
            for k in range(1, 4):
                vector.tensor_tensor(self.af(8192, 512), buf1[:, 512 * k:512 * (k + 1)],
                                     buf1[:, 512 * k:512 * (k + 1)], op=ALU.mult)
                HZ.inc(vector.reduce_sum(self.c_col[:, 5:6], self.af(8192, 512), axis=AXX))
                HZ.wait(vector, HZ.n)
                HZ.inc(vector.tensor_tensor(self.c_col[:, 0:1], self.c_col[:, 0:1],
                                            self.c_col[:, 5:6], op=ALU.add))
                HZ.wait(vector, HZ.n)
            R3.inc(vector.tensor_scalar(self.c_col[:, 0:1], self.c_col[:, 0:1],
                                        1.0 / E, 1e-6, op0=ALU.mult, op1=ALU.add))
            R3.wait(vector, 2 * (tt + 1))
            HZ.inc(vector.reciprocal(self.c_col[:, 2:3], self.c_col[:, 1:2]))
            HZ.wait(vector, HZ.n)
            vector.tensor_scalar(buf1, buf1, self.c_col[:, 2:3], None, op0=ALU.mult)
            vector.tensor_tensor(buf1, buf1, buf2, op=ALU.add)
            R1.inc(vector.tensor_tensor(buf1, buf1, bo_bc, op=ALU.add))
            L5.wait(vector, ldm[("x", tt)])
            R2.inc(vector.tensor_tensor(buf1, buf1, buf2, op=ALU.add))
            for k in range(4):
                HZ.inc(vector.bn_stats(self.c_stats[:, 6 * k:6 * k + 6],
                                       buf1[:, 512 * k:512 * (k + 1)]))
            HZ.wait(vector, HZ.n)
            HZ.inc(vector.bn_aggr(self.c_stats[:, 24:26], self.c_stats[:, 0:24]))
            HZ.wait(vector, HZ.n)
            R5.inc(vector.tensor_scalar(self.c_col[:, 3:4], self.c_stats[:, 25:26], 1e-5, None,
                                        op0=ALU.add))
            R5.wait(vector, 2 * (tt + 1))
            HZ.inc(vector.reciprocal(self.c_col[:, 5:6], self.c_col[:, 4:5]))
            HZ.wait(vector, HZ.n)
            vector.tensor_scalar(buf2, buf1, self.c_stats[:, 24:25], None, op0=ALU.subtract)
            R4.inc(vector.tensor_scalar(buf2, buf2, self.c_col[:, 5:6], None, op0=ALU.mult))
            for ec in range(ECH):
                g = tt * ECH + ec
                TT.wait(vector, g + 1)
                hi_ap = self.ab(o_xh + ec * TOK + 128 * tt, 128)
                lo_ap = self.ab(o_xl + ec * TOK + 128 * tt, 128)
                vector.tensor_copy(hi_ap, self.pb[4 + g % 2][:, 0:128])
                SP.inc(vector.tensor_tensor(lo_ap, self.pb[4 + g % 2][:, 0:128],
                                            hi_ap, op=ALU.subtract))


MK.ph5 = ph5


def ph6(self):
    nc, S, TOK, NT = self.nc, self.S, self.TOK, self.NT
    CC = self.sem("CC")
    LCT = self.sem("LCT")
    LW = [self.sem("LW0"), self.sem("LW1")]
    LQ = [self.sem("LQ0"), self.sem("LQ1")]
    L6KV = self.sem("L6KV")
    W6, M6, E6, Q6 = (self.sem(x) for x in ("P6W", "P6M", "P6E", "P6Q"))
    o_xh, o_xl = 0, ECH * TOK
    o_wah, o_wal = 2 * ECH * TOK, 2 * ECH * TOK + 8192
    wstg = [self.af(0, 512), self.af(512, 512)]
    cos_t = lambda tt: self.af(2048 + 128 * tt, 128)
    sin_t = lambda tt: self.af(2048 + 128 * NT + 128 * tt, 128)
    qstage = [self.af(4096, 512), self.af(5888, 512)]
    qa = [self.af(4864, 512), self.af(6400, 512)]
    rot = self.af(4608, 128)
    kst = lambda tt: self.af(6912 + 128 * tt, 128)
    vst = lambda tt: self.af(7936 + 128 * tt, 128)
    ldm = {}
    ogs = [4, 0, 1, 2, 3]

    @self.block.sync
    def _(sync):
        for tt in range(NT):
            LCT.inc(sync.dma_start(out=cos_t(tt), in_=self.i_cos[128 * tt:128 * (tt + 1), :]), 16)
            LCT.inc(sync.dma_start(out=sin_t(tt), in_=self.i_sin[128 * tt:128 * (tt + 1), :]), 16)
        ldm["cs"] = LCT.n
        for oi, og in enumerate(ogs):
            wdt = 256 if og == 4 else 512
            for ec in range(ECH):
                idx = oi * ECH + ec
                if idx >= 2:
                    W6.wait(sync, idx - 1)
                LW[idx % 2].inc(sync.dma_start(out=wstg[idx % 2][:, 0:wdt],
                                               in_=self.i_wa[128 * ec:128 * (ec + 1),
                                                             512 * og:512 * og + wdt]), 16)
                ldm[("w", idx)] = LW[idx % 2].n
            for tt in range(NT):
                idx = oi * NT + tt
                Q6.wait(sync, idx + 1)
                if og == 4:
                    L6KV.inc(sync.dma_start(out=self.d_k1in.ap()[128 * tt:128 * (tt + 1), :],
                                            in_=kst(tt)), 16)
                    L6KV.inc(sync.dma_start(out=self.d_v1in.ap()[128 * tt:128 * (tt + 1), :],
                                            in_=vst(tt)), 16)
                else:
                    LQ[idx % 2].inc(sync.dma_start(
                        out=self.d_q1.ap()[128 * tt:128 * (tt + 1), 512 * og:512 * (og + 1)],
                        in_=qa[idx % 2]), 16)
                    ldm[("st", idx)] = LQ[idx % 2].n
        ldm["kv_end"] = L6KV.n

    @self.block.tensor
    def _(tensor):
        for oi, og in enumerate(ogs):
            wdt = 256 if og == 4 else 512
            W6.wait(tensor, ECH * (oi + 1))
            for tt in range(NT):
                idx = oi * NT + tt
                if idx >= 2:
                    E6.wait(tensor, idx - 1)
                last = None
                for ec in range(ECH):
                    last = self.mm3(tensor, self.pb[idx % 2][:, 0:wdt],
                                    self.ab(o_xh + ec * TOK + 128 * tt, 128),
                                    self.ab(o_xl + ec * TOK + 128 * tt, 128),
                                    self.ab(o_wah + ec * 512, wdt),
                                    self.ab(o_wal + ec * 512, wdt),
                                    start=(ec == 0), stop=(ec == ECH - 1))
                M6.inc(last)

    @self.block.scalar
    def _(scalar):
        for oi, og in enumerate(ogs):
            wdt = 256 if og == 4 else 512
            sc = 1.0 if og == 4 else INV_SQRT_D
            for tt in range(NT):
                idx = oi * NT + tt
                M6.wait(scalar, idx + 1)
                if idx >= 2:
                    Q6.wait(scalar, idx - 1)
                E6.inc(scalar.activation(out=qstage[idx % 2][:, 0:wdt],
                                         in_=self.pb[idx % 2][:, 0:wdt],
                                         func=AF.Identity, bias=self.c_zero[:, 0:1], scale=sc))

    @self.block.vector
    def _(vector):
        for oi, og in enumerate(ogs):
            for ec in range(ECH):
                idx = oi * ECH + ec
                wdt = 256 if og == 4 else 512
                LW[idx % 2].wait(vector, ldm[("w", idx)])
                vector.tensor_copy(self.ab(o_wah + ec * 512, wdt), wstg[idx % 2][:, 0:wdt])
                W6.inc(vector.tensor_tensor(self.ab(o_wal + ec * 512, wdt),
                                            wstg[idx % 2][:, 0:wdt],
                                            self.ab(o_wah + ec * 512, wdt), op=ALU.subtract))
            HZ = self.sem("HZV")
            for tt in range(NT):
                idx = oi * NT + tt
                E6.wait(vector, idx + 1)
                if oi == 0 and tt == 0:
                    LCT.wait(vector, ldm["cs"])
                src = qstage[idx % 2]
                if og == 4:
                    vector.tensor_tensor(kst(tt), src[:, 0:128], cos_t(tt), op=ALU.mult)
                    vector.tensor_scalar(rot[:, 0:64], src[:, 64:128], -1.0, None, op0=ALU.mult)
                    HZ.inc(vector.tensor_copy(rot[:, 64:128], src[:, 0:64]))
                    HZ.wait(vector, HZ.n)
                    HZ.inc(vector.tensor_tensor(rot, rot, sin_t(tt), op=ALU.mult))
                    HZ.wait(vector, HZ.n)
                    vector.tensor_tensor(kst(tt), kst(tt), rot, op=ALU.add)
                    Q6.inc(vector.tensor_copy(vst(tt), src[:, 128:256]))
                else:
                    if idx >= 2 and ("st", idx - 2) in ldm:
                        LQ[idx % 2].wait(vector, ldm[("st", idx - 2)])
                    dst = qa[idx % 2]
                    last = None
                    for j in range(4):
                        js = slice(128 * j, 128 * (j + 1))
                        vector.tensor_tensor(dst[:, js], src[:, js], cos_t(tt), op=ALU.mult)
                        vector.tensor_scalar(rot[:, 0:64], src[:, 128 * j + 64:128 * (j + 1)],
                                             -1.0, None, op0=ALU.mult)
                        HZ.inc(vector.tensor_copy(rot[:, 64:128], src[:, 128 * j:128 * j + 64]))
                        HZ.wait(vector, HZ.n)
                        HZ.inc(vector.tensor_tensor(rot, rot, sin_t(tt), op=ALU.mult))
                        HZ.wait(vector, HZ.n)
                        last = vector.tensor_tensor(dst[:, js], dst[:, js], rot, op=ALU.add)
                    Q6.inc(last)

    @self.block.gpsimd
    def _(gpsimd):
        L6KV.wait(gpsimd, ldm["kv_end"])
        CC.inc(gpsimd.collective_compute(
            "AllGather", ALU.bypass, replica_groups=[list(range(NCORES))],
            ins=[self.d_k1in.ap().opt()], outs=[self.d_k1.ap().opt()]))
        CC.inc(gpsimd.collective_compute(
            "AllGather", ALU.bypass, replica_groups=[list(range(NCORES))],
            ins=[self.d_v1in.ap().opt()], outs=[self.d_v1.ap().opt()]))


MK.ph6 = ph6



def ph7(self):
    nc, S, TOK, NT = self.nc, self.S, self.TOK, self.NT
    QT1, QW1, KC = self.QT1, self.QW1, self.KC
    L7K, L7Q, L7S = self.sem("L7K"), self.sem("L7Q"), self.sem("L7S")
    KT, KE, VE = (self.sem(x) for x in ("P7KT", "P7KE", "P7VE"))
    KN1, KN2, KN3, KM, KSQ = (self.sem(x) for x in ("P7KN1", "P7KN2", "P7KN3", "P7KM", "P7KSQ"))
    QP, QS = self.sem("P7QP"), self.sem("P7QS")
    QN1, QN2, QN3, QM, QSQ = (self.sem(x) for x in ("P7QN1", "P7QN2", "P7QN3", "P7QM", "P7QSQ"))
    VM, TM, VC = (self.sem(x) for x in ("P7VM", "P7TM", "P7VC"))
    Asc, Aexp, Aspl, Apv = (self.sem(x) for x in ("A_sc", "A_exp", "A_spl", "A_pv"))
    Arden, Arb, Aqtev = (self.sem(x) for x in ("A_rden", "A_rb", "A_qtev"))
    o_kth, o_ktl, o_vh, o_vl = 0, S, 2 * S, 3 * S
    o_qh, o_ql = 4 * S, 4 * S + TOK
    o_p = 4 * S + 2 * TOK
    o_ostg = o_p + 3072
    o_q2 = o_ostg + 1024
    kth = lambda c0, w: self.ab(o_kth + c0, w)
    ktl = lambda c0, w: self.ab(o_ktl + c0, w)
    vh = lambda c0, w: self.ab(o_vh + c0, w)
    vl = lambda c0, w: self.ab(o_vl + c0, w)
    qhh = lambda c0, w: self.ab(o_qh + c0, w)
    qll = lambda c0, w: self.ab(o_ql + c0, w)
    p_hi = lambda g: self.ab(o_p + (g % 3) * 512, 512)
    p_lo = lambda g: self.ab(o_p + 1536 + (g % 3) * 512, 512)
    ostg_hi = self.ab(o_ostg, 512)
    ostg_lo = self.ab(o_ostg + 512, 512)
    q2t = self.ab(o_q2, 512)
    kstg = self.af(0, 128)
    qstg = self.af(128, 128)
    ptil = lambda g: self.af(3584 + 512 * (g % 3), 512)
    ostg_f = self.af(5120, 512)
    rb_tile = self.af(6656, 512)
    den_r = self.af(7168, 512, part0=0, nparts=1)
    rden_r = self.af(7680, 512, part0=0, nparts=1)
    negM = self.c_col[:, 3:4]
    ldm = {}
    NKB = S // 128
    NJK = S // 512
    NJ1 = max(1, TOK // 512)
    UPH = QT1 * KC
    g1b = self.g1

    def gidx(h, qt, kc):
        return g1b + h * UPH + qt * KC + kc

    @self.block.sync
    def _(sync):
        for kb in range(NKB):
            KT.wait(sync, kb)
            L7K.inc(sync.dma_start(out=kstg, in_=self.d_k1.ap()[128 * kb:128 * (kb + 1), :]), 16)
            ldm[("k", kb)] = L7K.n
        for kb in range(NKB):
            if kb == 0:
                KT.wait(sync, NKB)
            else:
                VE.wait(sync, kb)
            L7K.inc(sync.dma_start(out=kstg, in_=self.d_v1.ap()[128 * kb:128 * (kb + 1), :]), 16)
            ldm[("v", kb)] = L7K.n
        for h in range(H):
            for tt in range(NT):
                u = h * NT + tt
                QP.wait(sync, u)
                L7Q.inc(sync.dma_start(out=qstg,
                                       in_=self.d_q1.ap()[128 * tt:128 * (tt + 1),
                                                          128 * h:128 * (h + 1)]), 16)
                ldm[("q", u)] = L7Q.n
            for qt in range(QT1):
                qtg = h * QT1 + qt
                Aqtev.wait(sync, qtg + 1)
                L7S.inc(sync.dma_start(
                    out=self.d_o1h.ap()[128 * h:128 * (h + 1), QW1 * qt:QW1 * (qt + 1)],
                    in_=ostg_hi[:, 0:QW1]), 16)
                L7S.inc(sync.dma_start(
                    out=self.d_o1l.ap()[128 * h:128 * (h + 1), QW1 * qt:QW1 * (qt + 1)],
                    in_=ostg_lo[:, 0:QW1]), 16)
                ldm[("ost", qtg)] = L7S.n

    @self.block.tensor
    def _(tensor):
        def pv7(qt, kc, g, qtg):
            last = (kc == KC - 1)
            Aspl.wait(tensor, g + 1)
            if kc == 0:
                Aqtev.wait(tensor, qtg)
            tensor.matmul(self.pb[3][0:1, 0:QW1], self.c_onecb[:], p_hi(g)[:, 0:QW1],
                          start=(kc == 0), stop=False)
            tensor.matmul(self.pb[3][0:1, 0:QW1], self.c_onecb[:], p_lo(g)[:, 0:QW1],
                          start=False, stop=last)
            tensor.matmul(self.pb[2][:, 0:QW1], vh(kc * 128, 128), p_hi(g)[:, 0:QW1],
                          start=(kc == 0), stop=False)
            tensor.matmul(self.pb[2][:, 0:QW1], vl(kc * 128, 128), p_hi(g)[:, 0:QW1],
                          start=False, stop=False)
            Apv.inc(tensor.matmul(self.pb[2][:, 0:QW1], vh(kc * 128, 128), p_lo(g)[:, 0:QW1],
                                  start=False, stop=last))

        for kb in range(NKB):
            L7K.wait(tensor, ldm[("k", kb)])
            if kb >= 2:
                KE.wait(tensor, kb - 1)
            KT.inc(tensor.transpose(self.pb[5][:, 128 * (kb % 2):128 * (kb % 2) + 128],
                                    kstg, self.c_idf[:]))
        for j in range(NJK):
            KN1.wait(tensor, j + 1)
            if j >= 1:
                KN3.wait(tensor, j)
            KN2.inc(tensor.matmul(self.pb[3][0:1, 0:512], self.c_onecb[:], q2t,
                                  start=True, stop=True))
        for h in range(H):
            for tt in range(NT):
                u = h * NT + tt
                L7Q.wait(tensor, ldm[("q", u)])
                if tt == 0 and h >= 1:
                    Aexp.wait(tensor, h * UPH)
                if u >= 2:
                    QS.wait(tensor, u - 1)
                QP.inc(tensor.transpose(self.pb[6][:, 128 * (u % 2):128 * (u % 2) + 128],
                                        qstg, self.c_idf[:]))
            if h == 0:
                KN3.wait(tensor, NJK)
            Arden.wait(tensor, h * QT1)
            for j in range(NJ1):
                uj = h * NJ1 + j
                QN1.wait(tensor, uj + 1)
                if uj >= 1:
                    QN3.wait(tensor, uj)
                QN2.inc(tensor.matmul(self.pb[3][0:1, 0:QW1], self.c_onecb[:], q2t[:, 0:QW1],
                                      start=True, stop=True))
            VM.wait(tensor, h + 1)
            if h >= 1:
                VC.wait(tensor, h)
            TM.inc(tensor.matmul(self.pb[4][:, 0:1], self.c_onef[:], self.c_sc[0:1, 7:8],
                                 start=True, stop=True))
            QS.wait(tensor, (h + 1) * NT)
            for qt in range(QT1):
                qtg = h * QT1 + qt
                for kc in range(KC):
                    g = gidx(h, qt, kc)
                    Aexp.wait(tensor, max(0, g - 2))
                    Asc.inc(self.mm3(tensor, self.pb[(0, 1, 6)[g % 3]][:, 0:QW1],
                                     kth(kc * 128, 128), ktl(kc * 128, 128),
                                     qhh(QW1 * qt, QW1), qll(QW1 * qt, QW1),
                                     start=True, stop=True))
                    if kc >= 1:
                        pv7(qt, kc - 1, g - 1, qtg)
                pv7(qt, KC - 1, gidx(h, qt, KC - 1), qtg)
                Arden.wait(tensor, qtg + 1)
                Aqtev.wait(tensor, qtg)
                Arb.inc(tensor.matmul(self.pb[5][:, 0:QW1], self.c_onef[:],
                                      rden_r[0:1, 0:QW1], start=True, stop=True))

    @self.block.scalar
    def _(scalar):
        KM.wait(scalar, 1)
        KSQ.inc(scalar.activation(out=self.c_sc[0:1, 5:6], in_=self.c_sc[0:1, 2:3],
                                  func=AF.Sqrt, bias=self.c_zero[0:1, 0:1]))
        for h in range(H):
            QM.wait(scalar, h + 1)
            QSQ.inc(scalar.activation(out=self.c_sc[0:1, 4:5], in_=self.c_sc[0:1, 0:1],
                                      func=AF.Sqrt, bias=self.c_zero[0:1, 0:1]))
            for qt in range(QT1):
                for kc in range(KC):
                    g = gidx(h, qt, kc)
                    Asc.wait(scalar, g + 1)
                    Aspl.wait(scalar, max(0, g - 2))
                    VC.wait(scalar, h + 1)
                    Aexp.inc(scalar.activation(out=ptil(g)[:, 0:QW1],
                                               in_=self.pb[(0, 1, 6)[g % 3]][:, 0:QW1],
                                               func=AF.Exp, bias=negM, scale=1.0))

    @self.block.vector
    def _(vector):
        HZ = self.sem("HZV")
        for kb in range(NKB):
            KT.wait(vector, kb + 1)
            slot = self.pb[5][:, 128 * (kb % 2):128 * (kb % 2) + 128]
            vector.tensor_copy(kth(128 * kb, 128), slot)
            KE.inc(vector.tensor_tensor(ktl(128 * kb, 128), slot, kth(128 * kb, 128),
                                        op=ALU.subtract))
        for kb in range(NKB):
            L7K.wait(vector, ldm[("v", kb)])
            vector.tensor_copy(vh(128 * kb, 128), kstg)
            VE.inc(vector.tensor_tensor(vl(128 * kb, 128), kstg, vh(128 * kb, 128),
                                        op=ALU.subtract))
        for j in range(NJK):
            KN1.inc(vector.tensor_tensor(q2t, kth(512 * j, 512), kth(512 * j, 512), op=ALU.mult))
            KN2.wait(vector, j + 1)
            if j == 0:
                KN3.inc(vector.reduce_max(self.c_sc[0:1, 2:3], self.pb[3][0:1, 0:512], axis=AXX))
            else:
                HZ.inc(vector.reduce_max(self.c_sc[0:1, 6:7], self.pb[3][0:1, 0:512], axis=AXX))
                HZ.wait(vector, HZ.n)
                KN3.inc(vector.tensor_tensor(self.c_sc[0:1, 2:3], self.c_sc[0:1, 2:3],
                                             self.c_sc[0:1, 6:7], op=ALU.max))
            KN3.wait(vector, j + 1)
        KM.inc(vector.tensor_copy(self.c_sc[0:1, 16:17], self.c_sc[0:1, 2:3]))
        for h in range(H):
            for tt in range(NT):
                u = h * NT + tt
                QP.wait(vector, u + 1)
                slot = self.pb[6][:, 128 * (u % 2):128 * (u % 2) + 128]
                vector.tensor_copy(qhh(128 * tt, 128), slot)
                QS.inc(vector.tensor_tensor(qll(128 * tt, 128), slot, qhh(128 * tt, 128),
                                            op=ALU.subtract))
            for j in range(NJ1):
                uj = h * NJ1 + j
                QN1.inc(vector.tensor_tensor(q2t[:, 0:QW1], qhh(QW1 * j, QW1),
                                             qhh(QW1 * j, QW1), op=ALU.mult))
                QN2.wait(vector, uj + 1)
                if j == 0:
                    QN3.inc(vector.reduce_max(self.c_sc[0:1, 0:1], self.pb[3][0:1, 0:QW1],
                                              axis=AXX))
                else:
                    HZ.inc(vector.reduce_max(self.c_sc[0:1, 6:7], self.pb[3][0:1, 0:QW1],
                                             axis=AXX))
                    HZ.wait(vector, HZ.n)
                    QN3.inc(vector.tensor_tensor(self.c_sc[0:1, 0:1], self.c_sc[0:1, 0:1],
                                                 self.c_sc[0:1, 6:7], op=ALU.max))
                QN3.wait(vector, uj + 1)
            QM.inc(vector.tensor_copy(self.c_sc[0:1, 17:18], self.c_sc[0:1, 0:1]))
            QSQ.wait(vector, h + 1)
            if h == 0:
                KSQ.wait(vector, 1)
            HZ.inc(vector.tensor_tensor(self.c_sc[0:1, 6:7], self.c_sc[0:1, 4:5],
                                        self.c_sc[0:1, 5:6], op=ALU.mult))
            HZ.wait(vector, HZ.n)
            VM.inc(vector.tensor_scalar(self.c_sc[0:1, 7:8], self.c_sc[0:1, 6:7],
                                        -1.01, -0.5, op0=ALU.mult, op1=ALU.add))
            TM.wait(vector, h + 1)
            if h >= 1:
                Aexp.wait(vector, g1b + h * UPH)
            VC.inc(vector.tensor_copy(negM, self.pb[4][:, 0:1]))
            for qt in range(QT1):
                qtg = h * QT1 + qt
                for kc in range(KC):
                    g = gidx(h, qt, kc)
                    Aexp.wait(vector, g + 1)
                    Apv.wait(vector, max(0, g - 2))
                    vector.tensor_copy(p_hi(g)[:, 0:QW1], ptil(g)[:, 0:QW1])
                    Aspl.inc(vector.tensor_tensor(p_lo(g)[:, 0:QW1], ptil(g)[:, 0:QW1],
                                                  p_hi(g)[:, 0:QW1], op=ALU.subtract))
                g = gidx(h, qt, KC - 1)
                Apv.wait(vector, g + 1)
                HZ.inc(vector.tensor_copy(den_r[0:1, 0:QW1], self.pb[3][0:1, 0:QW1]))
                HZ.wait(vector, HZ.n)
                Arden.inc(vector.reciprocal(rden_r[0:1, 0:QW1], den_r[0:1, 0:QW1]))
                Arb.wait(vector, qtg + 1)
                vector.tensor_copy(rb_tile[:, 0:QW1], self.pb[5][:, 0:QW1])
                if qtg >= 1:
                    L7S.wait(vector, ldm[("ost", qtg - 1)])
                vector.tensor_tensor(ostg_f[:, 0:QW1], self.pb[2][:, 0:QW1],
                                     rb_tile[:, 0:QW1], op=ALU.mult)
                vector.tensor_copy(ostg_hi[:, 0:QW1], ostg_f[:, 0:QW1])
                Aqtev.inc(vector.tensor_tensor(ostg_lo[:, 0:QW1], ostg_f[:, 0:QW1],
                                               ostg_hi[:, 0:QW1], op=ALU.subtract))

    self.g1 += H * UPH


MK.ph7 = ph7


def ph8(self):
    nc, S, TOK, NT = self.nc, self.S, self.TOK, self.NT
    LW = [self.sem("LW0"), self.sem("LW1")]
    L8OH, L8X, L8S = self.sem("L8OH"), self.sem("L8X"), self.sem("L8S")
    W8, M8, E8, O8 = (self.sem(x) for x in ("P8W", "P8M", "P8E", "P8O"))
    o_wh, o_wl, o_oh = 0, ECH * 2048, 2 * ECH * 2048
    wo_h = lambda fc, c0, w: self.ab(o_wh + fc * 2048 + c0, w)
    wo_l = lambda fc, c0, w: self.ab(o_wl + fc * 2048 + c0, w)
    oh_hi = lambda fc: self.ab(o_oh + fc * 128, 128)
    oh_lo = lambda fc: self.ab(o_oh + 2048 + fc * 128, 128)
    wstg = [self.af(0, 2048), self.af(2048, 2048)]
    buf1 = self.af(4096, 2048)
    stage = self.af(6656, 2048)
    ldm = {}

    @self.block.sync
    def _(sync):
        for ec in range(ECH):
            if ec >= 2:
                W8.wait(sync, ec - 1)
            LW[ec % 2].inc(sync.dma_start(out=wstg[ec % 2],
                                          in_=self.i_wu[128 * ec:128 * (ec + 1), :]), 16)
            ldm[("w", ec)] = LW[ec % 2].n
        for tt in range(NT):
            M8.wait(sync, tt)
            for fc in range(ECH):
                L8OH.inc(sync.dma_start(out=oh_hi(fc),
                                        in_=self.d_o1h.ap()[128 * fc:128 * (fc + 1),
                                                            128 * tt:128 * (tt + 1)]), 16)
                L8OH.inc(sync.dma_start(out=oh_lo(fc),
                                        in_=self.d_o1l.ap()[128 * fc:128 * (fc + 1),
                                                            128 * tt:128 * (tt + 1)]), 16)
            ldm[("oh", tt)] = L8OH.n
            E8.wait(sync, tt)
            L8X.inc(sync.dma_start(out=buf1, in_=self.d_x1.ap()[128 * tt:128 * (tt + 1), :]), 16)
            ldm[("x1", tt)] = L8X.n
            E8.wait(sync, tt + 1)
            L8S.inc(sync.dma_start(out=self.d_x1p.ap()[128 * tt:128 * (tt + 1), :], in_=stage), 16)
            ldm[("st", tt)] = L8S.n

    @self.block.tensor
    def _(tensor):
        for tt in range(NT):
            L8OH.wait(tensor, ldm[("oh", tt)])
            if tt == 0:
                W8.wait(tensor, ECH)
            if tt >= 1:
                E8.wait(tensor, tt)
            last = None
            for fc in range(ECH):
                for og in range(4):
                    last = self.mm3(tensor, self.pb[og][:],
                                    oh_hi(fc), oh_lo(fc),
                                    wo_h(fc, 512 * og, 512), wo_l(fc, 512 * og, 512),
                                    start=(fc == 0), stop=(fc == ECH - 1))
            M8.inc(last)

    @self.block.vector
    def _(vector):
        for ec in range(ECH):
            LW[ec % 2].wait(vector, ldm[("w", ec)])
            vector.tensor_copy(self.ab(o_wh + ec * 2048, 2048), wstg[ec % 2])
            W8.inc(vector.tensor_tensor(self.ab(o_wl + ec * 2048, 2048), wstg[ec % 2],
                                        self.ab(o_wh + ec * 2048, 2048), op=ALU.subtract))
        for tt in range(NT):
            M8.wait(vector, tt + 1)
            L8X.wait(vector, ldm[("x1", tt)])
            if tt >= 1:
                L8S.wait(vector, ldm[("st", tt - 1)])
            last = None
            for og in range(4):
                last = vector.tensor_tensor(stage[:, 512 * og:512 * (og + 1)], self.pb[og][:],
                                            buf1[:, 512 * og:512 * (og + 1)], op=ALU.add)
            E8.inc(last)


def ph9(self):
    nc, S, TOK, NT = self.nc, self.S, self.TOK, self.NT
    L9A, L9W, L9X, L9O = (self.sem(x) for x in ("L9A", "L9W", "L9X", "L9O"))
    X9, W9, M9, E9, B9 = (self.sem(x) for x in ("P9X", "P9W", "P9M", "P9E", "P9B"))
    o_xh, o_xl = 0, ECH * TOK
    o_wt = 2 * ECH * TOK
    wt_t = lambda ec: self.ab(o_wt + ec * 512, 512)
    biasr = self.af(6656, 2048, part0=0, nparts=1)
    bias_bc = self.af(4096, 2048)
    x1p = self.af(2048, 512)
    outst = self.af(2560, 512)
    ldm = {}

    @self.block.sync
    def _(sync):
        for ec in range(ECH):
            L9A.inc(sync.dma_start(out=self.ab(o_xh + ec * TOK, TOK),
                                   in_=self.d_x1h.ap()[128 * ec:128 * (ec + 1), :]), 16)
            L9A.inc(sync.dma_start(out=self.ab(o_xl + ec * TOK, TOK),
                                   in_=self.d_x1l.ap()[128 * ec:128 * (ec + 1), :]), 16)
        L9A.inc(sync.dma_start(out=biasr, in_=self.i_bl1), 16)
        ldm["xn"] = L9A.n
        for og in range(4):
            for ec in range(ECH):
                idx = og * ECH + ec
                if og >= 1 and ec == 0:
                    M9.wait(sync, NT * og)
                L9W.inc(sync.dma_start(out=wt_t(ec),
                                       in_=self.d_wt.ap()[128 * ec:128 * (ec + 1),
                                                          512 * og:512 * (og + 1)]), 16)
            ldm[("wt", og)] = L9W.n
            for tt in range(NT):
                idx = og * NT + tt
                E9.wait(sync, idx)
                L9X.inc(sync.dma_start(out=x1p,
                                       in_=self.d_x1p.ap()[128 * tt:128 * (tt + 1),
                                                           512 * og:512 * (og + 1)]), 16)
                ldm[("x1p", idx)] = L9X.n
                E9.wait(sync, idx + 1)
                L9O.inc(sync.dma_start(out=self.o_y[128 * tt:128 * (tt + 1),
                                                    512 * og:512 * (og + 1)], in_=outst), 16)
                ldm[("out", idx)] = L9O.n

    @self.block.tensor
    def _(tensor):
        L9A.wait(tensor, ldm["xn"])
        for og in range(4):
            B9.inc(tensor.matmul(self.pb[2 + og][:], self.c_onef[:],
                                 biasr[0:1, 512 * og:512 * (og + 1)], start=True, stop=True))
        for og in range(4):
            L9W.wait(tensor, ldm[("wt", og)])
            for tt in range(NT):
                idx = og * NT + tt
                if idx >= 2:
                    E9.wait(tensor, idx - 1)
                last = None
                for ec in range(ECH):
                    tensor.matmul(self.pb[idx % 2][:],
                                  self.ab(o_xh + ec * TOK + 128 * tt, 128), wt_t(ec),
                                  start=(ec == 0), stop=False)
                    last = tensor.matmul(self.pb[idx % 2][:],
                                         self.ab(o_xl + ec * TOK + 128 * tt, 128), wt_t(ec),
                                         start=False, stop=(ec == ECH - 1))
                M9.inc(last)

    @self.block.vector
    def _(vector):
        for og in range(4):
            B9.wait(vector, og + 1)
            vector.tensor_copy(bias_bc[:, 512 * og:512 * (og + 1)], self.pb[2 + og][:])
        for og in range(4):
            for tt in range(NT):
                idx = og * NT + tt
                M9.wait(vector, idx + 1)
                L9X.wait(vector, ldm[("x1p", idx)])
                if idx >= 1:
                    L9O.wait(vector, ldm[("out", idx - 1)])
                vector.scalar_tensor_tensor(outst, self.pb[idx % 2][:], self.c_col[:, 6:7],
                                            x1p, op0=ALU.mult, op1=ALU.add)
                E9.inc(vector.tensor_tensor(outst, outst,
                                            bias_bc[:, 512 * og:512 * (og + 1)], op=ALU.add))

    @self.block.gpsimd
    def _(gpsimd):
        pass


MK.ph8 = ph8
MK.ph9 = ph9


def build_mk(S, nph=99, dumps=()):
    mk = MK(S)
    mk.declare()
    mk.alloc()
    dump_outs = []
    for i, name in enumerate(dumps):
        if isinstance(name, str) and "[" in name:
            base, idx = name[:-1].split("[")
            t = getattr(mk, base)[int(idx)]
        elif isinstance(name, str):
            t = getattr(mk, name)
        else:
            t = name
        sh = list(t.shape)
        dt = t.dtype
        dump_outs.append((mk.nc.dram_tensor(f"dbg{i}", sh, dt, kind="ExternalOutput").ap(), t))
    phases = [mk.ph0, mk.ph1, lambda: mk.ph2(0), lambda: mk.ph2(1), mk.ph3,
              mk.ph4, mk.ph5, mk.ph6, mk.ph7, mk.ph8, mk.ph9]
    with mk.nc.Block() as block:
        mk.block = block
        for i, ph in enumerate(phases):
            if i >= nph:
                break
            ph()
            mk.barrier()
        if dump_outs:
            LD = mk.sem("LD")

            @block.sync
            def _(sync):
                for o, t in dump_outs:
                    LD.inc(sync.dma_start(out=o, in_=t[:]), 16)
            mk.barrier()
    for name, sem in mk.sems.items():
        for w in sem.waits:
            assert w <= sem.n, f"sem {name}: wait {w} > total {sem.n}"
    mk.es.close()
    return mk


def kernel(**inputs):
    S = int(np.asarray(inputs["L0_attn_x"]).shape[1])
    mk = build_mk(S)
    in_maps = [host_prep(c, S, inputs) for c in range(NCORES)]
    res = run_bass_kernel_spmd(mk.nc, in_maps, core_ids=list(range(NCORES)))
    out = np.zeros((1, S, E), dtype=np.float32)
    for c in range(NCORES):
        out[0, owned_rows(c, S), :] = np.asarray(res.results[c]["y_c"], dtype=np.float32)
    return out


# version 05
# speedup vs baseline: 1.1140x; 1.0067x over previous
import os
import numpy as np
import ml_dtypes
import concourse.bass as bass
import concourse.mybir as mybir
from concourse.bass_utils import run_bass_kernel_spmd

F32 = mybir.dt.float32
BF16 = mybir.dt.bfloat16
F16 = mybir.dt.float16
I32 = mybir.dt.int32
AF = mybir.ActivationFunctionType
ALU = mybir.AluOpType
AXX = mybir.AxisListType.X

NCORES = 8
E = 2048
ECH = E // 128
H = 16
D = 128
HC = H // NCORES
ICR = 5632 // NCORES
ICP = 768
GU = 2 * ICP
NEG = -1e30
INV_SQRT_D = 1.0 / np.sqrt(128.0)
PARANOID = bool(int(os.environ.get("MK_PARANOID", "0")))

BF = ml_dtypes.bfloat16


def _hilo(a):
    hi = a.astype(BF)
    lo = (a - hi.astype(np.float32)).astype(BF)
    return hi, lo


class PhysSem:
    def __init__(self, es, nc, name):
        self.h = es.enter_context(nc.semaphore(name))
        self.n = 0


class Sem:

    def __init__(self, phys):
        self.p = phys
        self.base = phys.n
        self.waits = []

    @property
    def n(self):
        return self.p.n - self.base

    def inc(self, instr, k=1):
        instr.then_inc(self.p.h, k)
        self.p.n += k
        return self.n

    def wait(self, eng, value):
        assert value >= 0
        self.waits.append(value)
        eng.wait_ge(self.p.h, self.base + value)


SEM_ALIAS = {}
for _nm, _ph in {
    "P1A": "E00", "P1B": "E01", "P1C": "E02", "P1T": "E03", "P1S": "E04",
    "P2W": "E00", "P2M": "E01", "P2E": "E02", "P2V": "E03", "P2T": "E04",
    "P2VE": "E05", "P2N1": "E06", "P2N2": "E07", "P2N3": "E08", "P2N4": "E09",
    "P2N5": "E10", "D_sc": "E11", "D_bias": "E12", "D_exp": "E13", "D_spl": "E14",
    "D_pv": "E15", "D_brow": "E16", "D_bc": "E17", "D_btile": "E18",
    "D_rden": "E19", "D_rb": "E20", "D_qtev": "E21", "P2B": "E22",
    "P3M": "E23", "P3E": "E24", "P3W": "E25",
    "P4X": "E03", "P4B": "E04", "P4W": "E05", "P4G": "E06", "P4S": "E07",
    "P4H": "E08", "P4D2": "E09", "P4Y": "E10", "P4C1": "E11", "P4C2": "E12",
    "P4C3": "E13", "P4C4": "E14", "P4C5": "E15",
    "P5R1": "E16", "P5R2": "E17", "P5R3": "E18", "P5R4": "E19", "P5R5": "E20",
    "P5T": "E21", "P5S": "E22", "P5BO": "E06",
    "P6W": "E00", "P6M": "E01", "P6E": "E02", "P6Q": "E03",
    "P7KT": "E04", "P7KE": "E05", "P7VE": "E07", "P7KN1": "E08", "P7KN2": "E09",
    "P7KN3": "E10", "P7KM": "E11", "P7KSQ": "E12", "P7QP": "E13", "P7QS": "E14",
    "P7QN1": "E15", "P7QN2": "E16", "P7QN3": "E17", "P7QM": "E18", "P7QSQ": "E19",
    "P7VM": "E20", "P7TM": "E21", "P7VC": "E22",
    "A_sc": "E23", "A_exp": "E24", "A_spl": "E25", "A_pv": "E26",
    "A_rden": "E27", "A_rb": "E28", "A_qtev": "E29",
    "P8W": "E00", "P8M": "E01", "P8E": "E02", "P8O": "E05",
    "P9X": "E03", "P9W": "E04", "P9M": "E05", "P9E": "E06", "P9B": "E07",
    "L1X": "D6", "L1S": "D7",
    "LW0": "D0", "LW1": "D1", "LSL": "D2", "LAQ": "D3", "LOS": "D4", "LCT": "D5",
    "LOH": "D6", "LP3": "D7",
    "LDQ": "D3", "LY0": "D6", "LY1": "D7", "LL0": "D8", "LL1": "D9",
    "LWT0": "D4", "LWT1": "D10",
    "L5": "D6x", "L5S": "D7x", "L5P": "D8x",
    "LQ0": "D6", "LQ1": "D7", "L6KV": "D8",
    "L7K": "D6y", "L7Q": "D7y", "L7S": "D8y",
    "L8OH": "D6", "L8X": "D7", "L8S": "D8",
    "L9A": "D6z", "L9W": "D7z", "L9X": "D8z", "L9O": "D9",
}.items():
    SEM_ALIAS[_nm] = _ph


class MK:
    def __init__(self, S):
        assert S % 1024 == 0
        self.S = S
        self.TOK = S // NCORES
        self.NT = self.TOK // 128
        self.QT0 = S // 512
        self.QT1 = max(1, self.TOK // 512)
        self.QW1 = min(512, self.TOK)
        self.KC = S // 128
        self.SLAB = 256
        self.NSLAB = S // self.SLAB
        self.nc = bass.Bass("TRN2", target_bir_lowering=False,
                            dynamic_dma_scratch_size=8192,
                            detect_race_conditions=False)
        import contextlib
        self.es = contextlib.ExitStack()
        self.block = None
        self.sems = {}
        self.phys = {}
        self.g0 = 0
        self.qt0 = 0
        self.g1 = 0
        self._prev_ost = 0

    def sem(self, name):
        if name not in self.sems:
            ph = SEM_ALIAS.get(name, name)
            if ph not in self.phys:
                self.phys[ph] = PhysSem(self.es, self.nc, ph)
            self.sems[name] = Sem(self.phys[ph])
        return self.sems[name]

    def declare(self):
        nc, S, TOK = self.nc, self.S, self.TOK
        di = lambda n, sh, dt: nc.dram_tensor(n, sh, dt, kind="ExternalInput").ap()
        self.i_x = di("x_c", [TOK, E], F32)
        self.i_wq = di("wq_t", [E, HC * 128], F32)
        self.i_wk = di("wk_t", [E, HC * 128], F32)
        self.i_wv = di("wv_t", [E, HC * 128], F32)
        self.i_bqkv = di("bqkv", [128, HC * 3], F32)
        self.i_wo = di("wo_t", [HC * 128, E], F32)
        self.i_bo = di("bo", [1, E], F32)
        self.i_gup = di("gu_rep", [E, GU], I32)
        self.i_gus = di("gu_sc", [ECH, GU], F16)
        self.i_dwp = di("dw_rep", [ICP, E], I32)
        self.i_dws = di("dw_scsel", [2 * (ICP // 128), E], F16)
        self.i_dsel = di("dsel", [2, 128], F32)
        self.i_wa = di("w_attn", [E, 2304], F32)
        self.i_wu = di("w_out", [E, E], F32)
        self.i_wl1 = di("w_l1t", [E, E], F32)
        self.i_bl1 = di("b_l1", [1, E], F32)
        self.i_x2 = di("x_c2", [TOK, E], F32)
        self.i_idf = di("identf", [128, 128], F32)
        self.i_idb = di("identb", [128, 128], BF16)
        self.i_msk = di("masks", [128, 4 * 512], BF16)
        self.i_alq = di("al_q", [HC, S], F32)
        self.i_alk = di("al_k", [128, HC * (S // 128)], F32)
        self.i_shft = di("shifts", [128, 1], I32)
        self.i_nmask = di("nmask", [128, 1], I32)
        self.i_ninv = di("ninv", [128, 1], F32)
        self.i_cos = di("cos_t", [TOK, 128], F32)
        self.i_sin = di("sin_t", [TOK, 128], F32)

        self.o_y = nc.dram_tensor("y_c", [TOK, E], F32, kind="ExternalOutput").ap()

        dr = lambda n, sh, dt: nc.dram_tensor(n, sh, dt)
        self.d_xh_in = dr("xh_in", [E, TOK], BF16)
        self.d_xl_in = dr("xl_in", [E, TOK], BF16)
        self.d_xh = dr("xh_all", [NCORES * E, TOK], BF16)
        self.d_xl = dr("xl_all", [NCORES * E, TOK], BF16)
        self.d_ohh = dr("ohT_hi", [HC * 128, S], BF16)
        self.d_ohl = dr("ohT_lo", [HC * 128, S], BF16)
        self.NCH = S // 1024
        self.d_part = [dr(f"part{ch}", [1024, 2 * E], F32) for ch in range(self.NCH)]
        self.d_rs = [dr(f"rsout{ch}", [128, 2 * E], F32) for ch in range(self.NCH)]
        self.d_q1 = dr("q1_spill", [TOK, E], F32)
        self.d_k1in = dr("k1_in", [TOK, D], F32)
        self.d_v1in = dr("v1_in", [TOK, D], F32)
        self.d_k1 = dr("k1_all", [S, D], F32)
        self.d_v1 = dr("v1_all", [S, D], F32)
        self.d_x1h = dr("xn1T_hi", [E, TOK], BF16)
        self.d_x1l = dr("xn1T_lo", [E, TOK], BF16)
        self.d_x1 = dr("x1_spill", [TOK, E], F32)
        self.d_x1p = dr("x1p_spill", [TOK, E], F32)
        self.d_o1h = dr("o1T_hi", [E, TOK], BF16)
        self.d_o1l = dr("o1T_lo", [E, TOK], BF16)
        self.d_wt = dr("wt_spill", [E, E], BF16)
        self.d_bar = dr("bar_scratch", [128, 1], F32)

    def alloc(self):
        nc, es, S, TOK = self.nc, self.es, self.S, self.TOK
        sb = lambda n, sh, dt: es.enter_context(nc.sbuf_tensor(n, sh, dt))
        self.AB = sb("AB", [128, 84992], BF16)
        self.AF = sb("AF", [128, 9216], F32)
        self.c_i32a = sb("c_i32a", [128, 512], I32)
        self.c_i32b = sb("c_i32b", [128, 512], I32)
        self.c_idf = sb("c_idf", [128, 128], F32)
        self.c_idb = sb("c_idb", [128, 128], BF16)
        self.c_shft = sb("c_shft", [128, 1], I32)
        self.c_nmask = sb("c_nmask", [128, 1], I32)
        self.c_ninv = sb("c_ninv", [128, 1], F32)
        self.c_zero = sb("c_zero", [128, 1], F32)
        self.c_onef = sb("c_onef", [1, 128], F32)
        self.c_onecb = sb("c_onecb", [128, 1], BF16)
        self.c_onecf = sb("c_onecf", [128, 1], F32)
        self.c_dsel = sb("c_dsel", [2, 128], F32)
        self.c_bqkv = sb("c_bqkv", [128, HC * 3], F32)
        self.c_sc = sb("c_sc", [1, 32], F32)
        self.c_col = sb("c_col", [128, 8], F32)
        self.c_stats = sb("c_stats", [128, 26], F32)
        self.c_alk = sb("c_alk", [128, HC * (S // 128)], F32)
        self.c_bar = sb("c_bar", [128, 4], F32)
        self.c_f16 = sb("c_f16", [2, 2048], F16)
        self.pb = [es.enter_context(nc.psum_tensor(f"pb{i}", [128, 512], F32))
                   for i in range(7)]
        self.pt_b = es.enter_context(nc.psum_tensor("pt_b", [128, 512], BF16))

    def ab(self, off, cols, part0=None, nparts=None):
        assert off + cols <= 84992, (off, cols)
        if part0 is not None:
            return self.AB[part0:part0 + nparts, off:off + cols]
        return self.AB[:, off:off + cols]

    def af(self, off, cols, part0=None, nparts=None):
        assert off + cols <= 9216, (off, cols)
        if part0 is not None:
            return self.AF[part0:part0 + nparts, off:off + cols]
        return self.AF[:, off:off + cols]

    def barrier(self):
        BAR, LD, CC = self.sem("BAR"), self.sem("LD"), self.sem("CC")
        base = BAR.n
        ccn = CC.n
        r1 = base + 19
        tgt = base + 20

        BI = self.sem("BARINIT")

        @self.block.vector
        def _(vector):
            BAR.inc(vector.tensor_copy(self.c_bar[:, 0:1], self.c_bar[:, 0:1]), 1)

        @self.block.scalar
        def _(scalar):
            BI.wait(scalar, 1)
            BAR.inc(scalar.activation(out=self.c_bar[:, 1:2], in_=self.c_bar[:, 1:2],
                                      func=AF.Copy, bias=0.0), 1)

        @self.block.gpsimd
        def _(gpsimd):
            BI.wait(gpsimd, 1)
            BAR.inc(gpsimd.tensor_copy(self.c_bar[:, 2:3], self.c_bar[:, 2:3]), 1)

        @self.block.sync
        def _(sync):
            BI.wait(sync, 1)
            BAR.inc(sync.dma_start(out=self.d_bar.ap(), in_=self.c_bar[:, 3:4]), 16)

        @self.block.tensor
        def _(tensor):
            BAR.wait(tensor, r1)
            BAR.inc(tensor.matmul(self.pb[0][0:1, 0:1], self.c_onecb[:], self.c_onecb[:],
                                  start=True, stop=True), 1)

        dma_totals = [(sem, sem.n) for name, sem in self.sems.items()
                      if name.startswith("L") and sem.n > 0]
        for engname in ("sync", "gpsimd", "tensor", "vector", "scalar"):
            def mk(engname=engname):
                def _(eng):
                    BAR.wait(eng, tgt)
                    for sem, n in dma_totals:
                        sem.wait(eng, n)
                    if ccn:
                        CC.wait(eng, ccn)
                return _
            getattr(self.block, engname)(mk())

    def mm3(self, tensor, out, l_hi, l_lo, r_hi, r_lo, start, stop):
        tensor.matmul(out, l_hi, r_hi, start=start, stop=False)
        tensor.matmul(out, l_lo, r_hi, start=False, stop=False)
        return tensor.matmul(out, l_hi, r_lo, start=False, stop=stop)

    def split_hilo(self, vector, src, hi, lo):
        vector.tensor_copy(hi, src)
        vector.tensor_tensor(lo, src, hi, op=ALU.subtract)



def host_prep(core, S, ins):
    TOK = S // NCORES
    t0 = core * TOK
    f32 = np.float32
    out = {}
    x = np.asarray(ins["L0_attn_x"], dtype=f32)
    out["x_c"] = np.ascontiguousarray(x[0, t0:t0 + TOK, :])
    h0 = core * HC
    for nm, w in (("wq_t", "L0_attn_Wq"), ("wk_t", "L0_attn_Wk"), ("wv_t", "L0_attn_Wv")):
        out[nm] = np.ascontiguousarray(np.asarray(ins[w], f32).T[:, h0 * 128:(h0 + HC) * 128])
    bq = np.stack([np.asarray(ins["L0_attn_bq"], f32), np.asarray(ins["L0_attn_bk"], f32),
                   np.asarray(ins["L0_attn_bv"], f32)], axis=1)
    bqkv = np.zeros((128, HC * 3), f32)
    for h in range(HC):
        for j in range(3):
            bqkv[:, h * 3 + j] = bq[(h0 + h) * 128:(h0 + h + 1) * 128, j]
    out["bqkv"] = bqkv
    out["wo_t"] = np.ascontiguousarray(np.asarray(ins["L0_attn_Wo"], f32).T[h0 * 128:(h0 + HC) * 128, :])
    out["bo"] = np.asarray(ins["L0_attn_bo"], f32).reshape(1, E)

    gup = np.asarray(ins["L0_ffn_gate_up_packed"])
    gus = np.asarray(ins["L0_ffn_gate_up_scales"])
    I = gup.shape[0] // 2
    g_lo, g_hi = core * ICR, (core + 1) * ICR
    rows = np.concatenate([gup[g_lo:g_hi], np.zeros((ICP - ICR, E // 8), np.int32),
                           gup[I + g_lo:I + g_hi], np.zeros((ICP - ICR, E // 8), np.int32)], axis=0)
    out["gu_rep"] = np.ascontiguousarray(np.repeat(rows.T, 8, axis=0)).astype(np.int32)
    sc = np.concatenate([gus[:, g_lo:g_hi], np.zeros((ECH, ICP - ICR), gus.dtype),
                         gus[:, I + g_lo:I + g_hi], np.zeros((ECH, ICP - ICR), gus.dtype)], axis=1)
    out["gu_sc"] = np.ascontiguousarray(sc)
    dwp = np.asarray(ins["L0_ffn_down_packed"])
    dsc = np.asarray(ins["L0_ffn_down_scales"])
    w_lo8, w_hi8 = g_lo // 8, g_hi // 8
    drows = np.concatenate([dwp[:, w_lo8:w_hi8].T, np.zeros((ICP // 8 - ICR // 8, E), np.int32)], axis=0)
    out["dw_rep"] = np.ascontiguousarray(np.repeat(drows, 8, axis=0)).astype(np.int32)
    sel = np.zeros((2, 128), f32)
    sel[0, :64] = 1.0
    sel[1, 64:] = 1.0
    out["dsel"] = sel
    nic = ICP // 128
    scsel = np.zeros((2 * nic, E), dsc.dtype)
    for ic in range(nic):
        scsel[2 * ic + 0] = dsc[min((g_lo + ic * 128) // 128, dsc.shape[0] - 1)]
        scsel[2 * ic + 1] = dsc[min((g_lo + ic * 128 + 64) // 128, dsc.shape[0] - 1)]
    out["dw_scsel"] = np.ascontiguousarray(scsel)

    out["w_attn"] = np.ascontiguousarray(np.asarray(ins["L1_attn_w_attn"], f32))
    out["w_out"] = np.ascontiguousarray(np.asarray(ins["L1_attn_w_out"], f32))
    out["w_l1t"] = np.ascontiguousarray(np.asarray(ins["L1_ffn_weight"], f32).T)
    out["b_l1"] = np.asarray(ins["L1_ffn_bias"], f32).reshape(1, E)

    out["identf"] = np.eye(128, dtype=f32)
    out["identb"] = np.eye(128, dtype=f32).astype(BF)
    p = np.arange(128)[:, None]
    j = np.arange(512)[None, :]
    masks = np.concatenate([np.where(p + 128 * m <= j, 0.0, NEG).astype(f32) for m in range(4)], axis=1)
    out["masks"] = masks.astype(BF)
    slopes = np.array([2.0 ** (-0.5 * (i + 1)) for i in range(H)], dtype=np.float64)
    alq = np.stack([(-slopes[h0 + h] * np.arange(S)).astype(f32) for h in range(HC)], axis=0)
    out["al_q"] = np.ascontiguousarray(alq)
    alk = np.zeros((128, HC * (S // 128)), f32)
    for h in range(HC):
        for kc in range(S // 128):
            alk[:, h * (S // 128) + kc] = (slopes[h0 + h] * (kc * 128 + np.arange(128))).astype(f32)
    out["al_k"] = alk
    out["shifts"] = (4 * (np.arange(128) % 8)).astype(np.int32).reshape(128, 1)
    sh = 4 * (np.arange(128) % 8)
    out["nmask"] = (15 << sh).astype(np.int32).reshape(128, 1)
    out["ninv"] = (2.0 ** (-sh)).astype(f32).reshape(128, 1)
    idx = owned_rows(core, S)
    out["x_c2"] = np.ascontiguousarray(x[0, idx, :])
    hd = 128
    pos = 10000.0 ** ((-2.0 * np.arange(0, hd, 2, dtype=f32) - 1.0) / hd)
    t_idx = idx.astype(f32)[:, None]
    ang = t_idx * pos[None, :]
    rot = np.concatenate([ang, ang], axis=1)
    out["cos_t"] = np.cos(rot).astype(f32)
    out["sin_t"] = np.sin(rot).astype(f32)
    return out


def owned_rows(core, S):
    nch = S // 1024
    return np.concatenate([np.arange(1024 * ch + 128 * core, 1024 * ch + 128 * core + 128)
                           for ch in range(nch)])


def ph0(self):
    nc, LD = self.nc, self.sem("LD")
    ld_marks = {}

    @self.block.sync
    def _(sync):
        for dst, src in [(self.c_idf[:], self.i_idf), (self.c_idb[:], self.i_idb),
                         (self.c_nmask[:], self.i_nmask), (self.c_ninv[:], self.i_ninv),
                         (self.c_dsel[:], self.i_dsel),
                         (self.c_bqkv[:], self.i_bqkv), (self.c_alk[:], self.i_alk)]:
            LD.inc(sync.dma_start(out=dst, in_=src), 16)
        ld_marks["done"] = LD.n

    @self.block.vector
    def _(vector):
        BI = self.sem("BARINIT")
        vector.memset(self.c_zero[:], 0.0)
        vector.memset(self.c_onef[:], 1.0)
        vector.memset(self.c_onecb[:], 1.0)
        vector.memset(self.c_onecf[:], 1.0)
        BI.inc(vector.memset(self.c_bar[:], 0.0))


def ph1(self):
    nc, S, TOK, NT = self.nc, self.S, self.TOK, self.NT
    CC = self.sem("CC")
    L1X, L1S = self.sem("L1X"), self.sem("L1S")
    A = self.sem("P1A"); B = self.sem("P1B"); C = self.sem("P1C")
    T = self.sem("P1T"); SP = self.sem("P1S")
    xtile = self.af(0, 2048)
    xntile = self.af(2048, 2048)
    o_xh, o_xl = 0, ECH * TOK
    b0 = dict(A=A.n, B=B.n, C=C.n, T=T.n, SP=SP.n)
    ldm = {}

    @self.block.sync
    def _(sync):
        for tt in range(NT):
            if tt > 0:
                C.wait(sync, b0["C"] + tt)
            L1X.inc(sync.dma_start(out=xtile, in_=self.i_x[tt * 128:(tt + 1) * 128, :]), 16)
            ldm[("x", tt)] = L1X.n
            for ec in range(ECH):
                g = tt * ECH + ec
                SP.wait(sync, b0["SP"] + g + 1)
                L1S.inc(sync.dma_start(
                    out=self.d_xh_in.ap()[128 * ec:128 * (ec + 1), 128 * tt:128 * (tt + 1)],
                    in_=self.ab(o_xh + ec * TOK + 128 * tt, 128)), 16)
                L1S.inc(sync.dma_start(
                    out=self.d_xl_in.ap()[128 * ec:128 * (ec + 1), 128 * tt:128 * (tt + 1)],
                    in_=self.ab(o_xl + ec * TOK + 128 * tt, 128)), 16)
        ldm["end"] = L1S.n

    @self.block.tensor
    def _(tensor):
        for tt in range(NT):
            C.wait(tensor, b0["C"] + tt + 1)
            for ec in range(ECH):
                g = tt * ECH + ec
                if g >= 2:
                    SP.wait(tensor, b0["SP"] + g - 1)
                T.inc(tensor.transpose(self.pb[g % 2][:, 0:128],
                                       xntile[:, 128 * ec:128 * (ec + 1)], self.c_idf[:]))

    @self.block.scalar
    def _(scalar):
        for tt in range(NT):
            A.wait(scalar, b0["A"] + tt + 1)
            B.inc(scalar.activation(out=self.c_col[:, 1:2], in_=self.c_col[:, 0:1],
                                    func=AF.Sqrt, bias=self.c_zero[:, 0:1]))

    @self.block.vector
    def _(vector):
        HZ = self.sem("HZV")
        for tt in range(NT):
            L1X.wait(vector, ldm[("x", tt)])
            for k in range(4):
                HZ.inc(vector.bn_stats(self.c_stats[:, 6 * k:6 * k + 6],
                                       xtile[:, 512 * k:512 * (k + 1)]))
            HZ.wait(vector, HZ.n)
            HZ.inc(vector.bn_aggr(self.c_stats[:, 24:26], self.c_stats[:, 0:24]))
            HZ.wait(vector, HZ.n)
            A.inc(vector.tensor_scalar(self.c_col[:, 0:1], self.c_stats[:, 25:26], 1e-5, None,
                                       op0=ALU.add))
            B.wait(vector, b0["B"] + tt + 1)
            HZ.inc(vector.reciprocal(self.c_col[:, 2:3], self.c_col[:, 1:2]))
            HZ.wait(vector, HZ.n)
            vector.tensor_scalar(xntile, xtile, self.c_stats[:, 24:25], None, op0=ALU.subtract)
            C.inc(vector.tensor_scalar(xntile, xntile, self.c_col[:, 2:3], None, op0=ALU.mult))
            for ec in range(ECH):
                g = tt * ECH + ec
                T.wait(vector, b0["T"] + g + 1)
                hi_ap = self.ab(o_xh + ec * TOK + 128 * tt, 128)
                lo_ap = self.ab(o_xl + ec * TOK + 128 * tt, 128)
                vector.tensor_copy(hi_ap, self.pb[g % 2][:, 0:128])
                SP.inc(vector.tensor_tensor(lo_ap, self.pb[g % 2][:, 0:128],
                                            hi_ap, op=ALU.subtract))

    @self.block.gpsimd
    def _(gpsimd):
        L1S.wait(gpsimd, ldm["end"])
        CC.inc(gpsimd.collective_compute(
            "AllGather", ALU.bypass, replica_groups=[list(range(NCORES))],
            ins=[self.d_xh_in.ap().opt()], outs=[self.d_xh.ap().opt()]))
        CC.inc(gpsimd.collective_compute(
            "AllGather", ALU.bypass, replica_groups=[list(range(NCORES))],
            ins=[self.d_xl_in.ap().opt()], outs=[self.d_xl.ap().opt()]))


MK.ph0 = ph0
MK.ph1 = ph1


def ph2(self, h):
    nc, S, TOK, NT, QT0 = self.nc, self.S, self.TOK, self.NT, self.QT0
    NSL = S // 256
    LW = [self.sem("LW0"), self.sem("LW1")]
    LSL, LAQ, LOS, LCT = (self.sem(x) for x in ("LSL", "LAQ", "LOS", "LCT"))
    P2W, P2M, P2E, P2V, P2T, P2VE = (self.sem(x) for x in
                                     ("P2W", "P2M", "P2E", "P2V", "P2T", "P2VE"))
    N1, N2, N3, N4, N5 = (self.sem(x) for x in ("P2N1", "P2N2", "P2N3", "P2N4", "P2N5"))
    Dsc, Dbias, Dexp, Dspl, Dpv = (self.sem(x) for x in
                                   ("D_sc", "D_bias", "D_exp", "D_spl", "D_pv"))
    Dbrow, Dbc, Dbtile, Drden, Drb, Dqtev = (self.sem(x) for x in
        ("D_brow", "D_bc", "D_btile", "D_rden", "D_rb", "D_qtev"))

    o_q, o_ql, o_kh, o_kl, o_vh, o_vl = 0, S, 2 * S, 3 * S, 4 * S, 5 * S
    o_sl = 6 * S
    o_w = 6 * S + 16384
    o_p = o_w + 12288
    o_vstg = o_p + 3072
    o_ostg = o_vstg + 512
    o_q2 = o_ostg + 1024
    o_msk = o_q2 + 512
    qh = lambda c0, w: self.ab(o_q + c0, w)
    ql = lambda c0, w: self.ab(o_ql + c0, w)
    kh = lambda c0, w: self.ab(o_kh + c0, w)
    kl = lambda c0, w: self.ab(o_kl + c0, w)
    vh = lambda c0, w: self.ab(o_vh + c0, w)
    vl = lambda c0, w: self.ab(o_vl + c0, w)
    slh = lambda b, ec, c0=0, w=256: self.ab(o_sl + b * 8192 + ec * 256 + c0, w)
    sll = lambda b, ec, c0=0, w=256: self.ab(o_sl + b * 8192 + 4096 + ec * 256 + c0, w)
    w_hi = lambda j, ec: self.ab(o_w + j * 4096 + ec * 128, 128)
    w_lo = lambda j, ec: self.ab(o_w + j * 4096 + 2048 + ec * 128, 128)
    p_hi = lambda g: self.ab(o_p + (g % 3) * 512, 512)
    p_lo = lambda g: self.ab(o_p + 1536 + (g % 3) * 512, 512)
    vstg_hi = self.ab(o_vstg, 256)
    vstg_lo = self.ab(o_vstg + 256, 256)
    ostg_hi = self.ab(o_ostg, 512)
    ostg_lo = self.ab(o_ostg + 512, 512)
    q2t = self.ab(o_q2, 512)
    mskt = lambda m: self.ab(o_msk + 512 * m, 512)
    tmp = lambda g: self.af(2048 + 512 * (g % 3), 512)
    ptil = lambda g: self.af(3584 + 512 * (g % 3), 512)
    bias_tile = self.af(5120, 512)
    comb = lambda m: self.af(5632 + 512 * m, 512)
    rb_tile = self.af(5120, 512)
    qtmp = self.af(0, 256)
    ktmp = self.af(256, 256)
    vtmp = self.af(512, 256)
    wstg = [self.af(1024, 128), self.af(1152, 128)]
    den_r = self.af(7168, 512, part0=0, nparts=1)
    rden_r = self.af(7680, 512, part0=0, nparts=1)
    alq_r = self.af(8192, 512, part0=0, nparts=1)
    brow_r = self.af(8704, 512, part0=0, nparts=1)

    units = [(qt, kc) for qt in range(QT0) for kc in range(4 * qt + 4)]
    qt_last_g = {qt: max(lu for lu, (q2_, _) in enumerate(units) if q2_ == qt)
                 for qt in range(QT0)}
    g0 = self.g0
    qt0 = self.qt0
    bW, bM, bE, bV, bT, bVE = P2W.n, P2M.n, P2E.n, P2V.n, P2T.n, P2VE.n
    bN1, bN2, bN3, bN4, bN5 = N1.n, N2.n, N3.n, N4.n, N5.n
    ldm = {}
    NJ = S // 512

    def slab_pieces(sl):
        out = []
        t0g = 256 * sl
        done = 0
        while done < 256:
            t = t0g + done
            r = t // TOK
            lc = t - r * TOK
            w = min(256 - done, TOK - lc)
            out.append((r, lc, w, done))
            done += w
        return out

    @self.block.sync
    def _(sync):
        if h == 0:
            LCT.inc(sync.dma_start(out=self.ab(o_msk, 2048), in_=self.i_msk), 16)
            ldm["msk"] = LCT.n
        for j, src in enumerate([self.i_wq, self.i_wk, self.i_wv]):
            for ec in range(ECH):
                idx = j * ECH + ec
                if idx >= 2:
                    P2W.wait(sync, bW + idx - 1)
                LW[idx % 2].inc(sync.dma_start(
                    out=wstg[idx % 2],
                    in_=src[128 * ec:128 * (ec + 1), 128 * h:128 * (h + 1)]), 16)
                ldm[("w", idx)] = LW[idx % 2].n
        for sl in range(NSL):
            P2M.wait(sync, bM + max(0, 3 * (sl - 1)))
            b = sl % 2
            for (r, lc, w, doff) in slab_pieces(sl):
                src_h = self.d_xh.ap()[2048 * r:2048 * (r + 1), lc:lc + w] \
                    .rearrange("(n p) m -> p n m", p=128)
                src_l = self.d_xl.ap()[2048 * r:2048 * (r + 1), lc:lc + w] \
                    .rearrange("(n p) m -> p n m", p=128)
                if w == 256 and doff == 0:
                    LSL.inc(sync.dma_start(
                        out=self.ab(o_sl + b * 8192, 4096).rearrange("p (n m) -> p n m", n=ECH),
                        in_=src_h), 16)
                    LSL.inc(sync.dma_start(
                        out=self.ab(o_sl + b * 8192 + 4096, 4096).rearrange("p (n m) -> p n m", n=ECH),
                        in_=src_l), 16)
                else:
                    for ec in range(ECH):
                        LSL.inc(sync.dma_start(out=slh(b, ec, doff, w),
                                               in_=self.d_xh.ap()[2048 * r + 128 * ec:
                                                                  2048 * r + 128 * (ec + 1),
                                                                  lc:lc + w]), 16)
                        LSL.inc(sync.dma_start(out=sll(b, ec, doff, w),
                                               in_=self.d_xl.ap()[2048 * r + 128 * ec:
                                                                  2048 * r + 128 * (ec + 1),
                                                                  lc:lc + w]), 16)
            ldm[("slab", sl)] = LSL.n
        for qt in range(QT0):
            qtg = qt0 + qt
            Dbrow.wait(sync, qtg)
            LAQ.inc(sync.dma_start(out=alq_r, in_=self.i_alq[h:h + 1, 512 * qt:512 * (qt + 1)]), 16)
            ldm[("alq", qt)] = LAQ.n
            Dqtev.wait(sync, qtg + 1)
            LOS.inc(sync.dma_start(
                out=self.d_ohh.ap()[128 * h:128 * (h + 1), 512 * qt:512 * (qt + 1)],
                in_=ostg_hi), 16)
            LOS.inc(sync.dma_start(
                out=self.d_ohl.ap()[128 * h:128 * (h + 1), 512 * qt:512 * (qt + 1)],
                in_=ostg_lo), 16)
            ldm[("ost", qt)] = LOS.n

    @self.block.tensor
    def _(tensor):
        SBANK = lambda g: self.pb[(0, 1, 6)[g % 3]][:]

        def pv_emit(qt, kc, g):
            qtg = qt0 + qt
            last = (kc == 4 * qt + 3)
            Dspl.wait(tensor, g + 1)
            if kc == 0:
                Dqtev.wait(tensor, qtg)
            tensor.matmul(self.pb[3][0:1, 0:512], self.c_onecb[:], p_hi(g),
                          start=(kc == 0), stop=False)
            tensor.matmul(self.pb[3][0:1, 0:512], self.c_onecb[:], p_lo(g),
                          start=False, stop=last)
            tensor.matmul(self.pb[2][:], vh(kc * 128, 128), p_hi(g), start=(kc == 0), stop=False)
            tensor.matmul(self.pb[2][:], vl(kc * 128, 128), p_hi(g), start=False, stop=False)
            Dpv.inc(tensor.matmul(self.pb[2][:], vh(kc * 128, 128), p_lo(g),
                                  start=False, stop=last))

        for sl in range(NSL):
            P2E.wait(tensor, bE + 3 * sl)
            LSL.wait(tensor, ldm[("slab", sl)])
            if sl == 0:
                P2W.wait(tensor, bW + 3 * ECH)
            b = sl % 2
            for dst, wj, scale_note in ((0, 0, "q"), (1, 1, "k"), (2, 2, "v")):
                last = None
                for ec in range(ECH):
                    last = self.mm3(tensor, self.pb[dst][:, 0:256],
                                    w_hi(wj, ec), w_lo(wj, ec), slh(b, ec), sll(b, ec),
                                    start=(ec == 0), stop=(ec == ECH - 1))
                P2M.inc(last)
            if sl >= 1:
                s = sl - 1
                P2V.wait(tensor, bV + s + 1)
                P2VE.wait(tensor, bVE + s)
                for jb in range(2):
                    P2T.inc(tensor.transpose(self.pt_b[:, 128 * jb:128 * (jb + 1)],
                                             self.ab(o_vstg + 128 * jb, 128),
                                             self.c_idb[:]))
                    P2T.inc(tensor.transpose(self.pt_b[:, 256 + 128 * jb:384 + 128 * jb],
                                             self.ab(o_vstg + 256 + 128 * jb, 128),
                                             self.c_idb[:]))
        s = NSL - 1
        P2V.wait(tensor, bV + s + 1)
        P2VE.wait(tensor, bVE + s)
        for jb in range(2):
            P2T.inc(tensor.transpose(self.pt_b[:, 128 * jb:128 * (jb + 1)],
                                     self.ab(o_vstg + 128 * jb, 128), self.c_idb[:]))
            P2T.inc(tensor.transpose(self.pt_b[:, 256 + 128 * jb:384 + 128 * jb],
                                     self.ab(o_vstg + 256 + 128 * jb, 128), self.c_idb[:]))
        for j in range(2 * NJ):
            N1.wait(tensor, bN1 + j + 1)
            N3.wait(tensor, bN3 + j)
            N2.inc(tensor.matmul(self.pb[3][0:1, 0:512], self.c_onecb[:], q2t,
                                 start=True, stop=True))
        P2E.wait(tensor, bE + 3 * NSL)
        N3.wait(tensor, bN3 + 2 * NJ)
        for lu, (qt, kc) in enumerate(units):
            g = g0 + lu
            qtg = qt0 + qt
            if kc == 0:
                Dbrow.wait(tensor, qtg + 1)
                Dbtile.wait(tensor, qtg)
                Dbc.inc(tensor.matmul(self.pb[4][:], self.c_onef[:], brow_r,
                                      start=True, stop=True))
            if lu == 0:
                Dbias.wait(tensor, g0)
                Dqtev.wait(tensor, qt0)
            Dbias.wait(tensor, max(0, g - 2))
            Dsc.inc(self.mm3(tensor, SBANK(g),
                             kh(kc * 128, 128), kl(kc * 128, 128),
                             qh(512 * qt, 512), ql(512 * qt, 512),
                             start=True, stop=True))
            if kc >= 1:
                pv_emit(qt, kc - 1, g - 1)
            if kc == 4 * qt + 3:
                pv_emit(qt, kc, g)
                Drden.wait(tensor, qtg + 1)
                Drb.inc(tensor.matmul(self.pb[5][:], self.c_onef[:], rden_r,
                                      start=True, stop=True))

    @self.block.scalar
    def _(scalar):
        for sl in range(NSL):
            if sl == 0:
                self.sem("P2B").wait(scalar, h + 1)
            P2M.wait(scalar, bM + 3 * sl + 1)
            P2E.inc(scalar.activation(out=qtmp, in_=self.pb[0][:, 0:256], func=AF.Identity,
                                      bias=self.c_col[:, 3:4], scale=INV_SQRT_D))
            P2M.wait(scalar, bM + 3 * sl + 2)
            P2E.inc(scalar.activation(out=ktmp, in_=self.pb[1][:, 0:256], func=AF.Identity,
                                      bias=self.c_bqkv[:, 3 * h + 1:3 * h + 2], scale=1.0))
            P2M.wait(scalar, bM + 3 * sl + 3)
            P2E.inc(scalar.activation(out=vtmp, in_=self.pb[2][:, 0:256], func=AF.Identity,
                                      bias=self.c_bqkv[:, 3 * h + 2:3 * h + 3], scale=1.0))
        N3.wait(scalar, bN3 + 2 * NJ)
        scalar.activation(out=self.c_sc[0:1, 3:4], in_=self.c_sc[0:1, 0:1], func=AF.Sqrt,
                          bias=self.c_zero[0:1, 0:1])
        N4.inc(scalar.activation(out=self.c_sc[0:1, 4:5], in_=self.c_sc[0:1, 2:3], func=AF.Sqrt,
                                 bias=self.c_zero[0:1, 0:1]))
        for lu, (qt, kc) in enumerate(units):
            g = g0 + lu
            Dbias.wait(scalar, g + 1)
            Dspl.wait(scalar, max(0, g - 2))
            Dexp.inc(scalar.activation(
                out=ptil(g), in_=tmp(g), func=AF.Exp,
                bias=self.c_alk[:, h * (S // 128) + kc:h * (S // 128) + kc + 1], scale=1.0))

    @self.block.vector
    def _(vector):
        HZ = self.sem("HZV")
        for j in range(3):
            for ec in range(ECH):
                idx = j * ECH + ec
                LW[idx % 2].wait(vector, ldm[("w", idx)])
                vector.tensor_copy(w_hi(j, ec), wstg[idx % 2])
                P2W.inc(vector.tensor_tensor(w_lo(j, ec), wstg[idx % 2], w_hi(j, ec),
                                             op=ALU.subtract))
        self.sem("P2B").inc(vector.tensor_scalar(self.c_col[:, 3:4],
                                                 self.c_bqkv[:, 3 * h:3 * h + 1],
                                                 INV_SQRT_D, None, op0=ALU.mult))
        for sl in range(NSL):
            P2E.wait(vector, bE + 3 * sl + 1)
            self.split_hilo(vector, qtmp, qh(256 * sl, 256), ql(256 * sl, 256))
            P2E.wait(vector, bE + 3 * sl + 2)
            self.split_hilo(vector, ktmp, kh(256 * sl, 256), kl(256 * sl, 256))
            P2E.wait(vector, bE + 3 * sl + 3)
            P2T.wait(vector, bT + 4 * sl)
            vector.tensor_copy(vstg_hi, vtmp)
            P2V.inc(vector.tensor_tensor(vstg_lo, vtmp, vstg_hi, op=ALU.subtract))
            if sl >= 1:
                s = sl - 1
                P2T.wait(vector, bT + 4 * (s + 1))
                last = None
                for jb in range(2):
                    vector.tensor_copy(vh((2 * s + jb) * 128, 128),
                                       self.pt_b[:, 128 * jb:128 * (jb + 1)])
                    last = vector.tensor_copy(vl((2 * s + jb) * 128, 128),
                                              self.pt_b[:, 256 + 128 * jb:384 + 128 * jb])
                P2VE.inc(last)
        s = NSL - 1
        P2T.wait(vector, bT + 4 * (s + 1))
        last = None
        for jb in range(2):
            vector.tensor_copy(vh((2 * s + jb) * 128, 128), self.pt_b[:, 128 * jb:128 * (jb + 1)])
            last = vector.tensor_copy(vl((2 * s + jb) * 128, 128),
                                      self.pt_b[:, 256 + 128 * jb:384 + 128 * jb])
        P2VE.inc(last)
        for j in range(2 * NJ):
            if j < NJ:
                src = qh(512 * j, 512)
            else:
                src = kh(512 * (j - NJ), 512)
            N1.inc(vector.tensor_tensor(q2t, src, src, op=ALU.mult))
            N2.wait(vector, bN2 + j + 1)
            slot = 0 if j < NJ else 2
            if j % NJ == 0:
                N3.inc(vector.reduce_max(self.c_sc[0:1, slot:slot + 1],
                                         self.pb[3][0:1, 0:512], axis=AXX))
            else:
                HZ.inc(vector.reduce_max(self.c_sc[0:1, 1:2], self.pb[3][0:1, 0:512], axis=AXX))
                HZ.wait(vector, HZ.n)
                N3.inc(vector.tensor_tensor(self.c_sc[0:1, slot:slot + 1],
                                            self.c_sc[0:1, slot:slot + 1],
                                            self.c_sc[0:1, 1:2], op=ALU.max))
            N3.wait(vector, bN3 + j + 1)
        N4.wait(vector, bN4 + 1)
        HZ.inc(vector.tensor_tensor(self.c_sc[0:1, 1:2], self.c_sc[0:1, 3:4],
                                    self.c_sc[0:1, 4:5], op=ALU.mult))
        HZ.wait(vector, HZ.n)
        N5.inc(vector.tensor_scalar(self.c_sc[0:1, 5:6], self.c_sc[0:1, 1:2], 1.01, 0.5,
                                    op0=ALU.mult, op1=ALU.add))
        for lu, (qt, kc) in enumerate(units):
            g = g0 + lu
            qtg = qt0 + qt
            if kc == 0:
                LAQ.wait(vector, ldm[("alq", qt)])
                if h == 0 and qt == 0:
                    LCT.wait(vector, ldm["msk"])
                N5.wait(vector, bN5 + 1)
                Dbrow.inc(vector.tensor_scalar(brow_r, alq_r, self.c_sc[0:1, 5:6], None,
                                               op0=ALU.subtract))
                Dbc.wait(vector, qtg + 1)
                vector.tensor_copy(bias_tile, self.pb[4][:])
                last = None
                for m in range(4):
                    last = vector.tensor_tensor(comb(m), bias_tile, mskt(m), op=ALU.add)
                Dbtile.inc(last)
            Dsc.wait(vector, g + 1)
            Dexp.wait(vector, max(0, g - 2))
            sel = comb(kc - 4 * qt) if kc >= 4 * qt else bias_tile
            Dbias.inc(vector.tensor_tensor(tmp(g), self.pb[(0, 1, 6)[g % 3]][:], sel, op=ALU.add))
            Dexp.wait(vector, g + 1)
            Dpv.wait(vector, max(0, g - 2))
            vector.tensor_copy(p_hi(g), ptil(g))
            Dspl.inc(vector.tensor_tensor(p_lo(g), ptil(g), p_hi(g), op=ALU.subtract))
            if kc == 4 * qt + 3:
                Dpv.wait(vector, g + 1)
                HZ.inc(vector.tensor_copy(den_r, self.pb[3][0:1, 0:512]))
                HZ.wait(vector, HZ.n)
                Drden.inc(vector.reciprocal(rden_r, den_r))
                Drb.wait(vector, qtg + 1)
                vector.tensor_copy(rb_tile, self.pb[5][:])
                if qtg >= 1:
                    LOS.wait(vector, ldm[("ost", qt - 1)] if qt >= 1 else self._prev_ost)
                vector.tensor_tensor(self.af(0, 512),
                                     self.pb[2][:], rb_tile, op=ALU.mult)
                vector.tensor_copy(ostg_hi, self.af(0, 512))
                Dqtev.inc(vector.tensor_tensor(ostg_lo, self.af(0, 512), ostg_hi,
                                               op=ALU.subtract))

    self._prev_ost = ldm.get(("ost", QT0 - 1), 0)
    self.g0 += len(units)
    self.qt0 += QT0


MK.ph2 = ph2


def ph3(self):
    nc, S, TOK = self.nc, self.S, self.TOK
    LW = [self.sem("LW0"), self.sem("LW1")]
    LOH, LP3 = self.sem("LOH"), self.sem("LP3")
    WM, WE, WW = self.sem("P3M"), self.sem("P3E"), self.sem("P3W")
    bM, bE, bW = WM.n, WE.n, WW.n
    o_wh, o_wl, o_oh = 0, 4096, 8192
    wo_h = lambda fc, c0, w: self.ab(o_wh + fc * 2048 + c0, w)
    wo_l = lambda fc, c0, w: self.ab(o_wl + fc * 2048 + c0, w)
    oh_hi = lambda fc: self.ab(o_oh + fc * 128, 128)
    oh_lo = lambda fc: self.ab(o_oh + 256 + fc * 128, 128)
    stage = self.af(0, 2048)
    wstg = [self.af(2048, 2048), self.af(4096, 2048)]
    ldm = {}
    NTT = S // 128

    @self.block.sync
    def _(sync):
        for fc in range(HC):
            if fc >= 2:
                WW.wait(sync, bW + fc - 1)
            LW[fc % 2].inc(sync.dma_start(out=wstg[fc % 2],
                                          in_=self.i_wo[128 * fc:128 * (fc + 1), :]), 16)
            ldm[("w", fc)] = LW[fc % 2].n
        for t in range(NTT):
            WM.wait(sync, bM + t)
            for fc in range(HC):
                LOH.inc(sync.dma_start(out=oh_hi(fc),
                                       in_=self.d_ohh.ap()[128 * fc:128 * (fc + 1),
                                                           128 * t:128 * (t + 1)]), 16)
                LOH.inc(sync.dma_start(out=oh_lo(fc),
                                       in_=self.d_ohl.ap()[128 * fc:128 * (fc + 1),
                                                           128 * t:128 * (t + 1)]), 16)
            ldm[("oh", t)] = LOH.n
            WE.wait(sync, bE + t + 1)
            ch, ro = (128 * t) // 1024, (128 * t) % 1024
            LP3.inc(sync.dma_start(out=self.d_part[ch].ap()[ro:ro + 128, 0:2048], in_=stage), 16)
            ldm[("st", t)] = LP3.n

    @self.block.tensor
    def _(tensor):
        for t in range(NTT):
            LOH.wait(tensor, ldm[("oh", t)])
            if t == 0:
                WW.wait(tensor, bW + HC)
            if t >= 1:
                WE.wait(tensor, bE + t)
            last = None
            for og in range(4):
                for fc in range(HC):
                    last = self.mm3(tensor, self.pb[og][:],
                                    oh_hi(fc), oh_lo(fc),
                                    wo_h(fc, 512 * og, 512), wo_l(fc, 512 * og, 512),
                                    start=(fc == 0), stop=(fc == HC - 1))
            WM.inc(last)

    @self.block.vector
    def _(vector):
        for fc in range(HC):
            LW[fc % 2].wait(vector, ldm[("w", fc)])
            vector.tensor_copy(self.ab(o_wh + fc * 2048, 2048), wstg[fc % 2])
            WW.inc(vector.tensor_tensor(self.ab(o_wl + fc * 2048, 2048), wstg[fc % 2],
                                        self.ab(o_wh + fc * 2048, 2048), op=ALU.subtract))
        for t in range(NTT):
            WM.wait(vector, bM + t + 1)
            if t >= 1:
                LP3.wait(vector, ldm[("st", t - 1)])
            for og in range(3):
                vector.tensor_copy(stage[:, 512 * og:512 * (og + 1)], self.pb[og][:])
            WE.inc(vector.tensor_copy(stage[:, 1536:2048], self.pb[3][:]))


MK.ph3 = ph3


def ph4(self):
    nc, S, TOK = self.nc, self.S, self.TOK
    NSL = S // 256
    NIC = ICP // 128
    CC = self.sem("CC")
    LDQ, LSL = self.sem("LDQ"), self.sem("LSL")
    LY = [self.sem("LY0"), self.sem("LY1")]
    LL = [self.sem("LL0"), self.sem("LL1")]
    LWT = [self.sem("LWT0"), self.sem("LWT1")]
    QX, QB, QW = self.sem("P4X"), self.sem("P4B"), self.sem("P4W")
    G, SI, HH, D2, Y = (self.sem(x) for x in ("P4G", "P4S", "P4H", "P4D2", "P4Y"))
    C1, C2, C3, C4, C5 = (self.sem(x) for x in ("P4C1", "P4C2", "P4C3", "P4C4", "P4C5"))
    o_gh, o_gl, o_dh, o_dl = 0, 24576, 49152, 61440
    o_slh, o_sll = 73728, 77824
    o_h = 81920
    gu_h = lambda ec, c0, w: self.ab(o_gh + ec * 1536 + c0, w)
    gu_l = lambda ec, c0, w: self.ab(o_gl + ec * 1536 + c0, w)
    wd_h = lambda ic, c0, w: self.ab(o_dh + ic * 2048 + c0, w)
    wd_l = lambda ic, c0, w: self.ab(o_dl + ic * 2048 + c0, w)
    slh = lambda ec, c0=0, w=256: self.ab(o_slh + ec * 256 + c0, w)
    sll = lambda ec, c0=0, w=256: self.ab(o_sll + ec * 256 + c0, w)
    h_hi = lambda ic, c0, w: self.ab(o_h + ic * 256 + c0, w)
    h_lo = lambda ic, c0, w: self.ab(o_h + 1536 + ic * 256 + c0, w)
    nibf = self.af(2048, 512)
    t1 = self.af(2560, 512)
    t2 = self.af(3072, 512)
    val = self.af(3584, 512)
    silu_t = self.af(4096, 256)
    srow = self.af(8192, 512, part0=0, nparts=2)
    stage = [self.af(0, 2048), self.af(6656, 2048)]
    ldm = {}
    dq = [("gu", ec, ot) for ec in range(ECH) for ot in range(3)] + \
         [("dw", ic, eg) for ic in range(NIC) for eg in range(4)]

    def slab_pieces(sl):
        out, done = [], 0
        while done < 256:
            t = 256 * sl + done
            r, lc = t // TOK, t % TOK
            w = min(256 - done, TOK - lc)
            out.append((r, lc, w, done))
            done += w
        return out

    @self.block.sync
    def _(sync):
        for u, (kind, a, bb) in enumerate(dq):
            QX.wait(sync, u)
            src = self.i_gup if kind == "gu" else self.i_dwp
            LDQ.inc(sync.dma_start(out=self.c_i32a[:],
                                   in_=src[128 * a:128 * (a + 1), 512 * bb:512 * (bb + 1)]), 16)
            if kind == "gu":
                LDQ.inc(sync.dma_start(out=self.c_f16[0:1, 0:512],
                                       in_=self.i_gus[a:a + 1, 512 * bb:512 * (bb + 1)]), 16)
            else:
                LDQ.inc(sync.dma_start(out=self.c_f16[0:2, 0:512],
                                       in_=self.i_dws[2 * a:2 * a + 2, 512 * bb:512 * (bb + 1)]), 16)
            ldm[("dq", u)] = LDQ.n
        for sl in range(NSL):
            G.wait(sync, 6 * sl)
            for (r, lc, w, doff) in slab_pieces(sl):
                if w == 256 and doff == 0:
                    src_h = self.d_xh.ap()[2048 * r:2048 * (r + 1), lc:lc + w] \
                        .rearrange("(n p) m -> p n m", p=128)
                    src_l = self.d_xl.ap()[2048 * r:2048 * (r + 1), lc:lc + w] \
                        .rearrange("(n p) m -> p n m", p=128)
                    LSL.inc(sync.dma_start(
                        out=self.ab(o_slh, 4096).rearrange("p (n m) -> p n m", n=ECH),
                        in_=src_h), 16)
                    LSL.inc(sync.dma_start(
                        out=self.ab(o_sll, 4096).rearrange("p (n m) -> p n m", n=ECH),
                        in_=src_l), 16)
                else:
                    for ec in range(ECH):
                        LSL.inc(sync.dma_start(out=slh(ec, doff, w),
                                               in_=self.d_xh.ap()[2048 * r + 128 * ec:
                                                                  2048 * r + 128 * (ec + 1),
                                                                  lc:lc + w]), 16)
                        LSL.inc(sync.dma_start(out=sll(ec, doff, w),
                                               in_=self.d_xl.ap()[2048 * r + 128 * ec:
                                                                  2048 * r + 128 * (ec + 1),
                                                                  lc:lc + w]), 16)
            ldm[("slab", sl)] = LSL.n
            for tsub in range(2):
                ts_idx = 2 * sl + tsub
                Y.wait(sync, 8 * sl + 4 * tsub + 4)
                tg = 256 * sl + 128 * tsub
                ch, ro = tg // 1024, tg % 1024
                LY[ts_idx % 2].inc(sync.dma_start(
                    out=self.d_part[ch].ap()[ro:ro + 128, 2048:4096],
                    in_=stage[tsub % 2]), 16)
                ldm[("st", ts_idx)] = LY[ts_idx % 2].n
                ldm[("rs", ts_idx)] = (self.sem("LY0").n, self.sem("LY1").n)
        for ec in range(ECH):
            if ec < 2:
                LY[0].wait(sync, self.sem("LY0").n)
                LY[1].wait(sync, self.sem("LY1").n)
            else:
                C1.wait(sync, ec - 1)
            LL[ec % 2].inc(sync.dma_start(out=stage[ec % 2],
                                          in_=self.i_wl1[128 * ec:128 * (ec + 1), :]), 16)
            ldm[("l1w", ec)] = LL[ec % 2].n
        for ec in range(ECH):
            u = ECH + ec
            if ec < 2:
                C1.wait(sync, ECH)
            else:
                C5.wait(sync, ec - 1)
            LL[u % 2].inc(sync.dma_start(out=stage[u % 2],
                                         in_=self.i_wl1[128 * ec:128 * (ec + 1), :]), 16)
            ldm[("l1w", u)] = LL[u % 2].n
            C5.wait(sync, ec + 1)
            LWT[ec % 2].inc(sync.dma_start(out=self.d_wt.ap()[128 * ec:128 * (ec + 1), :],
                                           in_=self.ab(73728 + 2048 * (ec % 2), 2048)), 16)
            ldm[("wt", ec)] = LWT[ec % 2].n

    @self.block.tensor
    def _(tensor):
        for u, (kind, a, bb) in enumerate(dq):
            LDQ.wait(tensor, ldm[("dq", u)])
            QX.wait(tensor, u + 1)
            QW.wait(tensor, u)
            if kind == "gu":
                QB.inc(tensor.matmul(self.pb[4][:], self.c_onef[:], srow[0:1, 0:512],
                                     start=True, stop=True))
            else:
                QB.inc(tensor.matmul(self.pb[4][:], self.c_dsel[:], srow[0:2, 0:512],
                                     start=True, stop=True))
        for sl in range(NSL):
            LSL.wait(tensor, ldm[("slab", sl)])
            if sl == 0:
                QW.wait(tensor, len(dq))
            for i in range(6):
                p = 6 * sl + i
                pg, pu = (0, 1) if i % 2 == 0 else (5, 6)
                HH.wait(tensor, max(0, p - 1))
                SI.wait(tensor, max(0, p - 1))
                for ec in range(ECH):
                    self.mm3(tensor, self.pb[pg][:, 0:256],
                             gu_h(ec, 128 * i, 128), gu_l(ec, 128 * i, 128),
                             slh(ec), sll(ec), start=(ec == 0), stop=(ec == ECH - 1))
                last = None
                for ec in range(ECH):
                    last = self.mm3(tensor, self.pb[pu][:, 0:256],
                                    gu_h(ec, 768 + 128 * i, 128), gu_l(ec, 768 + 128 * i, 128),
                                    slh(ec), sll(ec), start=(ec == 0), stop=(ec == ECH - 1))
                G.inc(last)
            HH.wait(tensor, 6 * (sl + 1))
            for tsub in range(2):
                for eg in range(4):
                    d = 8 * sl + 4 * tsub + eg
                    Y.wait(tensor, max(0, d - 1))
                    last = None
                    for ic in range(NIC):
                        last = self.mm3(tensor, self.pb[2 + d % 2][:],
                                        h_hi(ic, 128 * tsub, 128), h_lo(ic, 128 * tsub, 128),
                                        wd_h(ic, 512 * eg, 512), wd_l(ic, 512 * eg, 512),
                                        start=(ic == 0), stop=(ic == NIC - 1))
                    D2.inc(last)
        HH.wait(tensor, 6 * NSL)
        C1.wait(tensor, ECH)
        C2.inc(tensor.matmul(self.pb[6][0:1, 0:1], self.c_col[:, 5:6], self.c_onecf[:],
                             start=True, stop=True))
        C4.wait(tensor, 1)
        tensor.matmul(self.pb[6][:, 1:2], self.c_onef[:], self.c_sc[0:1, 10:11],
                      start=True, stop=True)
        tensor.matmul(self.pb[6][:, 2:3], self.c_onef[:], self.c_sc[0:1, 11:12],
                      start=True, stop=True)
        C3.inc(tensor.matmul(self.pb[6][:, 3:4], self.c_onef[:], self.c_sc[0:1, 8:9],
                             start=True, stop=True))

    @self.block.scalar
    def _(scalar):
        for sl in range(NSL):
            for i in range(6):
                p = 6 * sl + i
                pg = 0 if i % 2 == 0 else 5
                G.wait(scalar, p + 1)
                if p >= 1:
                    HH.wait(scalar, p)
                SI.inc(scalar.activation(out=silu_t, in_=self.pb[pg][:, 0:256], func=AF.Silu,
                                         bias=self.c_zero[:, 0:1]))

    @self.block.vector
    def _(vector):
        HZ = self.sem("HZV")
        for u, (kind, a, bb) in enumerate(dq):
            LDQ.wait(vector, ldm[("dq", u)])
            nparts = 1 if kind == "gu" else 2
            vector.tensor_copy(srow[0:nparts, :], self.c_f16[0:nparts, 0:512])
            QX.inc(vector.tensor_scalar(self.c_i32b[:], self.c_i32a[:],
                                        self.c_nmask[:, 0:1], None,
                                        op0=ALU.bitwise_and))
            vector.tensor_copy(nibf, self.c_i32b[:])
            vector.tensor_scalar(nibf, nibf, self.c_ninv[:, 0:1], None, op0=ALU.mult)
            vector.tensor_scalar(t1, nibf, 8.0, None, op0=ALU.is_ge)
            vector.scalar_tensor_tensor(t2, t1, -8.0, nibf, op0=ALU.mult, op1=ALU.add)
            vector.tensor_scalar(val, t2, 4.0, 0.5, op0=ALU.min, op1=ALU.mult)
            vector.tensor_scalar(nibf, t2, 4.0, 0.0, op0=ALU.subtract, op1=ALU.max)
            vector.scalar_tensor_tensor(nibf, t2, 7.0, nibf, op0=ALU.is_ge, op1=ALU.add)
            vector.tensor_tensor(val, val, nibf, op=ALU.add)
            vector.tensor_scalar(t1, t1, -2.0, 1.0, op0=ALU.mult, op1=ALU.add)
            vector.tensor_tensor(val, val, t1, op=ALU.mult)
            QB.wait(vector, u + 1)
            vector.tensor_tensor(val, val, self.pb[4][:], op=ALU.mult)
            if kind == "gu":
                dh, dl = gu_h(a, 512 * bb, 512), gu_l(a, 512 * bb, 512)
            else:
                dh, dl = wd_h(a, 512 * bb, 512), wd_l(a, 512 * bb, 512)
            vector.tensor_copy(dh, val)
            QW.inc(vector.tensor_tensor(dl, val, dh, op=ALU.subtract))
        for sl in range(NSL):
            for i in range(6):
                p = 6 * sl + i
                pu = 1 if i % 2 == 0 else 6
                SI.wait(vector, p + 1)
                if sl >= 1 and i == 0:
                    D2.wait(vector, 8 * sl)
                vector.tensor_tensor(silu_t, silu_t, self.pb[pu][:, 0:256], op=ALU.mult)
                vector.tensor_copy(h_hi(i, 0, 256), silu_t)
                HH.inc(vector.tensor_tensor(h_lo(i, 0, 256), silu_t, h_hi(i, 0, 256),
                                            op=ALU.subtract))
            for tsub in range(2):
                for eg in range(4):
                    d = 8 * sl + 4 * tsub + eg
                    D2.wait(vector, d + 1)
                    if 2 * sl + tsub >= 2 and eg == 0:
                        LY[(2 * sl + tsub) % 2].wait(vector, ldm[("st", 2 * sl + tsub - 2)])
                    Y.inc(vector.tensor_copy(stage[tsub % 2][:, 512 * eg:512 * (eg + 1)],
                                              self.pb[2 + d % 2][:]))
        for ec in range(ECH):
            LL[ec % 2].wait(vector, ldm[("l1w", ec)])
            if ec == 0:
                C1.inc(vector.reduce_sum(self.c_col[:, 5:6], stage[0], axis=AXX,
                                         apply_absolute_value=True))
            else:
                HZ.inc(vector.reduce_sum(self.c_col[:, 4:5], stage[ec % 2], axis=AXX,
                                         apply_absolute_value=True))
                HZ.wait(vector, HZ.n)
                C1.inc(vector.tensor_tensor(self.c_col[:, 5:6], self.c_col[:, 5:6],
                                            self.c_col[:, 4:5], op=ALU.add))
            C1.wait(vector, ec + 1)
        C2.wait(vector, 1)
        HZ.inc(vector.tensor_scalar(self.c_sc[0:1, 8:9], self.pb[6][0:1, 0:1],
                                    1.0 / (E * E), None, op0=ALU.mult))
        HZ.wait(vector, HZ.n)
        HZ.inc(vector.tensor_scalar(self.c_sc[0:1, 10:11], self.c_sc[0:1, 8:9], 1e-10, 0.5,
                                    op0=ALU.add, op1=ALU.mult))
        HZ.wait(vector, HZ.n)
        C4.inc(vector.tensor_scalar(self.c_sc[0:1, 11:12], self.c_sc[0:1, 10:11],
                                    -1.0, None, op0=ALU.mult))
        C3.wait(vector, 1)
        vector.tensor_copy(self.c_col[:, 4:5], self.pb[6][:, 1:2])
        vector.tensor_copy(self.c_col[:, 5:6], self.pb[6][:, 2:3])
        HZ.inc(vector.tensor_copy(self.c_col[:, 6:7], self.pb[6][:, 3:4]))
        HZ.wait(vector, HZ.n)
        for ec in range(ECH):
            u = ECH + ec
            LL[u % 2].wait(vector, ldm[("l1w", u)])
            if ec >= 2:
                LWT[ec % 2].wait(vector, ldm[("wt", ec - 2)])
            vector.tensor_scalar(self.af(2048, 2048), stage[u % 2], self.c_col[:, 4:5],
                                 None, op0=ALU.is_gt)
            vector.tensor_scalar(self.af(4096, 2048), stage[u % 2], self.c_col[:, 5:6],
                                 None, op0=ALU.is_lt)
            C5.inc(vector.tensor_tensor(self.ab(73728 + 2048 * (ec % 2), 2048),
                                        self.af(2048, 2048), self.af(4096, 2048),
                                        op=ALU.subtract))

    @self.block.gpsimd
    def _(gpsimd):
        for ch in range(self.NCH):
            last_t128 = (1024 * (ch + 1) - 128) // 128
            n0, n1 = ldm[("rs", last_t128)]
            if n0:
                self.sem("LY0").wait(gpsimd, n0)
            if n1:
                self.sem("LY1").wait(gpsimd, n1)
            CC.inc(gpsimd.collective_compute(
                "ReduceScatter", ALU.add, replica_groups=[list(range(NCORES))],
                ins=[self.d_part[ch].ap().opt()], outs=[self.d_rs[ch].ap().opt()]))


MK.ph4 = ph4


def ph5(self):
    nc, S, TOK, NT = self.nc, self.S, self.TOK, self.NT
    L5, L5S, L5P, LCT = self.sem("L5"), self.sem("L5S"), self.sem("L5P"), self.sem("LCT")
    R2, R3, R4, R5 = (self.sem(x) for x in ("P5R2", "P5R3", "P5R4", "P5R5"))
    TT, SP = self.sem("P5T"), self.sem("P5S")
    BO = self.sem("P5BO")
    bufY = [self.af(0, 2048), self.af(2048, 2048)]
    bufA = self.af(4096, 2048)
    bufX = self.af(6144, 2048)
    bo_row = self.af(7168, 2048, part0=0, nparts=1)
    o_xh, o_xl = 0, ECH * TOK
    ldm = {}

    def spill_xnT(sync, tt):
        for ec in range(ECH):
            g = tt * ECH + ec
            SP.wait(sync, g + 1)
            L5S.inc(sync.dma_start(
                out=self.d_x1h.ap()[128 * ec:128 * (ec + 1), 128 * tt:128 * (tt + 1)],
                in_=self.ab(o_xh + ec * TOK + 128 * tt, 128)), 16)
            L5S.inc(sync.dma_start(
                out=self.d_x1l.ap()[128 * ec:128 * (ec + 1), 128 * tt:128 * (tt + 1)],
                in_=self.ab(o_xl + ec * TOK + 128 * tt, 128)), 16)

    @self.block.sync
    def _(sync):
        LCT.inc(sync.dma_start(out=bo_row, in_=self.i_bo), 16)
        ldm["bo"] = LCT.n
        L5.inc(sync.dma_start(out=bufY[0], in_=self.d_rs[0].ap()[:, 2048:4096]), 16)
        L5.inc(sync.dma_start(out=bufA, in_=self.d_rs[0].ap()[:, 0:2048]), 16)
        ldm[("y", 0)] = L5.n
        for tt in range(NT):
            if tt == 0:
                BO.wait(sync, 1)
            else:
                R2.wait(sync, tt)
            L5.inc(sync.dma_start(out=bufX, in_=self.i_x2[128 * tt:128 * (tt + 1), :]), 16)
            ldm[("x", tt)] = L5.n
            if tt > 0:
                R2.wait(sync, tt)
                L5P.inc(sync.dma_start(out=self.d_x1.ap()[128 * (tt - 1):128 * tt, :],
                                       in_=bufY[(tt - 1) % 2]), 16)
                ldm[("spw", tt - 1)] = L5P.n
                spill_xnT(sync, tt - 1)
            if tt + 1 < NT:
                R4.wait(sync, tt)
                if tt > 0:
                    L5P.wait(sync, ldm[("spw", tt - 1)])
                L5.inc(sync.dma_start(out=bufY[(tt + 1) % 2],
                                      in_=self.d_rs[tt + 1].ap()[:, 2048:4096]), 16)
                TT.wait(sync, ECH * (tt + 1))
                L5.inc(sync.dma_start(out=bufA, in_=self.d_rs[tt + 1].ap()[:, 0:2048]), 16)
                ldm[("y", tt + 1)] = L5.n
        R2.wait(sync, NT)
        L5P.inc(sync.dma_start(out=self.d_x1.ap()[128 * (NT - 1):128 * NT, :],
                               in_=bufY[(NT - 1) % 2]), 16)
        ldm[("spw", NT - 1)] = L5P.n
        spill_xnT(sync, NT - 1)

    @self.block.tensor
    def _(tensor):
        LCT.wait(tensor, ldm["bo"])
        last = None
        for og in range(4):
            last = tensor.matmul(self.pb[og][:], self.c_onef[:],
                                 bo_row[0:1, 512 * og:512 * (og + 1)], start=True, stop=True)
        BO.inc(last)
        for tt in range(NT):
            R4.wait(tensor, tt + 1)
            for ec in range(ECH):
                g = tt * ECH + ec
                if g >= 2:
                    SP.wait(tensor, g - 1)
                TT.inc(tensor.transpose(self.pb[4 + g % 2][:, 0:128],
                                        bufA[:, 128 * ec:128 * (ec + 1)], self.c_idf[:]))

    @self.block.scalar
    def _(scalar):
        for tt in range(NT):
            R3.wait(scalar, 2 * tt + 1)
            R3.inc(scalar.activation(out=self.c_col[:, 1:2], in_=self.c_col[:, 0:1],
                                     func=AF.Sqrt, bias=self.c_zero[:, 0:1]))
            R5.wait(scalar, 2 * tt + 1)
            R5.inc(scalar.activation(out=self.c_col[:, 4:5], in_=self.c_col[:, 3:4],
                                     func=AF.Sqrt, bias=self.c_zero[:, 0:1]))

    @self.block.vector
    def _(vector):
        HZ = self.sem("HZV")
        BO.wait(vector, 1)
        for tt in range(NT):
            b = tt % 2
            yb = bufY[b]
            L5.wait(vector, ldm[("y", tt)])
            vector.tensor_tensor(self.af(8192, 512), yb[:, 0:512], yb[:, 0:512], op=ALU.mult)
            HZ.inc(vector.reduce_sum(self.c_col[:, 0:1], self.af(8192, 512), axis=AXX))
            for k in range(1, 4):
                vector.tensor_tensor(self.af(8192, 512), yb[:, 512 * k:512 * (k + 1)],
                                     yb[:, 512 * k:512 * (k + 1)], op=ALU.mult)
                HZ.inc(vector.reduce_sum(self.c_col[:, 5:6], self.af(8192, 512), axis=AXX))
                HZ.wait(vector, HZ.n)
                HZ.inc(vector.tensor_tensor(self.c_col[:, 0:1], self.c_col[:, 0:1],
                                            self.c_col[:, 5:6], op=ALU.add))
                HZ.wait(vector, HZ.n)
            R3.inc(vector.tensor_scalar(self.c_col[:, 0:1], self.c_col[:, 0:1],
                                        1.0 / E, 1e-6, op0=ALU.mult, op1=ALU.add))
            R3.wait(vector, 2 * (tt + 1))
            HZ.inc(vector.reciprocal(self.c_col[:, 2:3], self.c_col[:, 1:2]))
            HZ.wait(vector, HZ.n)
            vector.tensor_scalar(yb, yb, self.c_col[:, 2:3], None, op0=ALU.mult)
            vector.tensor_tensor(yb, yb, bufA, op=ALU.add)
            for og in range(4):
                vector.tensor_tensor(yb[:, 512 * og:512 * (og + 1)],
                                     yb[:, 512 * og:512 * (og + 1)],
                                     self.pb[og][:], op=ALU.add)
            L5.wait(vector, ldm[("x", tt)])
            R2.inc(vector.tensor_tensor(yb, yb, bufX, op=ALU.add))
            for k in range(4):
                HZ.inc(vector.bn_stats(self.c_stats[:, 6 * k:6 * k + 6],
                                       yb[:, 512 * k:512 * (k + 1)]))
            HZ.wait(vector, HZ.n)
            HZ.inc(vector.bn_aggr(self.c_stats[:, 24:26], self.c_stats[:, 0:24]))
            HZ.wait(vector, HZ.n)
            R5.inc(vector.tensor_scalar(self.c_col[:, 3:4], self.c_stats[:, 25:26], 1e-5, None,
                                        op0=ALU.add))
            R5.wait(vector, 2 * (tt + 1))
            HZ.inc(vector.reciprocal(self.c_col[:, 5:6], self.c_col[:, 4:5]))
            HZ.wait(vector, HZ.n)
            vector.tensor_scalar(bufA, yb, self.c_stats[:, 24:25], None, op0=ALU.subtract)
            R4.inc(vector.tensor_scalar(bufA, bufA, self.c_col[:, 5:6], None, op0=ALU.mult))
            for ec in range(ECH):
                g = tt * ECH + ec
                TT.wait(vector, g + 1)
                hi_ap = self.ab(o_xh + ec * TOK + 128 * tt, 128)
                lo_ap = self.ab(o_xl + ec * TOK + 128 * tt, 128)
                vector.tensor_copy(hi_ap, self.pb[4 + g % 2][:, 0:128])
                SP.inc(vector.tensor_tensor(lo_ap, self.pb[4 + g % 2][:, 0:128],
                                            hi_ap, op=ALU.subtract))


MK.ph5 = ph5


def ph6(self):
    nc, S, TOK, NT = self.nc, self.S, self.TOK, self.NT
    CC = self.sem("CC")
    LCT = self.sem("LCT")
    LW = [self.sem("LW0"), self.sem("LW1")]
    LQ = [self.sem("LQ0"), self.sem("LQ1")]
    L6KV = self.sem("L6KV")
    W6, M6, E6, Q6 = (self.sem(x) for x in ("P6W", "P6M", "P6E", "P6Q"))
    o_xh, o_xl = 0, ECH * TOK
    o_wah, o_wal = 2 * ECH * TOK, 2 * ECH * TOK + 8192
    wstg = [self.af(0, 512), self.af(512, 512)]
    cos_t = lambda tt: self.af(2048 + 128 * tt, 128)
    sin_t = lambda tt: self.af(2048 + 128 * NT + 128 * tt, 128)
    qstage = [self.af(4096, 512), self.af(5888, 512)]
    qa = [self.af(4864, 512), self.af(6400, 512)]
    rot = self.af(4608, 128)
    kst = lambda tt: self.af(6912 + 128 * tt, 128)
    vst = lambda tt: self.af(7936 + 128 * tt, 128)
    ldm = {}
    ogs = [4, 0, 1, 2, 3]

    @self.block.sync
    def _(sync):
        for tt in range(NT):
            LCT.inc(sync.dma_start(out=cos_t(tt), in_=self.i_cos[128 * tt:128 * (tt + 1), :]), 16)
            LCT.inc(sync.dma_start(out=sin_t(tt), in_=self.i_sin[128 * tt:128 * (tt + 1), :]), 16)
        ldm["cs"] = LCT.n
        for oi, og in enumerate(ogs):
            wdt = 256 if og == 4 else 512
            for ec in range(ECH):
                idx = oi * ECH + ec
                if idx >= 2:
                    W6.wait(sync, idx - 1)
                LW[idx % 2].inc(sync.dma_start(out=wstg[idx % 2][:, 0:wdt],
                                               in_=self.i_wa[128 * ec:128 * (ec + 1),
                                                             512 * og:512 * og + wdt]), 16)
                ldm[("w", idx)] = LW[idx % 2].n
            for tt in range(NT):
                idx = oi * NT + tt
                Q6.wait(sync, idx + 1)
                if og == 4:
                    L6KV.inc(sync.dma_start(out=self.d_k1in.ap()[128 * tt:128 * (tt + 1), :],
                                            in_=kst(tt)), 16)
                    L6KV.inc(sync.dma_start(out=self.d_v1in.ap()[128 * tt:128 * (tt + 1), :],
                                            in_=vst(tt)), 16)
                else:
                    LQ[idx % 2].inc(sync.dma_start(
                        out=self.d_q1.ap()[128 * tt:128 * (tt + 1), 512 * og:512 * (og + 1)],
                        in_=qa[idx % 2]), 16)
                    ldm[("st", idx)] = LQ[idx % 2].n
        ldm["kv_end"] = L6KV.n

    @self.block.tensor
    def _(tensor):
        for oi, og in enumerate(ogs):
            wdt = 256 if og == 4 else 512
            W6.wait(tensor, ECH * (oi + 1))
            for tt in range(NT):
                idx = oi * NT + tt
                if idx >= 2:
                    E6.wait(tensor, idx - 1)
                last = None
                for ec in range(ECH):
                    last = self.mm3(tensor, self.pb[idx % 2][:, 0:wdt],
                                    self.ab(o_xh + ec * TOK + 128 * tt, 128),
                                    self.ab(o_xl + ec * TOK + 128 * tt, 128),
                                    self.ab(o_wah + ec * 512, wdt),
                                    self.ab(o_wal + ec * 512, wdt),
                                    start=(ec == 0), stop=(ec == ECH - 1))
                M6.inc(last)

    @self.block.scalar
    def _(scalar):
        for oi, og in enumerate(ogs):
            wdt = 256 if og == 4 else 512
            sc = 1.0 if og == 4 else INV_SQRT_D
            for tt in range(NT):
                idx = oi * NT + tt
                M6.wait(scalar, idx + 1)
                if idx >= 2:
                    Q6.wait(scalar, idx - 1)
                E6.inc(scalar.activation(out=qstage[idx % 2][:, 0:wdt],
                                         in_=self.pb[idx % 2][:, 0:wdt],
                                         func=AF.Identity, bias=self.c_zero[:, 0:1], scale=sc))

    @self.block.vector
    def _(vector):
        for oi, og in enumerate(ogs):
            for ec in range(ECH):
                idx = oi * ECH + ec
                wdt = 256 if og == 4 else 512
                LW[idx % 2].wait(vector, ldm[("w", idx)])
                vector.tensor_copy(self.ab(o_wah + ec * 512, wdt), wstg[idx % 2][:, 0:wdt])
                W6.inc(vector.tensor_tensor(self.ab(o_wal + ec * 512, wdt),
                                            wstg[idx % 2][:, 0:wdt],
                                            self.ab(o_wah + ec * 512, wdt), op=ALU.subtract))
            HZ = self.sem("HZV")
            for tt in range(NT):
                idx = oi * NT + tt
                E6.wait(vector, idx + 1)
                if oi == 0 and tt == 0:
                    LCT.wait(vector, ldm["cs"])
                src = qstage[idx % 2]
                if og == 4:
                    vector.tensor_tensor(kst(tt), src[:, 0:128], cos_t(tt), op=ALU.mult)
                    vector.tensor_scalar(rot[:, 0:64], src[:, 64:128], -1.0, None, op0=ALU.mult)
                    HZ.inc(vector.tensor_copy(rot[:, 64:128], src[:, 0:64]))
                    HZ.wait(vector, HZ.n)
                    HZ.inc(vector.tensor_tensor(rot, rot, sin_t(tt), op=ALU.mult))
                    HZ.wait(vector, HZ.n)
                    vector.tensor_tensor(kst(tt), kst(tt), rot, op=ALU.add)
                    Q6.inc(vector.tensor_copy(vst(tt), src[:, 128:256]))
                else:
                    if idx >= 2 and ("st", idx - 2) in ldm:
                        LQ[idx % 2].wait(vector, ldm[("st", idx - 2)])
                    dst = qa[idx % 2]
                    last = None
                    for j in range(4):
                        js = slice(128 * j, 128 * (j + 1))
                        vector.tensor_tensor(dst[:, js], src[:, js], cos_t(tt), op=ALU.mult)
                        vector.tensor_scalar(rot[:, 0:64], src[:, 128 * j + 64:128 * (j + 1)],
                                             -1.0, None, op0=ALU.mult)
                        HZ.inc(vector.tensor_copy(rot[:, 64:128], src[:, 128 * j:128 * j + 64]))
                        HZ.wait(vector, HZ.n)
                        HZ.inc(vector.tensor_tensor(rot, rot, sin_t(tt), op=ALU.mult))
                        HZ.wait(vector, HZ.n)
                        last = vector.tensor_tensor(dst[:, js], dst[:, js], rot, op=ALU.add)
                    Q6.inc(last)

    @self.block.gpsimd
    def _(gpsimd):
        L6KV.wait(gpsimd, ldm["kv_end"])
        CC.inc(gpsimd.collective_compute(
            "AllGather", ALU.bypass, replica_groups=[list(range(NCORES))],
            ins=[self.d_k1in.ap().opt()], outs=[self.d_k1.ap().opt()]))
        CC.inc(gpsimd.collective_compute(
            "AllGather", ALU.bypass, replica_groups=[list(range(NCORES))],
            ins=[self.d_v1in.ap().opt()], outs=[self.d_v1.ap().opt()]))


MK.ph6 = ph6



def ph7(self):
    nc, S, TOK, NT = self.nc, self.S, self.TOK, self.NT
    QT1, QW1, KC = self.QT1, self.QW1, self.KC
    L7K, L7Q, L7S = self.sem("L7K"), self.sem("L7Q"), self.sem("L7S")
    KT, KE, VE = (self.sem(x) for x in ("P7KT", "P7KE", "P7VE"))
    KN1, KN2, KN3, KM, KSQ = (self.sem(x) for x in ("P7KN1", "P7KN2", "P7KN3", "P7KM", "P7KSQ"))
    QP, QS = self.sem("P7QP"), self.sem("P7QS")
    QN1, QN2, QN3, QM, QSQ = (self.sem(x) for x in ("P7QN1", "P7QN2", "P7QN3", "P7QM", "P7QSQ"))
    VM, TM, VC = (self.sem(x) for x in ("P7VM", "P7TM", "P7VC"))
    Asc, Aexp, Aspl, Apv = (self.sem(x) for x in ("A_sc", "A_exp", "A_spl", "A_pv"))
    Arden, Arb, Aqtev = (self.sem(x) for x in ("A_rden", "A_rb", "A_qtev"))
    o_kth, o_ktl, o_vh, o_vl = 0, S, 2 * S, 3 * S
    o_qh, o_ql = 4 * S, 4 * S + TOK
    o_p = 4 * S + 2 * TOK
    o_ostg = o_p + 3072
    o_q2 = o_ostg + 1024
    kth = lambda c0, w: self.ab(o_kth + c0, w)
    ktl = lambda c0, w: self.ab(o_ktl + c0, w)
    vh = lambda c0, w: self.ab(o_vh + c0, w)
    vl = lambda c0, w: self.ab(o_vl + c0, w)
    qhh = lambda c0, w: self.ab(o_qh + c0, w)
    qll = lambda c0, w: self.ab(o_ql + c0, w)
    p_hi = lambda g: self.ab(o_p + (g % 3) * 512, 512)
    p_lo = lambda g: self.ab(o_p + 1536 + (g % 3) * 512, 512)
    ostg_hi = self.ab(o_ostg, 512)
    ostg_lo = self.ab(o_ostg + 512, 512)
    q2t = self.ab(o_q2, 512)
    kstg = self.af(0, 128)
    qstg = self.af(128, 128)
    ptil = lambda g: self.af(3584 + 512 * (g % 3), 512)
    ostg_f = self.af(5120, 512)
    rb_tile = self.af(6656, 512)
    den_r = self.af(7168, 512, part0=0, nparts=1)
    rden_r = self.af(7680, 512, part0=0, nparts=1)
    negM = self.c_col[:, 3:4]
    ldm = {}
    NKB = S // 128
    NJK = S // 512
    NJ1 = max(1, TOK // 512)
    UPH = QT1 * KC
    g1b = self.g1

    def gidx(h, qt, kc):
        return g1b + h * UPH + qt * KC + kc

    @self.block.sync
    def _(sync):
        for kb in range(NKB):
            KT.wait(sync, kb)
            L7K.inc(sync.dma_start(out=kstg, in_=self.d_k1.ap()[128 * kb:128 * (kb + 1), :]), 16)
            ldm[("k", kb)] = L7K.n
        for kb in range(NKB):
            if kb == 0:
                KT.wait(sync, NKB)
            else:
                VE.wait(sync, kb)
            L7K.inc(sync.dma_start(out=kstg, in_=self.d_v1.ap()[128 * kb:128 * (kb + 1), :]), 16)
            ldm[("v", kb)] = L7K.n
        for h in range(H):
            for tt in range(NT):
                u = h * NT + tt
                QP.wait(sync, u)
                L7Q.inc(sync.dma_start(out=qstg,
                                       in_=self.d_q1.ap()[128 * tt:128 * (tt + 1),
                                                          128 * h:128 * (h + 1)]), 16)
                ldm[("q", u)] = L7Q.n
            for qt in range(QT1):
                qtg = h * QT1 + qt
                Aqtev.wait(sync, qtg + 1)
                L7S.inc(sync.dma_start(
                    out=self.d_o1h.ap()[128 * h:128 * (h + 1), QW1 * qt:QW1 * (qt + 1)],
                    in_=ostg_hi[:, 0:QW1]), 16)
                L7S.inc(sync.dma_start(
                    out=self.d_o1l.ap()[128 * h:128 * (h + 1), QW1 * qt:QW1 * (qt + 1)],
                    in_=ostg_lo[:, 0:QW1]), 16)
                ldm[("ost", qtg)] = L7S.n

    @self.block.tensor
    def _(tensor):
        def pv7(qt, kc, g, qtg):
            last = (kc == KC - 1)
            Aspl.wait(tensor, g + 1)
            if kc == 0:
                Aqtev.wait(tensor, qtg)
            tensor.matmul(self.pb[3][0:1, 0:QW1], self.c_onecb[:], p_hi(g)[:, 0:QW1],
                          start=(kc == 0), stop=False)
            tensor.matmul(self.pb[3][0:1, 0:QW1], self.c_onecb[:], p_lo(g)[:, 0:QW1],
                          start=False, stop=last)
            tensor.matmul(self.pb[2][:, 0:QW1], vh(kc * 128, 128), p_hi(g)[:, 0:QW1],
                          start=(kc == 0), stop=False)
            tensor.matmul(self.pb[2][:, 0:QW1], vl(kc * 128, 128), p_hi(g)[:, 0:QW1],
                          start=False, stop=False)
            Apv.inc(tensor.matmul(self.pb[2][:, 0:QW1], vh(kc * 128, 128), p_lo(g)[:, 0:QW1],
                                  start=False, stop=last))

        for kb in range(NKB):
            L7K.wait(tensor, ldm[("k", kb)])
            if kb >= 2:
                KE.wait(tensor, kb - 1)
            KT.inc(tensor.transpose(self.pb[5][:, 128 * (kb % 2):128 * (kb % 2) + 128],
                                    kstg, self.c_idf[:]))
        for j in range(NJK):
            KN1.wait(tensor, j + 1)
            if j >= 1:
                KN3.wait(tensor, j)
            KN2.inc(tensor.matmul(self.pb[3][0:1, 0:512], self.c_onecb[:], q2t,
                                  start=True, stop=True))
        for h in range(H):
            for tt in range(NT):
                u = h * NT + tt
                L7Q.wait(tensor, ldm[("q", u)])
                if tt == 0 and h >= 1:
                    Aexp.wait(tensor, h * UPH)
                if u >= 2:
                    QS.wait(tensor, u - 1)
                QP.inc(tensor.transpose(self.pb[6][:, 128 * (u % 2):128 * (u % 2) + 128],
                                        qstg, self.c_idf[:]))
            if h == 0:
                KN3.wait(tensor, NJK)
            Arden.wait(tensor, h * QT1)
            for j in range(NJ1):
                uj = h * NJ1 + j
                QN1.wait(tensor, uj + 1)
                if uj >= 1:
                    QN3.wait(tensor, uj)
                QN2.inc(tensor.matmul(self.pb[3][0:1, 0:QW1], self.c_onecb[:], q2t[:, 0:QW1],
                                      start=True, stop=True))
            VM.wait(tensor, h + 1)
            if h >= 1:
                VC.wait(tensor, h)
            TM.inc(tensor.matmul(self.pb[4][:, 0:1], self.c_onef[:], self.c_sc[0:1, 7:8],
                                 start=True, stop=True))
            QS.wait(tensor, (h + 1) * NT)
            for qt in range(QT1):
                qtg = h * QT1 + qt
                for kc in range(KC):
                    g = gidx(h, qt, kc)
                    Aexp.wait(tensor, max(0, g - 2))
                    Asc.inc(self.mm3(tensor, self.pb[(0, 1, 6)[g % 3]][:, 0:QW1],
                                     kth(kc * 128, 128), ktl(kc * 128, 128),
                                     qhh(QW1 * qt, QW1), qll(QW1 * qt, QW1),
                                     start=True, stop=True))
                    if kc >= 1:
                        pv7(qt, kc - 1, g - 1, qtg)
                pv7(qt, KC - 1, gidx(h, qt, KC - 1), qtg)
                Arden.wait(tensor, qtg + 1)
                Aqtev.wait(tensor, qtg)
                Arb.inc(tensor.matmul(self.pb[5][:, 0:QW1], self.c_onef[:],
                                      rden_r[0:1, 0:QW1], start=True, stop=True))

    @self.block.scalar
    def _(scalar):
        KM.wait(scalar, 1)
        KSQ.inc(scalar.activation(out=self.c_sc[0:1, 5:6], in_=self.c_sc[0:1, 2:3],
                                  func=AF.Sqrt, bias=self.c_zero[0:1, 0:1]))
        for h in range(H):
            QM.wait(scalar, h + 1)
            QSQ.inc(scalar.activation(out=self.c_sc[0:1, 4:5], in_=self.c_sc[0:1, 0:1],
                                      func=AF.Sqrt, bias=self.c_zero[0:1, 0:1]))
            for qt in range(QT1):
                for kc in range(KC):
                    g = gidx(h, qt, kc)
                    Asc.wait(scalar, g + 1)
                    Aspl.wait(scalar, max(0, g - 2))
                    VC.wait(scalar, h + 1)
                    Aexp.inc(scalar.activation(out=ptil(g)[:, 0:QW1],
                                               in_=self.pb[(0, 1, 6)[g % 3]][:, 0:QW1],
                                               func=AF.Exp, bias=negM, scale=1.0))

    @self.block.vector
    def _(vector):
        HZ = self.sem("HZV")
        for kb in range(NKB):
            KT.wait(vector, kb + 1)
            slot = self.pb[5][:, 128 * (kb % 2):128 * (kb % 2) + 128]
            vector.tensor_copy(kth(128 * kb, 128), slot)
            KE.inc(vector.tensor_tensor(ktl(128 * kb, 128), slot, kth(128 * kb, 128),
                                        op=ALU.subtract))
        for kb in range(NKB):
            L7K.wait(vector, ldm[("v", kb)])
            vector.tensor_copy(vh(128 * kb, 128), kstg)
            VE.inc(vector.tensor_tensor(vl(128 * kb, 128), kstg, vh(128 * kb, 128),
                                        op=ALU.subtract))
        for j in range(NJK):
            KN1.inc(vector.tensor_tensor(q2t, kth(512 * j, 512), kth(512 * j, 512), op=ALU.mult))
            KN2.wait(vector, j + 1)
            if j == 0:
                KN3.inc(vector.reduce_max(self.c_sc[0:1, 2:3], self.pb[3][0:1, 0:512], axis=AXX))
            else:
                HZ.inc(vector.reduce_max(self.c_sc[0:1, 6:7], self.pb[3][0:1, 0:512], axis=AXX))
                HZ.wait(vector, HZ.n)
                KN3.inc(vector.tensor_tensor(self.c_sc[0:1, 2:3], self.c_sc[0:1, 2:3],
                                             self.c_sc[0:1, 6:7], op=ALU.max))
            KN3.wait(vector, j + 1)
        KM.inc(vector.tensor_copy(self.c_sc[0:1, 16:17], self.c_sc[0:1, 2:3]))
        for h in range(H):
            for tt in range(NT):
                u = h * NT + tt
                QP.wait(vector, u + 1)
                slot = self.pb[6][:, 128 * (u % 2):128 * (u % 2) + 128]
                vector.tensor_copy(qhh(128 * tt, 128), slot)
                QS.inc(vector.tensor_tensor(qll(128 * tt, 128), slot, qhh(128 * tt, 128),
                                            op=ALU.subtract))
            for j in range(NJ1):
                uj = h * NJ1 + j
                QN1.inc(vector.tensor_tensor(q2t[:, 0:QW1], qhh(QW1 * j, QW1),
                                             qhh(QW1 * j, QW1), op=ALU.mult))
                QN2.wait(vector, uj + 1)
                if j == 0:
                    QN3.inc(vector.reduce_max(self.c_sc[0:1, 0:1], self.pb[3][0:1, 0:QW1],
                                              axis=AXX))
                else:
                    HZ.inc(vector.reduce_max(self.c_sc[0:1, 6:7], self.pb[3][0:1, 0:QW1],
                                             axis=AXX))
                    HZ.wait(vector, HZ.n)
                    QN3.inc(vector.tensor_tensor(self.c_sc[0:1, 0:1], self.c_sc[0:1, 0:1],
                                                 self.c_sc[0:1, 6:7], op=ALU.max))
                QN3.wait(vector, uj + 1)
            QM.inc(vector.tensor_copy(self.c_sc[0:1, 17:18], self.c_sc[0:1, 0:1]))
            QSQ.wait(vector, h + 1)
            if h == 0:
                KSQ.wait(vector, 1)
            HZ.inc(vector.tensor_tensor(self.c_sc[0:1, 6:7], self.c_sc[0:1, 4:5],
                                        self.c_sc[0:1, 5:6], op=ALU.mult))
            HZ.wait(vector, HZ.n)
            VM.inc(vector.tensor_scalar(self.c_sc[0:1, 7:8], self.c_sc[0:1, 6:7],
                                        -1.01, -0.5, op0=ALU.mult, op1=ALU.add))
            TM.wait(vector, h + 1)
            if h >= 1:
                Aexp.wait(vector, g1b + h * UPH)
            VC.inc(vector.tensor_copy(negM, self.pb[4][:, 0:1]))
            for qt in range(QT1):
                qtg = h * QT1 + qt
                for kc in range(KC):
                    g = gidx(h, qt, kc)
                    Aexp.wait(vector, g + 1)
                    Apv.wait(vector, max(0, g - 2))
                    vector.tensor_copy(p_hi(g)[:, 0:QW1], ptil(g)[:, 0:QW1])
                    Aspl.inc(vector.tensor_tensor(p_lo(g)[:, 0:QW1], ptil(g)[:, 0:QW1],
                                                  p_hi(g)[:, 0:QW1], op=ALU.subtract))
                g = gidx(h, qt, KC - 1)
                Apv.wait(vector, g + 1)
                HZ.inc(vector.tensor_copy(den_r[0:1, 0:QW1], self.pb[3][0:1, 0:QW1]))
                HZ.wait(vector, HZ.n)
                Arden.inc(vector.reciprocal(rden_r[0:1, 0:QW1], den_r[0:1, 0:QW1]))
                Arb.wait(vector, qtg + 1)
                vector.tensor_copy(rb_tile[:, 0:QW1], self.pb[5][:, 0:QW1])
                if qtg >= 1:
                    L7S.wait(vector, ldm[("ost", qtg - 1)])
                vector.tensor_tensor(ostg_f[:, 0:QW1], self.pb[2][:, 0:QW1],
                                     rb_tile[:, 0:QW1], op=ALU.mult)
                vector.tensor_copy(ostg_hi[:, 0:QW1], ostg_f[:, 0:QW1])
                Aqtev.inc(vector.tensor_tensor(ostg_lo[:, 0:QW1], ostg_f[:, 0:QW1],
                                               ostg_hi[:, 0:QW1], op=ALU.subtract))

    self.g1 += H * UPH


MK.ph7 = ph7


def ph8(self):
    nc, S, TOK, NT = self.nc, self.S, self.TOK, self.NT
    LW = [self.sem("LW0"), self.sem("LW1")]
    L8OH, L8X, L8S = self.sem("L8OH"), self.sem("L8X"), self.sem("L8S")
    W8, M8, E8, O8 = (self.sem(x) for x in ("P8W", "P8M", "P8E", "P8O"))
    o_wh, o_wl, o_oh = 0, ECH * 2048, 2 * ECH * 2048
    wo_h = lambda fc, c0, w: self.ab(o_wh + fc * 2048 + c0, w)
    wo_l = lambda fc, c0, w: self.ab(o_wl + fc * 2048 + c0, w)
    oh_hi = lambda fc: self.ab(o_oh + fc * 128, 128)
    oh_lo = lambda fc: self.ab(o_oh + 2048 + fc * 128, 128)
    wstg = [self.af(0, 2048), self.af(2048, 2048)]
    buf1 = self.af(4096, 2048)
    stage = self.af(6656, 2048)
    ldm = {}

    @self.block.sync
    def _(sync):
        for ec in range(ECH):
            if ec >= 2:
                W8.wait(sync, ec - 1)
            LW[ec % 2].inc(sync.dma_start(out=wstg[ec % 2],
                                          in_=self.i_wu[128 * ec:128 * (ec + 1), :]), 16)
            ldm[("w", ec)] = LW[ec % 2].n
        for tt in range(NT):
            M8.wait(sync, tt)
            for fc in range(ECH):
                L8OH.inc(sync.dma_start(out=oh_hi(fc),
                                        in_=self.d_o1h.ap()[128 * fc:128 * (fc + 1),
                                                            128 * tt:128 * (tt + 1)]), 16)
                L8OH.inc(sync.dma_start(out=oh_lo(fc),
                                        in_=self.d_o1l.ap()[128 * fc:128 * (fc + 1),
                                                            128 * tt:128 * (tt + 1)]), 16)
            ldm[("oh", tt)] = L8OH.n
            E8.wait(sync, tt)
            L8X.inc(sync.dma_start(out=buf1, in_=self.d_x1.ap()[128 * tt:128 * (tt + 1), :]), 16)
            ldm[("x1", tt)] = L8X.n
            E8.wait(sync, tt + 1)
            L8S.inc(sync.dma_start(out=self.d_x1p.ap()[128 * tt:128 * (tt + 1), :], in_=stage), 16)
            ldm[("st", tt)] = L8S.n

    @self.block.tensor
    def _(tensor):
        for tt in range(NT):
            L8OH.wait(tensor, ldm[("oh", tt)])
            if tt == 0:
                W8.wait(tensor, ECH)
            if tt >= 1:
                E8.wait(tensor, tt)
            last = None
            for fc in range(ECH):
                for og in range(4):
                    last = self.mm3(tensor, self.pb[og][:],
                                    oh_hi(fc), oh_lo(fc),
                                    wo_h(fc, 512 * og, 512), wo_l(fc, 512 * og, 512),
                                    start=(fc == 0), stop=(fc == ECH - 1))
            M8.inc(last)

    @self.block.vector
    def _(vector):
        for ec in range(ECH):
            LW[ec % 2].wait(vector, ldm[("w", ec)])
            vector.tensor_copy(self.ab(o_wh + ec * 2048, 2048), wstg[ec % 2])
            W8.inc(vector.tensor_tensor(self.ab(o_wl + ec * 2048, 2048), wstg[ec % 2],
                                        self.ab(o_wh + ec * 2048, 2048), op=ALU.subtract))
        for tt in range(NT):
            M8.wait(vector, tt + 1)
            L8X.wait(vector, ldm[("x1", tt)])
            if tt >= 1:
                L8S.wait(vector, ldm[("st", tt - 1)])
            last = None
            for og in range(4):
                last = vector.tensor_tensor(stage[:, 512 * og:512 * (og + 1)], self.pb[og][:],
                                            buf1[:, 512 * og:512 * (og + 1)], op=ALU.add)
            E8.inc(last)


def ph9(self):
    nc, S, TOK, NT = self.nc, self.S, self.TOK, self.NT
    L9A, L9W, L9X, L9O = (self.sem(x) for x in ("L9A", "L9W", "L9X", "L9O"))
    X9, W9, M9, E9, B9 = (self.sem(x) for x in ("P9X", "P9W", "P9M", "P9E", "P9B"))
    o_xh, o_xl = 0, ECH * TOK
    o_wt = 2 * ECH * TOK
    wt_t = lambda ec: self.ab(o_wt + ec * 512, 512)
    biasr = self.af(6656, 2048, part0=0, nparts=1)
    bias_bc = self.af(4096, 2048)
    x1p = self.af(2048, 512)
    outst = self.af(2560, 512)
    ldm = {}

    @self.block.sync
    def _(sync):
        for ec in range(ECH):
            L9A.inc(sync.dma_start(out=self.ab(o_xh + ec * TOK, TOK),
                                   in_=self.d_x1h.ap()[128 * ec:128 * (ec + 1), :]), 16)
            L9A.inc(sync.dma_start(out=self.ab(o_xl + ec * TOK, TOK),
                                   in_=self.d_x1l.ap()[128 * ec:128 * (ec + 1), :]), 16)
        L9A.inc(sync.dma_start(out=biasr, in_=self.i_bl1), 16)
        ldm["xn"] = L9A.n
        for og in range(4):
            for ec in range(ECH):
                idx = og * ECH + ec
                if og >= 1 and ec == 0:
                    M9.wait(sync, NT * og)
                L9W.inc(sync.dma_start(out=wt_t(ec),
                                       in_=self.d_wt.ap()[128 * ec:128 * (ec + 1),
                                                          512 * og:512 * (og + 1)]), 16)
            ldm[("wt", og)] = L9W.n
            for tt in range(NT):
                idx = og * NT + tt
                E9.wait(sync, idx)
                L9X.inc(sync.dma_start(out=x1p,
                                       in_=self.d_x1p.ap()[128 * tt:128 * (tt + 1),
                                                           512 * og:512 * (og + 1)]), 16)
                ldm[("x1p", idx)] = L9X.n
                E9.wait(sync, idx + 1)
                L9O.inc(sync.dma_start(out=self.o_y[128 * tt:128 * (tt + 1),
                                                    512 * og:512 * (og + 1)], in_=outst), 16)
                ldm[("out", idx)] = L9O.n

    @self.block.tensor
    def _(tensor):
        L9A.wait(tensor, ldm["xn"])
        for og in range(4):
            B9.inc(tensor.matmul(self.pb[2 + og][:], self.c_onef[:],
                                 biasr[0:1, 512 * og:512 * (og + 1)], start=True, stop=True))
        for og in range(4):
            L9W.wait(tensor, ldm[("wt", og)])
            for tt in range(NT):
                idx = og * NT + tt
                if idx >= 2:
                    E9.wait(tensor, idx - 1)
                last = None
                for ec in range(ECH):
                    tensor.matmul(self.pb[idx % 2][:],
                                  self.ab(o_xh + ec * TOK + 128 * tt, 128), wt_t(ec),
                                  start=(ec == 0), stop=False)
                    last = tensor.matmul(self.pb[idx % 2][:],
                                         self.ab(o_xl + ec * TOK + 128 * tt, 128), wt_t(ec),
                                         start=False, stop=(ec == ECH - 1))
                M9.inc(last)

    @self.block.vector
    def _(vector):
        for og in range(4):
            B9.wait(vector, og + 1)
            vector.tensor_copy(bias_bc[:, 512 * og:512 * (og + 1)], self.pb[2 + og][:])
        for og in range(4):
            for tt in range(NT):
                idx = og * NT + tt
                M9.wait(vector, idx + 1)
                L9X.wait(vector, ldm[("x1p", idx)])
                if idx >= 1:
                    L9O.wait(vector, ldm[("out", idx - 1)])
                vector.scalar_tensor_tensor(outst, self.pb[idx % 2][:], self.c_col[:, 6:7],
                                            x1p, op0=ALU.mult, op1=ALU.add)
                E9.inc(vector.tensor_tensor(outst, outst,
                                            bias_bc[:, 512 * og:512 * (og + 1)], op=ALU.add))

    @self.block.gpsimd
    def _(gpsimd):
        pass


MK.ph8 = ph8
MK.ph9 = ph9


def build_mk(S, nph=99, dumps=()):
    mk = MK(S)
    mk.declare()
    mk.alloc()
    dump_outs = []
    for i, name in enumerate(dumps):
        if isinstance(name, str) and "[" in name:
            base, idx = name[:-1].split("[")
            t = getattr(mk, base)[int(idx)]
        elif isinstance(name, str):
            t = getattr(mk, name)
        else:
            t = name
        sh = list(t.shape)
        dt = t.dtype
        dump_outs.append((mk.nc.dram_tensor(f"dbg{i}", sh, dt, kind="ExternalOutput").ap(), t))
    phases = [mk.ph0, mk.ph1, lambda: mk.ph2(0), lambda: mk.ph2(1), mk.ph3,
              mk.ph4, mk.ph5, mk.ph6, mk.ph7, mk.ph8, mk.ph9]
    with mk.nc.Block() as block:
        mk.block = block
        for i, ph in enumerate(phases):
            if i >= nph:
                break
            ph()
            mk.barrier()
        if dump_outs:
            LD = mk.sem("LD")

            @block.sync
            def _(sync):
                for o, t in dump_outs:
                    LD.inc(sync.dma_start(out=o, in_=t[:]), 16)
            mk.barrier()
    for name, sem in mk.sems.items():
        for w in sem.waits:
            assert w <= sem.n, f"sem {name}: wait {w} > total {sem.n}"
    mk.es.close()
    return mk


def kernel(**inputs):
    S = int(np.asarray(inputs["L0_attn_x"]).shape[1])
    mk = build_mk(S)
    in_maps = [host_prep(c, S, inputs) for c in range(NCORES)]
    res = run_bass_kernel_spmd(mk.nc, in_maps, core_ids=list(range(NCORES)))
    out = np.zeros((1, S, E), dtype=np.float32)
    for c in range(NCORES):
        out[0, owned_rows(c, S), :] = np.asarray(res.results[c]["y_c"], dtype=np.float32)
    return out
